# version 15
# baseline (speedup 1.0000x reference)
import os

os.environ.setdefault("TRN_ALLOW_ICI_ALL_TO_ALL", "1")

import math
import numpy as np
import ml_dtypes

import concourse.bass as bass
import concourse.mybir as mybir
from concourse.bass import AutoArena, IndirectOffsetOnAxis
from concourse.bass_utils import run_bass_kernel_spmd

F32 = mybir.dt.float32
BF16 = mybir.dt.bfloat16
AX = mybir.AxisListType.X
OP = mybir.AluOpType
AF = mybir.ActivationFunctionType

C = 1024
HD = 128
SCL = 1.0 / math.sqrt(HD)
EPS = 1e-5
NEG = -1.0e9


def _dirs():
    t = np.array(
        [[(i >> 2 & 1) * 2 - 1, (i >> 1 & 1) * 2 - 1, (i & 1) * 2 - 1] for i in range(8)],
        dtype=np.float64,
    )
    return t / np.sqrt(3.0)


class Ctr:
    def __init__(self, sem, step=1):
        self.sem = sem
        self.step = step
        self.v = 0

    def bump(self, n=None):
        self.v += self.step if n is None else n
        return self.v


def build_program(T):
    TQ = T // 4
    NT = T // 512
    QG = T // 512
    PQ = max(QG // 4, 1)
    NC2 = TQ // 512
    KTT = T // 128

    nc = bass.Bass("TRN2", target_bir_lowering=False)

    xt = nc.dram_tensor("xt", [128, 8 * T], F32, kind="ExternalInput").ap()
    xqt = nc.dram_tensor("xqt", [128, 8 * TQ], F32, kind="ExternalInput").ap()
    xqlp = nc.dram_tensor("xqlp", [128, 8], F32, kind="ExternalInput").ap()
    wqkv = nc.dram_tensor("wqkv", [128, 6 * 8 * 128], F32, kind="ExternalInput").ap()
    wout = nc.dram_tensor("wout", [128, 8 * 8 * 128], F32, kind="ExternalInput").ap()
    wr = nc.dram_tensor("wr", [128, 8 * 8 * 128], F32, kind="ExternalInput").ap()
    wk = nc.dram_tensor("wk", [128, 8 * 8 * 128], F32, kind="ExternalInput").ap()
    wv = nc.dram_tensor("wv", [128, 8 * 8 * 128], F32, kind="ExternalInput").ap()
    mur = nc.dram_tensor("mur", [128, 8], F32, kind="ExternalInput").ap()
    muk = nc.dram_tensor("muk", [128, 8], F32, kind="ExternalInput").ap()
    hs2 = nc.dram_tensor("hs2", [1, 2], F32, kind="ExternalInput").ap()
    d3 = nc.dram_tensor("d3", [3, 6], F32, kind="ExternalInput").ap()
    identp = nc.dram_tensor("identp", [128, 128], BF16, kind="ExternalInput").ap()
    onesb = nc.dram_tensor("onesb", [128, 1], BF16, kind="ExternalInput").ap()
    onesbm = nc.dram_tensor("onesbm", [128, 128], BF16, kind="ExternalInput").ap()
    onescm = nc.dram_tensor("onescm", [128, 128], F32, kind="ExternalInput").ap()
    onesc = nc.dram_tensor("onesc", [128, 1], F32, kind="ExternalInput").ap()
    onesr = nc.dram_tensor("onesr", [1, 128], F32, kind="ExternalInput").ap()
    trim = nc.dram_tensor("trim", [128, 128], F32, kind="ExternalInput").ap()
    epsv = nc.dram_tensor("epsv", [128, 1], F32, kind="ExternalInput").ap()
    gidx = nc.dram_tensor("gidx", [4, 128], mybir.dt.int32, kind="ExternalInput").ap()
    out = nc.dram_tensor("out", [C, TQ], F32, kind="ExternalOutput").ap()

    wrd = nc.dram_tensor("wrd", [128, 8 * 8 * 128], BF16).ap()
    wkd = nc.dram_tensor("wkd", [128, 8 * 8 * 128], BF16).ap()
    wvd = nc.dram_tensor("wvd", [128, 8 * 8 * 128], BF16).ap()
    a2ai = [nc.dram_tensor(f"a2ai{j}", [8, 128, 2, 512 + (1 if j == NC2 - 1 else 0)],
                           BF16).ap() for j in range(NC2)]
    a2ao = [nc.dram_tensor(f"a2ao{j}", [8, 128, 2, 512 + (1 if j == NC2 - 1 else 0)],
                           BF16).ap() for j in range(NC2)]

    import contextlib

    st_ctx = contextlib.ExitStack()
    sbp = lambda name, shape, dt: st_ctx.enter_context(nc.sbuf_tensor(name, shape, dt))
    psx_ = lambda name: st_ctx.enter_context(nc.psum_tensor(name, [128, 512], F32))
    sem = lambda name: st_ctx.enter_context(nc.semaphore(name))

    c_id = sbp("c_id", [128, 128], BF16)
    c_1b = sbp("c_1b", [128, 1], BF16)
    c_1bm = sbp("c_1bm", [128, 128], BF16)
    c_1cm = sbp("c_1cm", [128, 128], F32)
    c_1c = sbp("c_1c", [128, 1], F32)
    c_1r = sbp("c_1r", [1, 128], F32)
    c_tri = sbp("c_tri", [128, 128], F32)
    c_d3 = sbp("c_d3", [3, 6], F32)
    c_hs = sbp("c_hs", [1, 2], F32)
    c_mur = sbp("c_mur", [128, 8], F32)
    c_muk = sbp("c_muk", [128, 8], F32)
    zcol = sbp("zcol", [128, 1], BF16)
    c_eps = sbp("c_eps", [128, 1], F32)
    c_gix = [sbp(f"c_gix{j}", [128, 1], mybir.dt.int32) for j in range(4)]
    lrec = sbp("lrec", [1, 512], F32)
    mx = sbp("mx", [1, 64], F32)
    hsr = sbp("hsr", [3, 2], F32)
    A3 = sbp("A3", [3, 6], BF16)
    cneg = sbp("cneg", [128, 2], F32)
    xql = sbp("xql", [128, 8], F32)
    wo_sb = sbp("wo_sb", [128, 8 * 8 * 128], BF16)
    vcol = sbp("vcol", [128, 8], F32)
    mi2 = sbp("mi2", [128, 2], F32)
    bsq8 = sbp("bsq8", [128, 8], F32)
    bcol = sbp("bcol", [128, 8], F32)

    arA = AutoArena(nc)
    wq_sb = arA.sbuf_tensor("wq_sb", [128, 6 * 8 * 128], BF16)
    xsb = arA.sbuf_tensor("xsb", [128, 2 * 8 * 512], BF16)
    Qt = arA.sbuf_tensor("Qt", [128, 2 * T], BF16)
    Kt = arA.sbuf_tensor("Kt", [128, 2 * T], BF16)
    Vt = arA.sbuf_tensor("Vt", [128, 2 * T], BF16)
    vstg = arA.sbuf_tensor("vstg", [128, 512], BF16)
    pt = arA.sbuf_tensor("pt", [128, 4 * 512], BF16)
    lsum = arA.sbuf_tensor("lsum", [128, 2 * 512], BF16)
    sq = arA.sbuf_tensor("sq", [128, 512], BF16)
    lb = arA.sbuf_tensor("lb", [128, 512], F32)
    aosg = arA.sbuf_tensor("aosg", [128, 2 * 512], BF16)
    arA.allocate()

    arA.free()
    arB = AutoArena(nc)
    aall = arB.sbuf_tensor("aall", [128, 4 * 1026], BF16)
    aall_b = arB.sbuf_tensor("aall_b", [128, 8], BF16)
    w2_sb = arB.sbuf_tensor("w2_sb", [128, 3 * 8 * 8 * 128], BF16)
    xq = arB.sbuf_tensor("xq", [128, 8 * 512], F32)
    x2c = arB.sbuf_tensor("x2c", [128, 8 * 513], F32)
    vtmp = arB.sbuf_tensor("vtmp", [128, 2 * 8 * 512], F32)
    sq2 = arB.sbuf_tensor("sq2", [128, 4 * 512], F32)
    mb_sb2 = arB.sbuf_tensor("mb_sb2", [128, 512], F32)
    ib_sb2 = arB.sbuf_tensor("ib_sb2", [128, 512], F32)
    xr_sb = arB.sbuf_tensor("xr_sb", [128, 8 * 512], BF16)
    xk_sb = arB.sbuf_tensor("xk_sb", [128, 8 * 512], BF16)
    ka_sb = arB.sbuf_tensor("ka_sb", [128, 8 * 512], BF16)
    rg_sb = arB.sbuf_tensor("rg_sb", [128, 8 * 512], F32)
    arB.allocate()
    t1_sb = None

    psA = psx_("psA")
    psB = psx_("psB")
    psO0 = psx_("psO0")
    psO1 = psx_("psO1")
    psL0 = psx_("psL0")
    psL1 = psx_("psL1")
    psBC = psx_("psBC")
    psX = psx_("psX")

    sC = Ctr(sem("sC"), 16)
    sWQ = Ctr(sem("sWQ"), 16)
    sX = [Ctr(sem("sX0"), 16), Ctr(sem("sX1"), 16)]
    sW2 = Ctr(sem("sW2"), 16)
    sAO = [Ctr(sem("sAO0"), 16), Ctr(sem("sAO1"), 16)]
    sZ = Ctr(sem("sZ"), 16)
    sLD = Ctr(sem("sLD"), 16)
    sLD2 = Ctr(sem("sLD2"), 16)
    sWO = Ctr(sem("sWO"), 16)
    sAB = Ctr(sem("sAB"), 16)
    sXQL = Ctr(sem("sXQL"), 16)
    sXQ = Ctr(sem("sXQ"), 16)
    sOUT = Ctr(sem("sOUT"), 16)
    sgp = Ctr(sem("sgp"), 1)
    st = Ctr(sem("st"), 1)
    sv = Ctr(sem("sv"), 1)
    sa = Ctr(sem("sa"), 1)
    scc = Ctr(sem("scc"), 1)

    prog = {k: [] for k in ("t", "v", "a", "g", "s")}
    highwater = {}

    def op(eng, builder, inc=None, waits=()):
        if eng == "ad":
            eng = "a"
        elif eng == "gc":
            eng = "g"
            waits = list(waits) + [(sgp, sgp.v)]
            inc = sgp
        elif eng == "v":
            waits = list(waits) + [(sv, sv.v)]
            inc = sv
        elif eng == "a":
            waits = list(waits) + [(sa, sa.v)]
            inc = sa
        needed = []
        for ctr, val in waits:
            if val <= 0:
                continue
            key = (eng, id(ctr))
            if highwater.get(key, 0) >= val:
                continue
            highwater[key] = val
            needed.append((ctr.sem, val))
        inc_sem = None
        inc_n = 0
        if inc is not None:
            inc.bump()
            inc_sem = inc.sem
            inc_n = inc.step

        def f(e, builder=builder, needed=tuple(needed), inc_sem=inc_sem, inc_n=inc_n):
            for s_, v_ in needed:
                e.wait_ge(s_, v_)
            inst = builder(e)
            if inc_sem is not None:
                inst.then_inc(inc_sem, inc_n)

        prog[eng].append(f)
        return inc.v if inc is not None else None

    for dst, src in ((c_id[:, :], identp), (c_1b[:, :], onesb), (c_1c[:, :], onesc),
                     (c_1bm[:, :], onesbm), (c_1cm[:, :], onescm),
                     (c_1r[:, :], onesr), (c_tri[:, :], trim),
                     (c_d3[:, :], d3), (c_hs[:, :], hs2), (c_mur[:, :], mur),
                     (c_muk[:, :], muk), (c_eps[:, :], epsv),
                     (c_gix[0][:, :], gidx[0:1, :]), (c_gix[1][:, :], gidx[1:2, :]),
                     (c_gix[2][:, :], gidx[2:3, :]), (c_gix[3][:, :], gidx[3:4, :]),
                     (xql[:, :], xqlp)):
        op("s", lambda e, d=dst, s=src: e.dma_start(out=d, in_=s), inc=sC)
    consts_s = sC.v
    op("g", lambda e: e.dma_start(out=wq_sb[:, :], in_=wqkv), inc=sWQ)
    op("g", lambda e: e.dma_start(out=wo_sb[:, :], in_=wout), inc=sWO)
    op("g", lambda e: e.dma_start(out=wrd, in_=wr), inc=sW2)
    op("g", lambda e: e.dma_start(out=wkd, in_=wk), inc=sW2)
    op("g", lambda e: e.dma_start(out=wvd, in_=wv), inc=sW2)
    w2d_g = sW2.v

    zv = op("v", lambda e: e.memset(zcol[:, :], 0.0), inc=sv)
    for sh in (0, 4):
        for u in range(2):
            op("s", lambda e, sh=sh, u=u: e.dma_start(
                out=a2ai[NC2 - 1][sh, :, u, 0:1], in_=zcol[:, :],
                allow_slow_non_contiguous=True),
               inc=sZ, waits=[(sv, zv)])
    zero_ss = sZ.v

    xt3 = xt.rearrange("p (tc a t) -> p tc a t", tc=NT, a=8)
    xsb3 = xsb[:, :].rearrange("p (par a t) -> p par a t", par=2, a=8)
    wq4 = wq_sb[:, :].rearrange("p (s a d) -> p s a d", s=6, a=8)
    Qt2 = Qt[:, :].rearrange("p (u t) -> p u t", u=2)
    Kt2 = Kt[:, :].rearrange("p (u t) -> p u t", u=2)
    Vt3 = Vt[:, :].rearrange("p (u kt d) -> p u kt d", u=2, kt=KTT)

    x_done = []
    x_cons = []
    bank_free = {0: 0, 1: 0}
    psx_free = 0

    for tc in range(NT):
        par = tc % 2
        w_ = [] if tc < 2 else [(st, x_cons[tc - 2])]
        op("g", lambda e, par=par, tc=tc:
           e.dma_start(out=xsb3[:, par, :, :], in_=xt3[:, tc, :, :]),
           inc=sX[par], waits=w_)
        x_done.append((sX[par], sX[par].v))

        for ob in range(6):
            bname = (tc * 6 + ob) % 2
            bank = psA if bname == 0 else psB

            def mms(e, bank=bank, ob=ob, par=par):
                for a in range(8):
                    inst = e.matmul(bank[:, :], wq4[:, ob, a, :], xsb3[:, par, a, :],
                                    start=(a == 0), stop=(a == 7))
                return inst
            mmc = op("t", mms, inc=st,
                     waits=[x_done[tc], (sWQ, 16), (sv, bank_free[bname])])

            u = ob % 2
            kind = ob // 2
            if kind == 0:
                dst = Qt2[:, u, tc * 512:(tc + 1) * 512]
            elif kind == 1:
                dst = Kt2[:, u, tc * 512:(tc + 1) * 512]
            else:
                dst = vstg[:, :]
            bank_free[bname] = op("v", lambda e, d=dst, b=bank: e.tensor_copy(d, b[:, :]),
                                  inc=sv, waits=[(st, mmc)])

            if kind == 2:
                vst = sv.v

                def vtr(e):
                    psXb = psX[:, :].bitcast(BF16)
                    for jj in range(4):
                        inst = e.transpose(psXb[:, jj * 128:(jj + 1) * 128],
                                           vstg[:, jj * 128:(jj + 1) * 128], c_id[:, :])
                    return inst
                trc = op("t", vtr, inc=st,
                         waits=[(sv, vst), (sv, psx_free), (sC, consts_s)])
                psx_free = op("v", lambda e, u=u, tc=tc: e.tensor_copy(
                    Vt3[:, u, tc * 4:(tc + 1) * 4, :].rearrange("p k d -> p (k d)"),
                    psX[:, :].bitcast(BF16)[:, 0:512]), inc=sv, waits=[(st, trc)])
        x_cons.append(st.v)

    ph1_v = sv.v

    psbc_free = 0
    psl0_free = 0
    sq_free = 0
    for u in range(2):
        hc = op("t", lambda e, u=u: e.matmul(psX[0:3, 0:1], c_1r[0:1, 0:3],
                                             c_hs[0:1, u:u + 1], start=True, stop=True),
                inc=st, waits=[(sC, consts_s), (sv, psx_free)])
        hv = op("v", lambda e, u=u: e.tensor_copy(hsr[:, u:u + 1], psX[0:3, 0:1]),
                inc=sv, waits=[(st, hc)])
        psx_free = hv
        a3done = op("v", lambda e, u=u: e.tensor_scalar(
            A3[:, 3 * u:3 * u + 3], c_d3[:, 3 * u:3 * u + 3], hsr[:, u:u + 1], None, OP.mult),
            inc=sv, waits=[(sv, hv), (sC, consts_s)])

        for tc in range(NT):
            amc = op("t", lambda e, u=u, tc=tc: e.matmul(
                psBC[0:3, 0:512], A3[:, 3 * u:3 * u + 3],
                Qt2[0:3, u, tc * 512:(tc + 1) * 512], start=True, stop=True),
                inc=st, waits=[(sv, a3done), (sv, ph1_v), (sv, psbc_free)])
            psbc_free = op("v", lambda e, u=u, tc=tc: e.tensor_tensor(
                Qt2[0:3, u, tc * 512:(tc + 1) * 512],
                Qt2[0:3, u, tc * 512:(tc + 1) * 512],
                psBC[0:3, 0:512], OP.add), inc=sv, waits=[(st, amc)])
        adj_v = sv.v

        for which in ("q", "k"):
            buf = Qt2 if which == "q" else Kt2
            base = 0 if which == "q" else 16
            dep = adj_v if which == "q" else ph1_v
            for tc in range(NT):
                sqc = op("a", lambda e, u=u, tc=tc, buf=buf: e.activation(
                    sq[:, :], buf[:, u, tc * 512:(tc + 1) * 512], AF.Square),
                    inc=sa, waits=[(sv, dep), (st, sq_free)])
                oc = op("t", lambda e: e.matmul(psL0[0:1, :], c_1b[:, 0:1], sq[:, :],
                                                start=True, stop=True),
                        inc=st, waits=[(sa, sqc), (sC, consts_s), (sv, psl0_free)])
                sq_free = oc
                psl0_free = op("v", lambda e, tc=tc, base=base: e.reduce_max(
                    mx[0:1, base + tc:base + tc + 1], psL0[0:1, 0:512], axis=AX),
                    inc=sv, waits=[(st, oc)])

        op("v", lambda e: e.reduce_max(mx[0:1, 32:33], mx[0:1, 0:NT], axis=AX),
           inc=sv, waits=[(sv, psl0_free)])
        f0 = op("v", lambda e: e.reduce_max(mx[0:1, 33:34], mx[0:1, 16:16 + NT], axis=AX),
                inc=sv)
        fc = op("v", lambda e: e.tensor_tensor(mx[0:1, 34:35], mx[0:1, 32:33],
                                               mx[0:1, 33:34], OP.mult),
                inc=sv, waits=[(sv, f0)])

        op("a", lambda e: e.activation(mx[0:1, 35:36], mx[0:1, 34:35], AF.Sqrt),
           waits=[(sv, fc)])
        cc_ = op("a", lambda e: e.mul(mx[0:1, 36:37], mx[0:1, 35:36], -SCL))
        cbc_c = op("t", lambda e: e.matmul(psBC[:, 0:1], c_1r[0:1, :], mx[0:1, 36:37],
                                           start=True, stop=True),
                   inc=st, waits=[(sa, cc_), (sv, psbc_free)])
        psbc_free = op("v", lambda e, u=u: e.tensor_copy(cneg[:, u:u + 1], psBC[:, 0:1]),
                       inc=sv, waits=[(st, cbc_c)])
    bounds_v = sv.v

    xqt3 = xqt.rearrange("p (ck a t) -> p ck a t", ck=NC2, a=8)

    wo4 = wo_sb[:, :].rearrange("p (a o d) -> p a o d", a=8, o=8)
    w24 = w2_sb[:, :].rearrange("p (w a o d) -> p w a o d", w=3, a=8, o=8)
    xq3 = xq[:, :].rearrange("p (a t) -> p a t", a=8)
    x2c3 = x2c[:, :].rearrange("p (a t) -> p a t", a=8)
    vtmp3p = vtmp[:, :].rearrange("p (par a t) -> p par a t", par=2, a=8)
    xr3 = xr_sb[:, :].rearrange("p (a t) -> p a t", a=8)
    xk3 = xk_sb[:, :].rearrange("p (a t) -> p a t", a=8)
    ka3 = ka_sb[:, :].rearrange("p (a t) -> p a t", a=8)
    rg3 = rg_sb[:, :].rearrange("p (a t) -> p a t", a=8)
    out3 = out.rearrange("(a p) t -> p a t", p=128)

    rounds = [NC2 - 1] + list(range(NC2 - 1))
    a2afl = [a2ao[j].rearrange("r p u t -> (r p) (u t)") for j in range(NC2)]
    scc_need = {jr: i + 1 for i, jr in enumerate(rounds)}
    prologue_state = {}

    def emit_prologue():
        nonlocal psx_free
        ab_ss = 0
        for j_ in range(4):
            ab_ss = op("g", lambda e, j_=j_: e.indirect_dma_start(
                aall[:, j_ * 1026:(j_ + 1) * 1026], None, a2afl[NC2 - 1][:, :],
                IndirectOffsetOnAxis(c_gix[j_][:, :], 0)),
                inc=sAB, waits=[(scc, scc_need[NC2 - 1]), (sC, consts_s)])
        abview = bass.AP(aall, 0, [[4104, 128], [1026, 4], [513, 2]])
        abcp = op("v", lambda e: e.tensor_copy(
            aall_b[:, :].rearrange("p (j u) -> p j u", u=2), abview),
            waits=[(sAB, ab_ss)])

        def bpmm(e):
            for o in range(8):
                for a in range(8):
                    inst = e.matmul(psX[:, o:o + 1], wo4[:, a, o, :],
                                    aall_b[:, a:a + 1],
                                    start=(a == 0), stop=(a == 7))
            return inst
        bp_c = op("t", bpmm, inc=st,
                  waits=[(sv, abcp), (sWO, 16), (sv, psx_free)])
        bres_c = op("v", lambda e: e.tensor_tensor(vcol[:, :], xql[:, :], psX[:, 0:8], OP.add),
                    inc=sv, waits=[(st, bp_c), (sC, consts_s)])
        psx_free = bres_c
        bsq_c = op("a", lambda e: e.activation(bsq8[:, :], vcol[:, :], AF.Square),
                   waits=[(sv, bres_c)])

        def bstat(e):
            e.matmul(psX[0:1, 16:24], c_1c[:, 0:1], vcol[:, :], start=True, stop=True)
            return e.matmul(psX[0:1, 32:40], c_1c[:, 0:1], bsq8[:, :], start=True, stop=True)
        bst_c = op("t", bstat, inc=st, waits=[(sa, bsq_c), (sv, bres_c)])

        op("v", lambda e: e.reduce_sum(mx[0:1, 40:41], psX[0:1, 16:24], axis=AX),
           inc=sv, waits=[(st, bst_c)])
        bm0 = op("v", lambda e: e.reduce_sum(mx[0:1, 41:42], psX[0:1, 32:40], axis=AX),
                 inc=sv)

        op("v", lambda e: e.tensor_scalar(mx[0:1, 40:41], mx[0:1, 40:41], 1.0 / C,
                                          None, OP.mult), waits=[(sv, bm0)])
        op("v", lambda e: e.tensor_scalar(mx[0:1, 41:42], mx[0:1, 41:42], 1.0 / C,
                                          None, OP.mult))
        op("v", lambda e: e.tensor_tensor(mx[0:1, 42:43], mx[0:1, 40:41], mx[0:1, 40:41],
                                          OP.mult))
        bmv_c = op("v", lambda e: e.tensor_tensor(mx[0:1, 41:42], mx[0:1, 41:42],
                                                  mx[0:1, 42:43], OP.subtract))
        bsi_c = op("a", lambda e: e.activation(mx[0:1, 41:42], mx[0:1, 41:42], AF.Sqrt, bias=c_eps[0:1, 0:1]),
                   inc=sa, waits=[(sv, bmv_c)])
        bir_c = op("v", lambda e: e.reciprocal(mx[0:1, 41:42], mx[0:1, 41:42]),
                   inc=sv, waits=[(sa, bsi_c)])
        bbc_c = op("t", lambda e: e.matmul(psX[:, 40:42], c_1r[0:1, :], mx[0:1, 40:42],
                                           start=True, stop=True),
                   inc=st, waits=[(sv, bir_c)])
        mi_ev = op("v", lambda e: e.tensor_copy(mi2[:, :], psX[:, 40:42]),
                   inc=sv, waits=[(st, bbc_c)])
        bnd_v = op("v", lambda e: e.tensor_scalar(
            bcol[:, :], vcol[:, :], mi2[:, 0:1], mi2[:, 1:2], OP.subtract, OP.mult),
            waits=[(sv, mi_ev)])

        prologue_state["bnd_v"] = bnd_v
        prologue_state["abcp"] = abcp

    pt2 = pt[:, :].rearrange("p (r t) -> p r t", r=4)
    ls2 = lsum[:, :].rearrange("p (r t) -> p r t", r=2)
    aosg2 = aosg[:, :].rearrange("p (r t) -> p r t", r=2)

    exp_ring = {0: 0, 1: 0, 2: 0}
    pt_ring = {0: 0, 1: 0, 2: 0, 3: 0}
    pt_ring_v = {0: 0, 1: 0, 2: 0, 3: 0}
    lsum_free = {0: 0, 1: 0}
    ol_free = {0: 0, 1: 0}
    ao_dma = {0: 0, 1: 0}
    lb_free = 0

    qi = -1
    a2a_cs = {}
    for jr in rounds:
        for u in range(2):
          for qtr in range(4):
            qg = qtr * PQ + jr
            qi += 1
            ktmax = 4 * qg + 4
            ngrp = ktmax // 4
            obank = psO0 if qi % 2 == 0 else psO1
            lbank = psL0 if qi % 2 == 0 else psL1

            smm_cs = {}
            exp_cs = {}
            grp_add = {}

            def geom(kt):
                diag = kt - 4 * qg
                if diag >= 0:
                    return 128 * diag, 512 - 128 * diag
                return 0, 512

            sbanks = (psA, psB, psBC)
            for step in range(ktmax + 2):
                tri_c = None
                if step < ktmax:
                    kt = step
                    sbank = sbanks[kt % 3]
                    o0, ncols = geom(kt)
                    qc0 = qg * 512 + o0
                    smm_cs[kt] = op("t", lambda e, u=u, kt=kt, sbank=sbank, qc0=qc0,
                                    ncols=ncols, o0=o0:
                                    e.matmul(sbank[:, o0:o0 + ncols],
                                             Kt2[:, u, kt * 128:(kt + 1) * 128],
                                             Qt2[:, u, qc0:qc0 + ncols],
                                             start=True, stop=True),
                                    inc=st,
                                    waits=[(sv, bounds_v), (sa, exp_ring[kt % 3]),
                                           (sv, psbc_free if kt % 3 == 2 else 0)])
                    if kt - 4 * qg >= 0:
                        tri_c = op("v", lambda e, sbank=sbank, o0=o0: e.tensor_tensor(
                            sbank[:, o0:o0 + 128], sbank[:, o0:o0 + 128],
                            c_tri[:, :], OP.add),
                            waits=[(st, smm_cs[kt]), (sC, consts_s)])
                if step >= 2:
                    kt = step - 2
                    o0, ncols = geom(kt)
                    ow = [(sa, exp_cs[kt])]
                    if kt == 0:
                        ow.append((sv, ol_free[qi % 2]))
                    pt_ring[kt % 4] = op("t", lambda e, u=u, kt=kt, o0=o0, ncols=ncols,
                                         obank=obank,
                                         first=(kt == 0), last=(kt == ktmax - 1):
                                         e.matmul(obank[:, o0:o0 + ncols],
                                                  Vt3[:, u, kt, :],
                                                  pt2[:, kt % 4, o0:o0 + ncols],
                                                  start=first, stop=last),
                                         inc=st, waits=ow)
                    if kt % 4 == 1 and kt // 4 < ngrp - 1:
                        g = kt // 4
                        a1 = op("v", lambda e, g=g: e.tensor_tensor(
                            ls2[:, g % 2, :], pt2[:, 0, :], pt2[:, 1, :], OP.add),
                            waits=[(sa, exp_cs[kt]), (st, lsum_free[g % 2])])
                        pt_ring_v[0] = a1
                        pt_ring_v[1] = a1
                    if kt % 4 == 3:
                        g = kt // 4
                        if g < ngrp - 1:
                            op("v", lambda e, g=g: e.tensor_tensor(
                                ls2[:, g % 2, :], ls2[:, g % 2, :], pt2[:, 2, :],
                                OP.add), waits=[(sa, exp_cs[kt])])
                            grp_add[g] = op("v", lambda e, g=g: e.tensor_tensor(
                                ls2[:, g % 2, :], ls2[:, g % 2, :], pt2[:, 3, :],
                                OP.add))
                            pt_ring_v[2] = grp_add[g]
                            pt_ring_v[3] = grp_add[g]
                            ow2 = [(sv, grp_add[g])]
                            if g == 0:
                                ow2.append((sv, ol_free[qi % 2]))
                            lsum_free[g % 2] = op("t", lambda e, g=g, lbank=lbank,
                                                  first=(g == 0):
                                                  e.matmul(lbank[:, :], c_1bm[:, :],
                                                           ls2[:, g % 2, :],
                                                           start=first, stop=False),
                                                  inc=st, waits=ow2)
                        else:
                            for kk in range(4 * g + 1, 4 * g + 4):
                                oo, nn = geom(kk)
                                op("v", lambda e, kk=kk, oo=oo: e.memset(
                                    pt2[:, kk % 4, 0:oo], 0.0),
                                    waits=[(sa, exp_cs[kt]), (st, pt_ring[kk % 4])])
                            op("v", lambda e, g=g: e.tensor_tensor(
                                ls2[:, g % 2, :], pt2[:, 0, :], pt2[:, 1, :], OP.add),
                                waits=[(st, lsum_free[g % 2])])
                            op("v", lambda e, g=g: e.tensor_tensor(
                                ls2[:, g % 2, :], ls2[:, g % 2, :], pt2[:, 2, :],
                                OP.add))
                            grp_add[g] = op("v", lambda e, g=g: e.tensor_tensor(
                                ls2[:, g % 2, :], ls2[:, g % 2, :], pt2[:, 3, :],
                                OP.add))
                            for kk in range(4 * g, 4 * g + 4):
                                pt_ring_v[kk % 4] = grp_add[g]
                            ow2 = [(sv, grp_add[g])]
                            if g == 0:
                                ow2.append((sv, ol_free[qi % 2]))
                            lsum_free[g % 2] = op("t", lambda e, g=g, lbank=lbank,
                                                  first=(g == 0):
                                                  e.matmul(lbank[:, :], c_1bm[:, :],
                                                           ls2[:, g % 2, :],
                                                           start=first, stop=True),
                                                  inc=st, waits=ow2)
                if step < ktmax:
                    kt = step
                    sbank = sbanks[kt % 3]
                    o0, ncols = geom(kt)
                    ew = [(st, smm_cs[kt]), (st, pt_ring[kt % 4]),
                          (sv, pt_ring_v[kt % 4])]
                    if tri_c is not None:
                        ew.append((sv, tri_c))
                    exp_cs[kt] = op("a", lambda e, u=u, sbank=sbank, kt=kt, o0=o0,
                                    ncols=ncols:
                                    e.activation(pt2[:, kt % 4, o0:o0 + ncols],
                                                 sbank[:, o0:o0 + ncols], AF.Exp,
                                                 bias=cneg[:, u:u + 1], scale=SCL),
                                    waits=ew)
                    exp_ring[kt % 3] = exp_cs[kt]

            kloop_t = st.v
            rec_c = op("v", lambda e, lbank=lbank: e.reciprocal(lb[:, :], lbank[:, :]),
                       waits=[(st, kloop_t), (sv, lb_free)])
            aev_c = op("v", lambda e, obank=obank, qi=qi: e.tensor_tensor(
                aosg2[:, qi % 2, :], obank[:, :], lb[:, :], OP.mult),
                waits=[(sv, rec_c), (sAO[qi % 2], ao_dma[qi % 2])])
            lb_free = aev_c
            ol_free[qi % 2] = aev_c

            extra = (jr == NC2 - 1) and (qtr < 3)
            sAO[qi % 2].bump((3 if extra else 1) * 16)
            bcol0 = 1 if jr == NC2 - 1 else 0

            def aodma(e, u=u, qi=qi, qtr=qtr, jr=jr, extra=extra, bcol0=bcol0,
                      sem_=sAO[qi % 2].sem):
                tin = a2ai[jr]
                if extra:
                    e.dma_start(out=tin[qtr + 1, :, u, 0:1],
                                in_=aosg2[:, qi % 2, 511:512],
                                allow_slow_non_contiguous=True).then_inc(sem_, 16)
                    e.dma_start(out=tin[qtr + 5, :, u, 0:1],
                                in_=aosg2[:, qi % 2, 511:512],
                                allow_slow_non_contiguous=True).then_inc(sem_, 16)
                e.dma_start(out=tin[qtr, :, u, bcol0:bcol0 + 512],
                            in_=aosg2[:, qi % 2, :]).then_inc(sem_, 16)
                return e.dma_start(
                    out=tin[qtr + 4, :, u, bcol0:bcol0 + 512], in_=aosg2[:, qi % 2, :])
            ao_dma[qi % 2] = op("s", aodma, inc=sAO[qi % 2], waits=[(sv, aev_c)])
        cw = [(sAO[0], sAO[0].v), (sAO[1], sAO[1].v)]
        if jr == NC2 - 1:
            cw.append((sZ, zero_ss))
        a2a_cs[jr] = op("g", lambda e, jr=jr: e.collective_compute(
            "AllToAll", OP.bypass, replica_groups=[[0, 1, 2, 3, 4, 5, 6, 7]],
            ins=[a2ai[jr].opt()], outs=[a2ao[jr].opt()]),
            inc=scc, waits=cw)
        if jr == rounds[0]:
            emit_prologue()

    attn_ao0 = sAO[0].v
    attn_ao1 = sAO[1].v

    sWR = Ctr(sem("sWR"), 16)
    sWK = Ctr(sem("sWK"), 16)
    sWV = Ctr(sem("sWV"), 16)
    aow = [(sAO[0], attn_ao0), (sAO[1], attn_ao1), (sW2, w2d_g)]
    w23 = w2_sb[:, :].rearrange("p (w x) -> p w x", w=3)
    op("g", lambda e: e.dma_start(out=w23[:, 0, :], in_=wrd), inc=sWR, waits=aow)
    op("g", lambda e: e.dma_start(out=w23[:, 1, :], in_=wkd), inc=sWK)
    op("g", lambda e: e.dma_start(out=w23[:, 2, :], in_=wvd), inc=sWV)


    psl_pair_free = 0
    sq2_t_free = 0

    def stats_and_apply(src3, dst3, dst_off, dep_v, bc_extra=(), pls=None):
        nonlocal psl_pair_free, psbc_free, psx_free, sq2_t_free
        pl0, pl1 = (psL0, psL1) if pls is None else pls
        mb_sb = mb_sb2[:, :]
        ib_sb = ib_sb2[:, :]
        sqcs = {}
        for h in range(2):
            sqcs[h] = op("a", lambda e, h=h, src3=src3: e.activation(
                sq2[:, :], src3[:, 4 * h:4 * h + 4, :].rearrange("p a t -> p (a t)"),
                AF.Square),
                waits=[(sv, dep_v), (st, sq2_t_free)])
            for a in range(4 * h, 4 * h + 4):
                def s2mm(e, a=a, src3=src3, pl0=pl0, pl1=pl1):
                    e.matmul(pl0[:, :], c_1cm[:, :], src3[:, a, :],
                             start=(a == 0), stop=(a == 7))
                    return e.matmul(pl1[:, :], c_1cm[:, :],
                                    sq2[:, (a % 4) * 512:(a % 4) * 512 + 512],
                                    start=(a == 0), stop=(a == 7))
                op("t", s2mm, inc=st,
                   waits=[(sa, sqcs[h]), (sv, dep_v)] + list(bc_extra))
            sq2_t_free = st.v
        stats_t = st.v

        op("v", lambda e, pl0=pl0, mb_sb=mb_sb: e.tensor_scalar(
            mb_sb, pl0[:, :], 1.0 / C, None, OP.mult), waits=[(st, stats_t)])
        op("v", lambda e, pl1=pl1, ib_sb=ib_sb: e.tensor_scalar(
            ib_sb, pl1[:, :], 1.0 / C, None, OP.mult))
        op("v", lambda e, mb_sb=mb_sb: e.tensor_tensor(sq2[:, 0:512], mb_sb, mb_sb,
                                                       OP.mult))
        mv_c = op("v", lambda e, ib_sb=ib_sb: e.tensor_tensor(ib_sb, ib_sb,
                                                              sq2[:, 0:512], OP.subtract))
        psl_pair_free = mv_c
        isq_c = op("a", lambda e, ib_sb=ib_sb: e.activation(ib_sb, ib_sb, AF.Sqrt,
                                                            bias=c_eps[:, 0:1]),
                   waits=[(sv, mv_c)])
        bce_c = op("v", lambda e, ib_sb=ib_sb: e.reciprocal(ib_sb, ib_sb),
                   waits=[(sa, isq_c)])

        mbB = bass.AP(mb_sb2, 0, [[512, 128], [0, 8], [1, 512]])
        ibB = bass.AP(ib_sb2, 0, [[512, 128], [0, 8], [1, 512]])
        op("v", lambda e, src3=src3, mbB=mbB: e.tensor_tensor(
            src3[:, :, :], src3[:, :, :], mbB, OP.subtract),
            waits=[(sv, bce_c)])
        last = op("v", lambda e, src3=src3, dst3=dst3, dst_off=dst_off, ibB=ibB:
                  e.tensor_tensor(dst3[:, :, dst_off:dst_off + 512],
                                  src3[:, :, :], ibB, OP.mult))
        return last

    pb_free = {0: ol_free[0], 1: ol_free[1]}
    xq_free = 0
    out_dma = {0: 0, 1: 0}
    out_dma_all = 0
    ln1_v = 0
    sig_c = {}
    ka_c = {}
    rbank_free = {0: 0, 1: 0}
    kbank_free = {0: 0, 1: 0}
    vb_free = {0: 0, 1: 0}
    aall_free = 0
    zblk_vs = {}
    resid_vs = {}
    ka_prev = 0

    def emit_A(ck):
        nonlocal xq_free, aall_free
        xq_ss = op("s", lambda e, ck=ck: e.dma_start(
            out=xq3[:, :, :], in_=xqt3[:, ck, :, :]),
            inc=sXQ, waits=[(sv, xq_free), (sAO[0], attn_ao0), (sAO[1], attn_ao1)])
        wid = 513 if ck == NC2 - 1 else 512
        for j_ in range(4):
            op("g", lambda e, ck=ck, j_=j_, wid=wid: e.indirect_dma_start(
                aall[:, j_ * 1026: j_ * 1026 + 2 * wid], None, a2afl[ck][:, :],
                IndirectOffsetOnAxis(c_gix[j_][:, :], 0)),
                inc=sLD, waits=[(st, aall_free), (scc, scc_need[ck]), (sC, consts_s),
                                (sv, abcp)])
        aall_ss = sLD.v

        par = ck % 2
        off = 1 if ck == NC2 - 1 else 0
        for o in range(8):
            bank = psO0 if o % 2 == 0 else psO1
            pw = [(sLD, aall_ss), (sWO, 16), (sv, pb_free[o % 2])]

            def pmm(e, o=o, bank=bank, off=off, wid=wid):
                for a in range(8):
                    base = (a // 2) * 1026 + (a % 2) * wid + off
                    inst = e.matmul(bank[:, :], wo4[:, a, o, :],
                                    aall[:, base:base + 512],
                                    start=(a == 0), stop=(a == 7))
                return inst
            pc_ = op("t", pmm, inc=st, waits=pw)
            pb_free[o % 2] = op("v", lambda e, o=o, par=par, bank=bank:
                                e.tensor_tensor(
                vtmp3p[:, par, o, :], xq3[:, o, :], bank[:, :], OP.add),
                waits=[(st, pc_), (sXQ, xq_ss), (sOUT, out_dma[par])])
        resid_vs[ck] = sv.v
        xq_free = sv.v
        aall_free = st.v

    abcp = prologue_state["abcp"]
    emit_A(0)
    bnd_v = op("v", lambda e: e.tensor_copy(x2c3[:, :, 0], bcol[:, :]),
               waits=[(sv, prologue_state["bnd_v"]), (sAO[0], attn_ao0),
                      (sAO[1], attn_ao1)])
    ln1_v = bnd_v
    for ck in range(NC2):
        par = ck % 2
        src3 = vtmp3p[:, par, :, :]

        if ck > 0:
            ln1_v = op("v", lambda e: e.tensor_copy(x2c3[:, :, 0], x2c3[:, :, 512]),
                       waits=[(sv, ln1_v)])

        ln1_v = stats_and_apply(src3, x2c3, 1, max(resid_vs[ck], ln1_v),
                                bc_extra=[(sa, ka_prev)])

        if ck + 1 < NC2:
            emit_A(ck + 1)

        mixed = None
        for h in range(2):
            op("v", lambda e, h=h: e.tensor_tensor(
                sq2[:, :].rearrange("p (a t) -> p a t", a=4),
                x2c3[:, 4 * h:4 * h + 4, 1:513],
                x2c3[:, 4 * h:4 * h + 4, 0:512], OP.subtract),
                waits=[(sv, ln1_v), (sC, consts_s), (st, sq2_t_free)])
            for a in range(4 * h, 4 * h + 4):
                op("v", lambda e, a=a: e.scalar_tensor_tensor(
                    xr3[:, a, :], sq2[:, (a % 4) * 512:(a % 4) * 512 + 512],
                    c_mur[:, a:a + 1], x2c3[:, a, 0:512], OP.mult, OP.add))
                mixed = op("v", lambda e, a=a: e.scalar_tensor_tensor(
                    xk3[:, a, :], sq2[:, (a % 4) * 512:(a % 4) * 512 + 512],
                    c_muk[:, a:a + 1], x2c3[:, a, 0:512], OP.mult, OP.add))

        for o in range(8):
            bank = psA if o % 2 == 0 else psB

            def rmm(e, o=o, bank=bank):
                for a in range(8):
                    inst = e.matmul(bank[:, :], w24[:, 0, a, o, :], xr3[:, a, :],
                                    start=(a == 0), stop=(a == 7))
                return inst
            rc = op("t", rmm, inc=st,
                    waits=[(sv, mixed), (sWR, 16), (sa, rbank_free[o % 2])])
            sig_c[o] = op("a", lambda e, o=o, bank=bank: e.activation(
                rg3[:, o, :], bank[:, :], AF.Sigmoid), waits=[(st, rc)])
            rbank_free[o % 2] = sig_c[o]

        for o in range(8):
            bank = psBC if o % 2 == 0 else psX

            def kmm(e, o=o, bank=bank):
                for a in range(8):
                    inst = e.matmul(bank[:, :], w24[:, 1, a, o, :], xk3[:, a, :],
                                    start=(a == 0), stop=(a == 7))
                return inst
            kc = op("t", kmm, inc=st,
                    waits=[(sv, mixed), (sWK, 16), (sa, kbank_free[o % 2])])

            op("a", lambda e, o=o, bank=bank: e.activation(
                sq2[:, (o % 2) * 512:(o % 2) * 512 + 512], bank[:, :], AF.Relu),
                waits=[(st, kc), (st, sq2_t_free), (sv, mixed)])
            ka_c[o] = op("a", lambda e, o=o: e.activation(
                ka3[:, o, :], sq2[:, (o % 2) * 512:(o % 2) * 512 + 512], AF.Square))
            kbank_free[o % 2] = ka_c[o]
        ka_done = sa.v
        ka_prev = ka_done


        for o in range(8):
            bank = psA if o % 2 == 0 else psB

            def vmm(e, o=o, bank=bank):
                for a in range(8):
                    inst = e.matmul(bank[:, :], w24[:, 2, a, o, :], ka3[:, a, :],
                                    start=(a == 0), stop=(a == 7))
                return inst
            vc = op("t", vmm, inc=st,
                    waits=[(sa, ka_done), (sWV, 16), (sa, rbank_free[o % 2]),
                           (sv, vb_free[o % 2])])

            op("v", lambda e, o=o, par=par, bank=bank: e.tensor_tensor(
                vtmp3p[:, par, o, :], rg3[:, o, :], bank[:, :], OP.mult),
                waits=[(st, vc)])
            vb_free[o % 2] = op("v", lambda e, o=o, par=par: e.tensor_tensor(
                vtmp3p[:, par, o, :], vtmp3p[:, par, o, :], x2c3[:, o, 1:513], OP.add))
        ffn_v = sv.v

        ln2_v = stats_and_apply(src3, src3, 0, ffn_v, bc_extra=[(sa, ka_done)],
                                pls=(psBC, psX))
        out_dma[par] = op("s", lambda e, ck=ck, par=par: e.dma_start(
            out=out3[:, :, ck * 512:(ck + 1) * 512], in_=vtmp3p[:, par, :, :]),
            inc=sOUT, waits=[(sv, ln2_v)])
        out_dma_all = sOUT.v

    final_ss = out_dma_all
    op("s", lambda e: e.nop(), waits=[(sOUT, final_ss)])

    with nc.Block() as block:
        @block.tensor
        def _(e):
            for f in prog["t"]:
                f(e)

        @block.vector
        def _(e):
            for f in prog["v"]:
                f(e)

        @block.scalar
        def _(e):
            for f in prog["a"]:
                f(e)

        @block.gpsimd
        def _(e):
            for f in prog["g"]:
                f(e)

        @block.sync
        def _(e):
            for f in prog["s"]:
                f(e)

    return nc, st_ctx


def stage_inputs(T, inputs):
    x = np.asarray(inputs["L0_attn_x"], dtype=np.float32)
    w_qkv = np.asarray(inputs["L0_attn_w_qkv"], dtype=np.float32)
    w_out = np.asarray(inputs["L0_attn_w_out"], dtype=np.float32)
    head_scales = np.asarray(inputs["L0_attn_head_scales"], dtype=np.float32)
    W_r = np.asarray(inputs["L0_ffn_W_r"], dtype=np.float32)
    W_k = np.asarray(inputs["L0_ffn_W_k"], dtype=np.float32)
    W_v = np.asarray(inputs["L0_ffn_W_v"], dtype=np.float32)
    mu_r = np.asarray(inputs["L0_ffn_mu_r"], dtype=np.float32)
    mu_k = np.asarray(inputs["L0_ffn_mu_k"], dtype=np.float32)

    TQ = T // 4
    dirs = _dirs()

    def wpre(w):
        return np.ascontiguousarray(
            w.reshape(8, 128, 8, 128).transpose(1, 0, 2, 3).reshape(128, 8192))

    ident = np.eye(128, dtype=np.float32).astype(ml_dtypes.bfloat16)
    onesb = np.ones((128, 1), dtype=np.float32).astype(ml_dtypes.bfloat16)
    onesc = np.ones((128, 1), dtype=np.float32)
    onesr = np.ones((1, 128), dtype=np.float32)
    tri = np.where(np.arange(128)[:, None] <= np.arange(128)[None, :], 0.0, NEG).astype(np.float32)
    mur_s = np.ascontiguousarray(mu_r.reshape(8, 128).T)
    muk_s = np.ascontiguousarray(mu_k.reshape(8, 128).T)
    wout_p = wpre(w_out)
    wr_p = wpre(np.ascontiguousarray(W_r.T))
    wk_p = wpre(np.ascontiguousarray(W_k.T))
    wv_p = wpre(np.ascontiguousarray(W_v.T))

    NTl = T // 512
    NC2l = TQ // 512
    in_maps = []
    for c in range(8):
        b, r = divmod(c, 4)
        h0, h1 = 2 * r, 2 * r + 1
        xt_ = np.ascontiguousarray(x[b].T)
        xt_pre = np.ascontiguousarray(
            xt_.reshape(8, 128, NTl, 512).transpose(1, 2, 0, 3).reshape(128, 8 * T))
        xq_slice = xt_[:, r * TQ:(r + 1) * TQ]
        xqt_pre = np.ascontiguousarray(
            xq_slice.reshape(8, 128, NC2l, 512).transpose(1, 2, 0, 3).reshape(128, 8 * TQ))
        if r == 0:
            xql_ = np.zeros((128, 8), dtype=np.float32)
        else:
            xql_ = np.ascontiguousarray(
                xt_[:, r * TQ - 1].reshape(8, 128).T)
        wq = np.empty((128, 6, 8, 128), dtype=np.float32)
        for si, (stype, h) in enumerate([(0, h0), (0, h1), (1, h0), (1, h1), (2, h0), (2, h1)]):
            blk = w_qkv[:, stype * 1024 + h * 128: stype * 1024 + h * 128 + 128]
            wq[:, si, :, :] = blk.reshape(8, 128, 128).transpose(1, 0, 2)
        hs = head_scales[[h0, h1]].reshape(1, 2).astype(np.float32)
        d3c = np.empty((3, 6), dtype=np.float32)
        for u, h in enumerate((h0, h1)):
            d = dirs[h]
            d3c[:, 3 * u:3 * u + 3] = (np.outer(d, d) / SCL).astype(np.float32)
        in_maps.append({
            "xt": xt_pre,
            "xqt": xqt_pre,
            "xqlp": xql_,
            "wqkv": np.ascontiguousarray(wq.reshape(128, 6 * 8 * 128)),
            "wout": wout_p,
            "wr": wr_p,
            "wk": wk_p,
            "wv": wv_p,
            "mur": mur_s,
            "muk": muk_s,
            "hs2": hs,
            "d3": d3c,
            "identp": ident,
            "onesb": onesb,
            "onesbm": np.ones((128, 128), dtype=np.float32).astype(ml_dtypes.bfloat16),
            "onescm": np.ones((128, 128), dtype=np.float32),
            "onesc": onesc,
            "onesr": onesr,
            "trim": tri,
            "epsv": np.full((128, 1), EPS, dtype=np.float32),
            "gidx": ((4 * b + np.arange(4, dtype=np.int32))[:, None] * 128
                     + np.arange(128, dtype=np.int32)[None, :]).astype(np.int32),
        })
    return in_maps


_CACHE = {}


def kernel(**inputs):
    T = np.asarray(inputs["L0_attn_x"]).shape[1]
    if T not in _CACHE:
        _CACHE[T] = build_program(T)
    nc, _ctx = _CACHE[T]
    in_maps = stage_inputs(T, inputs)
    res = run_bass_kernel_spmd(nc, in_maps, core_ids=list(range(8)))
    TQ = T // 4
    full = np.empty((2, T, 1024), dtype=np.float32)
    for c in range(8):
        b, r = divmod(c, 4)
        full[b, r * TQ:(r + 1) * TQ, :] = res.results[c]["out"].T
    return full


# version 17
# speedup vs baseline: 1.0343x; 1.0343x over previous
import os

os.environ.setdefault("TRN_ALLOW_ICI_ALL_TO_ALL", "1")

import math
import numpy as np
import ml_dtypes

import concourse.bass as bass
import concourse.mybir as mybir
from concourse.bass import AutoArena, IndirectOffsetOnAxis
from concourse import dve_ops
from concourse.bass_utils import run_bass_kernel_spmd

F32 = mybir.dt.float32
BF16 = mybir.dt.bfloat16
AX = mybir.AxisListType.X
OP = mybir.AluOpType
AF = mybir.ActivationFunctionType

C = 1024
HD = 128
SCL = 1.0 / math.sqrt(HD)
EPS = 1e-5
NEG = -1.0e9


def _dirs():
    t = np.array(
        [[(i >> 2 & 1) * 2 - 1, (i >> 1 & 1) * 2 - 1, (i & 1) * 2 - 1] for i in range(8)],
        dtype=np.float64,
    )
    return t / np.sqrt(3.0)


class Ctr:
    def __init__(self, sem, step=1):
        self.sem = sem
        self.step = step
        self.v = 0

    def bump(self, n=None):
        self.v += self.step if n is None else n
        return self.v


def build_program(T):
    TQ = T // 4
    NT = T // 512
    QG = T // 512
    PQ = max(QG // 4, 1)
    NC2 = TQ // 512
    KTT = T // 128

    nc = bass.Bass("TRN2", target_bir_lowering=False)

    xt = nc.dram_tensor("xt", [128, 8 * T], F32, kind="ExternalInput").ap()
    xqt = nc.dram_tensor("xqt", [128, 8 * TQ], F32, kind="ExternalInput").ap()
    xqlp = nc.dram_tensor("xqlp", [128, 8], F32, kind="ExternalInput").ap()
    wqkv = nc.dram_tensor("wqkv", [128, 6 * 8 * 128], F32, kind="ExternalInput").ap()
    wout = nc.dram_tensor("wout", [128, 8 * 8 * 128], F32, kind="ExternalInput").ap()
    wr = nc.dram_tensor("wr", [128, 8 * 8 * 128], F32, kind="ExternalInput").ap()
    wk = nc.dram_tensor("wk", [128, 8 * 8 * 128], F32, kind="ExternalInput").ap()
    wv = nc.dram_tensor("wv", [128, 8 * 8 * 128], F32, kind="ExternalInput").ap()
    mur = nc.dram_tensor("mur", [128, 8], F32, kind="ExternalInput").ap()
    muk = nc.dram_tensor("muk", [128, 8], F32, kind="ExternalInput").ap()
    hs2 = nc.dram_tensor("hs2", [1, 2], F32, kind="ExternalInput").ap()
    d3 = nc.dram_tensor("d3", [3, 6], F32, kind="ExternalInput").ap()
    identp = nc.dram_tensor("identp", [128, 128], BF16, kind="ExternalInput").ap()
    onesb = nc.dram_tensor("onesb", [128, 1], BF16, kind="ExternalInput").ap()
    onesbm = nc.dram_tensor("onesbm", [128, 128], BF16, kind="ExternalInput").ap()
    onescm = nc.dram_tensor("onescm", [128, 128], F32, kind="ExternalInput").ap()
    onesc = nc.dram_tensor("onesc", [128, 1], F32, kind="ExternalInput").ap()
    onesr = nc.dram_tensor("onesr", [1, 128], F32, kind="ExternalInput").ap()
    trim = nc.dram_tensor("trim", [128, 128], F32, kind="ExternalInput").ap()
    epsv = nc.dram_tensor("epsv", [128, 1], F32, kind="ExternalInput").ap()
    gidx = nc.dram_tensor("gidx", [4, 128], mybir.dt.int32, kind="ExternalInput").ap()
    out = nc.dram_tensor("out", [C, TQ], F32, kind="ExternalOutput").ap()

    wrd = nc.dram_tensor("wrd", [128, 8 * 8 * 128], BF16).ap()
    wkd = nc.dram_tensor("wkd", [128, 8 * 8 * 128], BF16).ap()
    wvd = nc.dram_tensor("wvd", [128, 8 * 8 * 128], BF16).ap()
    a2ai = [nc.dram_tensor(f"a2ai{j}", [8, 128, 2, 512 + (1 if j == NC2 - 1 else 0)],
                           BF16).ap() for j in range(NC2)]
    a2ao = [nc.dram_tensor(f"a2ao{j}", [8, 128, 2, 512 + (1 if j == NC2 - 1 else 0)],
                           BF16).ap() for j in range(NC2)]

    import contextlib

    st_ctx = contextlib.ExitStack()
    sbp = lambda name, shape, dt: st_ctx.enter_context(nc.sbuf_tensor(name, shape, dt))
    psx_ = lambda name: st_ctx.enter_context(nc.psum_tensor(name, [128, 512], F32))
    sem = lambda name: st_ctx.enter_context(nc.semaphore(name))

    c_id = sbp("c_id", [128, 128], BF16)
    c_1b = sbp("c_1b", [128, 1], BF16)
    c_1bm = sbp("c_1bm", [128, 128], BF16)
    c_1cm = sbp("c_1cm", [128, 128], F32)
    c_1c = sbp("c_1c", [128, 1], F32)
    c_1r = sbp("c_1r", [1, 128], F32)
    c_tri = sbp("c_tri", [128, 128], F32)
    c_d3 = sbp("c_d3", [3, 6], F32)
    c_hs = sbp("c_hs", [1, 2], F32)
    c_mur = sbp("c_mur", [128, 8], F32)
    c_muk = sbp("c_muk", [128, 8], F32)
    zcol = sbp("zcol", [128, 1], BF16)
    c_eps = sbp("c_eps", [128, 1], F32)
    c_gix = [sbp(f"c_gix{j}", [128, 1], mybir.dt.int32) for j in range(4)]
    lrec = sbp("lrec", [1, 512], F32)
    mx = sbp("mx", [1, 64], F32)
    hsr = sbp("hsr", [3, 2], F32)
    A3 = sbp("A3", [3, 6], BF16)
    cneg = sbp("cneg", [128, 2], F32)
    xql = sbp("xql", [128, 8], F32)
    wo_sb = sbp("wo_sb", [128, 8 * 8 * 128], BF16)
    vcol = sbp("vcol", [128, 8], F32)
    mi2 = sbp("mi2", [128, 2], F32)
    bsq8 = sbp("bsq8", [128, 8], F32)
    bcol = sbp("bcol", [128, 8], F32)

    arA = AutoArena(nc)
    wq_sb = arA.sbuf_tensor("wq_sb", [128, 6 * 8 * 128], BF16)
    xsb = arA.sbuf_tensor("xsb", [128, 2 * 8 * 512], BF16)
    Qt = arA.sbuf_tensor("Qt", [128, 2 * T], BF16)
    Kt = arA.sbuf_tensor("Kt", [128, 2 * T], BF16)
    Vt = arA.sbuf_tensor("Vt", [128, 2 * T], BF16)
    vstg = arA.sbuf_tensor("vstg", [128, 512], BF16)
    pt = arA.sbuf_tensor("pt", [128, 4 * 512], BF16)
    lsum = arA.sbuf_tensor("lsum", [128, 2 * 512], BF16)
    sq = arA.sbuf_tensor("sq", [128, 512], BF16)
    lb = arA.sbuf_tensor("lb", [128, 512], F32)
    rscr = arA.sbuf_tensor("rscr", [128, 512], F32)
    aosg = arA.sbuf_tensor("aosg", [128, 2 * 512], BF16)
    arA.allocate()

    arA.free()
    arB = AutoArena(nc)
    aall = arB.sbuf_tensor("aall", [128, 4 * 1026], BF16)
    aall_b = arB.sbuf_tensor("aall_b", [128, 8], BF16)
    w2_sb = arB.sbuf_tensor("w2_sb", [128, 3 * 8 * 8 * 128], BF16)
    xq = arB.sbuf_tensor("xq", [128, 8 * 512], F32)
    x2c = arB.sbuf_tensor("x2c", [128, 8 * 513], F32)
    vtmp = arB.sbuf_tensor("vtmp", [128, 2 * 8 * 512], F32)
    sq2 = arB.sbuf_tensor("sq2", [128, 4 * 512], F32)
    mb_sb2 = arB.sbuf_tensor("mb_sb2", [128, 512], F32)
    ib_sb2 = arB.sbuf_tensor("ib_sb2", [128, 512], F32)
    xr_sb = arB.sbuf_tensor("xr_sb", [128, 8 * 512], BF16)
    xk_sb = arB.sbuf_tensor("xk_sb", [128, 8 * 512], BF16)
    ka_sb = arB.sbuf_tensor("ka_sb", [128, 8 * 512], BF16)
    rg_sb = arB.sbuf_tensor("rg_sb", [128, 8 * 512], F32)
    arB.allocate()
    t1_sb = None

    psA = psx_("psA")
    psB = psx_("psB")
    psO0 = psx_("psO0")
    psO1 = psx_("psO1")
    psL0 = psx_("psL0")
    psL1 = psx_("psL1")
    psBC = psx_("psBC")
    psX = psx_("psX")

    sC = Ctr(sem("sC"), 16)
    sWQ = Ctr(sem("sWQ"), 16)
    sX = [Ctr(sem("sX0"), 16), Ctr(sem("sX1"), 16)]
    sW2 = Ctr(sem("sW2"), 16)
    sAO = [Ctr(sem("sAO0"), 16), Ctr(sem("sAO1"), 16)]
    sZ = Ctr(sem("sZ"), 16)
    sLD = Ctr(sem("sLD"), 16)
    sLD2 = Ctr(sem("sLD2"), 16)
    sWO = Ctr(sem("sWO"), 16)
    sAB = Ctr(sem("sAB"), 16)
    sXQL = Ctr(sem("sXQL"), 16)
    sXQ = Ctr(sem("sXQ"), 16)
    sOUT = Ctr(sem("sOUT"), 16)
    sgp = Ctr(sem("sgp"), 1)
    st = Ctr(sem("st"), 1)
    sv = Ctr(sem("sv"), 1)
    sa = Ctr(sem("sa"), 1)
    scc = Ctr(sem("scc"), 1)

    prog = {k: [] for k in ("t", "v", "a", "g", "s")}
    highwater = {}

    def op(eng, builder, inc=None, waits=()):
        if eng == "ad":
            eng = "a"
        elif eng == "gc":
            eng = "g"
            waits = list(waits) + [(sgp, sgp.v)]
            inc = sgp
        elif eng == "v":
            waits = list(waits) + [(sv, sv.v)]
            inc = sv
        elif eng == "a":
            waits = list(waits) + [(sa, sa.v)]
            inc = sa
        needed = []
        for ctr, val in waits:
            if val <= 0:
                continue
            key = (eng, id(ctr))
            if highwater.get(key, 0) >= val:
                continue
            highwater[key] = val
            needed.append((ctr.sem, val))
        inc_sem = None
        inc_n = 0
        if inc is not None:
            inc.bump()
            inc_sem = inc.sem
            inc_n = inc.step

        def f(e, builder=builder, needed=tuple(needed), inc_sem=inc_sem, inc_n=inc_n):
            for s_, v_ in needed:
                e.wait_ge(s_, v_)
            inst = builder(e)
            if inc_sem is not None:
                inst.then_inc(inc_sem, inc_n)

        prog[eng].append(f)
        return inc.v if inc is not None else None

    for dst, src in ((c_id[:, :], identp), (c_1b[:, :], onesb), (c_1c[:, :], onesc),
                     (c_1bm[:, :], onesbm), (c_1cm[:, :], onescm),
                     (c_1r[:, :], onesr), (c_tri[:, :], trim),
                     (c_d3[:, :], d3), (c_hs[:, :], hs2), (c_mur[:, :], mur),
                     (c_muk[:, :], muk), (c_eps[:, :], epsv),
                     (c_gix[0][:, :], gidx[0:1, :]), (c_gix[1][:, :], gidx[1:2, :]),
                     (c_gix[2][:, :], gidx[2:3, :]), (c_gix[3][:, :], gidx[3:4, :]),
                     (xql[:, :], xqlp)):
        op("s", lambda e, d=dst, s=src: e.dma_start(out=d, in_=s), inc=sC)
    consts_s = sC.v
    op("g", lambda e: e.dma_start(out=wq_sb[:, :], in_=wqkv), inc=sWQ)
    op("g", lambda e: e.dma_start(out=wo_sb[:, :], in_=wout), inc=sWO)
    op("g", lambda e: e.dma_start(out=wrd, in_=wr), inc=sW2)
    op("g", lambda e: e.dma_start(out=wkd, in_=wk), inc=sW2)
    op("g", lambda e: e.dma_start(out=wvd, in_=wv), inc=sW2)
    w2d_g = sW2.v

    zv = op("v", lambda e: e.memset(zcol[:, :], 0.0), inc=sv)
    for sh in (0, 4):
        for u in range(2):
            op("s", lambda e, sh=sh, u=u: e.dma_start(
                out=a2ai[NC2 - 1][sh, :, u, 0:1], in_=zcol[:, :],
                allow_slow_non_contiguous=True),
               inc=sZ, waits=[(sv, zv)])
    zero_ss = sZ.v

    xt3 = xt.rearrange("p (tc a t) -> p tc a t", tc=NT, a=8)
    xsb3 = xsb[:, :].rearrange("p (par a t) -> p par a t", par=2, a=8)
    wq4 = wq_sb[:, :].rearrange("p (s a d) -> p s a d", s=6, a=8)
    Qt2 = Qt[:, :].rearrange("p (u t) -> p u t", u=2)
    Kt2 = Kt[:, :].rearrange("p (u t) -> p u t", u=2)
    Vt3 = Vt[:, :].rearrange("p (u kt d) -> p u kt d", u=2, kt=KTT)

    x_done = []
    x_cons = []
    bank_free = {0: 0, 1: 0}
    psx_free = 0

    for tc in range(NT):
        par = tc % 2
        w_ = [] if tc < 2 else [(st, x_cons[tc - 2])]
        op("g", lambda e, par=par, tc=tc:
           e.dma_start(out=xsb3[:, par, :, :], in_=xt3[:, tc, :, :]),
           inc=sX[par], waits=w_)
        x_done.append((sX[par], sX[par].v))

        for ob in range(6):
            bname = (tc * 6 + ob) % 2
            bank = psA if bname == 0 else psB

            def mms(e, bank=bank, ob=ob, par=par):
                for a in range(8):
                    inst = e.matmul(bank[:, :], wq4[:, ob, a, :], xsb3[:, par, a, :],
                                    start=(a == 0), stop=(a == 7))
                return inst
            mmc = op("t", mms, inc=st,
                     waits=[x_done[tc], (sWQ, 16), (sv, bank_free[bname])])

            u = ob % 2
            kind = ob // 2
            if kind == 0:
                dst = Qt2[:, u, tc * 512:(tc + 1) * 512]
            elif kind == 1:
                dst = Kt2[:, u, tc * 512:(tc + 1) * 512]
            else:
                dst = vstg[:, :]
            bank_free[bname] = op("v", lambda e, d=dst, b=bank: e.tensor_copy(d, b[:, :]),
                                  inc=sv, waits=[(st, mmc)])

            if kind == 2:
                vst = sv.v

                def vtr(e):
                    psXb = psX[:, :].bitcast(BF16)
                    for jj in range(4):
                        inst = e.transpose(psXb[:, jj * 128:(jj + 1) * 128],
                                           vstg[:, jj * 128:(jj + 1) * 128], c_id[:, :])
                    return inst
                trc = op("t", vtr, inc=st,
                         waits=[(sv, vst), (sv, psx_free), (sC, consts_s)])
                psx_free = op("v", lambda e, u=u, tc=tc: e.tensor_copy(
                    Vt3[:, u, tc * 4:(tc + 1) * 4, :].rearrange("p k d -> p (k d)"),
                    psX[:, :].bitcast(BF16)[:, 0:512]), inc=sv, waits=[(st, trc)])
        x_cons.append(st.v)

    ph1_v = sv.v

    psbc_free = 0
    psl0_free = 0
    sq_free = 0
    for u in range(2):
        hc = op("t", lambda e, u=u: e.matmul(psX[0:3, 0:1], c_1r[0:1, 0:3],
                                             c_hs[0:1, u:u + 1], start=True, stop=True),
                inc=st, waits=[(sC, consts_s), (sv, psx_free)])
        hv = op("v", lambda e, u=u: e.tensor_copy(hsr[:, u:u + 1], psX[0:3, 0:1]),
                inc=sv, waits=[(st, hc)])
        psx_free = hv
        a3done = op("v", lambda e, u=u: e.tensor_scalar(
            A3[:, 3 * u:3 * u + 3], c_d3[:, 3 * u:3 * u + 3], hsr[:, u:u + 1], None, OP.mult),
            inc=sv, waits=[(sv, hv), (sC, consts_s)])

        for tc in range(NT):
            amc = op("t", lambda e, u=u, tc=tc: e.matmul(
                psBC[0:3, 0:512], A3[:, 3 * u:3 * u + 3],
                Qt2[0:3, u, tc * 512:(tc + 1) * 512], start=True, stop=True),
                inc=st, waits=[(sv, a3done), (sv, ph1_v), (sv, psbc_free)])
            psbc_free = op("v", lambda e, u=u, tc=tc: e.tensor_tensor(
                Qt2[0:3, u, tc * 512:(tc + 1) * 512],
                Qt2[0:3, u, tc * 512:(tc + 1) * 512],
                psBC[0:3, 0:512], OP.add), inc=sv, waits=[(st, amc)])
        adj_v = sv.v

        for which in ("q", "k"):
            buf = Qt2 if which == "q" else Kt2
            base = 0 if which == "q" else 16
            dep = adj_v if which == "q" else ph1_v
            for tc in range(NT):
                sqc = op("a", lambda e, u=u, tc=tc, buf=buf: e.activation(
                    sq[:, :], buf[:, u, tc * 512:(tc + 1) * 512], AF.Square),
                    inc=sa, waits=[(sv, dep), (st, sq_free)])
                oc = op("t", lambda e: e.matmul(psL0[0:1, :], c_1b[:, 0:1], sq[:, :],
                                                start=True, stop=True),
                        inc=st, waits=[(sa, sqc), (sC, consts_s), (sv, psl0_free)])
                sq_free = oc
                psl0_free = op("v", lambda e, tc=tc, base=base: e.reduce_max(
                    mx[0:1, base + tc:base + tc + 1], psL0[0:1, 0:512], axis=AX),
                    inc=sv, waits=[(st, oc)])

        op("v", lambda e: e.reduce_max(mx[0:1, 32:33], mx[0:1, 0:NT], axis=AX),
           inc=sv, waits=[(sv, psl0_free)])
        f0 = op("v", lambda e: e.reduce_max(mx[0:1, 33:34], mx[0:1, 16:16 + NT], axis=AX),
                inc=sv)
        fc = op("v", lambda e: e.tensor_tensor(mx[0:1, 34:35], mx[0:1, 32:33],
                                               mx[0:1, 33:34], OP.mult),
                inc=sv, waits=[(sv, f0)])

        op("a", lambda e: e.activation(mx[0:1, 35:36], mx[0:1, 34:35], AF.Sqrt),
           waits=[(sv, fc)])
        cc_ = op("a", lambda e: e.mul(mx[0:1, 36:37], mx[0:1, 35:36], -SCL))
        cbc_c = op("t", lambda e: e.matmul(psBC[:, 0:1], c_1r[0:1, :], mx[0:1, 36:37],
                                           start=True, stop=True),
                   inc=st, waits=[(sa, cc_), (sv, psbc_free)])
        psbc_free = op("v", lambda e, u=u: e.tensor_copy(cneg[:, u:u + 1], psBC[:, 0:1]),
                       inc=sv, waits=[(st, cbc_c)])
    bounds_v = sv.v

    xqt3 = xqt.rearrange("p (ck a t) -> p ck a t", ck=NC2, a=8)

    wo4 = wo_sb[:, :].rearrange("p (a o d) -> p a o d", a=8, o=8)
    w24 = w2_sb[:, :].rearrange("p (w a o d) -> p w a o d", w=3, a=8, o=8)
    xq3 = xq[:, :].rearrange("p (a t) -> p a t", a=8)
    x2c3 = x2c[:, :].rearrange("p (a t) -> p a t", a=8)
    vtmp3p = vtmp[:, :].rearrange("p (par a t) -> p par a t", par=2, a=8)
    xr3 = xr_sb[:, :].rearrange("p (a t) -> p a t", a=8)
    xk3 = xk_sb[:, :].rearrange("p (a t) -> p a t", a=8)
    ka3 = ka_sb[:, :].rearrange("p (a t) -> p a t", a=8)
    rg3 = rg_sb[:, :].rearrange("p (a t) -> p a t", a=8)
    out3 = out.rearrange("(a p) t -> p a t", p=128)

    rounds = [NC2 - 1] + list(range(NC2 - 1))
    a2afl = [a2ao[j].rearrange("r p u t -> (r p) (u t)") for j in range(NC2)]
    scc_need = {jr: i + 1 for i, jr in enumerate(rounds)}
    prologue_state = {}

    def emit_prologue():
        nonlocal psx_free
        ab_ss = 0
        for j_ in range(4):
            ab_ss = op("g", lambda e, j_=j_: e.indirect_dma_start(
                aall[:, j_ * 1026:(j_ + 1) * 1026], None, a2afl[NC2 - 1][:, :],
                IndirectOffsetOnAxis(c_gix[j_][:, :], 0)),
                inc=sAB, waits=[(scc, scc_need[NC2 - 1]), (sC, consts_s)])
        abview = bass.AP(aall, 0, [[4104, 128], [1026, 4], [513, 2]])
        abcp = op("v", lambda e: e.tensor_copy(
            aall_b[:, :].rearrange("p (j u) -> p j u", u=2), abview),
            waits=[(sAB, ab_ss)])

        def bpmm(e):
            for o in range(8):
                for a in range(8):
                    inst = e.matmul(psX[:, o:o + 1], wo4[:, a, o, :],
                                    aall_b[:, a:a + 1],
                                    start=(a == 0), stop=(a == 7))
            return inst
        bp_c = op("t", bpmm, inc=st,
                  waits=[(sv, abcp), (sWO, 16), (sv, psx_free)])
        bres_c = op("v", lambda e: e.tensor_tensor(vcol[:, :], xql[:, :], psX[:, 0:8], OP.add),
                    inc=sv, waits=[(st, bp_c), (sC, consts_s)])
        psx_free = bres_c
        bsq_c = op("a", lambda e: e.activation(bsq8[:, :], vcol[:, :], AF.Square),
                   waits=[(sv, bres_c)])

        def bstat(e):
            e.matmul(psX[0:1, 16:24], c_1c[:, 0:1], vcol[:, :], start=True, stop=True)
            return e.matmul(psX[0:1, 32:40], c_1c[:, 0:1], bsq8[:, :], start=True, stop=True)
        bst_c = op("t", bstat, inc=st, waits=[(sa, bsq_c), (sv, bres_c)])

        op("v", lambda e: e.reduce_sum(mx[0:1, 40:41], psX[0:1, 16:24], axis=AX),
           inc=sv, waits=[(st, bst_c)])
        bm0 = op("v", lambda e: e.reduce_sum(mx[0:1, 41:42], psX[0:1, 32:40], axis=AX),
                 inc=sv)

        op("v", lambda e: e.tensor_scalar(mx[0:1, 40:41], mx[0:1, 40:41], 1.0 / C,
                                          None, OP.mult), waits=[(sv, bm0)])
        op("v", lambda e: e.tensor_scalar(mx[0:1, 41:42], mx[0:1, 41:42], 1.0 / C,
                                          None, OP.mult))
        op("v", lambda e: e.tensor_tensor(mx[0:1, 42:43], mx[0:1, 40:41], mx[0:1, 40:41],
                                          OP.mult))
        bmv_c = op("v", lambda e: e.tensor_tensor(mx[0:1, 41:42], mx[0:1, 41:42],
                                                  mx[0:1, 42:43], OP.subtract))
        bsi_c = op("a", lambda e: e.activation(mx[0:1, 41:42], mx[0:1, 41:42], AF.Sqrt, bias=c_eps[0:1, 0:1]),
                   inc=sa, waits=[(sv, bmv_c)])
        bir_c = op("v", lambda e: e.reciprocal(mx[0:1, 41:42], mx[0:1, 41:42]),
                   inc=sv, waits=[(sa, bsi_c)])
        bbc_c = op("t", lambda e: e.matmul(psX[:, 40:42], c_1r[0:1, :], mx[0:1, 40:42],
                                           start=True, stop=True),
                   inc=st, waits=[(sv, bir_c)])
        mi_ev = op("v", lambda e: e.tensor_copy(mi2[:, :], psX[:, 40:42]),
                   inc=sv, waits=[(st, bbc_c)])
        bnd_v = op("v", lambda e: e.tensor_scalar(
            bcol[:, :], vcol[:, :], mi2[:, 0:1], mi2[:, 1:2], OP.subtract, OP.mult),
            waits=[(sv, mi_ev)])

        prologue_state["bnd_v"] = bnd_v
        prologue_state["abcp"] = abcp

    pt2 = pt[:, :].rearrange("p (r t) -> p r t", r=4)
    ls2 = lsum[:, :].rearrange("p (r t) -> p r t", r=2)
    aosg2 = aosg[:, :].rearrange("p (r t) -> p r t", r=2)

    exp_ring = {0: 0, 1: 0, 2: 0}
    pt_ring = {0: 0, 1: 0, 2: 0, 3: 0}
    pt_ring_v = {0: 0, 1: 0, 2: 0, 3: 0}
    lsum_free = {0: 0, 1: 0}
    ol_free = {0: 0, 1: 0}
    ao_dma = {0: 0, 1: 0}
    lb_free = 0

    qi = -1
    a2a_cs = {}
    for jr in rounds:
        for u in range(2):
          for qtr in range(4):
            qg = qtr * PQ + jr
            qi += 1
            ktmax = 4 * qg + 4
            ngrp = ktmax // 4
            obank = psO0 if qi % 2 == 0 else psO1
            lbank = psL0 if qi % 2 == 0 else psL1

            smm_cs = {}
            exp_cs = {}
            grp_add = {}

            def geom(kt):
                diag = kt - 4 * qg
                if diag >= 0:
                    return 128 * diag, 512 - 128 * diag
                return 0, 512

            sbanks = (psA, psB, psBC)
            for step in range(ktmax + 2):
                tri_c = None
                if step < ktmax:
                    kt = step
                    sbank = sbanks[kt % 3]
                    o0, ncols = geom(kt)
                    qc0 = qg * 512 + o0
                    smm_cs[kt] = op("t", lambda e, u=u, kt=kt, sbank=sbank, qc0=qc0,
                                    ncols=ncols, o0=o0:
                                    e.matmul(sbank[:, o0:o0 + ncols],
                                             Kt2[:, u, kt * 128:(kt + 1) * 128],
                                             Qt2[:, u, qc0:qc0 + ncols],
                                             start=True, stop=True),
                                    inc=st,
                                    waits=[(sv, bounds_v), (sa, exp_ring[kt % 3]),
                                           (sv, psbc_free if kt % 3 == 2 else 0)])
                    if kt - 4 * qg >= 0:
                        tri_c = op("v", lambda e, sbank=sbank, o0=o0: e.tensor_tensor(
                            sbank[:, o0:o0 + 128], sbank[:, o0:o0 + 128],
                            c_tri[:, :], OP.add),
                            waits=[(st, smm_cs[kt]), (sC, consts_s)])
                if step >= 2:
                    kt = step - 2
                    o0, ncols = geom(kt)
                    ow = [(sa, exp_cs[kt])]
                    if kt == 0:
                        ow.append((sv, ol_free[qi % 2]))
                    pt_ring[kt % 4] = op("t", lambda e, u=u, kt=kt, o0=o0, ncols=ncols,
                                         obank=obank,
                                         first=(kt == 0), last=(kt == ktmax - 1):
                                         e.matmul(obank[:, o0:o0 + ncols],
                                                  Vt3[:, u, kt, :],
                                                  pt2[:, kt % 4, o0:o0 + ncols],
                                                  start=first, stop=last),
                                         inc=st, waits=ow)
                    if kt % 4 == 1 and kt // 4 < ngrp - 1:
                        g = kt // 4
                        a1 = op("v", lambda e, g=g: e.tensor_tensor(
                            ls2[:, g % 2, :], pt2[:, 0, :], pt2[:, 1, :], OP.add),
                            waits=[(sa, exp_cs[kt]), (st, lsum_free[g % 2])])
                        pt_ring_v[0] = a1
                        pt_ring_v[1] = a1
                    if kt % 4 == 3:
                        g = kt // 4
                        if g < ngrp - 1:
                            op("v", lambda e, g=g: e.tensor_tensor(
                                ls2[:, g % 2, :], ls2[:, g % 2, :], pt2[:, 2, :],
                                OP.add), waits=[(sa, exp_cs[kt])])
                            grp_add[g] = op("v", lambda e, g=g: e.tensor_tensor(
                                ls2[:, g % 2, :], ls2[:, g % 2, :], pt2[:, 3, :],
                                OP.add))
                            pt_ring_v[2] = grp_add[g]
                            pt_ring_v[3] = grp_add[g]
                            ow2 = [(sv, grp_add[g])]
                            if g == 0:
                                ow2.append((sv, ol_free[qi % 2]))
                            lsum_free[g % 2] = op("t", lambda e, g=g, lbank=lbank,
                                                  first=(g == 0):
                                                  e.matmul(lbank[:, :], c_1bm[:, :],
                                                           ls2[:, g % 2, :],
                                                           start=first, stop=False),
                                                  inc=st, waits=ow2)
                        else:
                            for kk in range(4 * g + 1, 4 * g + 4):
                                oo, nn = geom(kk)
                                op("v", lambda e, kk=kk, oo=oo: e.memset(
                                    pt2[:, kk % 4, 0:oo], 0.0),
                                    waits=[(sa, exp_cs[kt]), (st, pt_ring[kk % 4])])
                            op("v", lambda e, g=g: e.tensor_tensor(
                                ls2[:, g % 2, :], pt2[:, 0, :], pt2[:, 1, :], OP.add),
                                waits=[(st, lsum_free[g % 2])])
                            op("v", lambda e, g=g: e.tensor_tensor(
                                ls2[:, g % 2, :], ls2[:, g % 2, :], pt2[:, 2, :],
                                OP.add))
                            grp_add[g] = op("v", lambda e, g=g: e.tensor_tensor(
                                ls2[:, g % 2, :], ls2[:, g % 2, :], pt2[:, 3, :],
                                OP.add))
                            for kk in range(4 * g, 4 * g + 4):
                                pt_ring_v[kk % 4] = grp_add[g]
                            ow2 = [(sv, grp_add[g])]
                            if g == 0:
                                ow2.append((sv, ol_free[qi % 2]))
                            lsum_free[g % 2] = op("t", lambda e, g=g, lbank=lbank,
                                                  first=(g == 0):
                                                  e.matmul(lbank[:, :], c_1bm[:, :],
                                                           ls2[:, g % 2, :],
                                                           start=first, stop=True),
                                                  inc=st, waits=ow2)
                if step < ktmax:
                    kt = step
                    sbank = sbanks[kt % 3]
                    o0, ncols = geom(kt)
                    ew = [(st, smm_cs[kt]), (st, pt_ring[kt % 4]),
                          (sv, pt_ring_v[kt % 4])]
                    if tri_c is not None:
                        ew.append((sv, tri_c))
                    exp_cs[kt] = op("a", lambda e, u=u, sbank=sbank, kt=kt, o0=o0,
                                    ncols=ncols:
                                    e.activation(pt2[:, kt % 4, o0:o0 + ncols],
                                                 sbank[:, o0:o0 + ncols], AF.Exp,
                                                 bias=cneg[:, u:u + 1], scale=SCL),
                                    waits=ew)
                    exp_ring[kt % 3] = exp_cs[kt]

            kloop_t = st.v
            rec_c = op("v", lambda e, lbank=lbank: e.reciprocal(lb[:, :], lbank[:, :]),
                       waits=[(st, kloop_t), (sv, lb_free)])
            aev_c = op("v", lambda e, obank=obank, qi=qi: e.tensor_tensor(
                aosg2[:, qi % 2, :], obank[:, :], lb[:, :], OP.mult),
                waits=[(sv, rec_c), (sAO[qi % 2], ao_dma[qi % 2])])
            lb_free = aev_c
            ol_free[qi % 2] = aev_c

            extra = (jr == NC2 - 1) and (qtr < 3)
            sAO[qi % 2].bump((3 if extra else 1) * 16)
            bcol0 = 1 if jr == NC2 - 1 else 0

            def aodma(e, u=u, qi=qi, qtr=qtr, jr=jr, extra=extra, bcol0=bcol0,
                      sem_=sAO[qi % 2].sem):
                tin = a2ai[jr]
                if extra:
                    e.dma_start(out=tin[qtr + 1, :, u, 0:1],
                                in_=aosg2[:, qi % 2, 511:512],
                                allow_slow_non_contiguous=True).then_inc(sem_, 16)
                    e.dma_start(out=tin[qtr + 5, :, u, 0:1],
                                in_=aosg2[:, qi % 2, 511:512],
                                allow_slow_non_contiguous=True).then_inc(sem_, 16)
                e.dma_start(out=tin[qtr, :, u, bcol0:bcol0 + 512],
                            in_=aosg2[:, qi % 2, :]).then_inc(sem_, 16)
                return e.dma_start(
                    out=tin[qtr + 4, :, u, bcol0:bcol0 + 512], in_=aosg2[:, qi % 2, :])
            ao_dma[qi % 2] = op("s", aodma, inc=sAO[qi % 2], waits=[(sv, aev_c)])
        cw = [(sAO[0], sAO[0].v), (sAO[1], sAO[1].v)]
        if jr == NC2 - 1:
            cw.append((sZ, zero_ss))
        a2a_cs[jr] = op("g", lambda e, jr=jr: e.collective_compute(
            "AllToAll", OP.bypass, replica_groups=[[0, 1, 2, 3, 4, 5, 6, 7]],
            ins=[a2ai[jr].opt()], outs=[a2ao[jr].opt()]),
            inc=scc, waits=cw)
        if jr == rounds[min(1, len(rounds) - 1)]:
            emit_prologue()

    attn_ao0 = sAO[0].v
    attn_ao1 = sAO[1].v

    sWR = Ctr(sem("sWR"), 16)
    sWK = Ctr(sem("sWK"), 16)
    sWV = Ctr(sem("sWV"), 16)
    aow = [(sAO[0], attn_ao0), (sAO[1], attn_ao1), (sW2, w2d_g)]
    w23 = w2_sb[:, :].rearrange("p (w x) -> p w x", w=3)
    op("g", lambda e: e.dma_start(out=w23[:, 0, :], in_=wrd), inc=sWR, waits=aow)
    op("g", lambda e: e.dma_start(out=w23[:, 1, :], in_=wkd), inc=sWK)
    op("g", lambda e: e.dma_start(out=w23[:, 2, :], in_=wvd), inc=sWV)


    psl_pair_free = 0
    sq2_t_free = 0

    def stats_and_apply(src3, dst3, dst_off, dep_v, bc_extra=(), pls=None):
        nonlocal psl_pair_free, psbc_free, psx_free, sq2_t_free
        pl0, pl1 = (psL0, psL1) if pls is None else pls
        mb_sb = mb_sb2[:, :]
        ib_sb = ib_sb2[:, :]
        sqcs = {}
        for h in range(2):
            sqcs[h] = op("a", lambda e, h=h, src3=src3: e.activation(
                sq2[:, :], src3[:, 4 * h:4 * h + 4, :].rearrange("p a t -> p (a t)"),
                AF.Square),
                waits=[(sv, dep_v), (st, sq2_t_free)])
            for a in range(4 * h, 4 * h + 4):
                def s2mm(e, a=a, src3=src3, pl0=pl0, pl1=pl1):
                    e.matmul(pl0[:, :], c_1cm[:, :], src3[:, a, :],
                             start=(a == 0), stop=(a == 7))
                    return e.matmul(pl1[:, :], c_1cm[:, :],
                                    sq2[:, (a % 4) * 512:(a % 4) * 512 + 512],
                                    start=(a == 0), stop=(a == 7))
                op("t", s2mm, inc=st,
                   waits=[(sa, sqcs[h]), (sv, dep_v)] + list(bc_extra))
            sq2_t_free = st.v
        stats_t = st.v

        op("v", lambda e, pl0=pl0, mb_sb=mb_sb: e.tensor_scalar(
            mb_sb, pl0[:, :], 1.0 / C, None, OP.mult), waits=[(st, stats_t)])
        op("v", lambda e, pl1=pl1, ib_sb=ib_sb: e.tensor_scalar(
            ib_sb, pl1[:, :], 1.0 / C, None, OP.mult))
        op("v", lambda e, mb_sb=mb_sb: e.tensor_tensor(sq2[:, 0:512], mb_sb, mb_sb,
                                                       OP.mult))
        mv_c = op("v", lambda e, ib_sb=ib_sb: e.tensor_tensor(ib_sb, ib_sb,
                                                              sq2[:, 0:512], OP.subtract))
        psl_pair_free = mv_c
        isq_c = op("a", lambda e, ib_sb=ib_sb: e.activation(ib_sb, ib_sb, AF.Sqrt,
                                                            bias=c_eps[:, 0:1]),
                   waits=[(sv, mv_c)])
        bce_c = op("v", lambda e, ib_sb=ib_sb: e.reciprocal(ib_sb, ib_sb),
                   waits=[(sa, isq_c)])

        mbB = bass.AP(mb_sb2, 0, [[512, 128], [0, 8], [1, 512]])
        ibB = bass.AP(ib_sb2, 0, [[512, 128], [0, 8], [1, 512]])
        op("v", lambda e, src3=src3, mbB=mbB: e.tensor_tensor(
            src3[:, :, :], src3[:, :, :], mbB, OP.subtract),
            waits=[(sv, bce_c)])
        last = op("v", lambda e, src3=src3, dst3=dst3, dst_off=dst_off, ibB=ibB:
                  e.tensor_tensor(dst3[:, :, dst_off:dst_off + 512],
                                  src3[:, :, :], ibB, OP.mult))
        return last

    pb_free = {0: ol_free[0], 1: ol_free[1]}
    xq_free = 0
    out_dma = {0: 0, 1: 0}
    out_dma_all = 0
    ln1_v = 0
    sig_c = {}
    ka_c = {}
    rbank_free = {0: 0, 1: 0}
    kbank_free = {0: 0, 1: 0}
    vb_free = {0: 0, 1: 0}
    aall_free = 0
    zblk_vs = {}
    resid_vs = {}
    ka_prev = 0

    def emit_A(ck):
        nonlocal xq_free, aall_free
        xq_ss = op("s", lambda e, ck=ck: e.dma_start(
            out=xq3[:, :, :], in_=xqt3[:, ck, :, :]),
            inc=sXQ, waits=[(sv, xq_free), (sAO[0], attn_ao0), (sAO[1], attn_ao1)])
        wid = 513 if ck == NC2 - 1 else 512
        for j_ in range(4):
            op("g", lambda e, ck=ck, j_=j_, wid=wid: e.indirect_dma_start(
                aall[:, j_ * 1026: j_ * 1026 + 2 * wid], None, a2afl[ck][:, :],
                IndirectOffsetOnAxis(c_gix[j_][:, :], 0)),
                inc=sLD, waits=[(st, aall_free), (scc, scc_need[ck]), (sC, consts_s),
                                (sv, abcp)])
        aall_ss = sLD.v

        par = ck % 2
        off = 1 if ck == NC2 - 1 else 0
        for o in range(8):
            bank = psO0 if o % 2 == 0 else psO1
            pw = [(sLD, aall_ss), (sWO, 16), (sv, pb_free[o % 2])]

            def pmm(e, o=o, bank=bank, off=off, wid=wid):
                for a in range(8):
                    base = (a // 2) * 1026 + (a % 2) * wid + off
                    inst = e.matmul(bank[:, :], wo4[:, a, o, :],
                                    aall[:, base:base + 512],
                                    start=(a == 0), stop=(a == 7))
                return inst
            pc_ = op("t", pmm, inc=st, waits=pw)
            pb_free[o % 2] = op("v", lambda e, o=o, par=par, bank=bank:
                                e.tensor_tensor(
                vtmp3p[:, par, o, :], xq3[:, o, :], bank[:, :], OP.add),
                waits=[(st, pc_), (sXQ, xq_ss), (sOUT, out_dma[par])])
        resid_vs[ck] = sv.v
        xq_free = sv.v
        aall_free = st.v

    abcp = prologue_state["abcp"]
    emit_A(0)
    bnd_v = op("v", lambda e: e.tensor_copy(x2c3[:, :, 0], bcol[:, :]),
               waits=[(sv, prologue_state["bnd_v"]), (sAO[0], attn_ao0),
                      (sAO[1], attn_ao1)])
    ln1_v = bnd_v
    for ck in range(NC2):
        par = ck % 2
        src3 = vtmp3p[:, par, :, :]

        if ck > 0:
            ln1_v = op("v", lambda e: e.tensor_copy(x2c3[:, :, 0], x2c3[:, :, 512]),
                       waits=[(sv, ln1_v)])

        ln1_v = stats_and_apply(src3, x2c3, 1, max(resid_vs[ck], ln1_v),
                                bc_extra=[(sv, ka_prev)])

        if ck + 1 < NC2:
            emit_A(ck + 1)

        mixed = None
        mixed_h = {}
        for h in range(2):
            op("v", lambda e, h=h: e.tensor_tensor(
                sq2[:, :].rearrange("p (a t) -> p a t", a=4),
                x2c3[:, 4 * h:4 * h + 4, 1:513],
                x2c3[:, 4 * h:4 * h + 4, 0:512], OP.subtract),
                waits=[(sv, ln1_v), (sC, consts_s), (st, sq2_t_free)])
            for a in range(4 * h, 4 * h + 4):
                op("v", lambda e, a=a: e.scalar_tensor_tensor(
                    xr3[:, a, :], sq2[:, (a % 4) * 512:(a % 4) * 512 + 512],
                    c_mur[:, a:a + 1], x2c3[:, a, 0:512], OP.mult, OP.add))
                mixed = op("v", lambda e, a=a: e.scalar_tensor_tensor(
                    xk3[:, a, :], sq2[:, (a % 4) * 512:(a % 4) * 512 + 512],
                    c_muk[:, a:a + 1], x2c3[:, a, 0:512], OP.mult, OP.add))
            mixed_h[h] = mixed

        for o in range(8):
            bank = psA if o % 2 == 0 else psB

            def rmm0(e, o=o, bank=bank):
                for a in range(4):
                    inst = e.matmul(bank[:, :], w24[:, 0, a, o, :], xr3[:, a, :],
                                    start=(a == 0), stop=False)
                return inst
            op("t", rmm0, inc=st,
               waits=[(sv, mixed_h[0]), (sWR, 16), (sa, rbank_free[o % 2])])

            def rmm1(e, o=o, bank=bank):
                for a in range(4, 8):
                    inst = e.matmul(bank[:, :], w24[:, 0, a, o, :], xr3[:, a, :],
                                    start=False, stop=(a == 7))
                return inst
            rc = op("t", rmm1, inc=st, waits=[(sv, mixed_h[1])])
            sig_c[o] = op("a", lambda e, o=o, bank=bank: e.activation(
                rg3[:, o, :], bank[:, :], AF.Sigmoid), waits=[(st, rc)])
            rbank_free[o % 2] = sig_c[o]

        for o in range(8):
            bank = psBC if o % 2 == 0 else psX

            def kmm(e, o=o, bank=bank):
                for a in range(8):
                    inst = e.matmul(bank[:, :], w24[:, 1, a, o, :], xk3[:, a, :],
                                    start=(a == 0), stop=(a == 7))
                return inst
            kc = op("t", kmm, inc=st,
                    waits=[(sv, mixed), (sWK, 16), (sv, kbank_free[o % 2])])

            op("v", lambda e, o=o, bank=bank: e.tensor_relu(
                sq2[:, (o % 2) * 512:(o % 2) * 512 + 512], bank[:, :]),
                waits=[(st, kc), (st, sq2_t_free), (sv, mixed)])
            ka_c[o] = op("v", lambda e, o=o: e.tensor_tensor(
                ka3[:, o, :], sq2[:, (o % 2) * 512:(o % 2) * 512 + 512],
                sq2[:, (o % 2) * 512:(o % 2) * 512 + 512], OP.mult))
            kbank_free[o % 2] = ka_c[o]
        ka_done = sv.v
        ka_prev = ka_done


        for o in range(8):
            bank = psA if o % 2 == 0 else psB

            def vmm(e, o=o, bank=bank):
                for a in range(8):
                    inst = e.matmul(bank[:, :], w24[:, 2, a, o, :], ka3[:, a, :],
                                    start=(a == 0), stop=(a == 7))
                return inst
            vc = op("t", vmm, inc=st,
                    waits=[(sv, ka_done), (sWV, 16), (sa, rbank_free[o % 2]),
                           (sv, vb_free[o % 2])])

            op("v", lambda e, o=o, par=par, bank=bank: e.tensor_tensor(
                vtmp3p[:, par, o, :], rg3[:, o, :], bank[:, :], OP.mult),
                waits=[(st, vc)])
            vb_free[o % 2] = op("v", lambda e, o=o, par=par: e.tensor_tensor(
                vtmp3p[:, par, o, :], vtmp3p[:, par, o, :], x2c3[:, o, 1:513], OP.add))
        ffn_v = sv.v

        ln2_v = stats_and_apply(src3, src3, 0, ffn_v, bc_extra=[(sv, ka_done)],
                                pls=(psBC, psX))
        out_dma[par] = op("s", lambda e, ck=ck, par=par: e.dma_start(
            out=out3[:, :, ck * 512:(ck + 1) * 512], in_=vtmp3p[:, par, :, :]),
            inc=sOUT, waits=[(sv, ln2_v)])
        out_dma_all = sOUT.v

    final_ss = out_dma_all
    op("s", lambda e: e.nop(), waits=[(sOUT, final_ss)])

    with nc.Block() as block:
        @block.tensor
        def _(e):
            for f in prog["t"]:
                f(e)

        @block.vector
        def _(e):
            for f in prog["v"]:
                f(e)

        @block.scalar
        def _(e):
            for f in prog["a"]:
                f(e)

        @block.gpsimd
        def _(e):
            for f in prog["g"]:
                f(e)

        @block.sync
        def _(e):
            for f in prog["s"]:
                f(e)

    return nc, st_ctx


def stage_inputs(T, inputs):
    x = np.asarray(inputs["L0_attn_x"], dtype=np.float32)
    w_qkv = np.asarray(inputs["L0_attn_w_qkv"], dtype=np.float32)
    w_out = np.asarray(inputs["L0_attn_w_out"], dtype=np.float32)
    head_scales = np.asarray(inputs["L0_attn_head_scales"], dtype=np.float32)
    W_r = np.asarray(inputs["L0_ffn_W_r"], dtype=np.float32)
    W_k = np.asarray(inputs["L0_ffn_W_k"], dtype=np.float32)
    W_v = np.asarray(inputs["L0_ffn_W_v"], dtype=np.float32)
    mu_r = np.asarray(inputs["L0_ffn_mu_r"], dtype=np.float32)
    mu_k = np.asarray(inputs["L0_ffn_mu_k"], dtype=np.float32)

    TQ = T // 4
    dirs = _dirs()

    def wpre(w):
        return np.ascontiguousarray(
            w.reshape(8, 128, 8, 128).transpose(1, 0, 2, 3).reshape(128, 8192))

    ident = np.eye(128, dtype=np.float32).astype(ml_dtypes.bfloat16)
    onesb = np.ones((128, 1), dtype=np.float32).astype(ml_dtypes.bfloat16)
    onesc = np.ones((128, 1), dtype=np.float32)
    onesr = np.ones((1, 128), dtype=np.float32)
    tri = np.where(np.arange(128)[:, None] <= np.arange(128)[None, :], 0.0, NEG).astype(np.float32)
    mur_s = np.ascontiguousarray(mu_r.reshape(8, 128).T)
    muk_s = np.ascontiguousarray(mu_k.reshape(8, 128).T)
    wout_p = wpre(w_out)
    wr_p = wpre(np.ascontiguousarray(W_r.T))
    wk_p = wpre(np.ascontiguousarray(W_k.T))
    wv_p = wpre(np.ascontiguousarray(W_v.T))

    NTl = T // 512
    NC2l = TQ // 512
    in_maps = []
    for c in range(8):
        b, r = divmod(c, 4)
        h0, h1 = 2 * r, 2 * r + 1
        xt_ = np.ascontiguousarray(x[b].T)
        xt_pre = np.ascontiguousarray(
            xt_.reshape(8, 128, NTl, 512).transpose(1, 2, 0, 3).reshape(128, 8 * T))
        xq_slice = xt_[:, r * TQ:(r + 1) * TQ]
        xqt_pre = np.ascontiguousarray(
            xq_slice.reshape(8, 128, NC2l, 512).transpose(1, 2, 0, 3).reshape(128, 8 * TQ))
        if r == 0:
            xql_ = np.zeros((128, 8), dtype=np.float32)
        else:
            xql_ = np.ascontiguousarray(
                xt_[:, r * TQ - 1].reshape(8, 128).T)
        wq = np.empty((128, 6, 8, 128), dtype=np.float32)
        for si, (stype, h) in enumerate([(0, h0), (0, h1), (1, h0), (1, h1), (2, h0), (2, h1)]):
            blk = w_qkv[:, stype * 1024 + h * 128: stype * 1024 + h * 128 + 128]
            wq[:, si, :, :] = blk.reshape(8, 128, 128).transpose(1, 0, 2)
        hs = head_scales[[h0, h1]].reshape(1, 2).astype(np.float32)
        d3c = np.empty((3, 6), dtype=np.float32)
        for u, h in enumerate((h0, h1)):
            d = dirs[h]
            d3c[:, 3 * u:3 * u + 3] = (np.outer(d, d) / SCL).astype(np.float32)
        in_maps.append({
            "xt": xt_pre,
            "xqt": xqt_pre,
            "xqlp": xql_,
            "wqkv": np.ascontiguousarray(wq.reshape(128, 6 * 8 * 128)),
            "wout": wout_p,
            "wr": wr_p,
            "wk": wk_p,
            "wv": wv_p,
            "mur": mur_s,
            "muk": muk_s,
            "hs2": hs,
            "d3": d3c,
            "identp": ident,
            "onesb": onesb,
            "onesbm": np.ones((128, 128), dtype=np.float32).astype(ml_dtypes.bfloat16),
            "onescm": np.ones((128, 128), dtype=np.float32),
            "onesc": onesc,
            "onesr": onesr,
            "trim": tri,
            "epsv": np.full((128, 1), EPS, dtype=np.float32),
            "gidx": ((4 * b + np.arange(4, dtype=np.int32))[:, None] * 128
                     + np.arange(128, dtype=np.int32)[None, :]).astype(np.int32),
        })
    return in_maps


_CACHE = {}


def kernel(**inputs):
    T = np.asarray(inputs["L0_attn_x"]).shape[1]
    if T not in _CACHE:
        _CACHE[T] = build_program(T)
    nc, _ctx = _CACHE[T]
    in_maps = stage_inputs(T, inputs)
    res = run_bass_kernel_spmd(nc, in_maps, core_ids=list(range(8)))
    TQ = T // 4
    full = np.empty((2, T, 1024), dtype=np.float32)
    for c in range(8):
        b, r = divmod(c, 4)
        full[b, r * TQ:(r + 1) * TQ, :] = res.results[c]["out"].T
    return full


# version 18
# speedup vs baseline: 1.0714x; 1.0359x over previous
import os

os.environ.setdefault("TRN_ALLOW_ICI_ALL_TO_ALL", "1")

import math
import numpy as np
import ml_dtypes

import concourse.bass as bass
import concourse.mybir as mybir
from concourse.bass import AutoArena, IndirectOffsetOnAxis
from concourse import dve_ops
from concourse.bass_utils import run_bass_kernel_spmd

F32 = mybir.dt.float32
BF16 = mybir.dt.bfloat16
AX = mybir.AxisListType.X
OP = mybir.AluOpType
AF = mybir.ActivationFunctionType

C = 1024
HD = 128
SCL = 1.0 / math.sqrt(HD)
EPS = 1e-5
NEG = -1.0e9


def _dirs():
    t = np.array(
        [[(i >> 2 & 1) * 2 - 1, (i >> 1 & 1) * 2 - 1, (i & 1) * 2 - 1] for i in range(8)],
        dtype=np.float64,
    )
    return t / np.sqrt(3.0)


class Ctr:
    def __init__(self, sem, step=1):
        self.sem = sem
        self.step = step
        self.v = 0

    def bump(self, n=None):
        self.v += self.step if n is None else n
        return self.v


def build_program(T):
    TQ = T // 4
    NT = T // 512
    QG = T // 512
    PQ = max(QG // 4, 1)
    NC2 = TQ // 512
    KTT = T // 128

    nc = bass.Bass("TRN2", target_bir_lowering=False)

    xt = nc.dram_tensor("xt", [128, 8 * T], F32, kind="ExternalInput").ap()
    xqt = nc.dram_tensor("xqt", [128, 8 * TQ], F32, kind="ExternalInput").ap()
    xqlp = nc.dram_tensor("xqlp", [128, 8], F32, kind="ExternalInput").ap()
    wqkv = nc.dram_tensor("wqkv", [128, 6 * 8 * 128], F32, kind="ExternalInput").ap()
    wout = nc.dram_tensor("wout", [128, 8 * 8 * 128], F32, kind="ExternalInput").ap()
    wr = nc.dram_tensor("wr", [128, 8 * 8 * 128], F32, kind="ExternalInput").ap()
    wk = nc.dram_tensor("wk", [128, 8 * 8 * 128], F32, kind="ExternalInput").ap()
    wv = nc.dram_tensor("wv", [128, 8 * 8 * 128], F32, kind="ExternalInput").ap()
    mur = nc.dram_tensor("mur", [128, 8], F32, kind="ExternalInput").ap()
    muk = nc.dram_tensor("muk", [128, 8], F32, kind="ExternalInput").ap()
    hs2 = nc.dram_tensor("hs2", [1, 2], F32, kind="ExternalInput").ap()
    d3 = nc.dram_tensor("d3", [3, 6], F32, kind="ExternalInput").ap()
    identp = nc.dram_tensor("identp", [128, 128], BF16, kind="ExternalInput").ap()
    onesb = nc.dram_tensor("onesb", [128, 1], BF16, kind="ExternalInput").ap()
    onesbm = nc.dram_tensor("onesbm", [128, 128], BF16, kind="ExternalInput").ap()
    onescm = nc.dram_tensor("onescm", [128, 128], F32, kind="ExternalInput").ap()
    onesc = nc.dram_tensor("onesc", [128, 1], F32, kind="ExternalInput").ap()
    onesr = nc.dram_tensor("onesr", [1, 128], F32, kind="ExternalInput").ap()
    trim = nc.dram_tensor("trim", [128, 128], F32, kind="ExternalInput").ap()
    epsv = nc.dram_tensor("epsv", [128, 1], F32, kind="ExternalInput").ap()
    gidx = nc.dram_tensor("gidx", [4, 128], mybir.dt.int32, kind="ExternalInput").ap()
    out = nc.dram_tensor("out", [C, TQ], F32, kind="ExternalOutput").ap()

    wrd = nc.dram_tensor("wrd", [128, 8 * 8 * 128], BF16).ap()
    wkd = nc.dram_tensor("wkd", [128, 8 * 8 * 128], BF16).ap()
    wvd = nc.dram_tensor("wvd", [128, 8 * 8 * 128], BF16).ap()
    a2ai = [nc.dram_tensor(f"a2ai{j}", [8, 128, 2, 512 + (1 if j == NC2 - 1 else 0)],
                           BF16).ap() for j in range(NC2)]
    a2ao = [nc.dram_tensor(f"a2ao{j}", [8, 128, 2, 512 + (1 if j == NC2 - 1 else 0)],
                           BF16).ap() for j in range(NC2)]

    import contextlib

    st_ctx = contextlib.ExitStack()
    sbp = lambda name, shape, dt: st_ctx.enter_context(nc.sbuf_tensor(name, shape, dt))
    psx_ = lambda name: st_ctx.enter_context(nc.psum_tensor(name, [128, 512], F32))
    sem = lambda name: st_ctx.enter_context(nc.semaphore(name))

    c_id = sbp("c_id", [128, 128], BF16)
    c_1b = sbp("c_1b", [128, 1], BF16)
    c_1bm = sbp("c_1bm", [128, 128], BF16)
    c_1cm = sbp("c_1cm", [128, 128], F32)
    c_1c = sbp("c_1c", [128, 1], F32)
    c_1r = sbp("c_1r", [1, 128], F32)
    c_tri = sbp("c_tri", [128, 128], F32)
    c_d3 = sbp("c_d3", [3, 6], F32)
    c_hs = sbp("c_hs", [1, 2], F32)
    c_mur = sbp("c_mur", [128, 8], F32)
    c_muk = sbp("c_muk", [128, 8], F32)
    zcol = sbp("zcol", [128, 1], BF16)
    c_eps = sbp("c_eps", [128, 1], F32)
    c_gix = [sbp(f"c_gix{j}", [128, 1], mybir.dt.int32) for j in range(4)]
    lrec = sbp("lrec", [1, 512], F32)
    mx = sbp("mx", [1, 64], F32)
    hsr = sbp("hsr", [3, 2], F32)
    A3 = sbp("A3", [3, 6], BF16)
    cneg = sbp("cneg", [128, 2], F32)
    xql = sbp("xql", [128, 8], F32)
    wo_sb = sbp("wo_sb", [128, 8 * 8 * 128], BF16)
    vcol = sbp("vcol", [128, 8], F32)
    mi2 = sbp("mi2", [128, 2], F32)
    bsq8 = sbp("bsq8", [128, 8], F32)
    bcol = sbp("bcol", [128, 8], F32)

    arA = AutoArena(nc)
    wq_sb = arA.sbuf_tensor("wq_sb", [128, 6 * 8 * 128], BF16)
    xsb = arA.sbuf_tensor("xsb", [128, 2 * 8 * 512], BF16)
    Qt = arA.sbuf_tensor("Qt", [128, 2 * T], BF16)
    Kt = arA.sbuf_tensor("Kt", [128, 2 * T], BF16)
    Vt = arA.sbuf_tensor("Vt", [128, 2 * T], BF16)
    vstg = arA.sbuf_tensor("vstg", [128, 512], BF16)
    pt = arA.sbuf_tensor("pt", [128, 4 * 512], BF16)
    lsum = arA.sbuf_tensor("lsum", [128, 2 * 512], BF16)
    sq = arA.sbuf_tensor("sq", [128, 2 * 512], BF16)
    lb = arA.sbuf_tensor("lb", [128, 512], F32)
    rscr = arA.sbuf_tensor("rscr", [128, 512], F32)
    aosg = arA.sbuf_tensor("aosg", [128, 2 * 512], BF16)
    arA.allocate()

    arA.free()
    arB = AutoArena(nc)
    aall = arB.sbuf_tensor("aall", [128, 4 * 1026], BF16)
    aall_b = arB.sbuf_tensor("aall_b", [128, 8], BF16)
    w2_sb = arB.sbuf_tensor("w2_sb", [128, 3 * 8 * 8 * 128], BF16)
    xq = arB.sbuf_tensor("xq", [128, 8 * 512], F32)
    x2c = arB.sbuf_tensor("x2c", [128, 8 * 513], F32)
    vtmp = arB.sbuf_tensor("vtmp", [128, 2 * 8 * 512], F32)
    sq2 = arB.sbuf_tensor("sq2", [128, 4 * 512], F32)
    mb_sb2 = arB.sbuf_tensor("mb_sb2", [128, 512], F32)
    ib_sb2 = arB.sbuf_tensor("ib_sb2", [128, 512], F32)
    xr_sb = arB.sbuf_tensor("xr_sb", [128, 8 * 512], BF16)
    xk_sb = arB.sbuf_tensor("xk_sb", [128, 8 * 512], BF16)
    ka_sb = arB.sbuf_tensor("ka_sb", [128, 8 * 512], BF16)
    rg_sb = arB.sbuf_tensor("rg_sb", [128, 8 * 512], F32)
    arB.allocate()
    t1_sb = None

    psA = psx_("psA")
    psB = psx_("psB")
    psO0 = psx_("psO0")
    psO1 = psx_("psO1")
    psL0 = psx_("psL0")
    psL1 = psx_("psL1")
    psBC = psx_("psBC")
    psX = psx_("psX")

    sC = Ctr(sem("sC"), 16)
    sWQ = Ctr(sem("sWQ"), 16)
    sX = [Ctr(sem("sX0"), 16), Ctr(sem("sX1"), 16)]
    sW2 = Ctr(sem("sW2"), 16)
    sAO = [Ctr(sem("sAO0"), 16), Ctr(sem("sAO1"), 16)]
    sZ = Ctr(sem("sZ"), 16)
    sLD = Ctr(sem("sLD"), 16)
    sLD2 = Ctr(sem("sLD2"), 16)
    sWO = Ctr(sem("sWO"), 16)
    sAB = Ctr(sem("sAB"), 16)
    sXQL = Ctr(sem("sXQL"), 16)
    sXQ = Ctr(sem("sXQ"), 16)
    sOUT = Ctr(sem("sOUT"), 16)
    sgp = Ctr(sem("sgp"), 1)
    st = Ctr(sem("st"), 1)
    sv = Ctr(sem("sv"), 1)
    sa = Ctr(sem("sa"), 1)
    scc = Ctr(sem("scc"), 1)

    prog = {k: [] for k in ("t", "v", "a", "g", "s")}
    highwater = {}

    def op(eng, builder, inc=None, waits=()):
        if eng == "ad":
            eng = "a"
        elif eng == "gc":
            eng = "g"
            waits = list(waits) + [(sgp, sgp.v)]
            inc = sgp
        elif eng == "v":
            waits = list(waits) + [(sv, sv.v)]
            inc = sv
        elif eng == "a":
            waits = list(waits) + [(sa, sa.v)]
            inc = sa
        needed = []
        for ctr, val in waits:
            if val <= 0:
                continue
            key = (eng, id(ctr))
            if highwater.get(key, 0) >= val:
                continue
            highwater[key] = val
            needed.append((ctr.sem, val))
        inc_sem = None
        inc_n = 0
        if inc is not None:
            inc.bump()
            inc_sem = inc.sem
            inc_n = inc.step

        def f(e, builder=builder, needed=tuple(needed), inc_sem=inc_sem, inc_n=inc_n):
            for s_, v_ in needed:
                e.wait_ge(s_, v_)
            inst = builder(e)
            if inc_sem is not None:
                inst.then_inc(inc_sem, inc_n)

        prog[eng].append(f)
        return inc.v if inc is not None else None

    for dst, src in ((c_id[:, :], identp), (c_1b[:, :], onesb), (c_1c[:, :], onesc),
                     (c_1bm[:, :], onesbm), (c_1cm[:, :], onescm),
                     (c_1r[:, :], onesr), (c_tri[:, :], trim),
                     (c_d3[:, :], d3), (c_hs[:, :], hs2), (c_mur[:, :], mur),
                     (c_muk[:, :], muk), (c_eps[:, :], epsv),
                     (c_gix[0][:, :], gidx[0:1, :]), (c_gix[1][:, :], gidx[1:2, :]),
                     (c_gix[2][:, :], gidx[2:3, :]), (c_gix[3][:, :], gidx[3:4, :]),
                     (xql[:, :], xqlp)):
        op("s", lambda e, d=dst, s=src: e.dma_start(out=d, in_=s), inc=sC)
    consts_s = sC.v
    op("g", lambda e: e.dma_start(out=wq_sb[:, :], in_=wqkv), inc=sWQ)
    op("g", lambda e: e.dma_start(out=wo_sb[:, :], in_=wout), inc=sWO)
    op("g", lambda e: e.dma_start(out=wrd, in_=wr), inc=sW2)
    op("g", lambda e: e.dma_start(out=wkd, in_=wk), inc=sW2)
    op("g", lambda e: e.dma_start(out=wvd, in_=wv), inc=sW2)
    w2d_g = sW2.v

    zv = op("v", lambda e: e.memset(zcol[:, :], 0.0), inc=sv)
    for sh in (0, 4):
        for u in range(2):
            op("s", lambda e, sh=sh, u=u: e.dma_start(
                out=a2ai[NC2 - 1][sh, :, u, 0:1], in_=zcol[:, :],
                allow_slow_non_contiguous=True),
               inc=sZ, waits=[(sv, zv)])
    zero_ss = sZ.v

    xt3 = xt.rearrange("p (tc a t) -> p tc a t", tc=NT, a=8)
    xsb3 = xsb[:, :].rearrange("p (par a t) -> p par a t", par=2, a=8)
    wq4 = wq_sb[:, :].rearrange("p (s a d) -> p s a d", s=6, a=8)
    Qt2 = Qt[:, :].rearrange("p (u t) -> p u t", u=2)
    Kt2 = Kt[:, :].rearrange("p (u t) -> p u t", u=2)
    Vt3 = Vt[:, :].rearrange("p (u kt d) -> p u kt d", u=2, kt=KTT)

    x_done = []
    x_cons = []
    bank_free = {0: 0, 1: 0}
    psx_free = 0

    for tc in range(NT):
        par = tc % 2
        w_ = [] if tc < 2 else [(st, x_cons[tc - 2])]
        op("g", lambda e, par=par, tc=tc:
           e.dma_start(out=xsb3[:, par, :, :], in_=xt3[:, tc, :, :]),
           inc=sX[par], waits=w_)
        x_done.append((sX[par], sX[par].v))

        for ob in range(6):
            bname = (tc * 6 + ob) % 2
            bank = psA if bname == 0 else psB

            def mms(e, bank=bank, ob=ob, par=par):
                for a in range(8):
                    inst = e.matmul(bank[:, :], wq4[:, ob, a, :], xsb3[:, par, a, :],
                                    start=(a == 0), stop=(a == 7))
                return inst
            mmc = op("t", mms, inc=st,
                     waits=[x_done[tc], (sWQ, 16), (sv, bank_free[bname])])

            u = ob % 2
            kind = ob // 2
            if kind == 0:
                dst = Qt2[:, u, tc * 512:(tc + 1) * 512]
            elif kind == 1:
                dst = Kt2[:, u, tc * 512:(tc + 1) * 512]
            else:
                dst = vstg[:, :]
            bank_free[bname] = op("v", lambda e, d=dst, b=bank: e.tensor_copy(d, b[:, :]),
                                  inc=sv, waits=[(st, mmc)])

            if kind == 2:
                vst = sv.v

                def vtr(e):
                    psXb = psX[:, :].bitcast(BF16)
                    for jj in range(4):
                        inst = e.transpose(psXb[:, jj * 128:(jj + 1) * 128],
                                           vstg[:, jj * 128:(jj + 1) * 128], c_id[:, :])
                    return inst
                trc = op("t", vtr, inc=st,
                         waits=[(sv, vst), (sv, psx_free), (sC, consts_s)])
                psx_free = op("v", lambda e, u=u, tc=tc: e.tensor_copy(
                    Vt3[:, u, tc * 4:(tc + 1) * 4, :].rearrange("p k d -> p (k d)"),
                    psX[:, :].bitcast(BF16)[:, 0:512]), inc=sv, waits=[(st, trc)])
        x_cons.append(st.v)

    ph1_v = sv.v

    psbc_free = 0
    adj_frees = {0: 0, 1: 0}
    psl_frees = {0: 0, 1: 0}
    sq_frees = {0: 0, 1: 0}
    for u in range(2):
        hc = op("t", lambda e, u=u: e.matmul(psX[0:3, 0:1], c_1r[0:1, 0:3],
                                             c_hs[0:1, u:u + 1], start=True, stop=True),
                inc=st, waits=[(sC, consts_s), (sv, psx_free)])
        hv = op("v", lambda e, u=u: e.tensor_copy(hsr[:, u:u + 1], psX[0:3, 0:1]),
                inc=sv, waits=[(st, hc)])
        psx_free = hv
        a3done = op("v", lambda e, u=u: e.tensor_scalar(
            A3[:, 3 * u:3 * u + 3], c_d3[:, 3 * u:3 * u + 3], hsr[:, u:u + 1], None, OP.mult),
            inc=sv, waits=[(sv, hv), (sC, consts_s)])

        for tc in range(NT):
            abank = psBC if tc % 2 == 0 else psX
            amc = op("t", lambda e, u=u, tc=tc, abank=abank: e.matmul(
                abank[0:3, 0:512], A3[:, 3 * u:3 * u + 3],
                Qt2[0:3, u, tc * 512:(tc + 1) * 512], start=True, stop=True),
                inc=st, waits=[(sv, a3done), (sv, ph1_v),
                               (sv, adj_frees[tc % 2])])
            adj_frees[tc % 2] = op("v", lambda e, u=u, tc=tc, abank=abank:
                                   e.tensor_tensor(
                Qt2[0:3, u, tc * 512:(tc + 1) * 512],
                Qt2[0:3, u, tc * 512:(tc + 1) * 512],
                abank[0:3, 0:512], OP.add), inc=sv, waits=[(st, amc)])
        adj_v = sv.v
        psbc_free = max(psbc_free, adj_frees[0])
        psx_free = max(psx_free, adj_frees[1])

        for which in ("q", "k"):
            buf = Qt2 if which == "q" else Kt2
            base = 0 if which == "q" else 16
            dep = adj_v if which == "q" else ph1_v
            for tc in range(NT):
                sl = tc % 2
                bank = psL0 if sl == 0 else psL1
                sqc = op("a", lambda e, u=u, tc=tc, sl=sl, buf=buf: e.activation(
                    sq[:, sl * 512:sl * 512 + 512],
                    buf[:, u, tc * 512:(tc + 1) * 512], AF.Square),
                    waits=[(sv, dep), (st, sq_frees[sl])])
                oc = op("t", lambda e, sl=sl, bank=bank: e.matmul(
                    bank[0:1, :], c_1b[:, 0:1], sq[:, sl * 512:sl * 512 + 512],
                    start=True, stop=True),
                    inc=st, waits=[(sa, sqc), (sC, consts_s), (sv, psl_frees[sl])])
                sq_frees[sl] = oc
                psl_frees[sl] = op("v", lambda e, tc=tc, sl=sl, base=base, bank=bank:
                                   e.reduce_max(
                    mx[0:1, base + tc:base + tc + 1], bank[0:1, 0:512], axis=AX),
                    inc=sv, waits=[(st, oc)])

        op("v", lambda e: e.reduce_max(mx[0:1, 32:33], mx[0:1, 0:NT], axis=AX),
           inc=sv, waits=[(sv, max(psl_frees[0], psl_frees[1]))])
        f0 = op("v", lambda e: e.reduce_max(mx[0:1, 33:34], mx[0:1, 16:16 + NT], axis=AX),
                inc=sv)
        fc = op("v", lambda e: e.tensor_tensor(mx[0:1, 34:35], mx[0:1, 32:33],
                                               mx[0:1, 33:34], OP.mult),
                inc=sv, waits=[(sv, f0)])

        op("a", lambda e: e.activation(mx[0:1, 35:36], mx[0:1, 34:35], AF.Sqrt),
           waits=[(sv, fc)])
        cc_ = op("a", lambda e: e.mul(mx[0:1, 36:37], mx[0:1, 35:36], -SCL))
        cbc_c = op("t", lambda e: e.matmul(psBC[:, 0:1], c_1r[0:1, :], mx[0:1, 36:37],
                                           start=True, stop=True),
                   inc=st, waits=[(sa, cc_), (sv, psbc_free)])
        psbc_free = op("v", lambda e, u=u: e.tensor_copy(cneg[:, u:u + 1], psBC[:, 0:1]),
                       inc=sv, waits=[(st, cbc_c)])
    bounds_v = sv.v

    xqt3 = xqt.rearrange("p (ck a t) -> p ck a t", ck=NC2, a=8)

    wo4 = wo_sb[:, :].rearrange("p (a o d) -> p a o d", a=8, o=8)
    w24 = w2_sb[:, :].rearrange("p (w a o d) -> p w a o d", w=3, a=8, o=8)
    xq3 = xq[:, :].rearrange("p (a t) -> p a t", a=8)
    x2c3 = x2c[:, :].rearrange("p (a t) -> p a t", a=8)
    vtmp3p = vtmp[:, :].rearrange("p (par a t) -> p par a t", par=2, a=8)
    xr3 = xr_sb[:, :].rearrange("p (a t) -> p a t", a=8)
    xk3 = xk_sb[:, :].rearrange("p (a t) -> p a t", a=8)
    ka3 = ka_sb[:, :].rearrange("p (a t) -> p a t", a=8)
    rg3 = rg_sb[:, :].rearrange("p (a t) -> p a t", a=8)
    out3 = out.rearrange("(a p) t -> p a t", p=128)

    rounds = [NC2 - 1] + list(range(NC2 - 1))
    a2afl = [a2ao[j].rearrange("r p u t -> (r p) (u t)") for j in range(NC2)]
    scc_need = {jr: i + 1 for i, jr in enumerate(rounds)}
    prologue_state = {}

    def emit_prologue():
        nonlocal psx_free
        ab_ss = 0
        for j_ in range(4):
            ab_ss = op("g", lambda e, j_=j_: e.indirect_dma_start(
                aall[:, j_ * 1026:(j_ + 1) * 1026], None, a2afl[NC2 - 1][:, :],
                IndirectOffsetOnAxis(c_gix[j_][:, :], 0)),
                inc=sAB, waits=[(scc, scc_need[NC2 - 1]), (sC, consts_s)])
        abview = bass.AP(aall, 0, [[4104, 128], [1026, 4], [513, 2]])
        abcp = op("v", lambda e: e.tensor_copy(
            aall_b[:, :].rearrange("p (j u) -> p j u", u=2), abview),
            waits=[(sAB, ab_ss)])

        def bpmm(e):
            for o in range(8):
                for a in range(8):
                    inst = e.matmul(psX[:, o:o + 1], wo4[:, a, o, :],
                                    aall_b[:, a:a + 1],
                                    start=(a == 0), stop=(a == 7))
            return inst
        bp_c = op("t", bpmm, inc=st,
                  waits=[(sv, abcp), (sWO, 16), (sv, psx_free)])
        bres_c = op("v", lambda e: e.tensor_tensor(vcol[:, :], xql[:, :], psX[:, 0:8], OP.add),
                    inc=sv, waits=[(st, bp_c), (sC, consts_s)])
        psx_free = bres_c
        bsq_c = op("a", lambda e: e.activation(bsq8[:, :], vcol[:, :], AF.Square),
                   waits=[(sv, bres_c)])

        def bstat(e):
            e.matmul(psX[0:1, 16:24], c_1c[:, 0:1], vcol[:, :], start=True, stop=True)
            return e.matmul(psX[0:1, 32:40], c_1c[:, 0:1], bsq8[:, :], start=True, stop=True)
        bst_c = op("t", bstat, inc=st, waits=[(sa, bsq_c), (sv, bres_c)])

        op("v", lambda e: e.reduce_sum(mx[0:1, 40:41], psX[0:1, 16:24], axis=AX),
           inc=sv, waits=[(st, bst_c)])
        bm0 = op("v", lambda e: e.reduce_sum(mx[0:1, 41:42], psX[0:1, 32:40], axis=AX),
                 inc=sv)

        op("v", lambda e: e.tensor_scalar(mx[0:1, 40:41], mx[0:1, 40:41], 1.0 / C,
                                          None, OP.mult), waits=[(sv, bm0)])
        op("v", lambda e: e.tensor_scalar(mx[0:1, 41:42], mx[0:1, 41:42], 1.0 / C,
                                          None, OP.mult))
        op("v", lambda e: e.tensor_tensor(mx[0:1, 42:43], mx[0:1, 40:41], mx[0:1, 40:41],
                                          OP.mult))
        bmv_c = op("v", lambda e: e.tensor_tensor(mx[0:1, 41:42], mx[0:1, 41:42],
                                                  mx[0:1, 42:43], OP.subtract))
        bsi_c = op("a", lambda e: e.activation(mx[0:1, 41:42], mx[0:1, 41:42], AF.Sqrt, bias=c_eps[0:1, 0:1]),
                   inc=sa, waits=[(sv, bmv_c)])
        bir_c = op("v", lambda e: e.reciprocal(mx[0:1, 41:42], mx[0:1, 41:42]),
                   inc=sv, waits=[(sa, bsi_c)])
        bbc_c = op("t", lambda e: e.matmul(psX[:, 40:42], c_1r[0:1, :], mx[0:1, 40:42],
                                           start=True, stop=True),
                   inc=st, waits=[(sv, bir_c)])
        mi_ev = op("v", lambda e: e.tensor_copy(mi2[:, :], psX[:, 40:42]),
                   inc=sv, waits=[(st, bbc_c)])
        bnd_v = op("v", lambda e: e.tensor_scalar(
            bcol[:, :], vcol[:, :], mi2[:, 0:1], mi2[:, 1:2], OP.subtract, OP.mult),
            waits=[(sv, mi_ev)])

        prologue_state["bnd_v"] = bnd_v
        prologue_state["abcp"] = abcp

    pt2 = pt[:, :].rearrange("p (r t) -> p r t", r=4)
    ls2 = lsum[:, :].rearrange("p (r t) -> p r t", r=2)
    aosg2 = aosg[:, :].rearrange("p (r t) -> p r t", r=2)

    exp_ring = {0: 0, 1: 0, 2: 0}
    pt_ring = {0: 0, 1: 0, 2: 0, 3: 0}
    pt_ring_v = {0: 0, 1: 0, 2: 0, 3: 0}
    lsum_free = {0: 0, 1: 0}
    ol_free = {0: 0, 1: 0}
    ao_dma = {0: 0, 1: 0}
    lb_free = 0

    qi = -1
    a2a_cs = {}
    for jr in rounds:
        for u in range(2):
          for qtr in range(4):
            qg = qtr * PQ + jr
            qi += 1
            ktmax = 4 * qg + 4
            ngrp = ktmax // 4
            obank = psO0 if qi % 2 == 0 else psO1
            lbank = psL0 if qi % 2 == 0 else psL1

            smm_cs = {}
            exp_cs = {}
            grp_add = {}

            def geom(kt):
                diag = kt - 4 * qg
                if diag >= 0:
                    return 128 * diag, 512 - 128 * diag
                return 0, 512

            sbanks = (psA, psB, psBC)
            for step in range(ktmax + 2):
                tri_c = None
                if step < ktmax:
                    kt = step
                    sbank = sbanks[kt % 3]
                    o0, ncols = geom(kt)
                    qc0 = qg * 512 + o0
                    smm_cs[kt] = op("t", lambda e, u=u, kt=kt, sbank=sbank, qc0=qc0,
                                    ncols=ncols, o0=o0:
                                    e.matmul(sbank[:, o0:o0 + ncols],
                                             Kt2[:, u, kt * 128:(kt + 1) * 128],
                                             Qt2[:, u, qc0:qc0 + ncols],
                                             start=True, stop=True),
                                    inc=st,
                                    waits=[(sv, bounds_v), (sa, exp_ring[kt % 3]),
                                           (sv, psbc_free if kt % 3 == 2 else 0)])
                    if kt - 4 * qg >= 0:
                        tri_c = op("v", lambda e, sbank=sbank, o0=o0: e.tensor_tensor(
                            sbank[:, o0:o0 + 128], sbank[:, o0:o0 + 128],
                            c_tri[:, :], OP.add),
                            waits=[(st, smm_cs[kt]), (sC, consts_s)])
                if step >= 2:
                    kt = step - 2
                    o0, ncols = geom(kt)
                    ow = [(sa, exp_cs[kt])]
                    if kt == 0:
                        ow.append((sv, ol_free[qi % 2]))
                    pt_ring[kt % 4] = op("t", lambda e, u=u, kt=kt, o0=o0, ncols=ncols,
                                         obank=obank,
                                         first=(kt == 0), last=(kt == ktmax - 1):
                                         e.matmul(obank[:, o0:o0 + ncols],
                                                  Vt3[:, u, kt, :],
                                                  pt2[:, kt % 4, o0:o0 + ncols],
                                                  start=first, stop=last),
                                         inc=st, waits=ow)
                    if kt % 4 == 1 and kt // 4 < ngrp - 1:
                        g = kt // 4
                        a1 = op("v", lambda e, g=g: e.tensor_tensor(
                            ls2[:, g % 2, :], pt2[:, 0, :], pt2[:, 1, :], OP.add),
                            waits=[(sa, exp_cs[kt]), (st, lsum_free[g % 2])])
                        pt_ring_v[0] = a1
                        pt_ring_v[1] = a1
                    if kt % 4 == 3:
                        g = kt // 4
                        if g < ngrp - 1:
                            op("v", lambda e, g=g: e.tensor_tensor(
                                ls2[:, g % 2, :], ls2[:, g % 2, :], pt2[:, 2, :],
                                OP.add), waits=[(sa, exp_cs[kt])])
                            grp_add[g] = op("v", lambda e, g=g: e.tensor_tensor(
                                ls2[:, g % 2, :], ls2[:, g % 2, :], pt2[:, 3, :],
                                OP.add))
                            pt_ring_v[2] = grp_add[g]
                            pt_ring_v[3] = grp_add[g]
                            ow2 = [(sv, grp_add[g])]
                            if g == 0:
                                ow2.append((sv, ol_free[qi % 2]))
                            lsum_free[g % 2] = op("t", lambda e, g=g, lbank=lbank,
                                                  first=(g == 0):
                                                  e.matmul(lbank[:, :], c_1bm[:, :],
                                                           ls2[:, g % 2, :],
                                                           start=first, stop=False),
                                                  inc=st, waits=ow2)
                        else:
                            for kk in range(4 * g + 1, 4 * g + 4):
                                oo, nn = geom(kk)
                                op("v", lambda e, kk=kk, oo=oo: e.memset(
                                    pt2[:, kk % 4, 0:oo], 0.0),
                                    waits=[(sa, exp_cs[kt]), (st, pt_ring[kk % 4])])
                            op("v", lambda e, g=g: e.tensor_tensor(
                                ls2[:, g % 2, :], pt2[:, 0, :], pt2[:, 1, :], OP.add),
                                waits=[(st, lsum_free[g % 2])])
                            op("v", lambda e, g=g: e.tensor_tensor(
                                ls2[:, g % 2, :], ls2[:, g % 2, :], pt2[:, 2, :],
                                OP.add))
                            grp_add[g] = op("v", lambda e, g=g: e.tensor_tensor(
                                ls2[:, g % 2, :], ls2[:, g % 2, :], pt2[:, 3, :],
                                OP.add))
                            for kk in range(4 * g, 4 * g + 4):
                                pt_ring_v[kk % 4] = grp_add[g]
                            ow2 = [(sv, grp_add[g])]
                            if g == 0:
                                ow2.append((sv, ol_free[qi % 2]))
                            lsum_free[g % 2] = op("t", lambda e, g=g, lbank=lbank,
                                                  first=(g == 0):
                                                  e.matmul(lbank[:, :], c_1bm[:, :],
                                                           ls2[:, g % 2, :],
                                                           start=first, stop=True),
                                                  inc=st, waits=ow2)
                if step < ktmax:
                    kt = step
                    sbank = sbanks[kt % 3]
                    o0, ncols = geom(kt)
                    ew = [(st, smm_cs[kt]), (st, pt_ring[kt % 4]),
                          (sv, pt_ring_v[kt % 4])]
                    if tri_c is not None:
                        ew.append((sv, tri_c))
                    exp_cs[kt] = op("a", lambda e, u=u, sbank=sbank, kt=kt, o0=o0,
                                    ncols=ncols:
                                    e.activation(pt2[:, kt % 4, o0:o0 + ncols],
                                                 sbank[:, o0:o0 + ncols], AF.Exp,
                                                 bias=cneg[:, u:u + 1], scale=SCL),
                                    waits=ew)
                    exp_ring[kt % 3] = exp_cs[kt]

            kloop_t = st.v
            rec_c = op("v", lambda e, lbank=lbank: e.reciprocal(lb[:, :], lbank[:, :]),
                       waits=[(st, kloop_t), (sv, lb_free)])
            aev_c = op("v", lambda e, obank=obank, qi=qi: e.tensor_tensor(
                aosg2[:, qi % 2, :], obank[:, :], lb[:, :], OP.mult),
                waits=[(sv, rec_c), (sAO[qi % 2], ao_dma[qi % 2])])
            lb_free = aev_c
            ol_free[qi % 2] = aev_c

            extra = (jr == NC2 - 1) and (qtr < 3)
            sAO[qi % 2].bump((3 if extra else 1) * 16)
            bcol0 = 1 if jr == NC2 - 1 else 0

            def aodma(e, u=u, qi=qi, qtr=qtr, jr=jr, extra=extra, bcol0=bcol0,
                      sem_=sAO[qi % 2].sem):
                tin = a2ai[jr]
                if extra:
                    e.dma_start(out=tin[qtr + 1, :, u, 0:1],
                                in_=aosg2[:, qi % 2, 511:512],
                                allow_slow_non_contiguous=True).then_inc(sem_, 16)
                    e.dma_start(out=tin[qtr + 5, :, u, 0:1],
                                in_=aosg2[:, qi % 2, 511:512],
                                allow_slow_non_contiguous=True).then_inc(sem_, 16)
                e.dma_start(out=tin[qtr, :, u, bcol0:bcol0 + 512],
                            in_=aosg2[:, qi % 2, :]).then_inc(sem_, 16)
                return e.dma_start(
                    out=tin[qtr + 4, :, u, bcol0:bcol0 + 512], in_=aosg2[:, qi % 2, :])
            ao_dma[qi % 2] = op("s", aodma, inc=sAO[qi % 2], waits=[(sv, aev_c)])
        cw = [(sAO[0], sAO[0].v), (sAO[1], sAO[1].v)]
        if jr == NC2 - 1:
            cw.append((sZ, zero_ss))
        a2a_cs[jr] = op("g", lambda e, jr=jr: e.collective_compute(
            "AllToAll", OP.bypass, replica_groups=[[0, 1, 2, 3, 4, 5, 6, 7]],
            ins=[a2ai[jr].opt()], outs=[a2ao[jr].opt()]),
            inc=scc, waits=cw)
        if jr == rounds[min(1, len(rounds) - 1)]:
            emit_prologue()

    attn_ao0 = sAO[0].v
    attn_ao1 = sAO[1].v

    sWR = Ctr(sem("sWR"), 16)
    sWK = Ctr(sem("sWK"), 16)
    sWV = Ctr(sem("sWV"), 16)
    aow = [(sAO[0], attn_ao0), (sAO[1], attn_ao1), (sW2, w2d_g)]
    w23 = w2_sb[:, :].rearrange("p (w x) -> p w x", w=3)
    op("g", lambda e: e.dma_start(out=w23[:, 0, :], in_=wrd), inc=sWR, waits=aow)
    op("g", lambda e: e.dma_start(out=w23[:, 1, :], in_=wkd), inc=sWK)
    op("g", lambda e: e.dma_start(out=w23[:, 2, :], in_=wvd), inc=sWV)


    psl_pair_free = 0
    sq2_t_free = 0

    def stats_and_apply(src3, dst3, dst_off, dep_v, bc_extra=(), pls=None):
        nonlocal psl_pair_free, psbc_free, psx_free, sq2_t_free
        pl0, pl1 = (psL0, psL1) if pls is None else pls
        mb_sb = mb_sb2[:, :]
        ib_sb = ib_sb2[:, :]
        sqcs = {}
        for h in range(2):
            sqcs[h] = op("a", lambda e, h=h, src3=src3: e.activation(
                sq2[:, :], src3[:, 4 * h:4 * h + 4, :].rearrange("p a t -> p (a t)"),
                AF.Square),
                waits=[(sv, dep_v), (st, sq2_t_free)])
            for a in range(4 * h, 4 * h + 4):
                def s2mm(e, a=a, src3=src3, pl0=pl0, pl1=pl1):
                    e.matmul(pl0[:, :], c_1cm[:, :], src3[:, a, :],
                             start=(a == 0), stop=(a == 7))
                    return e.matmul(pl1[:, :], c_1cm[:, :],
                                    sq2[:, (a % 4) * 512:(a % 4) * 512 + 512],
                                    start=(a == 0), stop=(a == 7))
                op("t", s2mm, inc=st,
                   waits=[(sa, sqcs[h]), (sv, dep_v)] + list(bc_extra))
            sq2_t_free = st.v
        stats_t = st.v

        op("v", lambda e, pl0=pl0, mb_sb=mb_sb: e.tensor_scalar(
            mb_sb, pl0[:, :], 1.0 / C, None, OP.mult), waits=[(st, stats_t)])
        op("v", lambda e, pl1=pl1, ib_sb=ib_sb: e.tensor_scalar(
            ib_sb, pl1[:, :], 1.0 / C, None, OP.mult))
        op("v", lambda e, mb_sb=mb_sb: e.tensor_tensor(sq2[:, 0:512], mb_sb, mb_sb,
                                                       OP.mult))
        mv_c = op("v", lambda e, ib_sb=ib_sb: e.tensor_tensor(ib_sb, ib_sb,
                                                              sq2[:, 0:512], OP.subtract))
        psl_pair_free = mv_c
        isq_c = op("a", lambda e, ib_sb=ib_sb: e.activation(ib_sb, ib_sb, AF.Sqrt,
                                                            bias=c_eps[:, 0:1]),
                   waits=[(sv, mv_c)])
        bce_c = op("v", lambda e, ib_sb=ib_sb: e.reciprocal(ib_sb, ib_sb),
                   waits=[(sa, isq_c)])

        mbB = bass.AP(mb_sb2, 0, [[512, 128], [0, 8], [1, 512]])
        ibB = bass.AP(ib_sb2, 0, [[512, 128], [0, 8], [1, 512]])
        op("v", lambda e, src3=src3, mbB=mbB: e.tensor_tensor(
            src3[:, :, :], src3[:, :, :], mbB, OP.subtract),
            waits=[(sv, bce_c)])
        last = op("v", lambda e, src3=src3, dst3=dst3, dst_off=dst_off, ibB=ibB:
                  e.tensor_tensor(dst3[:, :, dst_off:dst_off + 512],
                                  src3[:, :, :], ibB, OP.mult))
        return last

    pb_free = {0: ol_free[0], 1: ol_free[1]}
    xq_free = 0
    out_dma = {0: 0, 1: 0}
    out_dma_all = 0
    ln1_v = 0
    sig_c = {}
    ka_c = {}
    rbank_free = {0: 0, 1: 0}
    kbank_free = {0: 0, 1: 0}
    vb_free = {0: 0, 1: 0}
    aall_free = 0
    zblk_vs = {}
    resid_vs = {}
    ka_prev = 0

    def emit_A(ck):
        nonlocal xq_free, aall_free
        xq_ss = op("s", lambda e, ck=ck: e.dma_start(
            out=xq3[:, :, :], in_=xqt3[:, ck, :, :]),
            inc=sXQ, waits=[(sv, xq_free), (sAO[0], attn_ao0), (sAO[1], attn_ao1)])
        wid = 513 if ck == NC2 - 1 else 512
        for j_ in range(4):
            op("g", lambda e, ck=ck, j_=j_, wid=wid: e.indirect_dma_start(
                aall[:, j_ * 1026: j_ * 1026 + 2 * wid], None, a2afl[ck][:, :],
                IndirectOffsetOnAxis(c_gix[j_][:, :], 0)),
                inc=sLD, waits=[(st, aall_free), (scc, scc_need[ck]), (sC, consts_s),
                                (sv, abcp)])
        aall_ss = sLD.v

        par = ck % 2
        off = 1 if ck == NC2 - 1 else 0
        for o in range(8):
            bank = psO0 if o % 2 == 0 else psO1
            pw = [(sLD, aall_ss), (sWO, 16), (sv, pb_free[o % 2])]

            def pmm(e, o=o, bank=bank, off=off, wid=wid):
                for a in range(8):
                    base = (a // 2) * 1026 + (a % 2) * wid + off
                    inst = e.matmul(bank[:, :], wo4[:, a, o, :],
                                    aall[:, base:base + 512],
                                    start=(a == 0), stop=(a == 7))
                return inst
            pc_ = op("t", pmm, inc=st, waits=pw)
            pb_free[o % 2] = op("v", lambda e, o=o, par=par, bank=bank:
                                e.tensor_tensor(
                vtmp3p[:, par, o, :], xq3[:, o, :], bank[:, :], OP.add),
                waits=[(st, pc_), (sXQ, xq_ss), (sOUT, out_dma[par])])
        resid_vs[ck] = sv.v
        xq_free = sv.v
        aall_free = st.v

    abcp = prologue_state["abcp"]
    emit_A(0)
    bnd_v = op("v", lambda e: e.tensor_copy(x2c3[:, :, 0], bcol[:, :]),
               waits=[(sv, prologue_state["bnd_v"]), (sAO[0], attn_ao0),
                      (sAO[1], attn_ao1)])
    ln1_v = bnd_v
    for ck in range(NC2):
        par = ck % 2
        src3 = vtmp3p[:, par, :, :]

        if ck > 0:
            ln1_v = op("v", lambda e: e.tensor_copy(x2c3[:, :, 0], x2c3[:, :, 512]),
                       waits=[(sv, ln1_v)])

        ln1_v = stats_and_apply(src3, x2c3, 1, max(resid_vs[ck], ln1_v),
                                bc_extra=[(sv, ka_prev)])

        if ck + 1 < NC2:
            emit_A(ck + 1)

        mixed = None
        mixed_h = {}
        for h in range(2):
            op("v", lambda e, h=h: e.tensor_tensor(
                sq2[:, :].rearrange("p (a t) -> p a t", a=4),
                x2c3[:, 4 * h:4 * h + 4, 1:513],
                x2c3[:, 4 * h:4 * h + 4, 0:512], OP.subtract),
                waits=[(sv, ln1_v), (sC, consts_s), (st, sq2_t_free)])
            for a in range(4 * h, 4 * h + 4):
                op("v", lambda e, a=a: e.scalar_tensor_tensor(
                    xr3[:, a, :], sq2[:, (a % 4) * 512:(a % 4) * 512 + 512],
                    c_mur[:, a:a + 1], x2c3[:, a, 0:512], OP.mult, OP.add))
                mixed = op("v", lambda e, a=a: e.scalar_tensor_tensor(
                    xk3[:, a, :], sq2[:, (a % 4) * 512:(a % 4) * 512 + 512],
                    c_muk[:, a:a + 1], x2c3[:, a, 0:512], OP.mult, OP.add))
            mixed_h[h] = mixed

        for o in range(8):
            bank = psA if o % 2 == 0 else psB

            def rmm0(e, o=o, bank=bank):
                for a in range(4):
                    inst = e.matmul(bank[:, :], w24[:, 0, a, o, :], xr3[:, a, :],
                                    start=(a == 0), stop=False)
                return inst
            op("t", rmm0, inc=st,
               waits=[(sv, mixed_h[0]), (sWR, 16), (sa, rbank_free[o % 2])])

            def rmm1(e, o=o, bank=bank):
                for a in range(4, 8):
                    inst = e.matmul(bank[:, :], w24[:, 0, a, o, :], xr3[:, a, :],
                                    start=False, stop=(a == 7))
                return inst
            rc = op("t", rmm1, inc=st, waits=[(sv, mixed_h[1])])
            sig_c[o] = op("a", lambda e, o=o, bank=bank: e.activation(
                rg3[:, o, :], bank[:, :], AF.Sigmoid), waits=[(st, rc)])
            rbank_free[o % 2] = sig_c[o]

        for o in range(8):
            bank = psBC if o % 2 == 0 else psX

            def kmm(e, o=o, bank=bank):
                for a in range(8):
                    inst = e.matmul(bank[:, :], w24[:, 1, a, o, :], xk3[:, a, :],
                                    start=(a == 0), stop=(a == 7))
                return inst
            kc = op("t", kmm, inc=st,
                    waits=[(sv, mixed), (sWK, 16), (sv, kbank_free[o % 2])])

            op("v", lambda e, o=o, bank=bank: e.tensor_relu(
                sq2[:, (o % 2) * 512:(o % 2) * 512 + 512], bank[:, :]),
                waits=[(st, kc), (st, sq2_t_free), (sv, mixed)])
            ka_c[o] = op("v", lambda e, o=o: e.tensor_tensor(
                ka3[:, o, :], sq2[:, (o % 2) * 512:(o % 2) * 512 + 512],
                sq2[:, (o % 2) * 512:(o % 2) * 512 + 512], OP.mult))
            kbank_free[o % 2] = ka_c[o]
        ka_done = sv.v
        ka_prev = ka_done


        for o in range(8):
            bank = psA if o % 2 == 0 else psB

            def vmm(e, o=o, bank=bank):
                for a in range(8):
                    inst = e.matmul(bank[:, :], w24[:, 2, a, o, :], ka3[:, a, :],
                                    start=(a == 0), stop=(a == 7))
                return inst
            vc = op("t", vmm, inc=st,
                    waits=[(sv, ka_done), (sWV, 16), (sa, rbank_free[o % 2]),
                           (sv, vb_free[o % 2])])

            op("v", lambda e, o=o, par=par, bank=bank: e.tensor_tensor(
                vtmp3p[:, par, o, :], rg3[:, o, :], bank[:, :], OP.mult),
                waits=[(st, vc)])
            vb_free[o % 2] = op("v", lambda e, o=o, par=par: e.tensor_tensor(
                vtmp3p[:, par, o, :], vtmp3p[:, par, o, :], x2c3[:, o, 1:513], OP.add))
        ffn_v = sv.v

        ln2_v = stats_and_apply(src3, src3, 0, ffn_v, bc_extra=[(sv, ka_done)],
                                pls=(psBC, psX))
        out_dma[par] = op("s", lambda e, ck=ck, par=par: e.dma_start(
            out=out3[:, :, ck * 512:(ck + 1) * 512], in_=vtmp3p[:, par, :, :]),
            inc=sOUT, waits=[(sv, ln2_v)])
        out_dma_all = sOUT.v

    final_ss = out_dma_all
    op("s", lambda e: e.nop(), waits=[(sOUT, final_ss)])

    with nc.Block() as block:
        @block.tensor
        def _(e):
            for f in prog["t"]:
                f(e)

        @block.vector
        def _(e):
            for f in prog["v"]:
                f(e)

        @block.scalar
        def _(e):
            for f in prog["a"]:
                f(e)

        @block.gpsimd
        def _(e):
            for f in prog["g"]:
                f(e)

        @block.sync
        def _(e):
            for f in prog["s"]:
                f(e)

    return nc, st_ctx


def stage_inputs(T, inputs):
    x = np.asarray(inputs["L0_attn_x"], dtype=np.float32)
    w_qkv = np.asarray(inputs["L0_attn_w_qkv"], dtype=np.float32)
    w_out = np.asarray(inputs["L0_attn_w_out"], dtype=np.float32)
    head_scales = np.asarray(inputs["L0_attn_head_scales"], dtype=np.float32)
    W_r = np.asarray(inputs["L0_ffn_W_r"], dtype=np.float32)
    W_k = np.asarray(inputs["L0_ffn_W_k"], dtype=np.float32)
    W_v = np.asarray(inputs["L0_ffn_W_v"], dtype=np.float32)
    mu_r = np.asarray(inputs["L0_ffn_mu_r"], dtype=np.float32)
    mu_k = np.asarray(inputs["L0_ffn_mu_k"], dtype=np.float32)

    TQ = T // 4
    dirs = _dirs()

    def wpre(w):
        return np.ascontiguousarray(
            w.reshape(8, 128, 8, 128).transpose(1, 0, 2, 3).reshape(128, 8192))

    ident = np.eye(128, dtype=np.float32).astype(ml_dtypes.bfloat16)
    onesb = np.ones((128, 1), dtype=np.float32).astype(ml_dtypes.bfloat16)
    onesc = np.ones((128, 1), dtype=np.float32)
    onesr = np.ones((1, 128), dtype=np.float32)
    tri = np.where(np.arange(128)[:, None] <= np.arange(128)[None, :], 0.0, NEG).astype(np.float32)
    mur_s = np.ascontiguousarray(mu_r.reshape(8, 128).T)
    muk_s = np.ascontiguousarray(mu_k.reshape(8, 128).T)
    wout_p = wpre(w_out)
    wr_p = wpre(np.ascontiguousarray(W_r.T))
    wk_p = wpre(np.ascontiguousarray(W_k.T))
    wv_p = wpre(np.ascontiguousarray(W_v.T))

    NTl = T // 512
    NC2l = TQ // 512
    in_maps = []
    for c in range(8):
        b, r = divmod(c, 4)
        h0, h1 = 2 * r, 2 * r + 1
        xt_ = np.ascontiguousarray(x[b].T)
        xt_pre = np.ascontiguousarray(
            xt_.reshape(8, 128, NTl, 512).transpose(1, 2, 0, 3).reshape(128, 8 * T))
        xq_slice = xt_[:, r * TQ:(r + 1) * TQ]
        xqt_pre = np.ascontiguousarray(
            xq_slice.reshape(8, 128, NC2l, 512).transpose(1, 2, 0, 3).reshape(128, 8 * TQ))
        if r == 0:
            xql_ = np.zeros((128, 8), dtype=np.float32)
        else:
            xql_ = np.ascontiguousarray(
                xt_[:, r * TQ - 1].reshape(8, 128).T)
        wq = np.empty((128, 6, 8, 128), dtype=np.float32)
        for si, (stype, h) in enumerate([(0, h0), (0, h1), (1, h0), (1, h1), (2, h0), (2, h1)]):
            blk = w_qkv[:, stype * 1024 + h * 128: stype * 1024 + h * 128 + 128]
            wq[:, si, :, :] = blk.reshape(8, 128, 128).transpose(1, 0, 2)
        hs = head_scales[[h0, h1]].reshape(1, 2).astype(np.float32)
        d3c = np.empty((3, 6), dtype=np.float32)
        for u, h in enumerate((h0, h1)):
            d = dirs[h]
            d3c[:, 3 * u:3 * u + 3] = (np.outer(d, d) / SCL).astype(np.float32)
        in_maps.append({
            "xt": xt_pre,
            "xqt": xqt_pre,
            "xqlp": xql_,
            "wqkv": np.ascontiguousarray(wq.reshape(128, 6 * 8 * 128)),
            "wout": wout_p,
            "wr": wr_p,
            "wk": wk_p,
            "wv": wv_p,
            "mur": mur_s,
            "muk": muk_s,
            "hs2": hs,
            "d3": d3c,
            "identp": ident,
            "onesb": onesb,
            "onesbm": np.ones((128, 128), dtype=np.float32).astype(ml_dtypes.bfloat16),
            "onescm": np.ones((128, 128), dtype=np.float32),
            "onesc": onesc,
            "onesr": onesr,
            "trim": tri,
            "epsv": np.full((128, 1), EPS, dtype=np.float32),
            "gidx": ((4 * b + np.arange(4, dtype=np.int32))[:, None] * 128
                     + np.arange(128, dtype=np.int32)[None, :]).astype(np.int32),
        })
    return in_maps


_CACHE = {}


def kernel(**inputs):
    T = np.asarray(inputs["L0_attn_x"]).shape[1]
    if T not in _CACHE:
        _CACHE[T] = build_program(T)
    nc, _ctx = _CACHE[T]
    in_maps = stage_inputs(T, inputs)
    res = run_bass_kernel_spmd(nc, in_maps, core_ids=list(range(8)))
    TQ = T // 4
    full = np.empty((2, T, 1024), dtype=np.float32)
    for c in range(8):
        b, r = divmod(c, 4)
        full[b, r * TQ:(r + 1) * TQ, :] = res.results[c]["out"].T
    return full


# version 19
# speedup vs baseline: 1.0926x; 1.0198x over previous
import os

os.environ.setdefault("TRN_ALLOW_ICI_ALL_TO_ALL", "1")

import math
import numpy as np
import ml_dtypes

import concourse.bass as bass
import concourse.mybir as mybir
from concourse.bass import AutoArena, IndirectOffsetOnAxis
from concourse import dve_ops
from concourse.bass_utils import run_bass_kernel_spmd

F32 = mybir.dt.float32
BF16 = mybir.dt.bfloat16
AX = mybir.AxisListType.X
OP = mybir.AluOpType
AF = mybir.ActivationFunctionType

C = 1024
HD = 128
SCL = 1.0 / math.sqrt(HD)
EPS = 1e-5
NEG = -1.0e9


def _dirs():
    t = np.array(
        [[(i >> 2 & 1) * 2 - 1, (i >> 1 & 1) * 2 - 1, (i & 1) * 2 - 1] for i in range(8)],
        dtype=np.float64,
    )
    return t / np.sqrt(3.0)


class Ctr:
    def __init__(self, sem, step=1):
        self.sem = sem
        self.step = step
        self.v = 0

    def bump(self, n=None):
        self.v += self.step if n is None else n
        return self.v


def build_program(T):
    TQ = T // 4
    NT = T // 512
    QG = T // 512
    PQ = max(QG // 4, 1)
    NC2 = TQ // 512
    KTT = T // 128

    nc = bass.Bass("TRN2", target_bir_lowering=False)

    xt = nc.dram_tensor("xt", [128, 8 * T], F32, kind="ExternalInput").ap()
    xqt = nc.dram_tensor("xqt", [128, 8 * TQ], F32, kind="ExternalInput").ap()
    xqlp = nc.dram_tensor("xqlp", [128, 8], F32, kind="ExternalInput").ap()
    wqkv = nc.dram_tensor("wqkv", [128, 6 * 8 * 128], F32, kind="ExternalInput").ap()
    wout = nc.dram_tensor("wout", [128, 8 * 8 * 128], F32, kind="ExternalInput").ap()
    wr = nc.dram_tensor("wr", [128, 8 * 8 * 128], F32, kind="ExternalInput").ap()
    wk = nc.dram_tensor("wk", [128, 8 * 8 * 128], F32, kind="ExternalInput").ap()
    wv = nc.dram_tensor("wv", [128, 8 * 8 * 128], F32, kind="ExternalInput").ap()
    mur = nc.dram_tensor("mur", [128, 8], F32, kind="ExternalInput").ap()
    muk = nc.dram_tensor("muk", [128, 8], F32, kind="ExternalInput").ap()
    hs2 = nc.dram_tensor("hs2", [1, 2], F32, kind="ExternalInput").ap()
    d3 = nc.dram_tensor("d3", [3, 6], F32, kind="ExternalInput").ap()
    identp = nc.dram_tensor("identp", [128, 128], BF16, kind="ExternalInput").ap()
    onesb = nc.dram_tensor("onesb", [128, 1], BF16, kind="ExternalInput").ap()
    onesbm = nc.dram_tensor("onesbm", [128, 128], BF16, kind="ExternalInput").ap()
    onescm = nc.dram_tensor("onescm", [128, 128], F32, kind="ExternalInput").ap()
    onesc = nc.dram_tensor("onesc", [128, 1], F32, kind="ExternalInput").ap()
    onesr = nc.dram_tensor("onesr", [1, 128], F32, kind="ExternalInput").ap()
    trim = nc.dram_tensor("trim", [128, 128], F32, kind="ExternalInput").ap()
    epsv = nc.dram_tensor("epsv", [128, 1], F32, kind="ExternalInput").ap()
    gidx = nc.dram_tensor("gidx", [4, 128], mybir.dt.int32, kind="ExternalInput").ap()
    out = nc.dram_tensor("out", [C, TQ], F32, kind="ExternalOutput").ap()

    wrd = nc.dram_tensor("wrd", [128, 8 * 8 * 128], BF16).ap()
    wkd = nc.dram_tensor("wkd", [128, 8 * 8 * 128], BF16).ap()
    wvd = nc.dram_tensor("wvd", [128, 8 * 8 * 128], BF16).ap()
    a2ai = [nc.dram_tensor(f"a2ai{j}", [8, 128, 2, 512 + (1 if j == NC2 - 1 else 0)],
                           BF16).ap() for j in range(NC2)]
    a2ao = [nc.dram_tensor(f"a2ao{j}", [8, 128, 2, 512 + (1 if j == NC2 - 1 else 0)],
                           BF16).ap() for j in range(NC2)]

    import contextlib

    st_ctx = contextlib.ExitStack()
    sbp = lambda name, shape, dt: st_ctx.enter_context(nc.sbuf_tensor(name, shape, dt))
    psx_ = lambda name: st_ctx.enter_context(nc.psum_tensor(name, [128, 512], F32))
    sem = lambda name: st_ctx.enter_context(nc.semaphore(name))

    c_id = sbp("c_id", [128, 128], BF16)
    c_1b = sbp("c_1b", [128, 1], BF16)
    c_1bm = sbp("c_1bm", [128, 128], BF16)
    c_1cm = sbp("c_1cm", [128, 128], F32)
    c_1c = sbp("c_1c", [128, 1], F32)
    c_1r = sbp("c_1r", [1, 128], F32)
    c_tri = sbp("c_tri", [128, 128], F32)
    c_d3 = sbp("c_d3", [3, 6], F32)
    c_hs = sbp("c_hs", [1, 2], F32)
    c_mur = sbp("c_mur", [128, 8], F32)
    c_muk = sbp("c_muk", [128, 8], F32)
    zcol = sbp("zcol", [128, 1], BF16)
    c_eps = sbp("c_eps", [128, 1], F32)
    c_gix = [sbp(f"c_gix{j}", [128, 1], mybir.dt.int32) for j in range(4)]
    lrec = sbp("lrec", [1, 512], F32)
    mx = sbp("mx", [1, 64], F32)
    hsr = sbp("hsr", [3, 2], F32)
    A3 = sbp("A3", [3, 6], BF16)
    cneg = sbp("cneg", [128, 2], F32)
    xql = sbp("xql", [128, 8], F32)
    wo_sb = sbp("wo_sb", [128, 8 * 8 * 128], BF16)
    vcol = sbp("vcol", [128, 8], F32)
    mi2 = sbp("mi2", [128, 2], F32)
    bsq8 = sbp("bsq8", [128, 8], F32)
    bcol = sbp("bcol", [128, 8], F32)

    arA = AutoArena(nc)
    wq_sb = arA.sbuf_tensor("wq_sb", [128, 6 * 8 * 128], BF16)
    xsb = arA.sbuf_tensor("xsb", [128, 2 * 8 * 512], BF16)
    Qt = arA.sbuf_tensor("Qt", [128, 2 * T], BF16)
    Kt = arA.sbuf_tensor("Kt", [128, 2 * T], BF16)
    Vt = arA.sbuf_tensor("Vt", [128, 2 * T], BF16)
    vstg = arA.sbuf_tensor("vstg", [128, 512], BF16)
    pt = arA.sbuf_tensor("pt", [128, 4 * 512], BF16)
    lsum = arA.sbuf_tensor("lsum", [128, 2 * 512], BF16)
    sq = arA.sbuf_tensor("sq", [128, 2 * 512], BF16)
    lb = arA.sbuf_tensor("lb", [128, 512], F32)
    rscr = arA.sbuf_tensor("rscr", [128, 512], F32)
    aosg = arA.sbuf_tensor("aosg", [128, 2 * 512], BF16)
    arA.allocate()

    arA.free()
    arB = AutoArena(nc)
    aall = arB.sbuf_tensor("aall", [128, 4 * 1026], BF16)
    aall_b = arB.sbuf_tensor("aall_b", [128, 8], BF16)
    w2_sb = arB.sbuf_tensor("w2_sb", [128, 3 * 8 * 8 * 128], BF16)
    xq = arB.sbuf_tensor("xq", [128, 8 * 512], F32)
    x2c = arB.sbuf_tensor("x2c", [128, 8 * 513], F32)
    vtmp = arB.sbuf_tensor("vtmp", [128, 2 * 8 * 512], F32)
    sq2 = arB.sbuf_tensor("sq2", [128, 4 * 512], F32)
    mb_sb2 = arB.sbuf_tensor("mb_sb2", [128, 512], F32)
    ib_sb2 = arB.sbuf_tensor("ib_sb2", [128, 512], F32)
    xr_sb = arB.sbuf_tensor("xr_sb", [128, 8 * 512], BF16)
    xk_sb = arB.sbuf_tensor("xk_sb", [128, 8 * 512], BF16)
    ka_sb = arB.sbuf_tensor("ka_sb", [128, 8 * 512], BF16)
    rg_sb = arB.sbuf_tensor("rg_sb", [128, 8 * 512], F32)
    arB.allocate()
    t1_sb = None

    psA = psx_("psA")
    psB = psx_("psB")
    psO0 = psx_("psO0")
    psO1 = psx_("psO1")
    psL0 = psx_("psL0")
    psL1 = psx_("psL1")
    psBC = psx_("psBC")
    psX = psx_("psX")

    sC = Ctr(sem("sC"), 16)
    sWQ = Ctr(sem("sWQ"), 16)
    sX = [Ctr(sem("sX0"), 16), Ctr(sem("sX1"), 16)]
    sW2 = Ctr(sem("sW2"), 16)
    sAO = [Ctr(sem("sAO0"), 16), Ctr(sem("sAO1"), 16)]
    sZ = Ctr(sem("sZ"), 16)
    sLD = Ctr(sem("sLD"), 16)
    sLD2 = Ctr(sem("sLD2"), 16)
    sWO = Ctr(sem("sWO"), 16)
    sAB = Ctr(sem("sAB"), 16)
    sXQL = Ctr(sem("sXQL"), 16)
    sXQ = Ctr(sem("sXQ"), 16)
    sOUT = Ctr(sem("sOUT"), 16)
    sgp = Ctr(sem("sgp"), 1)
    st = Ctr(sem("st"), 1)
    sv = Ctr(sem("sv"), 1)
    sa = Ctr(sem("sa"), 1)
    scc = Ctr(sem("scc"), 1)

    prog = {k: [] for k in ("t", "v", "a", "g", "s")}
    highwater = {}

    def op(eng, builder, inc=None, waits=()):
        if eng == "ad":
            eng = "a"
        elif eng == "gc":
            eng = "g"
            waits = list(waits) + [(sgp, sgp.v)]
            inc = sgp
        elif eng == "v":
            waits = list(waits) + [(sv, sv.v)]
            inc = sv
        elif eng == "a":
            waits = list(waits) + [(sa, sa.v)]
            inc = sa
        needed = []
        for ctr, val in waits:
            if val <= 0:
                continue
            key = (eng, id(ctr))
            if highwater.get(key, 0) >= val:
                continue
            highwater[key] = val
            needed.append((ctr.sem, val))
        inc_sem = None
        inc_n = 0
        if inc is not None:
            inc.bump()
            inc_sem = inc.sem
            inc_n = inc.step

        def f(e, builder=builder, needed=tuple(needed), inc_sem=inc_sem, inc_n=inc_n):
            for s_, v_ in needed:
                e.wait_ge(s_, v_)
            inst = builder(e)
            if inc_sem is not None:
                inst.then_inc(inc_sem, inc_n)

        prog[eng].append(f)
        return inc.v if inc is not None else None

    for dst, src in ((c_id[:, :], identp), (c_1b[:, :], onesb), (c_1c[:, :], onesc),
                     (c_1bm[:, :], onesbm), (c_1cm[:, :], onescm),
                     (c_1r[:, :], onesr), (c_tri[:, :], trim),
                     (c_d3[:, :], d3), (c_hs[:, :], hs2), (c_mur[:, :], mur),
                     (c_muk[:, :], muk), (c_eps[:, :], epsv),
                     (c_gix[0][:, :], gidx[0:1, :]), (c_gix[1][:, :], gidx[1:2, :]),
                     (c_gix[2][:, :], gidx[2:3, :]), (c_gix[3][:, :], gidx[3:4, :]),
                     (xql[:, :], xqlp)):
        op("s", lambda e, d=dst, s=src: e.dma_start(out=d, in_=s), inc=sC)
    consts_s = sC.v
    op("g", lambda e: e.dma_start(out=wq_sb[:, :], in_=wqkv), inc=sWQ)
    op("g", lambda e: e.dma_start(out=wo_sb[:, :], in_=wout), inc=sWO)
    op("g", lambda e: e.dma_start(out=wrd, in_=wr), inc=sW2)
    op("g", lambda e: e.dma_start(out=wkd, in_=wk), inc=sW2)
    op("g", lambda e: e.dma_start(out=wvd, in_=wv), inc=sW2)
    w2d_g = sW2.v

    zv = op("v", lambda e: e.memset(zcol[:, :], 0.0), inc=sv)
    for sh in (0, 4):
        for u in range(2):
            op("s", lambda e, sh=sh, u=u: e.dma_start(
                out=a2ai[NC2 - 1][sh, :, u, 0:1], in_=zcol[:, :],
                allow_slow_non_contiguous=True),
               inc=sZ, waits=[(sv, zv)])
    zero_ss = sZ.v

    xt3 = xt.rearrange("p (tc a t) -> p tc a t", tc=NT, a=8)
    xsb3 = xsb[:, :].rearrange("p (par a t) -> p par a t", par=2, a=8)
    wq4 = wq_sb[:, :].rearrange("p (s a d) -> p s a d", s=6, a=8)
    Qt2 = Qt[:, :].rearrange("p (u t) -> p u t", u=2)
    Kt2 = Kt[:, :].rearrange("p (u t) -> p u t", u=2)
    Vt3 = Vt[:, :].rearrange("p (u kt d) -> p u kt d", u=2, kt=KTT)

    x_done = []
    x_cons = []
    bank_free = {0: 0, 1: 0}
    psx_free = 0

    for tc in range(NT):
        par = tc % 2
        w_ = [] if tc < 2 else [(st, x_cons[tc - 2])]
        op("g", lambda e, par=par, tc=tc:
           e.dma_start(out=xsb3[:, par, :, :], in_=xt3[:, tc, :, :]),
           inc=sX[par], waits=w_)
        x_done.append((sX[par], sX[par].v))

        for ob in range(6):
            bname = (tc * 6 + ob) % 2
            bank = psA if bname == 0 else psB

            def mms(e, bank=bank, ob=ob, par=par):
                for a in range(8):
                    inst = e.matmul(bank[:, :], wq4[:, ob, a, :], xsb3[:, par, a, :],
                                    start=(a == 0), stop=(a == 7))
                return inst
            mmc = op("t", mms, inc=st,
                     waits=[x_done[tc], (sWQ, 16), (sv, bank_free[bname])])

            u = ob % 2
            kind = ob // 2
            if kind == 0:
                dst = Qt2[:, u, tc * 512:(tc + 1) * 512]
            elif kind == 1:
                dst = Kt2[:, u, tc * 512:(tc + 1) * 512]
            else:
                dst = vstg[:, :]
            bank_free[bname] = op("v", lambda e, d=dst, b=bank: e.tensor_copy(d, b[:, :]),
                                  inc=sv, waits=[(st, mmc)])

            if kind == 2:
                vst = sv.v

                def vtr(e):
                    psXb = psX[:, :].bitcast(BF16)
                    for jj in range(4):
                        inst = e.transpose(psXb[:, jj * 128:(jj + 1) * 128],
                                           vstg[:, jj * 128:(jj + 1) * 128], c_id[:, :])
                    return inst
                trc = op("t", vtr, inc=st,
                         waits=[(sv, vst), (sv, psx_free), (sC, consts_s)])
                psx_free = op("v", lambda e, u=u, tc=tc: e.tensor_copy(
                    Vt3[:, u, tc * 4:(tc + 1) * 4, :].rearrange("p k d -> p (k d)"),
                    psX[:, :].bitcast(BF16)[:, 0:512]), inc=sv, waits=[(st, trc)])
        x_cons.append(st.v)

    ph1_v = sv.v

    psbc_free = 0
    adj_frees = {0: 0, 1: 0}
    psl_frees = {0: 0, 1: 0}
    sq_frees = {0: 0, 1: 0}
    for u in range(2):
        hc = op("t", lambda e, u=u: e.matmul(psX[0:3, 0:1], c_1r[0:1, 0:3],
                                             c_hs[0:1, u:u + 1], start=True, stop=True),
                inc=st, waits=[(sC, consts_s), (sv, psx_free)])
        hv = op("v", lambda e, u=u: e.tensor_copy(hsr[:, u:u + 1], psX[0:3, 0:1]),
                inc=sv, waits=[(st, hc)])
        psx_free = hv
        a3done = op("v", lambda e, u=u: e.tensor_scalar(
            A3[:, 3 * u:3 * u + 3], c_d3[:, 3 * u:3 * u + 3], hsr[:, u:u + 1], None, OP.mult),
            inc=sv, waits=[(sv, hv), (sC, consts_s)])

        for tc in range(NT):
            abank = psBC if tc % 2 == 0 else psX
            amc = op("t", lambda e, u=u, tc=tc, abank=abank: e.matmul(
                abank[0:3, 0:512], A3[:, 3 * u:3 * u + 3],
                Qt2[0:3, u, tc * 512:(tc + 1) * 512], start=True, stop=True),
                inc=st, waits=[(sv, a3done), (sv, ph1_v),
                               (sv, adj_frees[tc % 2])])
            adj_frees[tc % 2] = op("v", lambda e, u=u, tc=tc, abank=abank:
                                   e.tensor_tensor(
                Qt2[0:3, u, tc * 512:(tc + 1) * 512],
                Qt2[0:3, u, tc * 512:(tc + 1) * 512],
                abank[0:3, 0:512], OP.add), inc=sv, waits=[(st, amc)])
        adj_v = sv.v
        psbc_free = max(psbc_free, adj_frees[0])
        psx_free = max(psx_free, adj_frees[1])

        for which in ("q", "k"):
            buf = Qt2 if which == "q" else Kt2
            base = 0 if which == "q" else 16
            dep = adj_v if which == "q" else ph1_v
            for tc in range(NT):
                sl = tc % 2
                bank = psL0 if sl == 0 else psL1
                sqc = op("a", lambda e, u=u, tc=tc, sl=sl, buf=buf: e.activation(
                    sq[:, sl * 512:sl * 512 + 512],
                    buf[:, u, tc * 512:(tc + 1) * 512], AF.Square),
                    waits=[(sv, dep), (st, sq_frees[sl])])
                oc = op("t", lambda e, sl=sl, bank=bank: e.matmul(
                    bank[0:1, :], c_1b[:, 0:1], sq[:, sl * 512:sl * 512 + 512],
                    start=True, stop=True),
                    inc=st, waits=[(sa, sqc), (sC, consts_s), (sv, psl_frees[sl])])
                sq_frees[sl] = oc
                psl_frees[sl] = op("v", lambda e, tc=tc, sl=sl, base=base, bank=bank:
                                   e.reduce_max(
                    mx[0:1, base + tc:base + tc + 1], bank[0:1, 0:512], axis=AX),
                    inc=sv, waits=[(st, oc)])

        op("v", lambda e: e.reduce_max(mx[0:1, 32:33], mx[0:1, 0:NT], axis=AX),
           inc=sv, waits=[(sv, max(psl_frees[0], psl_frees[1]))])
        f0 = op("v", lambda e: e.reduce_max(mx[0:1, 33:34], mx[0:1, 16:16 + NT], axis=AX),
                inc=sv)
        fc = op("v", lambda e: e.tensor_tensor(mx[0:1, 34:35], mx[0:1, 32:33],
                                               mx[0:1, 33:34], OP.mult),
                inc=sv, waits=[(sv, f0)])

        op("a", lambda e: e.activation(mx[0:1, 35:36], mx[0:1, 34:35], AF.Sqrt),
           waits=[(sv, fc)])
        cc_ = op("a", lambda e: e.mul(mx[0:1, 36:37], mx[0:1, 35:36], -SCL))
        cbc_c = op("t", lambda e: e.matmul(psBC[:, 0:1], c_1r[0:1, :], mx[0:1, 36:37],
                                           start=True, stop=True),
                   inc=st, waits=[(sa, cc_), (sv, psbc_free)])
        psbc_free = op("v", lambda e, u=u: e.tensor_copy(cneg[:, u:u + 1], psBC[:, 0:1]),
                       inc=sv, waits=[(st, cbc_c)])
    bounds_v = sv.v

    xqt3 = xqt.rearrange("p (ck a t) -> p ck a t", ck=NC2, a=8)

    wo4 = wo_sb[:, :].rearrange("p (a o d) -> p a o d", a=8, o=8)
    w24 = w2_sb[:, :].rearrange("p (w a o d) -> p w a o d", w=3, a=8, o=8)
    xq3 = xq[:, :].rearrange("p (a t) -> p a t", a=8)
    x2c3 = x2c[:, :].rearrange("p (a t) -> p a t", a=8)
    vtmp3p = vtmp[:, :].rearrange("p (par a t) -> p par a t", par=2, a=8)
    xr3 = xr_sb[:, :].rearrange("p (a t) -> p a t", a=8)
    xk3 = xk_sb[:, :].rearrange("p (a t) -> p a t", a=8)
    ka3 = ka_sb[:, :].rearrange("p (a t) -> p a t", a=8)
    rg3 = rg_sb[:, :].rearrange("p (a t) -> p a t", a=8)
    out3 = out.rearrange("(a p) t -> p a t", p=128)

    rounds = [NC2 - 1] + list(range(NC2 - 1))
    a2afl = [a2ao[j].rearrange("r p u t -> (r p) (u t)") for j in range(NC2)]
    scc_need = {jr: i + 1 for i, jr in enumerate(rounds)}
    prologue_state = {}

    def emit_prologue():
        nonlocal psx_free
        ab_ss = 0
        for j_ in range(4):
            ab_ss = op("g", lambda e, j_=j_: e.indirect_dma_start(
                aall[:, j_ * 1026:(j_ + 1) * 1026], None, a2afl[NC2 - 1][:, :],
                IndirectOffsetOnAxis(c_gix[j_][:, :], 0)),
                inc=sAB, waits=[(scc, scc_need[NC2 - 1]), (sC, consts_s)])
        abview = bass.AP(aall, 0, [[4104, 128], [1026, 4], [513, 2]])
        abcp = op("v", lambda e: e.tensor_copy(
            aall_b[:, :].rearrange("p (j u) -> p j u", u=2), abview),
            waits=[(sAB, ab_ss)])

        def bpmm(e):
            for o in range(8):
                for a in range(8):
                    inst = e.matmul(psX[:, o:o + 1], wo4[:, a, o, :],
                                    aall_b[:, a:a + 1],
                                    start=(a == 0), stop=(a == 7))
            return inst
        bp_c = op("t", bpmm, inc=st,
                  waits=[(sv, abcp), (sWO, 16), (sv, psx_free)])
        bres_c = op("v", lambda e: e.tensor_tensor(vcol[:, :], xql[:, :], psX[:, 0:8], OP.add),
                    inc=sv, waits=[(st, bp_c), (sC, consts_s)])
        psx_free = bres_c
        bsq_c = op("a", lambda e: e.activation(bsq8[:, :], vcol[:, :], AF.Square),
                   waits=[(sv, bres_c)])

        def bstat(e):
            e.matmul(psX[0:1, 16:24], c_1c[:, 0:1], vcol[:, :], start=True, stop=True)
            return e.matmul(psX[0:1, 32:40], c_1c[:, 0:1], bsq8[:, :], start=True, stop=True)
        bst_c = op("t", bstat, inc=st, waits=[(sa, bsq_c), (sv, bres_c)])

        op("v", lambda e: e.reduce_sum(mx[0:1, 40:41], psX[0:1, 16:24], axis=AX),
           inc=sv, waits=[(st, bst_c)])
        bm0 = op("v", lambda e: e.reduce_sum(mx[0:1, 41:42], psX[0:1, 32:40], axis=AX),
                 inc=sv)

        op("v", lambda e: e.tensor_scalar(mx[0:1, 40:41], mx[0:1, 40:41], 1.0 / C,
                                          None, OP.mult), waits=[(sv, bm0)])
        op("v", lambda e: e.tensor_scalar(mx[0:1, 41:42], mx[0:1, 41:42], 1.0 / C,
                                          None, OP.mult))
        op("v", lambda e: e.tensor_tensor(mx[0:1, 42:43], mx[0:1, 40:41], mx[0:1, 40:41],
                                          OP.mult))
        bmv_c = op("v", lambda e: e.tensor_tensor(mx[0:1, 41:42], mx[0:1, 41:42],
                                                  mx[0:1, 42:43], OP.subtract))
        bsi_c = op("a", lambda e: e.activation(mx[0:1, 41:42], mx[0:1, 41:42], AF.Sqrt, bias=c_eps[0:1, 0:1]),
                   inc=sa, waits=[(sv, bmv_c)])
        bir_c = op("v", lambda e: e.reciprocal(mx[0:1, 41:42], mx[0:1, 41:42]),
                   inc=sv, waits=[(sa, bsi_c)])
        bbc_c = op("t", lambda e: e.matmul(psX[:, 40:42], c_1r[0:1, :], mx[0:1, 40:42],
                                           start=True, stop=True),
                   inc=st, waits=[(sv, bir_c)])
        mi_ev = op("v", lambda e: e.tensor_copy(mi2[:, :], psX[:, 40:42]),
                   inc=sv, waits=[(st, bbc_c)])
        bnd_v = op("v", lambda e: e.tensor_scalar(
            bcol[:, :], vcol[:, :], mi2[:, 0:1], mi2[:, 1:2], OP.subtract, OP.mult),
            waits=[(sv, mi_ev)])

        prologue_state["bnd_v"] = bnd_v
        prologue_state["abcp"] = abcp

    pt2 = pt[:, :].rearrange("p (r t) -> p r t", r=4)
    ls2 = lsum[:, :].rearrange("p (r t) -> p r t", r=2)
    aosg2 = aosg[:, :].rearrange("p (r t) -> p r t", r=2)

    exp_ring = {0: 0, 1: 0, 2: 0}
    pt_ring = {0: 0, 1: 0, 2: 0, 3: 0}
    pt_ring_v = {0: 0, 1: 0, 2: 0, 3: 0}
    lsum_free = {0: 0, 1: 0}
    ol_free = {0: 0, 1: 0}
    ao_dma = {0: 0, 1: 0}
    lb_free = 0

    qi = -1
    a2a_cs = {}
    for jr in rounds:
        for u in range(2):
          for qtr in range(4):
            qg = qtr * PQ + jr
            qi += 1
            ktmax = 4 * qg + 4
            ngrp = ktmax // 4
            obank = psO0 if qi % 2 == 0 else psO1
            lbank = psL0 if qi % 2 == 0 else psL1

            smm_cs = {}
            exp_cs = {}
            grp_add = {}

            def geom(kt):
                diag = kt - 4 * qg
                if diag >= 0:
                    return 128 * diag, 512 - 128 * diag
                return 0, 512

            sbanks = (psA, psB, psBC)
            for step in range(ktmax + 2):
                tri_c = None
                if step < ktmax:
                    kt = step
                    sbank = sbanks[kt % 3]
                    o0, ncols = geom(kt)
                    qc0 = qg * 512 + o0
                    smm_cs[kt] = op("t", lambda e, u=u, kt=kt, sbank=sbank, qc0=qc0,
                                    ncols=ncols, o0=o0:
                                    e.matmul(sbank[:, o0:o0 + ncols],
                                             Kt2[:, u, kt * 128:(kt + 1) * 128],
                                             Qt2[:, u, qc0:qc0 + ncols],
                                             start=True, stop=True),
                                    inc=st,
                                    waits=[(sv, bounds_v), (sa, exp_ring[kt % 3]),
                                           (sv, psbc_free if kt % 3 == 2 else 0)])
                    if kt - 4 * qg >= 0:
                        tri_c = op("v", lambda e, sbank=sbank, o0=o0: e.tensor_tensor(
                            sbank[:, o0:o0 + 128], sbank[:, o0:o0 + 128],
                            c_tri[:, :], OP.add),
                            waits=[(st, smm_cs[kt]), (sC, consts_s)])
                if step >= 2:
                    kt = step - 2
                    o0, ncols = geom(kt)
                    ow = [(sa, exp_cs[kt])]
                    if kt == 0:
                        ow.append((sv, ol_free[qi % 2]))
                    pt_ring[kt % 4] = op("t", lambda e, u=u, kt=kt, o0=o0, ncols=ncols,
                                         obank=obank,
                                         first=(kt == 0), last=(kt == ktmax - 1):
                                         e.matmul(obank[:, o0:o0 + ncols],
                                                  Vt3[:, u, kt, :],
                                                  pt2[:, kt % 4, o0:o0 + ncols],
                                                  start=first, stop=last),
                                         inc=st, waits=ow)
                    if kt % 4 == 1 and kt // 4 < ngrp - 1:
                        g = kt // 4
                        a1 = op("v", lambda e, g=g: e.tensor_tensor(
                            ls2[:, g % 2, :], pt2[:, 0, :], pt2[:, 1, :], OP.add),
                            waits=[(sa, exp_cs[kt]), (st, lsum_free[g % 2])])
                        pt_ring_v[0] = a1
                        pt_ring_v[1] = a1
                    if kt % 4 == 3:
                        g = kt // 4
                        if g < ngrp - 1:
                            op("v", lambda e, g=g: e.tensor_tensor(
                                ls2[:, g % 2, :], ls2[:, g % 2, :], pt2[:, 2, :],
                                OP.add), waits=[(sa, exp_cs[kt])])
                            grp_add[g] = op("v", lambda e, g=g: e.tensor_tensor(
                                ls2[:, g % 2, :], ls2[:, g % 2, :], pt2[:, 3, :],
                                OP.add))
                            pt_ring_v[2] = grp_add[g]
                            pt_ring_v[3] = grp_add[g]
                            ow2 = [(sv, grp_add[g])]
                            if g == 0:
                                ow2.append((sv, ol_free[qi % 2]))
                            lsum_free[g % 2] = op("t", lambda e, g=g, lbank=lbank,
                                                  first=(g == 0):
                                                  e.matmul(lbank[:, :], c_1bm[:, :],
                                                           ls2[:, g % 2, :],
                                                           start=first, stop=False),
                                                  inc=st, waits=ow2)
                        else:
                            for kk in range(4 * g + 1, 4 * g + 4):
                                oo, nn = geom(kk)
                                op("v", lambda e, kk=kk, oo=oo: e.memset(
                                    pt2[:, kk % 4, 0:oo], 0.0),
                                    waits=[(sa, exp_cs[kt]), (st, pt_ring[kk % 4])])
                            op("v", lambda e, g=g: e.tensor_tensor(
                                ls2[:, g % 2, :], pt2[:, 0, :], pt2[:, 1, :], OP.add),
                                waits=[(st, lsum_free[g % 2])])
                            op("v", lambda e, g=g: e.tensor_tensor(
                                ls2[:, g % 2, :], ls2[:, g % 2, :], pt2[:, 2, :],
                                OP.add))
                            grp_add[g] = op("v", lambda e, g=g: e.tensor_tensor(
                                ls2[:, g % 2, :], ls2[:, g % 2, :], pt2[:, 3, :],
                                OP.add))
                            for kk in range(4 * g, 4 * g + 4):
                                pt_ring_v[kk % 4] = grp_add[g]
                            ow2 = [(sv, grp_add[g])]
                            if g == 0:
                                ow2.append((sv, ol_free[qi % 2]))
                            lsum_free[g % 2] = op("t", lambda e, g=g, lbank=lbank,
                                                  first=(g == 0):
                                                  e.matmul(lbank[:, :], c_1bm[:, :],
                                                           ls2[:, g % 2, :],
                                                           start=first, stop=True),
                                                  inc=st, waits=ow2)
                if step < ktmax:
                    kt = step
                    sbank = sbanks[kt % 3]
                    o0, ncols = geom(kt)
                    ew = [(st, smm_cs[kt]), (st, pt_ring[kt % 4]),
                          (sv, pt_ring_v[kt % 4])]
                    if tri_c is not None:
                        ew.append((sv, tri_c))
                    exp_cs[kt] = op("a", lambda e, u=u, sbank=sbank, kt=kt, o0=o0,
                                    ncols=ncols:
                                    e.activation(pt2[:, kt % 4, o0:o0 + ncols],
                                                 sbank[:, o0:o0 + ncols], AF.Exp,
                                                 bias=cneg[:, u:u + 1], scale=SCL),
                                    waits=ew)
                    exp_ring[kt % 3] = exp_cs[kt]

            kloop_t = st.v
            rec_c = op("v", lambda e, lbank=lbank: e.reciprocal(lb[:, :], lbank[:, :]),
                       waits=[(st, kloop_t), (sv, lb_free)])
            aev_c = op("v", lambda e, obank=obank, qi=qi: e.tensor_tensor(
                aosg2[:, qi % 2, :], obank[:, :], lb[:, :], OP.mult),
                waits=[(sv, rec_c), (sAO[qi % 2], ao_dma[qi % 2])])
            lb_free = aev_c
            ol_free[qi % 2] = aev_c

            extra = (jr == NC2 - 1) and (qtr < 3)
            sAO[qi % 2].bump((3 if extra else 1) * 16)
            bcol0 = 1 if jr == NC2 - 1 else 0

            def aodma(e, u=u, qi=qi, qtr=qtr, jr=jr, extra=extra, bcol0=bcol0,
                      sem_=sAO[qi % 2].sem):
                tin = a2ai[jr]
                if extra:
                    e.dma_start(out=tin[qtr + 1, :, u, 0:1],
                                in_=aosg2[:, qi % 2, 511:512],
                                allow_slow_non_contiguous=True).then_inc(sem_, 16)
                    e.dma_start(out=tin[qtr + 5, :, u, 0:1],
                                in_=aosg2[:, qi % 2, 511:512],
                                allow_slow_non_contiguous=True).then_inc(sem_, 16)
                e.dma_start(out=tin[qtr, :, u, bcol0:bcol0 + 512],
                            in_=aosg2[:, qi % 2, :]).then_inc(sem_, 16)
                return e.dma_start(
                    out=tin[qtr + 4, :, u, bcol0:bcol0 + 512], in_=aosg2[:, qi % 2, :])
            ao_dma[qi % 2] = op("s", aodma, inc=sAO[qi % 2], waits=[(sv, aev_c)])
        cw = [(sAO[0], sAO[0].v), (sAO[1], sAO[1].v)]
        if jr == NC2 - 1:
            cw.append((sZ, zero_ss))
        a2a_cs[jr] = op("g", lambda e, jr=jr: e.collective_compute(
            "AllToAll", OP.bypass, replica_groups=[[0, 1, 2, 3, 4, 5, 6, 7]],
            ins=[a2ai[jr].opt()], outs=[a2ao[jr].opt()]),
            inc=scc, waits=cw)
        if jr == rounds[min(2, len(rounds) - 1)]:
            emit_prologue()

    attn_ao0 = sAO[0].v
    attn_ao1 = sAO[1].v

    sWR = Ctr(sem("sWR"), 16)
    sWK = Ctr(sem("sWK"), 16)
    sWV = Ctr(sem("sWV"), 16)
    aow = [(sAO[0], attn_ao0), (sAO[1], attn_ao1), (sW2, w2d_g)]
    w23 = w2_sb[:, :].rearrange("p (w x) -> p w x", w=3)
    op("g", lambda e: e.dma_start(out=w23[:, 0, :], in_=wrd), inc=sWR, waits=aow)
    op("g", lambda e: e.dma_start(out=w23[:, 1, :], in_=wkd), inc=sWK)
    op("g", lambda e: e.dma_start(out=w23[:, 2, :], in_=wvd), inc=sWV)


    psl_pair_free = 0
    sq2_t_free = 0

    def stats_and_apply(src3, dst3, dst_off, dep_v, bc_extra=(), pls=None):
        nonlocal psl_pair_free, psbc_free, psx_free, sq2_t_free
        pl0, pl1 = (psL0, psL1) if pls is None else pls
        mb_sb = mb_sb2[:, :]
        ib_sb = ib_sb2[:, :]
        sqcs = {}
        for h in range(2):
            sqcs[h] = op("a", lambda e, h=h, src3=src3: e.activation(
                sq2[:, :], src3[:, 4 * h:4 * h + 4, :].rearrange("p a t -> p (a t)"),
                AF.Square),
                waits=[(sv, dep_v), (st, sq2_t_free)])
            for a in range(4 * h, 4 * h + 4):
                def s2mm(e, a=a, src3=src3, pl0=pl0, pl1=pl1):
                    e.matmul(pl0[:, :], c_1cm[:, :], src3[:, a, :],
                             start=(a == 0), stop=(a == 7))
                    return e.matmul(pl1[:, :], c_1cm[:, :],
                                    sq2[:, (a % 4) * 512:(a % 4) * 512 + 512],
                                    start=(a == 0), stop=(a == 7))
                op("t", s2mm, inc=st,
                   waits=[(sa, sqcs[h]), (sv, dep_v)] + list(bc_extra))
            sq2_t_free = st.v
        stats_t = st.v

        op("v", lambda e, pl0=pl0, mb_sb=mb_sb: e.tensor_scalar(
            mb_sb, pl0[:, :], 1.0 / C, None, OP.mult), waits=[(st, stats_t)])
        op("v", lambda e, pl1=pl1, ib_sb=ib_sb: e.tensor_scalar(
            ib_sb, pl1[:, :], 1.0 / C, None, OP.mult))
        op("v", lambda e, mb_sb=mb_sb: e.tensor_tensor(sq2[:, 0:512], mb_sb, mb_sb,
                                                       OP.mult))
        mv_c = op("v", lambda e, ib_sb=ib_sb: e.tensor_tensor(ib_sb, ib_sb,
                                                              sq2[:, 0:512], OP.subtract))
        psl_pair_free = mv_c
        isq_c = op("a", lambda e, ib_sb=ib_sb: e.activation(ib_sb, ib_sb, AF.Sqrt,
                                                            bias=c_eps[:, 0:1]),
                   waits=[(sv, mv_c)])
        bce_c = op("v", lambda e, ib_sb=ib_sb: e.reciprocal(ib_sb, ib_sb),
                   waits=[(sa, isq_c)])

        mbB = bass.AP(mb_sb2, 0, [[512, 128], [0, 8], [1, 512]])
        ibB = bass.AP(ib_sb2, 0, [[512, 128], [0, 8], [1, 512]])
        op("v", lambda e, src3=src3, mbB=mbB: e.tensor_tensor(
            src3[:, :, :], src3[:, :, :], mbB, OP.subtract),
            waits=[(sv, bce_c)])
        last = op("v", lambda e, src3=src3, dst3=dst3, dst_off=dst_off, ibB=ibB:
                  e.tensor_tensor(dst3[:, :, dst_off:dst_off + 512],
                                  src3[:, :, :], ibB, OP.mult))
        return last

    pb_free = {0: ol_free[0], 1: ol_free[1]}
    xq_free = 0
    out_dma = {0: 0, 1: 0}
    out_dma_all = 0
    ln1_v = 0
    sig_c = {}
    ka_c = {}
    rbank_free = {0: 0, 1: 0}
    kbank_free = {0: 0, 1: 0}
    vb_free = {0: 0, 1: 0}
    aall_free = 0
    zblk_vs = {}
    resid_vs = {}
    ka_prev = 0

    def emit_A(ck):
        nonlocal xq_free, aall_free
        xq_ss = op("s", lambda e, ck=ck: e.dma_start(
            out=xq3[:, :, :], in_=xqt3[:, ck, :, :]),
            inc=sXQ, waits=[(sv, xq_free), (sAO[0], attn_ao0), (sAO[1], attn_ao1)])
        wid = 513 if ck == NC2 - 1 else 512
        for j_ in range(4):
            op("g", lambda e, ck=ck, j_=j_, wid=wid: e.indirect_dma_start(
                aall[:, j_ * 1026: j_ * 1026 + 2 * wid], None, a2afl[ck][:, :],
                IndirectOffsetOnAxis(c_gix[j_][:, :], 0)),
                inc=sLD, waits=[(st, aall_free), (scc, scc_need[ck]), (sC, consts_s),
                                (sv, abcp)])
        aall_ss = sLD.v

        par = ck % 2
        off = 1 if ck == NC2 - 1 else 0
        for o in range(8):
            bank = psO0 if o % 2 == 0 else psO1
            pw = [(sLD, aall_ss), (sWO, 16), (sv, pb_free[o % 2])]

            def pmm(e, o=o, bank=bank, off=off, wid=wid):
                for a in range(8):
                    base = (a // 2) * 1026 + (a % 2) * wid + off
                    inst = e.matmul(bank[:, :], wo4[:, a, o, :],
                                    aall[:, base:base + 512],
                                    start=(a == 0), stop=(a == 7))
                return inst
            pc_ = op("t", pmm, inc=st, waits=pw)
            pb_free[o % 2] = op("v", lambda e, o=o, par=par, bank=bank:
                                e.tensor_tensor(
                vtmp3p[:, par, o, :], xq3[:, o, :], bank[:, :], OP.add),
                waits=[(st, pc_), (sXQ, xq_ss), (sOUT, out_dma[par])])
        resid_vs[ck] = sv.v
        xq_free = sv.v
        aall_free = st.v

    abcp = prologue_state["abcp"]
    emit_A(0)
    bnd_v = op("v", lambda e: e.tensor_copy(x2c3[:, :, 0], bcol[:, :]),
               waits=[(sv, prologue_state["bnd_v"]), (sAO[0], attn_ao0),
                      (sAO[1], attn_ao1)])
    ln1_v = bnd_v
    for ck in range(NC2):
        par = ck % 2
        src3 = vtmp3p[:, par, :, :]

        if ck > 0:
            ln1_v = op("v", lambda e: e.tensor_copy(x2c3[:, :, 0], x2c3[:, :, 512]),
                       waits=[(sv, ln1_v)])

        ln1_v = stats_and_apply(src3, x2c3, 1, max(resid_vs[ck], ln1_v),
                                bc_extra=[(sv, ka_prev)])

        if ck + 1 < NC2:
            emit_A(ck + 1)

        mixed = None
        mixed_h = {}
        for h in range(2):
            op("v", lambda e, h=h: e.tensor_tensor(
                sq2[:, :].rearrange("p (a t) -> p a t", a=4),
                x2c3[:, 4 * h:4 * h + 4, 1:513],
                x2c3[:, 4 * h:4 * h + 4, 0:512], OP.subtract),
                waits=[(sv, ln1_v), (sC, consts_s), (st, sq2_t_free)])
            for a in range(4 * h, 4 * h + 4):
                op("v", lambda e, a=a: e.scalar_tensor_tensor(
                    xr3[:, a, :], sq2[:, (a % 4) * 512:(a % 4) * 512 + 512],
                    c_mur[:, a:a + 1], x2c3[:, a, 0:512], OP.mult, OP.add))
                mixed = op("v", lambda e, a=a: e.scalar_tensor_tensor(
                    xk3[:, a, :], sq2[:, (a % 4) * 512:(a % 4) * 512 + 512],
                    c_muk[:, a:a + 1], x2c3[:, a, 0:512], OP.mult, OP.add))
            mixed_h[h] = mixed

        for o in range(8):
            bank = psA if o % 2 == 0 else psB

            def rmm0(e, o=o, bank=bank):
                for a in range(4):
                    inst = e.matmul(bank[:, :], w24[:, 0, a, o, :], xr3[:, a, :],
                                    start=(a == 0), stop=False)
                return inst
            op("t", rmm0, inc=st,
               waits=[(sv, mixed_h[0]), (sWR, 16), (sa, rbank_free[o % 2])])

            def rmm1(e, o=o, bank=bank):
                for a in range(4, 8):
                    inst = e.matmul(bank[:, :], w24[:, 0, a, o, :], xr3[:, a, :],
                                    start=False, stop=(a == 7))
                return inst
            rc = op("t", rmm1, inc=st, waits=[(sv, mixed_h[1])])
            sig_c[o] = op("a", lambda e, o=o, bank=bank: e.activation(
                rg3[:, o, :], bank[:, :], AF.Sigmoid), waits=[(st, rc)])
            rbank_free[o % 2] = sig_c[o]

        for o in range(8):
            bank = psBC if o % 2 == 0 else psX

            def kmm(e, o=o, bank=bank):
                for a in range(8):
                    inst = e.matmul(bank[:, :], w24[:, 1, a, o, :], xk3[:, a, :],
                                    start=(a == 0), stop=(a == 7))
                return inst
            kc = op("t", kmm, inc=st,
                    waits=[(sv, mixed), (sWK, 16), (sv, kbank_free[o % 2])])

            op("v", lambda e, o=o, bank=bank: e.tensor_relu(
                sq2[:, (o % 2) * 512:(o % 2) * 512 + 512], bank[:, :]),
                waits=[(st, kc), (st, sq2_t_free), (sv, mixed)])
            ka_c[o] = op("v", lambda e, o=o: e.tensor_tensor(
                ka3[:, o, :], sq2[:, (o % 2) * 512:(o % 2) * 512 + 512],
                sq2[:, (o % 2) * 512:(o % 2) * 512 + 512], OP.mult))
            kbank_free[o % 2] = ka_c[o]
        ka_done = sv.v
        ka_prev = ka_done


        for o in range(8):
            bank = psA if o % 2 == 0 else psB

            def vmm(e, o=o, bank=bank):
                for a in range(8):
                    inst = e.matmul(bank[:, :], w24[:, 2, a, o, :], ka3[:, a, :],
                                    start=(a == 0), stop=(a == 7))
                return inst
            vc = op("t", vmm, inc=st,
                    waits=[(sv, ka_done), (sWV, 16), (sa, rbank_free[o % 2]),
                           (sv, vb_free[o % 2])])

            op("v", lambda e, o=o, par=par, bank=bank: e.tensor_tensor(
                vtmp3p[:, par, o, :], rg3[:, o, :], bank[:, :], OP.mult),
                waits=[(st, vc)])
            vb_free[o % 2] = op("v", lambda e, o=o, par=par: e.tensor_tensor(
                vtmp3p[:, par, o, :], vtmp3p[:, par, o, :], x2c3[:, o, 1:513], OP.add))
        ffn_v = sv.v

        ln2_v = stats_and_apply(src3, src3, 0, ffn_v, bc_extra=[(sv, ka_done)],
                                pls=(psBC, psX))
        out_dma[par] = op("s", lambda e, ck=ck, par=par: e.dma_start(
            out=out3[:, :, ck * 512:(ck + 1) * 512], in_=vtmp3p[:, par, :, :]),
            inc=sOUT, waits=[(sv, ln2_v)])
        out_dma_all = sOUT.v

    final_ss = out_dma_all
    op("s", lambda e: e.nop(), waits=[(sOUT, final_ss)])

    with nc.Block() as block:
        @block.tensor
        def _(e):
            for f in prog["t"]:
                f(e)

        @block.vector
        def _(e):
            for f in prog["v"]:
                f(e)

        @block.scalar
        def _(e):
            for f in prog["a"]:
                f(e)

        @block.gpsimd
        def _(e):
            for f in prog["g"]:
                f(e)

        @block.sync
        def _(e):
            for f in prog["s"]:
                f(e)

    return nc, st_ctx


def stage_inputs(T, inputs):
    x = np.asarray(inputs["L0_attn_x"], dtype=np.float32)
    w_qkv = np.asarray(inputs["L0_attn_w_qkv"], dtype=np.float32)
    w_out = np.asarray(inputs["L0_attn_w_out"], dtype=np.float32)
    head_scales = np.asarray(inputs["L0_attn_head_scales"], dtype=np.float32)
    W_r = np.asarray(inputs["L0_ffn_W_r"], dtype=np.float32)
    W_k = np.asarray(inputs["L0_ffn_W_k"], dtype=np.float32)
    W_v = np.asarray(inputs["L0_ffn_W_v"], dtype=np.float32)
    mu_r = np.asarray(inputs["L0_ffn_mu_r"], dtype=np.float32)
    mu_k = np.asarray(inputs["L0_ffn_mu_k"], dtype=np.float32)

    TQ = T // 4
    dirs = _dirs()

    def wpre(w):
        return np.ascontiguousarray(
            w.reshape(8, 128, 8, 128).transpose(1, 0, 2, 3).reshape(128, 8192))

    ident = np.eye(128, dtype=np.float32).astype(ml_dtypes.bfloat16)
    onesb = np.ones((128, 1), dtype=np.float32).astype(ml_dtypes.bfloat16)
    onesc = np.ones((128, 1), dtype=np.float32)
    onesr = np.ones((1, 128), dtype=np.float32)
    tri = np.where(np.arange(128)[:, None] <= np.arange(128)[None, :], 0.0, NEG).astype(np.float32)
    mur_s = np.ascontiguousarray(mu_r.reshape(8, 128).T)
    muk_s = np.ascontiguousarray(mu_k.reshape(8, 128).T)
    wout_p = wpre(w_out)
    wr_p = wpre(np.ascontiguousarray(W_r.T))
    wk_p = wpre(np.ascontiguousarray(W_k.T))
    wv_p = wpre(np.ascontiguousarray(W_v.T))

    NTl = T // 512
    NC2l = TQ // 512
    in_maps = []
    for c in range(8):
        b, r = divmod(c, 4)
        h0, h1 = 2 * r, 2 * r + 1
        xt_ = np.ascontiguousarray(x[b].T)
        xt_pre = np.ascontiguousarray(
            xt_.reshape(8, 128, NTl, 512).transpose(1, 2, 0, 3).reshape(128, 8 * T))
        xq_slice = xt_[:, r * TQ:(r + 1) * TQ]
        xqt_pre = np.ascontiguousarray(
            xq_slice.reshape(8, 128, NC2l, 512).transpose(1, 2, 0, 3).reshape(128, 8 * TQ))
        if r == 0:
            xql_ = np.zeros((128, 8), dtype=np.float32)
        else:
            xql_ = np.ascontiguousarray(
                xt_[:, r * TQ - 1].reshape(8, 128).T)
        wq = np.empty((128, 6, 8, 128), dtype=np.float32)
        for si, (stype, h) in enumerate([(0, h0), (0, h1), (1, h0), (1, h1), (2, h0), (2, h1)]):
            blk = w_qkv[:, stype * 1024 + h * 128: stype * 1024 + h * 128 + 128]
            wq[:, si, :, :] = blk.reshape(8, 128, 128).transpose(1, 0, 2)
        hs = head_scales[[h0, h1]].reshape(1, 2).astype(np.float32)
        d3c = np.empty((3, 6), dtype=np.float32)
        for u, h in enumerate((h0, h1)):
            d = dirs[h]
            d3c[:, 3 * u:3 * u + 3] = (np.outer(d, d) / SCL).astype(np.float32)
        in_maps.append({
            "xt": xt_pre,
            "xqt": xqt_pre,
            "xqlp": xql_,
            "wqkv": np.ascontiguousarray(wq.reshape(128, 6 * 8 * 128)),
            "wout": wout_p,
            "wr": wr_p,
            "wk": wk_p,
            "wv": wv_p,
            "mur": mur_s,
            "muk": muk_s,
            "hs2": hs,
            "d3": d3c,
            "identp": ident,
            "onesb": onesb,
            "onesbm": np.ones((128, 128), dtype=np.float32).astype(ml_dtypes.bfloat16),
            "onescm": np.ones((128, 128), dtype=np.float32),
            "onesc": onesc,
            "onesr": onesr,
            "trim": tri,
            "epsv": np.full((128, 1), EPS, dtype=np.float32),
            "gidx": ((4 * b + np.arange(4, dtype=np.int32))[:, None] * 128
                     + np.arange(128, dtype=np.int32)[None, :]).astype(np.int32),
        })
    return in_maps


_CACHE = {}


def kernel(**inputs):
    T = np.asarray(inputs["L0_attn_x"]).shape[1]
    if T not in _CACHE:
        _CACHE[T] = build_program(T)
    nc, _ctx = _CACHE[T]
    in_maps = stage_inputs(T, inputs)
    res = run_bass_kernel_spmd(nc, in_maps, core_ids=list(range(8)))
    TQ = T // 4
    full = np.empty((2, T, 1024), dtype=np.float32)
    for c in range(8):
        b, r = divmod(c, 4)
        full[b, r * TQ:(r + 1) * TQ, :] = res.results[c]["out"].T
    return full


# version 20
# speedup vs baseline: 1.1406x; 1.0440x over previous
import os

os.environ.setdefault("TRN_ALLOW_ICI_ALL_TO_ALL", "1")

import math
import numpy as np
import ml_dtypes

import concourse.bass as bass
import concourse.mybir as mybir
from concourse.bass import AutoArena, IndirectOffsetOnAxis
from concourse import dve_ops
from concourse.bass_utils import run_bass_kernel_spmd

F32 = mybir.dt.float32
BF16 = mybir.dt.bfloat16
AX = mybir.AxisListType.X
OP = mybir.AluOpType
AF = mybir.ActivationFunctionType

C = 1024
HD = 128
SCL = 1.0 / math.sqrt(HD)
EPS = 1e-5
NEG = -1.0e9


def _dirs():
    t = np.array(
        [[(i >> 2 & 1) * 2 - 1, (i >> 1 & 1) * 2 - 1, (i & 1) * 2 - 1] for i in range(8)],
        dtype=np.float64,
    )
    return t / np.sqrt(3.0)


class Ctr:
    def __init__(self, sem, step=1):
        self.sem = sem
        self.step = step
        self.v = 0

    def bump(self, n=None):
        self.v += self.step if n is None else n
        return self.v


def build_program(T):
    TQ = T // 4
    NT = T // 512
    QG = T // 512
    PQ = max(QG // 4, 1)
    NC2 = TQ // 512
    KTT = T // 128

    nc = bass.Bass("TRN2", target_bir_lowering=False)

    xt = nc.dram_tensor("xt", [128, 8 * T], F32, kind="ExternalInput").ap()
    xqt = nc.dram_tensor("xqt", [128, 8 * TQ], F32, kind="ExternalInput").ap()
    xqlp = nc.dram_tensor("xqlp", [128, 8], F32, kind="ExternalInput").ap()
    wqkv = nc.dram_tensor("wqkv", [128, 6 * 8 * 128], F32, kind="ExternalInput").ap()
    wout = nc.dram_tensor("wout", [128, 8 * 8 * 128], F32, kind="ExternalInput").ap()
    wr = nc.dram_tensor("wr", [128, 8 * 8 * 128], F32, kind="ExternalInput").ap()
    wk = nc.dram_tensor("wk", [128, 8 * 8 * 128], F32, kind="ExternalInput").ap()
    wv = nc.dram_tensor("wv", [128, 8 * 8 * 128], F32, kind="ExternalInput").ap()
    mur = nc.dram_tensor("mur", [128, 8], F32, kind="ExternalInput").ap()
    muk = nc.dram_tensor("muk", [128, 8], F32, kind="ExternalInput").ap()
    hs2 = nc.dram_tensor("hs2", [1, 2], F32, kind="ExternalInput").ap()
    d3 = nc.dram_tensor("d3", [3, 6], F32, kind="ExternalInput").ap()
    identp = nc.dram_tensor("identp", [128, 128], BF16, kind="ExternalInput").ap()
    onesb = nc.dram_tensor("onesb", [128, 1], BF16, kind="ExternalInput").ap()
    onesbm = nc.dram_tensor("onesbm", [128, 128], BF16, kind="ExternalInput").ap()
    onescm = nc.dram_tensor("onescm", [128, 128], F32, kind="ExternalInput").ap()
    onesc = nc.dram_tensor("onesc", [128, 1], F32, kind="ExternalInput").ap()
    onesr = nc.dram_tensor("onesr", [1, 128], F32, kind="ExternalInput").ap()
    trim = nc.dram_tensor("trim", [128, 128], F32, kind="ExternalInput").ap()
    epsv = nc.dram_tensor("epsv", [128, 1], F32, kind="ExternalInput").ap()
    gidx = nc.dram_tensor("gidx", [4, 128], mybir.dt.int32, kind="ExternalInput").ap()
    out = nc.dram_tensor("out", [C, TQ], F32, kind="ExternalOutput").ap()

    wrd = nc.dram_tensor("wrd", [128, 8 * 8 * 128], BF16).ap()
    wkd = nc.dram_tensor("wkd", [128, 8 * 8 * 128], BF16).ap()
    wvd = nc.dram_tensor("wvd", [128, 8 * 8 * 128], BF16).ap()
    a2ai = [nc.dram_tensor(f"a2ai{j}", [8, 128, 2, 512 + (1 if j == NC2 - 1 else 0)],
                           BF16).ap() for j in range(NC2)]
    a2ao = [nc.dram_tensor(f"a2ao{j}", [8, 128, 2, 512 + (1 if j == NC2 - 1 else 0)],
                           BF16).ap() for j in range(NC2)]

    import contextlib

    st_ctx = contextlib.ExitStack()
    sbp = lambda name, shape, dt: st_ctx.enter_context(nc.sbuf_tensor(name, shape, dt))
    psx_ = lambda name: st_ctx.enter_context(nc.psum_tensor(name, [128, 512], F32))
    sem = lambda name: st_ctx.enter_context(nc.semaphore(name))

    c_id = sbp("c_id", [128, 128], BF16)
    c_1b = sbp("c_1b", [128, 1], BF16)
    c_1bm = sbp("c_1bm", [128, 128], BF16)
    c_1cm = sbp("c_1cm", [128, 128], F32)
    c_1c = sbp("c_1c", [128, 1], F32)
    c_1r = sbp("c_1r", [1, 128], F32)
    c_tri = sbp("c_tri", [128, 128], F32)
    c_d3 = sbp("c_d3", [3, 6], F32)
    c_hs = sbp("c_hs", [1, 2], F32)
    c_mur = sbp("c_mur", [128, 8], F32)
    c_muk = sbp("c_muk", [128, 8], F32)
    zcol = sbp("zcol", [128, 1], BF16)
    c_eps = sbp("c_eps", [128, 1], F32)
    c_gix = [sbp(f"c_gix{j}", [128, 1], mybir.dt.int32) for j in range(4)]
    lrec = sbp("lrec", [1, 512], F32)
    mx = sbp("mx", [1, 64], F32)
    hsr = sbp("hsr", [3, 2], F32)
    A3 = sbp("A3", [3, 6], BF16)
    cneg = sbp("cneg", [128, 2], F32)
    xql = sbp("xql", [128, 8], F32)
    wo_sb = sbp("wo_sb", [128, 8 * 8 * 128], BF16)
    vcol = sbp("vcol", [128, 8], F32)
    mi2 = sbp("mi2", [128, 2], F32)
    bsq8 = sbp("bsq8", [128, 8], F32)
    bcol = sbp("bcol", [128, 8], F32)

    arA = AutoArena(nc)
    wq_sb = arA.sbuf_tensor("wq_sb", [128, 6 * 8 * 128], BF16)
    xsb = arA.sbuf_tensor("xsb", [128, 2 * 8 * 512], BF16)
    Qt = arA.sbuf_tensor("Qt", [128, 2 * T], BF16)
    Kt = arA.sbuf_tensor("Kt", [128, 2 * T], BF16)
    Vt = arA.sbuf_tensor("Vt", [128, 2 * T], BF16)
    vstg = arA.sbuf_tensor("vstg", [128, 512], BF16)
    pt = arA.sbuf_tensor("pt", [128, 8 * 512], BF16)
    lsum = arA.sbuf_tensor("lsum", [128, 2 * 512], BF16)
    sq = arA.sbuf_tensor("sq", [128, 2 * 512], BF16)
    lb = arA.sbuf_tensor("lb", [128, 512], F32)
    rscr = arA.sbuf_tensor("rscr", [128, 512], F32)
    aosg = arA.sbuf_tensor("aosg", [128, 2 * 512], BF16)
    arA.allocate()

    arA.free()
    arB = AutoArena(nc)
    aall = arB.sbuf_tensor("aall", [128, 4 * 1026], BF16)
    aall_b = arB.sbuf_tensor("aall_b", [128, 8], BF16)
    w2_sb = arB.sbuf_tensor("w2_sb", [128, 3 * 8 * 8 * 128], BF16)
    xq = arB.sbuf_tensor("xq", [128, 8 * 512], F32)
    x2c = arB.sbuf_tensor("x2c", [128, 8 * 513], F32)
    vtmp = arB.sbuf_tensor("vtmp", [128, 2 * 8 * 512], F32)
    sq2 = arB.sbuf_tensor("sq2", [128, 4 * 512], F32)
    mb_sb2 = arB.sbuf_tensor("mb_sb2", [128, 512], F32)
    ib_sb2 = arB.sbuf_tensor("ib_sb2", [128, 512], F32)
    xr_sb = arB.sbuf_tensor("xr_sb", [128, 8 * 512], BF16)
    xk_sb = arB.sbuf_tensor("xk_sb", [128, 8 * 512], BF16)
    ka_sb = arB.sbuf_tensor("ka_sb", [128, 8 * 512], BF16)
    rg_sb = arB.sbuf_tensor("rg_sb", [128, 8 * 512], F32)
    arB.allocate()
    t1_sb = None

    psA = psx_("psA")
    psB = psx_("psB")
    psO0 = psx_("psO0")
    psO1 = psx_("psO1")
    psL0 = psx_("psL0")
    psL1 = psx_("psL1")
    psBC = psx_("psBC")
    psX = psx_("psX")

    sC = Ctr(sem("sC"), 16)
    sWQ = Ctr(sem("sWQ"), 16)
    sX = [Ctr(sem("sX0"), 16), Ctr(sem("sX1"), 16)]
    sW2 = Ctr(sem("sW2"), 16)
    sAO = [Ctr(sem("sAO0"), 16), Ctr(sem("sAO1"), 16)]
    sZ = Ctr(sem("sZ"), 16)
    sLD = Ctr(sem("sLD"), 16)
    sLD2 = Ctr(sem("sLD2"), 16)
    sWO = Ctr(sem("sWO"), 16)
    sAB = Ctr(sem("sAB"), 16)
    sXQL = Ctr(sem("sXQL"), 16)
    sXQ = Ctr(sem("sXQ"), 16)
    sOUT = Ctr(sem("sOUT"), 16)
    sgp = Ctr(sem("sgp"), 1)
    st = Ctr(sem("st"), 1)
    sv = Ctr(sem("sv"), 1)
    sa = Ctr(sem("sa"), 1)
    scc = Ctr(sem("scc"), 1)

    prog = {k: [] for k in ("t", "v", "a", "g", "s")}
    highwater = {}

    def op(eng, builder, inc=None, waits=()):
        if eng == "ad":
            eng = "a"
        elif eng == "gc":
            eng = "g"
            waits = list(waits) + [(sgp, sgp.v)]
            inc = sgp
        elif eng == "v":
            waits = list(waits) + [(sv, sv.v)]
            inc = sv
        elif eng == "a":
            waits = list(waits) + [(sa, sa.v)]
            inc = sa
        needed = []
        for ctr, val in waits:
            if val <= 0:
                continue
            key = (eng, id(ctr))
            if highwater.get(key, 0) >= val:
                continue
            highwater[key] = val
            needed.append((ctr.sem, val))
        inc_sem = None
        inc_n = 0
        if inc is not None:
            inc.bump()
            inc_sem = inc.sem
            inc_n = inc.step

        def f(e, builder=builder, needed=tuple(needed), inc_sem=inc_sem, inc_n=inc_n):
            for s_, v_ in needed:
                e.wait_ge(s_, v_)
            inst = builder(e)
            if inc_sem is not None:
                inst.then_inc(inc_sem, inc_n)

        prog[eng].append(f)
        return inc.v if inc is not None else None

    for dst, src in ((c_id[:, :], identp), (c_1b[:, :], onesb), (c_1c[:, :], onesc),
                     (c_1bm[:, :], onesbm), (c_1cm[:, :], onescm),
                     (c_1r[:, :], onesr), (c_tri[:, :], trim),
                     (c_d3[:, :], d3), (c_hs[:, :], hs2), (c_mur[:, :], mur),
                     (c_muk[:, :], muk), (c_eps[:, :], epsv),
                     (c_gix[0][:, :], gidx[0:1, :]), (c_gix[1][:, :], gidx[1:2, :]),
                     (c_gix[2][:, :], gidx[2:3, :]), (c_gix[3][:, :], gidx[3:4, :]),
                     (xql[:, :], xqlp)):
        op("s", lambda e, d=dst, s=src: e.dma_start(out=d, in_=s), inc=sC)
    consts_s = sC.v
    op("g", lambda e: e.dma_start(out=wq_sb[:, :], in_=wqkv), inc=sWQ)
    op("g", lambda e: e.dma_start(out=wo_sb[:, :], in_=wout), inc=sWO)
    op("g", lambda e: e.dma_start(out=wrd, in_=wr), inc=sW2)
    op("g", lambda e: e.dma_start(out=wkd, in_=wk), inc=sW2)
    op("g", lambda e: e.dma_start(out=wvd, in_=wv), inc=sW2)
    w2d_g = sW2.v

    zv = op("v", lambda e: e.memset(zcol[:, :], 0.0), inc=sv)
    for sh in (0, 4):
        for u in range(2):
            op("s", lambda e, sh=sh, u=u: e.dma_start(
                out=a2ai[NC2 - 1][sh, :, u, 0:1], in_=zcol[:, :],
                allow_slow_non_contiguous=True),
               inc=sZ, waits=[(sv, zv)])
    zero_ss = sZ.v

    xt3 = xt.rearrange("p (tc a t) -> p tc a t", tc=NT, a=8)
    xsb3 = xsb[:, :].rearrange("p (par a t) -> p par a t", par=2, a=8)
    wq4 = wq_sb[:, :].rearrange("p (s a d) -> p s a d", s=6, a=8)
    Qt2 = Qt[:, :].rearrange("p (u t) -> p u t", u=2)
    Kt2 = Kt[:, :].rearrange("p (u t) -> p u t", u=2)
    Vt3 = Vt[:, :].rearrange("p (u kt d) -> p u kt d", u=2, kt=KTT)

    x_done = []
    x_cons = []
    bank_free = {0: 0, 1: 0}
    psx_free = 0

    for tc in range(NT):
        par = tc % 2
        w_ = [] if tc < 2 else [(st, x_cons[tc - 2])]
        op("g", lambda e, par=par, tc=tc:
           e.dma_start(out=xsb3[:, par, :, :], in_=xt3[:, tc, :, :]),
           inc=sX[par], waits=w_)
        x_done.append((sX[par], sX[par].v))

        for ob in range(6):
            bname = (tc * 6 + ob) % 2
            bank = psA if bname == 0 else psB

            def mms(e, bank=bank, ob=ob, par=par):
                for a in range(8):
                    inst = e.matmul(bank[:, :], wq4[:, ob, a, :], xsb3[:, par, a, :],
                                    start=(a == 0), stop=(a == 7))
                return inst
            mmc = op("t", mms, inc=st,
                     waits=[x_done[tc], (sWQ, 16), (sv, bank_free[bname])])

            u = ob % 2
            kind = ob // 2
            if kind == 0:
                dst = Qt2[:, u, tc * 512:(tc + 1) * 512]
            elif kind == 1:
                dst = Kt2[:, u, tc * 512:(tc + 1) * 512]
            else:
                dst = vstg[:, :]
            bank_free[bname] = op("v", lambda e, d=dst, b=bank: e.tensor_copy(d, b[:, :]),
                                  inc=sv, waits=[(st, mmc)])

            if kind == 2:
                vst = sv.v

                def vtr(e):
                    psXb = psX[:, :].bitcast(BF16)
                    for jj in range(4):
                        inst = e.transpose(psXb[:, jj * 128:(jj + 1) * 128],
                                           vstg[:, jj * 128:(jj + 1) * 128], c_id[:, :])
                    return inst
                trc = op("t", vtr, inc=st,
                         waits=[(sv, vst), (sv, psx_free), (sC, consts_s)])
                psx_free = op("v", lambda e, u=u, tc=tc: e.tensor_copy(
                    Vt3[:, u, tc * 4:(tc + 1) * 4, :].rearrange("p k d -> p (k d)"),
                    psX[:, :].bitcast(BF16)[:, 0:512]), inc=sv, waits=[(st, trc)])
        x_cons.append(st.v)

    ph1_v = sv.v

    psbc_free = 0
    adj_frees = {0: 0, 1: 0}
    psl_frees = {0: 0, 1: 0}
    sq_frees = {0: 0, 1: 0}
    for u in range(2):
        hc = op("t", lambda e, u=u: e.matmul(psX[0:3, 0:1], c_1r[0:1, 0:3],
                                             c_hs[0:1, u:u + 1], start=True, stop=True),
                inc=st, waits=[(sC, consts_s), (sv, psx_free)])
        hv = op("v", lambda e, u=u: e.tensor_copy(hsr[:, u:u + 1], psX[0:3, 0:1]),
                inc=sv, waits=[(st, hc)])
        psx_free = hv
        a3done = op("v", lambda e, u=u: e.tensor_scalar(
            A3[:, 3 * u:3 * u + 3], c_d3[:, 3 * u:3 * u + 3], hsr[:, u:u + 1], None, OP.mult),
            inc=sv, waits=[(sv, hv), (sC, consts_s)])

        for tc in range(NT):
            abank = psBC if tc % 2 == 0 else psX
            amc = op("t", lambda e, u=u, tc=tc, abank=abank: e.matmul(
                abank[0:3, 0:512], A3[:, 3 * u:3 * u + 3],
                Qt2[0:3, u, tc * 512:(tc + 1) * 512], start=True, stop=True),
                inc=st, waits=[(sv, a3done), (sv, ph1_v),
                               (sv, adj_frees[tc % 2])])
            adj_frees[tc % 2] = op("v", lambda e, u=u, tc=tc, abank=abank:
                                   e.tensor_tensor(
                Qt2[0:3, u, tc * 512:(tc + 1) * 512],
                Qt2[0:3, u, tc * 512:(tc + 1) * 512],
                abank[0:3, 0:512], OP.add), inc=sv, waits=[(st, amc)])
        adj_v = sv.v
        psbc_free = max(psbc_free, adj_frees[0])
        psx_free = max(psx_free, adj_frees[1])

        for which in ("q", "k"):
            buf = Qt2 if which == "q" else Kt2
            base = 0 if which == "q" else 16
            dep = adj_v if which == "q" else ph1_v
            for tc in range(NT):
                sl = tc % 2
                bank = psL0 if sl == 0 else psL1
                sqc = op("a", lambda e, u=u, tc=tc, sl=sl, buf=buf: e.activation(
                    sq[:, sl * 512:sl * 512 + 512],
                    buf[:, u, tc * 512:(tc + 1) * 512], AF.Square),
                    waits=[(sv, dep), (st, sq_frees[sl])])
                oc = op("t", lambda e, sl=sl, bank=bank: e.matmul(
                    bank[0:1, :], c_1b[:, 0:1], sq[:, sl * 512:sl * 512 + 512],
                    start=True, stop=True),
                    inc=st, waits=[(sa, sqc), (sC, consts_s), (sv, psl_frees[sl])])
                sq_frees[sl] = oc
                psl_frees[sl] = op("v", lambda e, tc=tc, sl=sl, base=base, bank=bank:
                                   e.reduce_max(
                    mx[0:1, base + tc:base + tc + 1], bank[0:1, 0:512], axis=AX),
                    inc=sv, waits=[(st, oc)])

        op("v", lambda e: e.reduce_max(mx[0:1, 32:33], mx[0:1, 0:NT], axis=AX),
           inc=sv, waits=[(sv, max(psl_frees[0], psl_frees[1]))])
        f0 = op("v", lambda e: e.reduce_max(mx[0:1, 33:34], mx[0:1, 16:16 + NT], axis=AX),
                inc=sv)
        fc = op("v", lambda e: e.tensor_tensor(mx[0:1, 34:35], mx[0:1, 32:33],
                                               mx[0:1, 33:34], OP.mult),
                inc=sv, waits=[(sv, f0)])

        op("a", lambda e: e.activation(mx[0:1, 35:36], mx[0:1, 34:35], AF.Sqrt),
           waits=[(sv, fc)])
        cc_ = op("a", lambda e: e.mul(mx[0:1, 36:37], mx[0:1, 35:36], -SCL))
        cbc_c = op("t", lambda e: e.matmul(psBC[:, 0:1], c_1r[0:1, :], mx[0:1, 36:37],
                                           start=True, stop=True),
                   inc=st, waits=[(sa, cc_), (sv, psbc_free)])
        psbc_free = op("v", lambda e, u=u: e.tensor_copy(cneg[:, u:u + 1], psBC[:, 0:1]),
                       inc=sv, waits=[(st, cbc_c)])
    bounds_v = sv.v

    xqt3 = xqt.rearrange("p (ck a t) -> p ck a t", ck=NC2, a=8)

    wo4 = wo_sb[:, :].rearrange("p (a o d) -> p a o d", a=8, o=8)
    w24 = w2_sb[:, :].rearrange("p (w a o d) -> p w a o d", w=3, a=8, o=8)
    xq3 = xq[:, :].rearrange("p (a t) -> p a t", a=8)
    x2c3 = x2c[:, :].rearrange("p (a t) -> p a t", a=8)
    vtmp3p = vtmp[:, :].rearrange("p (par a t) -> p par a t", par=2, a=8)
    xr3 = xr_sb[:, :].rearrange("p (a t) -> p a t", a=8)
    xk3 = xk_sb[:, :].rearrange("p (a t) -> p a t", a=8)
    ka3 = ka_sb[:, :].rearrange("p (a t) -> p a t", a=8)
    rg3 = rg_sb[:, :].rearrange("p (a t) -> p a t", a=8)
    out3 = out.rearrange("(a p) t -> p a t", p=128)

    rounds = [NC2 - 1] + list(range(NC2 - 1))
    a2afl = [a2ao[j].rearrange("r p u t -> (r p) (u t)") for j in range(NC2)]
    scc_need = {jr: i + 1 for i, jr in enumerate(rounds)}
    prologue_state = {}

    def emit_prologue():
        nonlocal psx_free
        ab_ss = 0
        for j_ in range(4):
            ab_ss = op("g", lambda e, j_=j_: e.indirect_dma_start(
                aall[:, j_ * 1026:(j_ + 1) * 1026], None, a2afl[NC2 - 1][:, :],
                IndirectOffsetOnAxis(c_gix[j_][:, :], 0)),
                inc=sAB, waits=[(scc, scc_need[NC2 - 1]), (sC, consts_s)])
        abview = bass.AP(aall, 0, [[4104, 128], [1026, 4], [513, 2]])
        abcp = op("v", lambda e: e.tensor_copy(
            aall_b[:, :].rearrange("p (j u) -> p j u", u=2), abview),
            waits=[(sAB, ab_ss)])

        def bpmm(e):
            for o in range(8):
                for a in range(8):
                    inst = e.matmul(psX[:, o:o + 1], wo4[:, a, o, :],
                                    aall_b[:, a:a + 1],
                                    start=(a == 0), stop=(a == 7))
            return inst
        bp_c = op("t", bpmm, inc=st,
                  waits=[(sv, abcp), (sWO, 16), (sv, psx_free)])
        bres_c = op("v", lambda e: e.tensor_tensor(vcol[:, :], xql[:, :], psX[:, 0:8], OP.add),
                    inc=sv, waits=[(st, bp_c), (sC, consts_s)])
        psx_free = bres_c
        bsq_c = op("a", lambda e: e.activation(bsq8[:, :], vcol[:, :], AF.Square),
                   waits=[(sv, bres_c)])

        def bstat(e):
            e.matmul(psX[0:1, 16:24], c_1c[:, 0:1], vcol[:, :], start=True, stop=True)
            return e.matmul(psX[0:1, 32:40], c_1c[:, 0:1], bsq8[:, :], start=True, stop=True)
        bst_c = op("t", bstat, inc=st, waits=[(sa, bsq_c), (sv, bres_c)])

        op("v", lambda e: e.reduce_sum(mx[0:1, 40:41], psX[0:1, 16:24], axis=AX),
           inc=sv, waits=[(st, bst_c)])
        bm0 = op("v", lambda e: e.reduce_sum(mx[0:1, 41:42], psX[0:1, 32:40], axis=AX),
                 inc=sv)

        op("v", lambda e: e.tensor_scalar(mx[0:1, 40:41], mx[0:1, 40:41], 1.0 / C,
                                          None, OP.mult), waits=[(sv, bm0)])
        op("v", lambda e: e.tensor_scalar(mx[0:1, 41:42], mx[0:1, 41:42], 1.0 / C,
                                          None, OP.mult))
        op("v", lambda e: e.tensor_tensor(mx[0:1, 42:43], mx[0:1, 40:41], mx[0:1, 40:41],
                                          OP.mult))
        bmv_c = op("v", lambda e: e.tensor_tensor(mx[0:1, 41:42], mx[0:1, 41:42],
                                                  mx[0:1, 42:43], OP.subtract))
        bsi_c = op("a", lambda e: e.activation(mx[0:1, 41:42], mx[0:1, 41:42], AF.Sqrt, bias=c_eps[0:1, 0:1]),
                   inc=sa, waits=[(sv, bmv_c)])
        bir_c = op("v", lambda e: e.reciprocal(mx[0:1, 41:42], mx[0:1, 41:42]),
                   inc=sv, waits=[(sa, bsi_c)])
        bbc_c = op("t", lambda e: e.matmul(psX[:, 40:42], c_1r[0:1, :], mx[0:1, 40:42],
                                           start=True, stop=True),
                   inc=st, waits=[(sv, bir_c)])
        mi_ev = op("v", lambda e: e.tensor_copy(mi2[:, :], psX[:, 40:42]),
                   inc=sv, waits=[(st, bbc_c)])
        bnd_v = op("v", lambda e: e.tensor_scalar(
            bcol[:, :], vcol[:, :], mi2[:, 0:1], mi2[:, 1:2], OP.subtract, OP.mult),
            waits=[(sv, mi_ev)])

        prologue_state["bnd_v"] = bnd_v
        prologue_state["abcp"] = abcp

    pt2 = pt[:, :].rearrange("p (r t) -> p r t", r=8)
    ls2 = lsum[:, :].rearrange("p (r t) -> p r t", r=2)
    aosg2 = aosg[:, :].rearrange("p (r t) -> p r t", r=2)

    exp_ring = {0: 0, 1: 0, 2: 0}
    pt_ring = {k: 0 for k in range(8)}
    pt_ring_v = {k: 0 for k in range(8)}
    lsum_free = {0: 0, 1: 0}
    ol_free = {0: 0, 1: 0}
    ao_dma = {0: 0, 1: 0}
    lb_free = 0

    qi = -1
    a2a_cs = {}
    for jr in rounds:
        for u in range(2):
          for qtr in range(4):
            qg = qtr * PQ + jr
            qi += 1
            ktmax = 4 * qg + 4
            ngrp = ktmax // 4
            obank = psO0 if qi % 2 == 0 else psO1
            lbank = psL0 if qi % 2 == 0 else psL1

            smm_cs = {}
            exp_cs = {}
            grp_add = {}

            def geom(kt):
                diag = kt - 4 * qg
                if diag >= 0:
                    return 128 * diag, 512 - 128 * diag
                return 0, 512

            sbanks = (psA, psB, psBC)
            for step in range(ktmax + 2):
                tri_c = None
                if step < ktmax:
                    kt = step
                    sbank = sbanks[kt % 3]
                    o0, ncols = geom(kt)
                    qc0 = qg * 512 + o0
                    smm_cs[kt] = op("t", lambda e, u=u, kt=kt, sbank=sbank, qc0=qc0,
                                    ncols=ncols, o0=o0:
                                    e.matmul(sbank[:, o0:o0 + ncols],
                                             Kt2[:, u, kt * 128:(kt + 1) * 128],
                                             Qt2[:, u, qc0:qc0 + ncols],
                                             start=True, stop=True),
                                    inc=st,
                                    waits=[(sv, bounds_v), (sa, exp_ring[kt % 3]),
                                           (sv, psbc_free if kt % 3 == 2 else 0)])
                    if kt - 4 * qg >= 0:
                        tri_c = op("v", lambda e, sbank=sbank, o0=o0: e.tensor_tensor(
                            sbank[:, o0:o0 + 128], sbank[:, o0:o0 + 128],
                            c_tri[:, :], OP.add),
                            waits=[(st, smm_cs[kt]), (sC, consts_s)])
                if step >= 2:
                    kt = step - 2
                    o0, ncols = geom(kt)
                    ow = [(sa, exp_cs[kt])]
                    if kt == 0:
                        ow.append((sv, ol_free[qi % 2]))
                    pt_ring[kt % 8] = op("t", lambda e, u=u, kt=kt, o0=o0, ncols=ncols,
                                         obank=obank,
                                         first=(kt == 0), last=(kt == ktmax - 1):
                                         e.matmul(obank[:, o0:o0 + ncols],
                                                  Vt3[:, u, kt, :],
                                                  pt2[:, kt % 8, o0:o0 + ncols],
                                                  start=first, stop=last),
                                         inc=st, waits=ow)
                    if kt % 4 == 1 and kt // 4 >= 1:
                        g = kt // 4 - 1
                        gp = 4 * (g % 2)
                        a1 = op("v", lambda e, g=g, gp=gp: e.tensor_tensor(
                            ls2[:, g % 2, :], pt2[:, gp, :], pt2[:, gp + 1, :],
                            OP.add),
                            waits=[(sa, exp_cs[4 * g + 3]),
                                   (st, lsum_free[g % 2])])
                        pt_ring_v[gp] = a1
                        pt_ring_v[gp + 1] = a1
                    if kt % 4 == 3 and kt // 4 >= 1:
                        g = kt // 4 - 1
                        gp = 4 * (g % 2)
                        op("v", lambda e, g=g, gp=gp: e.tensor_tensor(
                            ls2[:, g % 2, :], ls2[:, g % 2, :], pt2[:, gp + 2, :],
                            OP.add))
                        grp_add[g] = op("v", lambda e, g=g, gp=gp: e.tensor_tensor(
                            ls2[:, g % 2, :], ls2[:, g % 2, :], pt2[:, gp + 3, :],
                            OP.add))
                        pt_ring_v[gp + 2] = grp_add[g]
                        pt_ring_v[gp + 3] = grp_add[g]
                        ow2 = [(sv, grp_add[g])]
                        if g == 0:
                            ow2.append((sv, ol_free[qi % 2]))
                        lsum_free[g % 2] = op("t", lambda e, g=g, lbank=lbank,
                                              first=(g == 0):
                                              e.matmul(lbank[:, :], c_1bm[:, :],
                                                       ls2[:, g % 2, :],
                                                       start=first, stop=False),
                                              inc=st, waits=ow2)
                if step < ktmax:
                    kt = step
                    sbank = sbanks[kt % 3]
                    o0, ncols = geom(kt)
                    ew = [(st, smm_cs[kt]), (st, pt_ring[kt % 8]),
                          (sv, pt_ring_v[kt % 8])]
                    if tri_c is not None:
                        ew.append((sv, tri_c))
                    exp_cs[kt] = op("a", lambda e, u=u, sbank=sbank, kt=kt, o0=o0,
                                    ncols=ncols:
                                    e.activation(pt2[:, kt % 8, o0:o0 + ncols],
                                                 sbank[:, o0:o0 + ncols], AF.Exp,
                                                 bias=cneg[:, u:u + 1], scale=SCL),
                                    waits=ew)
                    exp_ring[kt % 3] = exp_cs[kt]

            g = ngrp - 1
            gp = 4 * (g % 2)
            for kk in range(4 * g + 1, 4 * g + 4):
                oo, nn = geom(kk)
                op("v", lambda e, kk=kk, oo=oo, gp=gp: e.memset(
                    pt2[:, gp + (kk % 4), 0:oo], 0.0),
                    waits=[(sa, exp_cs[kk]), (st, pt_ring[kk % 8])])
            op("v", lambda e, g=g, gp=gp: e.tensor_tensor(
                ls2[:, g % 2, :], pt2[:, gp, :], pt2[:, gp + 1, :], OP.add),
                waits=[(st, lsum_free[g % 2])])
            op("v", lambda e, g=g, gp=gp: e.tensor_tensor(
                ls2[:, g % 2, :], ls2[:, g % 2, :], pt2[:, gp + 2, :], OP.add))
            grp_add[g] = op("v", lambda e, g=g, gp=gp: e.tensor_tensor(
                ls2[:, g % 2, :], ls2[:, g % 2, :], pt2[:, gp + 3, :], OP.add))
            for kk in range(4 * g, 4 * g + 4):
                pt_ring_v[gp + (kk % 4)] = grp_add[g]
            ow2 = [(sv, grp_add[g])]
            if g == 0:
                ow2.append((sv, ol_free[qi % 2]))
            lsum_free[g % 2] = op("t", lambda e, g=g, lbank=lbank,
                                  first=(g == 0):
                                  e.matmul(lbank[:, :], c_1bm[:, :],
                                           ls2[:, g % 2, :],
                                           start=first, stop=True),
                                  inc=st, waits=ow2)

            kloop_t = st.v
            rec_c = op("v", lambda e, lbank=lbank: e.reciprocal(lb[:, :], lbank[:, :]),
                       waits=[(st, kloop_t), (sv, lb_free)])
            aev_c = op("v", lambda e, obank=obank, qi=qi: e.tensor_tensor(
                aosg2[:, qi % 2, :], obank[:, :], lb[:, :], OP.mult),
                waits=[(sv, rec_c), (sAO[qi % 2], ao_dma[qi % 2])])
            lb_free = aev_c
            ol_free[qi % 2] = aev_c

            extra = (jr == NC2 - 1) and (qtr < 3)
            sAO[qi % 2].bump((3 if extra else 1) * 16)
            bcol0 = 1 if jr == NC2 - 1 else 0

            def aodma(e, u=u, qi=qi, qtr=qtr, jr=jr, extra=extra, bcol0=bcol0,
                      sem_=sAO[qi % 2].sem):
                tin = a2ai[jr]
                if extra:
                    e.dma_start(out=tin[qtr + 1, :, u, 0:1],
                                in_=aosg2[:, qi % 2, 511:512],
                                allow_slow_non_contiguous=True).then_inc(sem_, 16)
                    e.dma_start(out=tin[qtr + 5, :, u, 0:1],
                                in_=aosg2[:, qi % 2, 511:512],
                                allow_slow_non_contiguous=True).then_inc(sem_, 16)
                e.dma_start(out=tin[qtr, :, u, bcol0:bcol0 + 512],
                            in_=aosg2[:, qi % 2, :]).then_inc(sem_, 16)
                return e.dma_start(
                    out=tin[qtr + 4, :, u, bcol0:bcol0 + 512], in_=aosg2[:, qi % 2, :])
            ao_dma[qi % 2] = op("s", aodma, inc=sAO[qi % 2], waits=[(sv, aev_c)])
        cw = [(sAO[0], sAO[0].v), (sAO[1], sAO[1].v)]
        if jr == NC2 - 1:
            cw.append((sZ, zero_ss))
        a2a_cs[jr] = op("g", lambda e, jr=jr: e.collective_compute(
            "AllToAll", OP.bypass, replica_groups=[[0, 1, 2, 3, 4, 5, 6, 7]],
            ins=[a2ai[jr].opt()], outs=[a2ao[jr].opt()]),
            inc=scc, waits=cw)
        if jr == rounds[min(2, len(rounds) - 1)]:
            emit_prologue()

    attn_ao0 = sAO[0].v
    attn_ao1 = sAO[1].v

    sWR = Ctr(sem("sWR"), 16)
    sWK = Ctr(sem("sWK"), 16)
    sWV = Ctr(sem("sWV"), 16)
    aow = [(sAO[0], attn_ao0), (sAO[1], attn_ao1), (sW2, w2d_g)]
    w23 = w2_sb[:, :].rearrange("p (w x) -> p w x", w=3)
    op("g", lambda e: e.dma_start(out=w23[:, 0, :], in_=wrd), inc=sWR, waits=aow)
    op("g", lambda e: e.dma_start(out=w23[:, 1, :], in_=wkd), inc=sWK)
    op("g", lambda e: e.dma_start(out=w23[:, 2, :], in_=wvd), inc=sWV)


    psl_pair_free = 0
    sq2_t_free = 0

    def stats_and_apply(src3, dst3, dst_off, dep_v, bc_extra=(), pls=None):
        nonlocal psl_pair_free, psbc_free, psx_free, sq2_t_free
        pl0, pl1 = (psL0, psL1) if pls is None else pls
        mb_sb = mb_sb2[:, :]
        ib_sb = ib_sb2[:, :]
        sqcs = {}
        for h in range(2):
            sqcs[h] = op("a", lambda e, h=h, src3=src3: e.activation(
                sq2[:, :], src3[:, 4 * h:4 * h + 4, :].rearrange("p a t -> p (a t)"),
                AF.Square),
                waits=[(sv, dep_v), (st, sq2_t_free)])
            for a in range(4 * h, 4 * h + 4):
                def s2mm(e, a=a, src3=src3, pl0=pl0, pl1=pl1):
                    e.matmul(pl0[:, :], c_1cm[:, :], src3[:, a, :],
                             start=(a == 0), stop=(a == 7))
                    return e.matmul(pl1[:, :], c_1cm[:, :],
                                    sq2[:, (a % 4) * 512:(a % 4) * 512 + 512],
                                    start=(a == 0), stop=(a == 7))
                op("t", s2mm, inc=st,
                   waits=[(sa, sqcs[h]), (sv, dep_v)] + list(bc_extra))
            sq2_t_free = st.v
        stats_t = st.v

        op("v", lambda e, pl0=pl0, mb_sb=mb_sb: e.tensor_scalar(
            mb_sb, pl0[:, :], 1.0 / C, None, OP.mult), waits=[(st, stats_t)])
        op("v", lambda e, pl1=pl1, ib_sb=ib_sb: e.tensor_scalar(
            ib_sb, pl1[:, :], 1.0 / C, None, OP.mult))
        op("v", lambda e, mb_sb=mb_sb: e.tensor_tensor(sq2[:, 0:512], mb_sb, mb_sb,
                                                       OP.mult))
        mv_c = op("v", lambda e, ib_sb=ib_sb: e.tensor_tensor(ib_sb, ib_sb,
                                                              sq2[:, 0:512], OP.subtract))
        psl_pair_free = mv_c
        isq_c = op("a", lambda e, ib_sb=ib_sb: e.activation(ib_sb, ib_sb, AF.Sqrt,
                                                            bias=c_eps[:, 0:1]),
                   waits=[(sv, mv_c)])
        bce_c = op("v", lambda e, ib_sb=ib_sb: e.reciprocal(ib_sb, ib_sb),
                   waits=[(sa, isq_c)])

        mbB = bass.AP(mb_sb2, 0, [[512, 128], [0, 8], [1, 512]])
        ibB = bass.AP(ib_sb2, 0, [[512, 128], [0, 8], [1, 512]])
        op("v", lambda e, src3=src3, mbB=mbB: e.tensor_tensor(
            src3[:, :, :], src3[:, :, :], mbB, OP.subtract),
            waits=[(sv, bce_c)])
        last = op("v", lambda e, src3=src3, dst3=dst3, dst_off=dst_off, ibB=ibB:
                  e.tensor_tensor(dst3[:, :, dst_off:dst_off + 512],
                                  src3[:, :, :], ibB, OP.mult))
        return last

    pb_free = {0: ol_free[0], 1: ol_free[1]}
    xq_free = 0
    out_dma = {0: 0, 1: 0}
    out_dma_all = 0
    ln1_v = 0
    sig_c = {}
    ka_c = {}
    rbank_free = {0: 0, 1: 0}
    kbank_free = {0: 0, 1: 0}
    vb_free = {0: 0, 1: 0}
    aall_free = 0
    zblk_vs = {}
    resid_vs = {}
    ka_prev = 0

    def emit_A(ck):
        nonlocal xq_free, aall_free
        xq_ss = op("s", lambda e, ck=ck: e.dma_start(
            out=xq3[:, :, :], in_=xqt3[:, ck, :, :]),
            inc=sXQ, waits=[(sv, xq_free), (sAO[0], attn_ao0), (sAO[1], attn_ao1)])
        wid = 513 if ck == NC2 - 1 else 512
        for j_ in range(4):
            op("g", lambda e, ck=ck, j_=j_, wid=wid: e.indirect_dma_start(
                aall[:, j_ * 1026: j_ * 1026 + 2 * wid], None, a2afl[ck][:, :],
                IndirectOffsetOnAxis(c_gix[j_][:, :], 0)),
                inc=sLD, waits=[(st, aall_free), (scc, scc_need[ck]), (sC, consts_s),
                                (sv, abcp)])
        aall_ss = sLD.v

        par = ck % 2
        off = 1 if ck == NC2 - 1 else 0
        for o in range(8):
            bank = psO0 if o % 2 == 0 else psO1
            pw = [(sLD, aall_ss), (sWO, 16), (sv, pb_free[o % 2])]

            def pmm(e, o=o, bank=bank, off=off, wid=wid):
                for a in range(8):
                    base = (a // 2) * 1026 + (a % 2) * wid + off
                    inst = e.matmul(bank[:, :], wo4[:, a, o, :],
                                    aall[:, base:base + 512],
                                    start=(a == 0), stop=(a == 7))
                return inst
            pc_ = op("t", pmm, inc=st, waits=pw)
            pb_free[o % 2] = op("v", lambda e, o=o, par=par, bank=bank:
                                e.tensor_tensor(
                vtmp3p[:, par, o, :], xq3[:, o, :], bank[:, :], OP.add),
                waits=[(st, pc_), (sXQ, xq_ss), (sOUT, out_dma[par])])
        resid_vs[ck] = sv.v
        xq_free = sv.v
        aall_free = st.v

    abcp = prologue_state["abcp"]
    emit_A(0)
    bnd_v = op("v", lambda e: e.tensor_copy(x2c3[:, :, 0], bcol[:, :]),
               waits=[(sv, prologue_state["bnd_v"]), (sAO[0], attn_ao0),
                      (sAO[1], attn_ao1)])
    ln1_v = bnd_v
    for ck in range(NC2):
        par = ck % 2
        src3 = vtmp3p[:, par, :, :]

        if ck > 0:
            ln1_v = op("v", lambda e: e.tensor_copy(x2c3[:, :, 0], x2c3[:, :, 512]),
                       waits=[(sv, ln1_v)])

        ln1_v = stats_and_apply(src3, x2c3, 1, max(resid_vs[ck], ln1_v),
                                bc_extra=[(sv, ka_prev)])

        if ck + 1 < NC2:
            emit_A(ck + 1)

        mixed = None
        mixed_h = {}
        for h in range(2):
            op("v", lambda e, h=h: e.tensor_tensor(
                sq2[:, :].rearrange("p (a t) -> p a t", a=4),
                x2c3[:, 4 * h:4 * h + 4, 1:513],
                x2c3[:, 4 * h:4 * h + 4, 0:512], OP.subtract),
                waits=[(sv, ln1_v), (sC, consts_s), (st, sq2_t_free)])
            for a in range(4 * h, 4 * h + 4):
                op("v", lambda e, a=a: e.scalar_tensor_tensor(
                    xr3[:, a, :], sq2[:, (a % 4) * 512:(a % 4) * 512 + 512],
                    c_mur[:, a:a + 1], x2c3[:, a, 0:512], OP.mult, OP.add))
                mixed = op("v", lambda e, a=a: e.scalar_tensor_tensor(
                    xk3[:, a, :], sq2[:, (a % 4) * 512:(a % 4) * 512 + 512],
                    c_muk[:, a:a + 1], x2c3[:, a, 0:512], OP.mult, OP.add))
            mixed_h[h] = mixed

        for o in range(8):
            bank = psA if o % 2 == 0 else psB

            def rmm0(e, o=o, bank=bank):
                for a in range(4):
                    inst = e.matmul(bank[:, :], w24[:, 0, a, o, :], xr3[:, a, :],
                                    start=(a == 0), stop=False)
                return inst
            op("t", rmm0, inc=st,
               waits=[(sv, mixed_h[0]), (sWR, 16), (sa, rbank_free[o % 2])])

            def rmm1(e, o=o, bank=bank):
                for a in range(4, 8):
                    inst = e.matmul(bank[:, :], w24[:, 0, a, o, :], xr3[:, a, :],
                                    start=False, stop=(a == 7))
                return inst
            rc = op("t", rmm1, inc=st, waits=[(sv, mixed_h[1])])
            sig_c[o] = op("a", lambda e, o=o, bank=bank: e.activation(
                rg3[:, o, :], bank[:, :], AF.Sigmoid), waits=[(st, rc)])
            rbank_free[o % 2] = sig_c[o]

        for o in range(8):
            bank = psBC if o % 2 == 0 else psX

            def kmm(e, o=o, bank=bank):
                for a in range(8):
                    inst = e.matmul(bank[:, :], w24[:, 1, a, o, :], xk3[:, a, :],
                                    start=(a == 0), stop=(a == 7))
                return inst
            kc = op("t", kmm, inc=st,
                    waits=[(sv, mixed), (sWK, 16), (sv, kbank_free[o % 2])])

            op("v", lambda e, o=o, bank=bank: e.tensor_relu(
                sq2[:, (o % 2) * 512:(o % 2) * 512 + 512], bank[:, :]),
                waits=[(st, kc), (st, sq2_t_free), (sv, mixed)])
            ka_c[o] = op("v", lambda e, o=o: e.tensor_tensor(
                ka3[:, o, :], sq2[:, (o % 2) * 512:(o % 2) * 512 + 512],
                sq2[:, (o % 2) * 512:(o % 2) * 512 + 512], OP.mult))
            kbank_free[o % 2] = ka_c[o]
        ka_done = sv.v
        ka_prev = ka_done


        for o in range(8):
            bank = psA if o % 2 == 0 else psB

            def vmm(e, o=o, bank=bank):
                for a in range(8):
                    inst = e.matmul(bank[:, :], w24[:, 2, a, o, :], ka3[:, a, :],
                                    start=(a == 0), stop=(a == 7))
                return inst
            vc = op("t", vmm, inc=st,
                    waits=[(sv, ka_done), (sWV, 16), (sa, rbank_free[o % 2]),
                           (sv, vb_free[o % 2])])

            op("v", lambda e, o=o, par=par, bank=bank: e.tensor_tensor(
                vtmp3p[:, par, o, :], rg3[:, o, :], bank[:, :], OP.mult),
                waits=[(st, vc)])
            vb_free[o % 2] = op("v", lambda e, o=o, par=par: e.tensor_tensor(
                vtmp3p[:, par, o, :], vtmp3p[:, par, o, :], x2c3[:, o, 1:513], OP.add))
        ffn_v = sv.v

        ln2_v = stats_and_apply(src3, src3, 0, ffn_v, bc_extra=[(sv, ka_done)],
                                pls=(psBC, psX))
        out_dma[par] = op("s", lambda e, ck=ck, par=par: e.dma_start(
            out=out3[:, :, ck * 512:(ck + 1) * 512], in_=vtmp3p[:, par, :, :]),
            inc=sOUT, waits=[(sv, ln2_v)])
        out_dma_all = sOUT.v

    final_ss = out_dma_all
    op("s", lambda e: e.nop(), waits=[(sOUT, final_ss)])

    with nc.Block() as block:
        @block.tensor
        def _(e):
            for f in prog["t"]:
                f(e)

        @block.vector
        def _(e):
            for f in prog["v"]:
                f(e)

        @block.scalar
        def _(e):
            for f in prog["a"]:
                f(e)

        @block.gpsimd
        def _(e):
            for f in prog["g"]:
                f(e)

        @block.sync
        def _(e):
            for f in prog["s"]:
                f(e)

    return nc, st_ctx


def stage_inputs(T, inputs):
    x = np.asarray(inputs["L0_attn_x"], dtype=np.float32)
    w_qkv = np.asarray(inputs["L0_attn_w_qkv"], dtype=np.float32)
    w_out = np.asarray(inputs["L0_attn_w_out"], dtype=np.float32)
    head_scales = np.asarray(inputs["L0_attn_head_scales"], dtype=np.float32)
    W_r = np.asarray(inputs["L0_ffn_W_r"], dtype=np.float32)
    W_k = np.asarray(inputs["L0_ffn_W_k"], dtype=np.float32)
    W_v = np.asarray(inputs["L0_ffn_W_v"], dtype=np.float32)
    mu_r = np.asarray(inputs["L0_ffn_mu_r"], dtype=np.float32)
    mu_k = np.asarray(inputs["L0_ffn_mu_k"], dtype=np.float32)

    TQ = T // 4
    dirs = _dirs()

    def wpre(w):
        return np.ascontiguousarray(
            w.reshape(8, 128, 8, 128).transpose(1, 0, 2, 3).reshape(128, 8192))

    ident = np.eye(128, dtype=np.float32).astype(ml_dtypes.bfloat16)
    onesb = np.ones((128, 1), dtype=np.float32).astype(ml_dtypes.bfloat16)
    onesc = np.ones((128, 1), dtype=np.float32)
    onesr = np.ones((1, 128), dtype=np.float32)
    tri = np.where(np.arange(128)[:, None] <= np.arange(128)[None, :], 0.0, NEG).astype(np.float32)
    mur_s = np.ascontiguousarray(mu_r.reshape(8, 128).T)
    muk_s = np.ascontiguousarray(mu_k.reshape(8, 128).T)
    wout_p = wpre(w_out)
    wr_p = wpre(np.ascontiguousarray(W_r.T))
    wk_p = wpre(np.ascontiguousarray(W_k.T))
    wv_p = wpre(np.ascontiguousarray(W_v.T))

    NTl = T // 512
    NC2l = TQ // 512
    in_maps = []
    for c in range(8):
        b, r = divmod(c, 4)
        h0, h1 = 2 * r, 2 * r + 1
        xt_ = np.ascontiguousarray(x[b].T)
        xt_pre = np.ascontiguousarray(
            xt_.reshape(8, 128, NTl, 512).transpose(1, 2, 0, 3).reshape(128, 8 * T))
        xq_slice = xt_[:, r * TQ:(r + 1) * TQ]
        xqt_pre = np.ascontiguousarray(
            xq_slice.reshape(8, 128, NC2l, 512).transpose(1, 2, 0, 3).reshape(128, 8 * TQ))
        if r == 0:
            xql_ = np.zeros((128, 8), dtype=np.float32)
        else:
            xql_ = np.ascontiguousarray(
                xt_[:, r * TQ - 1].reshape(8, 128).T)
        wq = np.empty((128, 6, 8, 128), dtype=np.float32)
        for si, (stype, h) in enumerate([(0, h0), (0, h1), (1, h0), (1, h1), (2, h0), (2, h1)]):
            blk = w_qkv[:, stype * 1024 + h * 128: stype * 1024 + h * 128 + 128]
            wq[:, si, :, :] = blk.reshape(8, 128, 128).transpose(1, 0, 2)
        hs = head_scales[[h0, h1]].reshape(1, 2).astype(np.float32)
        d3c = np.empty((3, 6), dtype=np.float32)
        for u, h in enumerate((h0, h1)):
            d = dirs[h]
            d3c[:, 3 * u:3 * u + 3] = (np.outer(d, d) / SCL).astype(np.float32)
        in_maps.append({
            "xt": xt_pre,
            "xqt": xqt_pre,
            "xqlp": xql_,
            "wqkv": np.ascontiguousarray(wq.reshape(128, 6 * 8 * 128)),
            "wout": wout_p,
            "wr": wr_p,
            "wk": wk_p,
            "wv": wv_p,
            "mur": mur_s,
            "muk": muk_s,
            "hs2": hs,
            "d3": d3c,
            "identp": ident,
            "onesb": onesb,
            "onesbm": np.ones((128, 128), dtype=np.float32).astype(ml_dtypes.bfloat16),
            "onescm": np.ones((128, 128), dtype=np.float32),
            "onesc": onesc,
            "onesr": onesr,
            "trim": tri,
            "epsv": np.full((128, 1), EPS, dtype=np.float32),
            "gidx": ((4 * b + np.arange(4, dtype=np.int32))[:, None] * 128
                     + np.arange(128, dtype=np.int32)[None, :]).astype(np.int32),
        })
    return in_maps


_CACHE = {}


def kernel(**inputs):
    T = np.asarray(inputs["L0_attn_x"]).shape[1]
    if T not in _CACHE:
        _CACHE[T] = build_program(T)
    nc, _ctx = _CACHE[T]
    in_maps = stage_inputs(T, inputs)
    res = run_bass_kernel_spmd(nc, in_maps, core_ids=list(range(8)))
    TQ = T // 4
    full = np.empty((2, T, 1024), dtype=np.float32)
    for c in range(8):
        b, r = divmod(c, 4)
        full[b, r * TQ:(r + 1) * TQ, :] = res.results[c]["out"].T
    return full


# version 21
# speedup vs baseline: 1.1440x; 1.0030x over previous
import os

os.environ.setdefault("TRN_ALLOW_ICI_ALL_TO_ALL", "1")

import math
import numpy as np
import ml_dtypes

import concourse.bass as bass
import concourse.mybir as mybir
from concourse.bass import AutoArena, IndirectOffsetOnAxis
from concourse import dve_ops
from concourse.bass_utils import run_bass_kernel_spmd

F32 = mybir.dt.float32
BF16 = mybir.dt.bfloat16
AX = mybir.AxisListType.X
OP = mybir.AluOpType
AF = mybir.ActivationFunctionType

C = 1024
HD = 128
SCL = 1.0 / math.sqrt(HD)
EPS = 1e-5
NEG = -1.0e9


def _dirs():
    t = np.array(
        [[(i >> 2 & 1) * 2 - 1, (i >> 1 & 1) * 2 - 1, (i & 1) * 2 - 1] for i in range(8)],
        dtype=np.float64,
    )
    return t / np.sqrt(3.0)


class Ctr:
    def __init__(self, sem, step=1):
        self.sem = sem
        self.step = step
        self.v = 0

    def bump(self, n=None):
        self.v += self.step if n is None else n
        return self.v


def build_program(T):
    TQ = T // 4
    NT = T // 512
    QG = T // 512
    PQ = max(QG // 4, 1)
    NC2 = TQ // 512
    KTT = T // 128

    nc = bass.Bass("TRN2", target_bir_lowering=False)

    xt = nc.dram_tensor("xt", [128, 8 * T], F32, kind="ExternalInput").ap()
    xqt = nc.dram_tensor("xqt", [128, 8 * TQ], F32, kind="ExternalInput").ap()
    xqlp = nc.dram_tensor("xqlp", [128, 8], F32, kind="ExternalInput").ap()
    wqkv = nc.dram_tensor("wqkv", [128, 6 * 8 * 128], F32, kind="ExternalInput").ap()
    wout = nc.dram_tensor("wout", [128, 8 * 8 * 128], F32, kind="ExternalInput").ap()
    wr = nc.dram_tensor("wr", [128, 8 * 8 * 128], F32, kind="ExternalInput").ap()
    wk = nc.dram_tensor("wk", [128, 8 * 8 * 128], F32, kind="ExternalInput").ap()
    wv = nc.dram_tensor("wv", [128, 8 * 8 * 128], F32, kind="ExternalInput").ap()
    mur = nc.dram_tensor("mur", [128, 8], F32, kind="ExternalInput").ap()
    muk = nc.dram_tensor("muk", [128, 8], F32, kind="ExternalInput").ap()
    hs2 = nc.dram_tensor("hs2", [1, 2], F32, kind="ExternalInput").ap()
    d3 = nc.dram_tensor("d3", [3, 6], F32, kind="ExternalInput").ap()
    identp = nc.dram_tensor("identp", [128, 128], BF16, kind="ExternalInput").ap()
    onesb = nc.dram_tensor("onesb", [128, 1], BF16, kind="ExternalInput").ap()
    onesbm = nc.dram_tensor("onesbm", [128, 128], BF16, kind="ExternalInput").ap()
    onescm = nc.dram_tensor("onescm", [128, 128], F32, kind="ExternalInput").ap()
    onesc = nc.dram_tensor("onesc", [128, 1], F32, kind="ExternalInput").ap()
    onesr = nc.dram_tensor("onesr", [1, 128], F32, kind="ExternalInput").ap()
    trim = nc.dram_tensor("trim", [128, 128], F32, kind="ExternalInput").ap()
    epsv = nc.dram_tensor("epsv", [128, 1], F32, kind="ExternalInput").ap()
    gidx = nc.dram_tensor("gidx", [4, 128], mybir.dt.int32, kind="ExternalInput").ap()
    out = nc.dram_tensor("out", [C, TQ], F32, kind="ExternalOutput").ap()

    wrd = nc.dram_tensor("wrd", [128, 8 * 8 * 128], BF16).ap()
    wkd = nc.dram_tensor("wkd", [128, 8 * 8 * 128], BF16).ap()
    wvd = nc.dram_tensor("wvd", [128, 8 * 8 * 128], BF16).ap()
    a2ai = [nc.dram_tensor(f"a2ai{j}", [8, 128, 2, 512 + (1 if j == NC2 - 1 else 0)],
                           BF16).ap() for j in range(NC2)]
    a2ao = [nc.dram_tensor(f"a2ao{j}", [8, 128, 2, 512 + (1 if j == NC2 - 1 else 0)],
                           BF16).ap() for j in range(NC2)]

    import contextlib

    st_ctx = contextlib.ExitStack()
    sbp = lambda name, shape, dt: st_ctx.enter_context(nc.sbuf_tensor(name, shape, dt))
    psx_ = lambda name: st_ctx.enter_context(nc.psum_tensor(name, [128, 512], F32))
    sem = lambda name: st_ctx.enter_context(nc.semaphore(name))

    c_id = sbp("c_id", [128, 128], BF16)
    c_1b = sbp("c_1b", [128, 1], BF16)
    c_1bm = sbp("c_1bm", [128, 128], BF16)
    c_1cm = sbp("c_1cm", [128, 128], F32)
    c_1c = sbp("c_1c", [128, 1], F32)
    c_1r = sbp("c_1r", [1, 128], F32)
    c_tri = sbp("c_tri", [128, 128], F32)
    c_d3 = sbp("c_d3", [3, 6], F32)
    c_hs = sbp("c_hs", [1, 2], F32)
    c_mur = sbp("c_mur", [128, 8], F32)
    c_muk = sbp("c_muk", [128, 8], F32)
    zcol = sbp("zcol", [128, 1], BF16)
    c_eps = sbp("c_eps", [128, 1], F32)
    c_gix = [sbp(f"c_gix{j}", [128, 1], mybir.dt.int32) for j in range(4)]
    lrec = sbp("lrec", [1, 512], F32)
    mx = sbp("mx", [1, 64], F32)
    hsr = sbp("hsr", [3, 2], F32)
    A3 = sbp("A3", [3, 6], BF16)
    cneg = sbp("cneg", [128, 2], F32)
    xql = sbp("xql", [128, 8], F32)
    wo_sb = sbp("wo_sb", [128, 8 * 8 * 128], BF16)
    vcol = sbp("vcol", [128, 8], F32)
    mi2 = sbp("mi2", [128, 2], F32)
    bsq8 = sbp("bsq8", [128, 8], F32)
    bcol = sbp("bcol", [128, 8], F32)
    aall0 = sbp("aall0", [128, 4 * 1024], BF16)

    arA = AutoArena(nc)
    wq_sb = arA.sbuf_tensor("wq_sb", [128, 6 * 8 * 128], BF16)
    xsb = arA.sbuf_tensor("xsb", [128, 2 * 8 * 512], BF16)
    Qt = arA.sbuf_tensor("Qt", [128, 2 * T], BF16)
    Kt = arA.sbuf_tensor("Kt", [128, 2 * T], BF16)
    Vt = arA.sbuf_tensor("Vt", [128, 2 * T], BF16)
    vstg = arA.sbuf_tensor("vstg", [128, 512], BF16)
    pt = arA.sbuf_tensor("pt", [128, 8 * 512], BF16)
    lsum = arA.sbuf_tensor("lsum", [128, 2 * 512], BF16)
    sq = arA.sbuf_tensor("sq", [128, 2 * 512], BF16)
    lb = arA.sbuf_tensor("lb", [128, 512], F32)
    rscr = arA.sbuf_tensor("rscr", [128, 512], F32)
    aosg = arA.sbuf_tensor("aosg", [128, 2 * 512], BF16)
    arA.allocate()

    arA.free()
    arB = AutoArena(nc)
    aall = arB.sbuf_tensor("aall", [128, 4 * 1026], BF16)
    aall_b = arB.sbuf_tensor("aall_b", [128, 8], BF16)
    w2_sb = arB.sbuf_tensor("w2_sb", [128, 3 * 8 * 8 * 128], BF16)
    xq = arB.sbuf_tensor("xq", [128, 8 * 512], F32)
    x2c = arB.sbuf_tensor("x2c", [128, 8 * 513], F32)
    vtmp = arB.sbuf_tensor("vtmp", [128, 2 * 8 * 512], F32)
    sq2 = arB.sbuf_tensor("sq2", [128, 4 * 512], F32)
    mb_sb2 = arB.sbuf_tensor("mb_sb2", [128, 512], F32)
    ib_sb2 = arB.sbuf_tensor("ib_sb2", [128, 512], F32)
    xr_sb = arB.sbuf_tensor("xr_sb", [128, 8 * 512], BF16)
    xk_sb = arB.sbuf_tensor("xk_sb", [128, 8 * 512], BF16)
    ka_sb = arB.sbuf_tensor("ka_sb", [128, 8 * 512], BF16)
    rg_sb = arB.sbuf_tensor("rg_sb", [128, 8 * 512], F32)
    arB.allocate()
    t1_sb = None

    psA = psx_("psA")
    psB = psx_("psB")
    psO0 = psx_("psO0")
    psO1 = psx_("psO1")
    psL0 = psx_("psL0")
    psL1 = psx_("psL1")
    psBC = psx_("psBC")
    psX = psx_("psX")

    sC = Ctr(sem("sC"), 16)
    sWQ = Ctr(sem("sWQ"), 16)
    sX = [Ctr(sem("sX0"), 16), Ctr(sem("sX1"), 16)]
    sW2 = Ctr(sem("sW2"), 16)
    sAO = [Ctr(sem("sAO0"), 16), Ctr(sem("sAO1"), 16)]
    sZ = Ctr(sem("sZ"), 16)
    sLD = Ctr(sem("sLD"), 16)
    sLD2 = Ctr(sem("sLD2"), 16)
    sWO = Ctr(sem("sWO"), 16)
    sAB = Ctr(sem("sAB"), 16)
    sXQL = Ctr(sem("sXQL"), 16)
    sXQ = Ctr(sem("sXQ"), 16)
    sOUT = Ctr(sem("sOUT"), 16)
    sgp = Ctr(sem("sgp"), 1)
    st = Ctr(sem("st"), 1)
    sv = Ctr(sem("sv"), 1)
    sa = Ctr(sem("sa"), 1)
    scc = Ctr(sem("scc"), 1)

    prog = {k: [] for k in ("t", "v", "a", "g", "s")}
    highwater = {}

    def op(eng, builder, inc=None, waits=()):
        if eng == "ad":
            eng = "a"
        elif eng == "gc":
            eng = "g"
            waits = list(waits) + [(sgp, sgp.v)]
            inc = sgp
        elif eng == "v":
            waits = list(waits) + [(sv, sv.v)]
            inc = sv
        elif eng == "a":
            waits = list(waits) + [(sa, sa.v)]
            inc = sa
        needed = []
        for ctr, val in waits:
            if val <= 0:
                continue
            key = (eng, id(ctr))
            if highwater.get(key, 0) >= val:
                continue
            highwater[key] = val
            needed.append((ctr.sem, val))
        inc_sem = None
        inc_n = 0
        if inc is not None:
            inc.bump()
            inc_sem = inc.sem
            inc_n = inc.step

        def f(e, builder=builder, needed=tuple(needed), inc_sem=inc_sem, inc_n=inc_n):
            for s_, v_ in needed:
                e.wait_ge(s_, v_)
            inst = builder(e)
            if inc_sem is not None:
                inst.then_inc(inc_sem, inc_n)

        prog[eng].append(f)
        return inc.v if inc is not None else None

    for dst, src in ((c_id[:, :], identp), (c_1b[:, :], onesb), (c_1c[:, :], onesc),
                     (c_1bm[:, :], onesbm), (c_1cm[:, :], onescm),
                     (c_1r[:, :], onesr), (c_tri[:, :], trim),
                     (c_d3[:, :], d3), (c_hs[:, :], hs2), (c_mur[:, :], mur),
                     (c_muk[:, :], muk), (c_eps[:, :], epsv),
                     (c_gix[0][:, :], gidx[0:1, :]), (c_gix[1][:, :], gidx[1:2, :]),
                     (c_gix[2][:, :], gidx[2:3, :]), (c_gix[3][:, :], gidx[3:4, :]),
                     (xql[:, :], xqlp)):
        op("s", lambda e, d=dst, s=src: e.dma_start(out=d, in_=s), inc=sC)
    consts_s = sC.v
    op("g", lambda e: e.dma_start(out=wq_sb[:, :], in_=wqkv), inc=sWQ)
    op("g", lambda e: e.dma_start(out=wo_sb[:, :], in_=wout), inc=sWO)
    op("g", lambda e: e.dma_start(out=wrd, in_=wr), inc=sW2)
    op("g", lambda e: e.dma_start(out=wkd, in_=wk), inc=sW2)
    op("g", lambda e: e.dma_start(out=wvd, in_=wv), inc=sW2)
    w2d_g = sW2.v

    zv = op("v", lambda e: e.memset(zcol[:, :], 0.0), inc=sv)
    for sh in (0, 4):
        for u in range(2):
            op("s", lambda e, sh=sh, u=u: e.dma_start(
                out=a2ai[NC2 - 1][sh, :, u, 0:1], in_=zcol[:, :],
                allow_slow_non_contiguous=True),
               inc=sZ, waits=[(sv, zv)])
    zero_ss = sZ.v

    xt3 = xt.rearrange("p (tc a t) -> p tc a t", tc=NT, a=8)
    xsb3 = xsb[:, :].rearrange("p (par a t) -> p par a t", par=2, a=8)
    wq4 = wq_sb[:, :].rearrange("p (s a d) -> p s a d", s=6, a=8)
    Qt2 = Qt[:, :].rearrange("p (u t) -> p u t", u=2)
    Kt2 = Kt[:, :].rearrange("p (u t) -> p u t", u=2)
    Vt3 = Vt[:, :].rearrange("p (u kt d) -> p u kt d", u=2, kt=KTT)

    x_done = []
    x_cons = []
    bank_free = {0: 0, 1: 0}
    psx_free = 0

    for tc in range(NT):
        par = tc % 2
        w_ = [] if tc < 2 else [(st, x_cons[tc - 2])]
        op("g", lambda e, par=par, tc=tc:
           e.dma_start(out=xsb3[:, par, :, :], in_=xt3[:, tc, :, :]),
           inc=sX[par], waits=w_)
        x_done.append((sX[par], sX[par].v))

        for ob in range(6):
            bname = (tc * 6 + ob) % 2
            bank = psA if bname == 0 else psB

            def mms(e, bank=bank, ob=ob, par=par):
                for a in range(8):
                    inst = e.matmul(bank[:, :], wq4[:, ob, a, :], xsb3[:, par, a, :],
                                    start=(a == 0), stop=(a == 7))
                return inst
            mmc = op("t", mms, inc=st,
                     waits=[x_done[tc], (sWQ, 16), (sv, bank_free[bname])])

            u = ob % 2
            kind = ob // 2
            if kind == 0:
                dst = Qt2[:, u, tc * 512:(tc + 1) * 512]
            elif kind == 1:
                dst = Kt2[:, u, tc * 512:(tc + 1) * 512]
            else:
                dst = vstg[:, :]
            bank_free[bname] = op("v", lambda e, d=dst, b=bank: e.tensor_copy(d, b[:, :]),
                                  inc=sv, waits=[(st, mmc)])

            if kind == 2:
                vst = sv.v

                def vtr(e):
                    psXb = psX[:, :].bitcast(BF16)
                    for jj in range(4):
                        inst = e.transpose(psXb[:, jj * 128:(jj + 1) * 128],
                                           vstg[:, jj * 128:(jj + 1) * 128], c_id[:, :])
                    return inst
                trc = op("t", vtr, inc=st,
                         waits=[(sv, vst), (sv, psx_free), (sC, consts_s)])
                psx_free = op("v", lambda e, u=u, tc=tc: e.tensor_copy(
                    Vt3[:, u, tc * 4:(tc + 1) * 4, :].rearrange("p k d -> p (k d)"),
                    psX[:, :].bitcast(BF16)[:, 0:512]), inc=sv, waits=[(st, trc)])
        x_cons.append(st.v)

    ph1_v = sv.v

    psbc_free = 0
    adj_frees = {0: 0, 1: 0}
    psl_frees = {0: 0, 1: 0}
    sq_frees = {0: 0, 1: 0}
    for u in range(2):
        hc = op("t", lambda e, u=u: e.matmul(psX[0:3, 0:1], c_1r[0:1, 0:3],
                                             c_hs[0:1, u:u + 1], start=True, stop=True),
                inc=st, waits=[(sC, consts_s), (sv, psx_free)])
        hv = op("v", lambda e, u=u: e.tensor_copy(hsr[:, u:u + 1], psX[0:3, 0:1]),
                inc=sv, waits=[(st, hc)])
        psx_free = hv
        a3done = op("v", lambda e, u=u: e.tensor_scalar(
            A3[:, 3 * u:3 * u + 3], c_d3[:, 3 * u:3 * u + 3], hsr[:, u:u + 1], None, OP.mult),
            inc=sv, waits=[(sv, hv), (sC, consts_s)])

        for tc in range(NT):
            abank = psBC if tc % 2 == 0 else psX
            amc = op("t", lambda e, u=u, tc=tc, abank=abank: e.matmul(
                abank[0:3, 0:512], A3[:, 3 * u:3 * u + 3],
                Qt2[0:3, u, tc * 512:(tc + 1) * 512], start=True, stop=True),
                inc=st, waits=[(sv, a3done), (sv, ph1_v),
                               (sv, adj_frees[tc % 2])])
            adj_frees[tc % 2] = op("v", lambda e, u=u, tc=tc, abank=abank:
                                   e.tensor_tensor(
                Qt2[0:3, u, tc * 512:(tc + 1) * 512],
                Qt2[0:3, u, tc * 512:(tc + 1) * 512],
                abank[0:3, 0:512], OP.add), inc=sv, waits=[(st, amc)])
        adj_v = sv.v
        psbc_free = max(psbc_free, adj_frees[0])
        psx_free = max(psx_free, adj_frees[1])

        for which in ("q", "k"):
            buf = Qt2 if which == "q" else Kt2
            base = 0 if which == "q" else 16
            dep = adj_v if which == "q" else ph1_v
            for tc in range(NT):
                sl = tc % 2
                bank = psL0 if sl == 0 else psL1
                sqc = op("a", lambda e, u=u, tc=tc, sl=sl, buf=buf: e.activation(
                    sq[:, sl * 512:sl * 512 + 512],
                    buf[:, u, tc * 512:(tc + 1) * 512], AF.Square),
                    waits=[(sv, dep), (st, sq_frees[sl])])
                oc = op("t", lambda e, sl=sl, bank=bank: e.matmul(
                    bank[0:1, :], c_1b[:, 0:1], sq[:, sl * 512:sl * 512 + 512],
                    start=True, stop=True),
                    inc=st, waits=[(sa, sqc), (sC, consts_s), (sv, psl_frees[sl])])
                sq_frees[sl] = oc
                psl_frees[sl] = op("v", lambda e, tc=tc, sl=sl, base=base, bank=bank:
                                   e.reduce_max(
                    mx[0:1, base + tc:base + tc + 1], bank[0:1, 0:512], axis=AX),
                    inc=sv, waits=[(st, oc)])

        op("v", lambda e: e.reduce_max(mx[0:1, 32:33], mx[0:1, 0:NT], axis=AX),
           inc=sv, waits=[(sv, max(psl_frees[0], psl_frees[1]))])
        f0 = op("v", lambda e: e.reduce_max(mx[0:1, 33:34], mx[0:1, 16:16 + NT], axis=AX),
                inc=sv)
        fc = op("v", lambda e: e.tensor_tensor(mx[0:1, 34:35], mx[0:1, 32:33],
                                               mx[0:1, 33:34], OP.mult),
                inc=sv, waits=[(sv, f0)])

        op("a", lambda e: e.activation(mx[0:1, 35:36], mx[0:1, 34:35], AF.Sqrt),
           waits=[(sv, fc)])
        cc_ = op("a", lambda e: e.mul(mx[0:1, 36:37], mx[0:1, 35:36], -SCL))
        cbc_c = op("t", lambda e: e.matmul(psBC[:, 0:1], c_1r[0:1, :], mx[0:1, 36:37],
                                           start=True, stop=True),
                   inc=st, waits=[(sa, cc_), (sv, psbc_free)])
        psbc_free = op("v", lambda e, u=u: e.tensor_copy(cneg[:, u:u + 1], psBC[:, 0:1]),
                       inc=sv, waits=[(st, cbc_c)])
    bounds_v = sv.v

    xqt3 = xqt.rearrange("p (ck a t) -> p ck a t", ck=NC2, a=8)

    wo4 = wo_sb[:, :].rearrange("p (a o d) -> p a o d", a=8, o=8)
    w24 = w2_sb[:, :].rearrange("p (w a o d) -> p w a o d", w=3, a=8, o=8)
    xq3 = xq[:, :].rearrange("p (a t) -> p a t", a=8)
    x2c3 = x2c[:, :].rearrange("p (a t) -> p a t", a=8)
    vtmp3p = vtmp[:, :].rearrange("p (par a t) -> p par a t", par=2, a=8)
    xr3 = xr_sb[:, :].rearrange("p (a t) -> p a t", a=8)
    xk3 = xk_sb[:, :].rearrange("p (a t) -> p a t", a=8)
    ka3 = ka_sb[:, :].rearrange("p (a t) -> p a t", a=8)
    rg3 = rg_sb[:, :].rearrange("p (a t) -> p a t", a=8)
    out3 = out.rearrange("(a p) t -> p a t", p=128)

    rounds = [NC2 - 1] + list(range(NC2 - 1))
    a2afl = [a2ao[j].rearrange("r p u t -> (r p) (u t)") for j in range(NC2)]
    scc_need = {jr: i + 1 for i, jr in enumerate(rounds)}
    prologue_state = {}

    def emit_prologue():
        nonlocal psx_free
        ab_ss = 0
        for j_ in range(4):
            ab_ss = op("g", lambda e, j_=j_: e.indirect_dma_start(
                aall[:, j_ * 1026:(j_ + 1) * 1026], None, a2afl[NC2 - 1][:, :],
                IndirectOffsetOnAxis(c_gix[j_][:, :], 0)),
                inc=sAB, waits=[(scc, scc_need[NC2 - 1]), (sC, consts_s)])
        abview = bass.AP(aall, 0, [[4104, 128], [1026, 4], [513, 2]])
        abcp = op("v", lambda e: e.tensor_copy(
            aall_b[:, :].rearrange("p (j u) -> p j u", u=2), abview),
            waits=[(sAB, ab_ss)])

        def bpmm(e):
            for o in range(8):
                for a in range(8):
                    inst = e.matmul(psX[:, o:o + 1], wo4[:, a, o, :],
                                    aall_b[:, a:a + 1],
                                    start=(a == 0), stop=(a == 7))
            return inst
        bp_c = op("t", bpmm, inc=st,
                  waits=[(sv, abcp), (sWO, 16), (sv, psx_free)])
        bres_c = op("v", lambda e: e.tensor_tensor(vcol[:, :], xql[:, :], psX[:, 0:8], OP.add),
                    inc=sv, waits=[(st, bp_c), (sC, consts_s)])
        psx_free = bres_c
        prologue_state["bnd_v"] = bres_c
        prologue_state["abcp"] = abcp

    def emit_prologue2():
        bres_c = prologue_state["bnd_v"]
        bsq_c = op("a", lambda e: e.activation(bsq8[:, :], vcol[:, :], AF.Square),
                   waits=[(sv, bres_c)])

        def bstat(e):
            e.matmul(psX[0:1, 16:24], c_1c[:, 0:1], vcol[:, :], start=True, stop=True)
            return e.matmul(psX[0:1, 32:40], c_1c[:, 0:1], bsq8[:, :], start=True, stop=True)
        bst_c = op("t", bstat, inc=st, waits=[(sa, bsq_c), (sv, bres_c)])

        op("v", lambda e: e.reduce_sum(mx[0:1, 40:41], psX[0:1, 16:24], axis=AX),
           inc=sv, waits=[(st, bst_c)])
        bm0 = op("v", lambda e: e.reduce_sum(mx[0:1, 41:42], psX[0:1, 32:40], axis=AX),
                 inc=sv)

        op("v", lambda e: e.tensor_scalar(mx[0:1, 40:41], mx[0:1, 40:41], 1.0 / C,
                                          None, OP.mult), waits=[(sv, bm0)])
        op("v", lambda e: e.tensor_scalar(mx[0:1, 41:42], mx[0:1, 41:42], 1.0 / C,
                                          None, OP.mult))
        op("v", lambda e: e.tensor_tensor(mx[0:1, 42:43], mx[0:1, 40:41], mx[0:1, 40:41],
                                          OP.mult))
        bmv_c = op("v", lambda e: e.tensor_tensor(mx[0:1, 41:42], mx[0:1, 41:42],
                                                  mx[0:1, 42:43], OP.subtract))
        bsi_c = op("a", lambda e: e.activation(mx[0:1, 41:42], mx[0:1, 41:42],
                                               AF.Sqrt, bias=c_eps[0:1, 0:1]),
                   inc=sa, waits=[(sv, bmv_c)])
        bir_c = op("v", lambda e: e.reciprocal(mx[0:1, 41:42], mx[0:1, 41:42]),
                   inc=sv, waits=[(sa, bsi_c)])
        bbc_c = op("t", lambda e: e.matmul(psX[:, 40:42], c_1r[0:1, :], mx[0:1, 40:42],
                                           start=True, stop=True),
                   inc=st, waits=[(sv, bir_c)])
        mi_ev = op("v", lambda e: e.tensor_copy(mi2[:, :], psX[:, 40:42]),
                   inc=sv, waits=[(st, bbc_c)])
        bnd_v = op("v", lambda e: e.tensor_scalar(
            bcol[:, :], vcol[:, :], mi2[:, 0:1], mi2[:, 1:2], OP.subtract, OP.mult),
            waits=[(sv, mi_ev)])
        prologue_state["bnd_v"] = bnd_v

    pt2 = pt[:, :].rearrange("p (r t) -> p r t", r=8)
    ls2 = lsum[:, :].rearrange("p (r t) -> p r t", r=2)
    aosg2 = aosg[:, :].rearrange("p (r t) -> p r t", r=2)

    exp_ring = {0: 0, 1: 0, 2: 0}
    pt_ring = {k: 0 for k in range(8)}
    pt_ring_v = {k: 0 for k in range(8)}
    lsum_free = {0: 0, 1: 0}
    ol_free = {0: 0, 1: 0}
    ao_dma = {0: 0, 1: 0}
    lb_free = 0

    qi = -1
    a2a_cs = {}
    for jr in rounds:
        for u in range(2):
          for qtr in range(4):
            qg = qtr * PQ + jr
            qi += 1
            ktmax = 4 * qg + 4
            ngrp = ktmax // 4
            obank = psO0 if qi % 2 == 0 else psO1
            lbank = psL0 if qi % 2 == 0 else psL1

            smm_cs = {}
            exp_cs = {}
            grp_add = {}

            def geom(kt):
                diag = kt - 4 * qg
                if diag >= 0:
                    return 128 * diag, 512 - 128 * diag
                return 0, 512

            sbanks = (psA, psB, psBC)
            for step in range(ktmax + 2):
                tri_c = None
                if step < ktmax:
                    kt = step
                    sbank = sbanks[kt % 3]
                    o0, ncols = geom(kt)
                    qc0 = qg * 512 + o0
                    smm_cs[kt] = op("t", lambda e, u=u, kt=kt, sbank=sbank, qc0=qc0,
                                    ncols=ncols, o0=o0:
                                    e.matmul(sbank[:, o0:o0 + ncols],
                                             Kt2[:, u, kt * 128:(kt + 1) * 128],
                                             Qt2[:, u, qc0:qc0 + ncols],
                                             start=True, stop=True),
                                    inc=st,
                                    waits=[(sv, bounds_v), (sa, exp_ring[kt % 3]),
                                           (sv, psbc_free if kt % 3 == 2 else 0)])
                    if kt - 4 * qg >= 0:
                        tri_c = op("v", lambda e, sbank=sbank, o0=o0: e.tensor_tensor(
                            sbank[:, o0:o0 + 128], sbank[:, o0:o0 + 128],
                            c_tri[:, :], OP.add),
                            waits=[(st, smm_cs[kt]), (sC, consts_s)])
                if step >= 2:
                    kt = step - 2
                    o0, ncols = geom(kt)
                    ow = [(sa, exp_cs[kt])]
                    if kt == 0:
                        ow.append((sv, ol_free[qi % 2]))
                    pt_ring[kt % 8] = op("t", lambda e, u=u, kt=kt, o0=o0, ncols=ncols,
                                         obank=obank,
                                         first=(kt == 0), last=(kt == ktmax - 1):
                                         e.matmul(obank[:, o0:o0 + ncols],
                                                  Vt3[:, u, kt, :],
                                                  pt2[:, kt % 8, o0:o0 + ncols],
                                                  start=first, stop=last),
                                         inc=st, waits=ow)
                    if kt % 4 == 1 and kt // 4 >= 1:
                        g = kt // 4 - 1
                        gp = 4 * (g % 2)
                        a1 = op("v", lambda e, g=g, gp=gp: e.tensor_tensor(
                            ls2[:, g % 2, :], pt2[:, gp, :], pt2[:, gp + 1, :],
                            OP.add),
                            waits=[(sa, exp_cs[4 * g + 3]),
                                   (st, lsum_free[g % 2])])
                        pt_ring_v[gp] = a1
                        pt_ring_v[gp + 1] = a1
                    if kt % 4 == 3 and kt // 4 >= 1:
                        g = kt // 4 - 1
                        gp = 4 * (g % 2)
                        op("v", lambda e, g=g, gp=gp: e.tensor_tensor(
                            ls2[:, g % 2, :], ls2[:, g % 2, :], pt2[:, gp + 2, :],
                            OP.add))
                        grp_add[g] = op("v", lambda e, g=g, gp=gp: e.tensor_tensor(
                            ls2[:, g % 2, :], ls2[:, g % 2, :], pt2[:, gp + 3, :],
                            OP.add))
                        pt_ring_v[gp + 2] = grp_add[g]
                        pt_ring_v[gp + 3] = grp_add[g]
                        ow2 = [(sv, grp_add[g])]
                        if g == 0:
                            ow2.append((sv, ol_free[qi % 2]))
                        lsum_free[g % 2] = op("t", lambda e, g=g, lbank=lbank,
                                              first=(g == 0):
                                              e.matmul(lbank[:, :], c_1bm[:, :],
                                                       ls2[:, g % 2, :],
                                                       start=first, stop=False),
                                              inc=st, waits=ow2)
                if step < ktmax:
                    kt = step
                    sbank = sbanks[kt % 3]
                    o0, ncols = geom(kt)
                    ew = [(st, smm_cs[kt]), (st, pt_ring[kt % 8]),
                          (sv, pt_ring_v[kt % 8])]
                    if tri_c is not None:
                        ew.append((sv, tri_c))
                    exp_cs[kt] = op("a", lambda e, u=u, sbank=sbank, kt=kt, o0=o0,
                                    ncols=ncols:
                                    e.activation(pt2[:, kt % 8, o0:o0 + ncols],
                                                 sbank[:, o0:o0 + ncols], AF.Exp,
                                                 bias=cneg[:, u:u + 1], scale=SCL),
                                    waits=ew)
                    exp_ring[kt % 3] = exp_cs[kt]

            g = ngrp - 1
            gp = 4 * (g % 2)
            for kk in range(4 * g + 1, 4 * g + 4):
                oo, nn = geom(kk)
                op("v", lambda e, kk=kk, oo=oo, gp=gp: e.memset(
                    pt2[:, gp + (kk % 4), 0:oo], 0.0),
                    waits=[(sa, exp_cs[kk]), (st, pt_ring[kk % 8])])
            op("v", lambda e, g=g, gp=gp: e.tensor_tensor(
                ls2[:, g % 2, :], pt2[:, gp, :], pt2[:, gp + 1, :], OP.add),
                waits=[(st, lsum_free[g % 2])])
            op("v", lambda e, g=g, gp=gp: e.tensor_tensor(
                ls2[:, g % 2, :], ls2[:, g % 2, :], pt2[:, gp + 2, :], OP.add))
            grp_add[g] = op("v", lambda e, g=g, gp=gp: e.tensor_tensor(
                ls2[:, g % 2, :], ls2[:, g % 2, :], pt2[:, gp + 3, :], OP.add))
            for kk in range(4 * g, 4 * g + 4):
                pt_ring_v[gp + (kk % 4)] = grp_add[g]
            ow2 = [(sv, grp_add[g])]
            if g == 0:
                ow2.append((sv, ol_free[qi % 2]))
            lsum_free[g % 2] = op("t", lambda e, g=g, lbank=lbank,
                                  first=(g == 0):
                                  e.matmul(lbank[:, :], c_1bm[:, :],
                                           ls2[:, g % 2, :],
                                           start=first, stop=True),
                                  inc=st, waits=ow2)

            kloop_t = st.v
            rec_c = op("v", lambda e, lbank=lbank: e.reciprocal(lb[:, :], lbank[:, :]),
                       waits=[(st, kloop_t), (sv, lb_free)])
            aev_c = op("v", lambda e, obank=obank, qi=qi: e.tensor_tensor(
                aosg2[:, qi % 2, :], obank[:, :], lb[:, :], OP.mult),
                waits=[(sv, rec_c), (sAO[qi % 2], ao_dma[qi % 2])])
            lb_free = aev_c
            ol_free[qi % 2] = aev_c

            extra = (jr == NC2 - 1) and (qtr < 3)
            sAO[qi % 2].bump((3 if extra else 1) * 16)
            bcol0 = 1 if jr == NC2 - 1 else 0

            def aodma(e, u=u, qi=qi, qtr=qtr, jr=jr, extra=extra, bcol0=bcol0,
                      sem_=sAO[qi % 2].sem):
                tin = a2ai[jr]
                if extra:
                    e.dma_start(out=tin[qtr + 1, :, u, 0:1],
                                in_=aosg2[:, qi % 2, 511:512],
                                allow_slow_non_contiguous=True).then_inc(sem_, 16)
                    e.dma_start(out=tin[qtr + 5, :, u, 0:1],
                                in_=aosg2[:, qi % 2, 511:512],
                                allow_slow_non_contiguous=True).then_inc(sem_, 16)
                e.dma_start(out=tin[qtr, :, u, bcol0:bcol0 + 512],
                            in_=aosg2[:, qi % 2, :]).then_inc(sem_, 16)
                return e.dma_start(
                    out=tin[qtr + 4, :, u, bcol0:bcol0 + 512], in_=aosg2[:, qi % 2, :])
            ao_dma[qi % 2] = op("s", aodma, inc=sAO[qi % 2], waits=[(sv, aev_c)])
        cw = [(sAO[0], sAO[0].v), (sAO[1], sAO[1].v)]
        if jr == NC2 - 1:
            cw.append((sZ, zero_ss))
        a2a_cs[jr] = op("g", lambda e, jr=jr: e.collective_compute(
            "AllToAll", OP.bypass, replica_groups=[[0, 1, 2, 3, 4, 5, 6, 7]],
            ins=[a2ai[jr].opt()], outs=[a2ao[jr].opt()]),
            inc=scc, waits=cw)
        if jr == 0 and NC2 > 1:
            for j_ in range(4):
                op("g", lambda e, j_=j_: e.indirect_dma_start(
                    aall0[:, j_ * 1024:(j_ + 1) * 1024], None, a2afl[0][:, :],
                    IndirectOffsetOnAxis(c_gix[j_][:, :], 0)),
                    inc=sLD, waits=[(scc, scc_need[0]), (sC, consts_s)])
            prologue_state["aall0_ss"] = sLD.v
        if jr == rounds[min(2, len(rounds) - 1)]:
            emit_prologue()

    attn_ao0 = sAO[0].v
    attn_ao1 = sAO[1].v

    sWR = Ctr(sem("sWR"), 16)
    sWK = Ctr(sem("sWK"), 16)
    sWV = Ctr(sem("sWV"), 16)
    aow = [(sAO[0], attn_ao0), (sAO[1], attn_ao1), (sW2, w2d_g)]
    w23 = w2_sb[:, :].rearrange("p (w x) -> p w x", w=3)
    op("g", lambda e: e.dma_start(out=w23[:, 0, :], in_=wrd), inc=sWR, waits=aow)
    op("g", lambda e: e.dma_start(out=w23[:, 1, :], in_=wkd), inc=sWK)
    op("g", lambda e: e.dma_start(out=w23[:, 2, :], in_=wvd), inc=sWV)


    psl_pair_free = 0
    sq2_t_free = 0

    def stats_and_apply(src3, dst3, dst_off, dep_v, bc_extra=(), pls=None):
        nonlocal psl_pair_free, psbc_free, psx_free, sq2_t_free
        pl0, pl1 = (psL0, psL1) if pls is None else pls
        mb_sb = mb_sb2[:, :]
        ib_sb = ib_sb2[:, :]
        sqcs = {}
        for h in range(2):
            sqcs[h] = op("a", lambda e, h=h, src3=src3: e.activation(
                sq2[:, :], src3[:, 4 * h:4 * h + 4, :].rearrange("p a t -> p (a t)"),
                AF.Square),
                waits=[(sv, dep_v), (st, sq2_t_free)])
            for a in range(4 * h, 4 * h + 4):
                def s2mm(e, a=a, src3=src3, pl0=pl0, pl1=pl1):
                    e.matmul(pl0[:, :], c_1cm[:, :], src3[:, a, :],
                             start=(a == 0), stop=(a == 7))
                    return e.matmul(pl1[:, :], c_1cm[:, :],
                                    sq2[:, (a % 4) * 512:(a % 4) * 512 + 512],
                                    start=(a == 0), stop=(a == 7))
                op("t", s2mm, inc=st,
                   waits=[(sa, sqcs[h]), (sv, dep_v)] + list(bc_extra))
            sq2_t_free = st.v
        stats_t = st.v

        op("v", lambda e, pl0=pl0, mb_sb=mb_sb: e.tensor_scalar(
            mb_sb, pl0[:, :], 1.0 / C, None, OP.mult), waits=[(st, stats_t)])
        op("v", lambda e, pl1=pl1, ib_sb=ib_sb: e.tensor_scalar(
            ib_sb, pl1[:, :], 1.0 / C, None, OP.mult))
        op("v", lambda e, mb_sb=mb_sb: e.tensor_tensor(sq2[:, 0:512], mb_sb, mb_sb,
                                                       OP.mult))
        mv_c = op("v", lambda e, ib_sb=ib_sb: e.tensor_tensor(ib_sb, ib_sb,
                                                              sq2[:, 0:512], OP.subtract))
        psl_pair_free = mv_c
        isq_c = op("a", lambda e, ib_sb=ib_sb: e.activation(ib_sb, ib_sb, AF.Sqrt,
                                                            bias=c_eps[:, 0:1]),
                   waits=[(sv, mv_c)])
        bce_c = op("v", lambda e, ib_sb=ib_sb: e.reciprocal(ib_sb, ib_sb),
                   waits=[(sa, isq_c)])

        mbB = bass.AP(mb_sb2, 0, [[512, 128], [0, 4], [1, 512]])
        ibB = bass.AP(ib_sb2, 0, [[512, 128], [0, 4], [1, 512]])
        halves = {}
        for hh in range(2):
            op("v", lambda e, hh=hh, src3=src3, mbB=mbB: e.tensor_tensor(
                src3[:, 4 * hh:4 * hh + 4, :], src3[:, 4 * hh:4 * hh + 4, :],
                mbB, OP.subtract),
                waits=([(sv, bce_c)] if hh == 0 else []))
            halves[hh] = op("v", lambda e, hh=hh, src3=src3, dst3=dst3,
                            dst_off=dst_off, ibB=ibB:
                            e.tensor_tensor(
                                dst3[:, 4 * hh:4 * hh + 4, dst_off:dst_off + 512],
                                src3[:, 4 * hh:4 * hh + 4, :], ibB, OP.mult))
        return halves

    pb_free = {0: ol_free[0], 1: ol_free[1]}
    xq_free = 0
    out_dma = {0: 0, 1: 0}
    out_dma_all = 0
    ln1_v = 0
    sig_c = {}
    ka_c = {}
    rbank_free = {0: 0, 1: 0}
    kbank_free = {0: 0, 1: 0}
    vb_free = {0: 0, 1: 0}
    aall_free = 0
    zblk_vs = {}
    resid_vs = {}
    ka_prev = 0

    def emit_A(ck):
        nonlocal xq_free, aall_free
        xq_ss = op("s", lambda e, ck=ck: e.dma_start(
            out=xq3[:, :, :], in_=xqt3[:, ck, :, :]),
            inc=sXQ, waits=[(sv, xq_free), (sAO[0], attn_ao0), (sAO[1], attn_ao1)])
        wid = 513 if ck == NC2 - 1 else 512
        if ck == 0 and NC2 > 1:
            abuf, spac = aall0, 1024
            aall_ss = prologue_state["aall0_ss"]
        else:
            abuf, spac = aall, 1026
            for j_ in range(4):
                op("g", lambda e, ck=ck, j_=j_, wid=wid: e.indirect_dma_start(
                    aall[:, j_ * 1026: j_ * 1026 + 2 * wid], None, a2afl[ck][:, :],
                    IndirectOffsetOnAxis(c_gix[j_][:, :], 0)),
                    inc=sLD, waits=[(st, aall_free), (scc, scc_need[ck]),
                                    (sC, consts_s), (sv, abcp)])
            aall_ss = sLD.v

        par = ck % 2
        off = 1 if ck == NC2 - 1 else 0
        for o in range(8):
            bank = psO0 if o % 2 == 0 else psO1
            pw = [(sLD, aall_ss), (sWO, 16), (sv, pb_free[o % 2])]

            def pmm(e, o=o, bank=bank, off=off, wid=wid, abuf=abuf, spac=spac):
                for a in range(8):
                    base = (a // 2) * spac + (a % 2) * wid + off
                    inst = e.matmul(bank[:, :], wo4[:, a, o, :],
                                    abuf[:, base:base + 512],
                                    start=(a == 0), stop=(a == 7))
                return inst
            pc_ = op("t", pmm, inc=st, waits=pw)
            pb_free[o % 2] = op("v", lambda e, o=o, par=par, bank=bank:
                                e.tensor_tensor(
                vtmp3p[:, par, o, :], xq3[:, o, :], bank[:, :], OP.add),
                waits=[(st, pc_), (sXQ, xq_ss), (sOUT, out_dma[par])])
        resid_vs[ck] = sv.v
        xq_free = sv.v
        aall_free = st.v

    abcp = prologue_state["abcp"]
    emit_A(0)
    emit_prologue2()
    bnd_v = op("v", lambda e: e.tensor_copy(x2c3[:, :, 0], bcol[:, :]),
               waits=[(sv, prologue_state["bnd_v"]), (sAO[0], attn_ao0),
                      (sAO[1], attn_ao1)])
    ln1_v = bnd_v
    for ck in range(NC2):
        par = ck % 2
        src3 = vtmp3p[:, par, :, :]

        if ck > 0:
            ln1_v = op("v", lambda e: e.tensor_copy(x2c3[:, :, 0], x2c3[:, :, 512]),
                       waits=[(sv, ln1_v)])

        ln1_h = stats_and_apply(src3, x2c3, 1, max(resid_vs[ck], ln1_v),
                                bc_extra=[(sv, ka_prev)])
        ln1_v = ln1_h[1]

        if ck + 1 < NC2:
            emit_A(ck + 1)

        mixed = None
        mixed_h = {}
        for h in range(2):
            op("v", lambda e, h=h: e.tensor_tensor(
                sq2[:, :].rearrange("p (a t) -> p a t", a=4),
                x2c3[:, 4 * h:4 * h + 4, 1:513],
                x2c3[:, 4 * h:4 * h + 4, 0:512], OP.subtract),
                waits=[(sv, ln1_h[h]), (sC, consts_s), (st, sq2_t_free)])
            for a in range(4 * h, 4 * h + 4):
                op("v", lambda e, a=a: e.scalar_tensor_tensor(
                    xr3[:, a, :], sq2[:, (a % 4) * 512:(a % 4) * 512 + 512],
                    c_mur[:, a:a + 1], x2c3[:, a, 0:512], OP.mult, OP.add))
                mixed = op("v", lambda e, a=a: e.scalar_tensor_tensor(
                    xk3[:, a, :], sq2[:, (a % 4) * 512:(a % 4) * 512 + 512],
                    c_muk[:, a:a + 1], x2c3[:, a, 0:512], OP.mult, OP.add))
            mixed_h[h] = mixed

        for o in range(8):
            bank = psA if o % 2 == 0 else psB

            def rmm0(e, o=o, bank=bank):
                for a in range(4):
                    inst = e.matmul(bank[:, :], w24[:, 0, a, o, :], xr3[:, a, :],
                                    start=(a == 0), stop=False)
                return inst
            op("t", rmm0, inc=st,
               waits=[(sv, mixed_h[0]), (sWR, 16), (sa, rbank_free[o % 2])])

            def rmm1(e, o=o, bank=bank):
                for a in range(4, 8):
                    inst = e.matmul(bank[:, :], w24[:, 0, a, o, :], xr3[:, a, :],
                                    start=False, stop=(a == 7))
                return inst
            rc = op("t", rmm1, inc=st, waits=[(sv, mixed_h[1])])
            sig_c[o] = op("a", lambda e, o=o, bank=bank: e.activation(
                rg3[:, o, :], bank[:, :], AF.Sigmoid), waits=[(st, rc)])
            rbank_free[o % 2] = sig_c[o]

        for o in range(8):
            bank = psBC if o % 2 == 0 else psX

            def kmm(e, o=o, bank=bank):
                for a in range(8):
                    inst = e.matmul(bank[:, :], w24[:, 1, a, o, :], xk3[:, a, :],
                                    start=(a == 0), stop=(a == 7))
                return inst
            kc = op("t", kmm, inc=st,
                    waits=[(sv, mixed), (sWK, 16), (sv, kbank_free[o % 2])])

            op("v", lambda e, o=o, bank=bank: e.tensor_relu(
                sq2[:, (o % 2) * 512:(o % 2) * 512 + 512], bank[:, :]),
                waits=[(st, kc), (st, sq2_t_free), (sv, mixed)])
            ka_c[o] = op("v", lambda e, o=o: e.tensor_tensor(
                ka3[:, o, :], sq2[:, (o % 2) * 512:(o % 2) * 512 + 512],
                sq2[:, (o % 2) * 512:(o % 2) * 512 + 512], OP.mult))
            kbank_free[o % 2] = ka_c[o]
        ka_done = sv.v
        ka_prev = ka_done


        for o in range(8):
            bank = psA if o % 2 == 0 else psB

            def vmm(e, o=o, bank=bank):
                for a in range(8):
                    inst = e.matmul(bank[:, :], w24[:, 2, a, o, :], ka3[:, a, :],
                                    start=(a == 0), stop=(a == 7))
                return inst
            vc = op("t", vmm, inc=st,
                    waits=[(sv, ka_done), (sWV, 16), (sa, rbank_free[o % 2]),
                           (sv, vb_free[o % 2])])

            op("v", lambda e, o=o, par=par, bank=bank: e.tensor_tensor(
                vtmp3p[:, par, o, :], rg3[:, o, :], bank[:, :], OP.mult),
                waits=[(st, vc)])
            vb_free[o % 2] = op("v", lambda e, o=o, par=par: e.tensor_tensor(
                vtmp3p[:, par, o, :], vtmp3p[:, par, o, :], x2c3[:, o, 1:513], OP.add))
        ffn_v = sv.v

        ln2_v = stats_and_apply(src3, src3, 0, ffn_v, bc_extra=[(sv, ka_done)],
                                pls=(psBC, psX))[1]
        out_dma[par] = op("s", lambda e, ck=ck, par=par: e.dma_start(
            out=out3[:, :, ck * 512:(ck + 1) * 512], in_=vtmp3p[:, par, :, :]),
            inc=sOUT, waits=[(sv, ln2_v)])
        out_dma_all = sOUT.v

    final_ss = out_dma_all
    op("s", lambda e: e.nop(), waits=[(sOUT, final_ss)])

    with nc.Block() as block:
        @block.tensor
        def _(e):
            for f in prog["t"]:
                f(e)

        @block.vector
        def _(e):
            for f in prog["v"]:
                f(e)

        @block.scalar
        def _(e):
            for f in prog["a"]:
                f(e)

        @block.gpsimd
        def _(e):
            for f in prog["g"]:
                f(e)

        @block.sync
        def _(e):
            for f in prog["s"]:
                f(e)

    return nc, st_ctx


def stage_inputs(T, inputs):
    x = np.asarray(inputs["L0_attn_x"], dtype=np.float32)
    w_qkv = np.asarray(inputs["L0_attn_w_qkv"], dtype=np.float32)
    w_out = np.asarray(inputs["L0_attn_w_out"], dtype=np.float32)
    head_scales = np.asarray(inputs["L0_attn_head_scales"], dtype=np.float32)
    W_r = np.asarray(inputs["L0_ffn_W_r"], dtype=np.float32)
    W_k = np.asarray(inputs["L0_ffn_W_k"], dtype=np.float32)
    W_v = np.asarray(inputs["L0_ffn_W_v"], dtype=np.float32)
    mu_r = np.asarray(inputs["L0_ffn_mu_r"], dtype=np.float32)
    mu_k = np.asarray(inputs["L0_ffn_mu_k"], dtype=np.float32)

    TQ = T // 4
    dirs = _dirs()

    def wpre(w):
        return np.ascontiguousarray(
            w.reshape(8, 128, 8, 128).transpose(1, 0, 2, 3).reshape(128, 8192))

    ident = np.eye(128, dtype=np.float32).astype(ml_dtypes.bfloat16)
    onesb = np.ones((128, 1), dtype=np.float32).astype(ml_dtypes.bfloat16)
    onesc = np.ones((128, 1), dtype=np.float32)
    onesr = np.ones((1, 128), dtype=np.float32)
    tri = np.where(np.arange(128)[:, None] <= np.arange(128)[None, :], 0.0, NEG).astype(np.float32)
    mur_s = np.ascontiguousarray(mu_r.reshape(8, 128).T)
    muk_s = np.ascontiguousarray(mu_k.reshape(8, 128).T)
    wout_p = wpre(w_out)
    wr_p = wpre(np.ascontiguousarray(W_r.T))
    wk_p = wpre(np.ascontiguousarray(W_k.T))
    wv_p = wpre(np.ascontiguousarray(W_v.T))

    NTl = T // 512
    NC2l = TQ // 512
    in_maps = []
    for c in range(8):
        b, r = divmod(c, 4)
        h0, h1 = 2 * r, 2 * r + 1
        xt_ = np.ascontiguousarray(x[b].T)
        xt_pre = np.ascontiguousarray(
            xt_.reshape(8, 128, NTl, 512).transpose(1, 2, 0, 3).reshape(128, 8 * T))
        xq_slice = xt_[:, r * TQ:(r + 1) * TQ]
        xqt_pre = np.ascontiguousarray(
            xq_slice.reshape(8, 128, NC2l, 512).transpose(1, 2, 0, 3).reshape(128, 8 * TQ))
        if r == 0:
            xql_ = np.zeros((128, 8), dtype=np.float32)
        else:
            xql_ = np.ascontiguousarray(
                xt_[:, r * TQ - 1].reshape(8, 128).T)
        wq = np.empty((128, 6, 8, 128), dtype=np.float32)
        for si, (stype, h) in enumerate([(0, h0), (0, h1), (1, h0), (1, h1), (2, h0), (2, h1)]):
            blk = w_qkv[:, stype * 1024 + h * 128: stype * 1024 + h * 128 + 128]
            wq[:, si, :, :] = blk.reshape(8, 128, 128).transpose(1, 0, 2)
        hs = head_scales[[h0, h1]].reshape(1, 2).astype(np.float32)
        d3c = np.empty((3, 6), dtype=np.float32)
        for u, h in enumerate((h0, h1)):
            d = dirs[h]
            d3c[:, 3 * u:3 * u + 3] = (np.outer(d, d) / SCL).astype(np.float32)
        in_maps.append({
            "xt": xt_pre,
            "xqt": xqt_pre,
            "xqlp": xql_,
            "wqkv": np.ascontiguousarray(wq.reshape(128, 6 * 8 * 128)),
            "wout": wout_p,
            "wr": wr_p,
            "wk": wk_p,
            "wv": wv_p,
            "mur": mur_s,
            "muk": muk_s,
            "hs2": hs,
            "d3": d3c,
            "identp": ident,
            "onesb": onesb,
            "onesbm": np.ones((128, 128), dtype=np.float32).astype(ml_dtypes.bfloat16),
            "onescm": np.ones((128, 128), dtype=np.float32),
            "onesc": onesc,
            "onesr": onesr,
            "trim": tri,
            "epsv": np.full((128, 1), EPS, dtype=np.float32),
            "gidx": ((4 * b + np.arange(4, dtype=np.int32))[:, None] * 128
                     + np.arange(128, dtype=np.int32)[None, :]).astype(np.int32),
        })
    return in_maps


_CACHE = {}


def kernel(**inputs):
    T = np.asarray(inputs["L0_attn_x"]).shape[1]
    if T not in _CACHE:
        _CACHE[T] = build_program(T)
    nc, _ctx = _CACHE[T]
    in_maps = stage_inputs(T, inputs)
    res = run_bass_kernel_spmd(nc, in_maps, core_ids=list(range(8)))
    TQ = T // 4
    full = np.empty((2, T, 1024), dtype=np.float32)
    for c in range(8):
        b, r = divmod(c, 4)
        full[b, r * TQ:(r + 1) * TQ, :] = res.results[c]["out"].T
    return full


# version 22
# speedup vs baseline: 1.1842x; 1.0352x over previous
import os

os.environ.setdefault("TRN_ALLOW_ICI_ALL_TO_ALL", "1")

import math
import numpy as np
import ml_dtypes

import concourse.bass as bass
import concourse.mybir as mybir
from concourse.bass import AutoArena, IndirectOffsetOnAxis
from concourse import dve_ops
from concourse.bass_utils import run_bass_kernel_spmd

F32 = mybir.dt.float32
BF16 = mybir.dt.bfloat16
AX = mybir.AxisListType.X
OP = mybir.AluOpType
AF = mybir.ActivationFunctionType

C = 1024
HD = 128
SCL = 1.0 / math.sqrt(HD)
EPS = 1e-5
NEG = -1.0e9


def _dirs():
    t = np.array(
        [[(i >> 2 & 1) * 2 - 1, (i >> 1 & 1) * 2 - 1, (i & 1) * 2 - 1] for i in range(8)],
        dtype=np.float64,
    )
    return t / np.sqrt(3.0)


class Ctr:
    def __init__(self, sem, step=1):
        self.sem = sem
        self.step = step
        self.v = 0

    def bump(self, n=None):
        self.v += self.step if n is None else n
        return self.v


def build_program(T):
    TQ = T // 4
    NT = T // 512
    QG = T // 512
    PQ = max(QG // 4, 1)
    NC2 = TQ // 512
    KTT = T // 128

    nc = bass.Bass("TRN2", target_bir_lowering=False)

    xt = nc.dram_tensor("xt", [128, 8 * T], F32, kind="ExternalInput").ap()
    xqt = nc.dram_tensor("xqt", [128, 8 * TQ], F32, kind="ExternalInput").ap()
    xqlp = nc.dram_tensor("xqlp", [128, 8], F32, kind="ExternalInput").ap()
    wqkv = nc.dram_tensor("wqkv", [128, 6 * 8 * 128], F32, kind="ExternalInput").ap()
    wout = nc.dram_tensor("wout", [128, 8 * 8 * 128], F32, kind="ExternalInput").ap()
    wr = nc.dram_tensor("wr", [128, 8 * 8 * 128], F32, kind="ExternalInput").ap()
    wk = nc.dram_tensor("wk", [128, 8 * 8 * 128], F32, kind="ExternalInput").ap()
    wv = nc.dram_tensor("wv", [128, 8 * 8 * 128], F32, kind="ExternalInput").ap()
    mur = nc.dram_tensor("mur", [128, 8], F32, kind="ExternalInput").ap()
    muk = nc.dram_tensor("muk", [128, 8], F32, kind="ExternalInput").ap()
    hs2 = nc.dram_tensor("hs2", [1, 2], F32, kind="ExternalInput").ap()
    d3 = nc.dram_tensor("d3", [3, 6], F32, kind="ExternalInput").ap()
    identp = nc.dram_tensor("identp", [128, 128], BF16, kind="ExternalInput").ap()
    onesb = nc.dram_tensor("onesb", [128, 1], BF16, kind="ExternalInput").ap()
    onesbm = nc.dram_tensor("onesbm", [128, 128], BF16, kind="ExternalInput").ap()
    onescm = nc.dram_tensor("onescm", [128, 128], F32, kind="ExternalInput").ap()
    onesc = nc.dram_tensor("onesc", [128, 1], F32, kind="ExternalInput").ap()
    onesr = nc.dram_tensor("onesr", [1, 128], F32, kind="ExternalInput").ap()
    trim = nc.dram_tensor("trim", [128, 128], F32, kind="ExternalInput").ap()
    epsv = nc.dram_tensor("epsv", [128, 1], F32, kind="ExternalInput").ap()
    gidx = nc.dram_tensor("gidx", [4, 128], mybir.dt.int32, kind="ExternalInput").ap()
    out = nc.dram_tensor("out", [C, TQ], F32, kind="ExternalOutput").ap()

    wrd = nc.dram_tensor("wrd", [128, 8 * 8 * 128], BF16).ap()
    wkd = nc.dram_tensor("wkd", [128, 8 * 8 * 128], BF16).ap()
    wvd = nc.dram_tensor("wvd", [128, 8 * 8 * 128], BF16).ap()
    a2ai = [nc.dram_tensor(f"a2ai{j}", [8, 128, 2, 512 + (1 if j == NC2 - 1 else 0)],
                           BF16).ap() for j in range(NC2)]
    a2ao = [nc.dram_tensor(f"a2ao{j}", [8, 128, 2, 512 + (1 if j == NC2 - 1 else 0)],
                           BF16).ap() for j in range(NC2)]

    import contextlib

    st_ctx = contextlib.ExitStack()
    sbp = lambda name, shape, dt: st_ctx.enter_context(nc.sbuf_tensor(name, shape, dt))
    psx_ = lambda name: st_ctx.enter_context(nc.psum_tensor(name, [128, 512], F32))
    sem = lambda name: st_ctx.enter_context(nc.semaphore(name))

    c_id = sbp("c_id", [128, 128], BF16)
    c_1b = sbp("c_1b", [128, 1], BF16)
    c_1bm = sbp("c_1bm", [128, 128], BF16)
    c_1cm = sbp("c_1cm", [128, 128], F32)
    c_1c = sbp("c_1c", [128, 1], F32)
    c_1r = sbp("c_1r", [1, 128], F32)
    c_tri = sbp("c_tri", [128, 128], F32)
    c_d3 = sbp("c_d3", [3, 6], F32)
    c_hs = sbp("c_hs", [1, 2], F32)
    c_mur = sbp("c_mur", [128, 8], F32)
    c_muk = sbp("c_muk", [128, 8], F32)
    zcol = sbp("zcol", [128, 1], BF16)
    c_eps = sbp("c_eps", [128, 1], F32)
    c_gix = [sbp(f"c_gix{j}", [128, 1], mybir.dt.int32) for j in range(4)]
    mx = sbp("mx", [1, 64], F32)
    hsr = sbp("hsr", [3, 2], F32)
    A3 = sbp("A3", [3, 6], BF16)
    cneg = sbp("cneg", [128, 2], F32)
    xql = sbp("xql", [128, 8], F32)
    wo_sb = sbp("wo_sb", [128, 8 * 8 * 128], BF16)
    vcol = sbp("vcol", [128, 8], F32)
    mi2 = sbp("mi2", [128, 2], F32)
    bsq8 = sbp("bsq8", [128, 8], F32)
    bcol = sbp("bcol", [128, 8], F32)
    aall0 = sbp("aall0", [128, 4 * 1024], BF16)
    sqB = sbp("sqB", [128, 4 * 512], F32)

    arA = AutoArena(nc)
    wq_sb = arA.sbuf_tensor("wq_sb", [128, 6 * 8 * 128], BF16)
    xsb = arA.sbuf_tensor("xsb", [128, 2 * 8 * 512], BF16)
    Qt = arA.sbuf_tensor("Qt", [128, 2 * T], BF16)
    Kt = arA.sbuf_tensor("Kt", [128, 2 * T], BF16)
    Vt = arA.sbuf_tensor("Vt", [128, 2 * T], BF16)
    vstg = arA.sbuf_tensor("vstg", [128, 512], BF16)
    pt = arA.sbuf_tensor("pt", [128, 8 * 512], BF16)
    lsum = arA.sbuf_tensor("lsum", [128, 2 * 512], BF16)
    sq = arA.sbuf_tensor("sq", [128, 2 * 512], BF16)
    lb = arA.sbuf_tensor("lb", [128, 512], F32)
    rscr = arA.sbuf_tensor("rscr", [128, 512], F32)
    aosg = arA.sbuf_tensor("aosg", [128, 2 * 512], BF16)
    arA.allocate()

    arA.free()
    arB = AutoArena(nc)
    aall = arB.sbuf_tensor("aall", [128, 4 * 1026], BF16)
    aall_b = arB.sbuf_tensor("aall_b", [128, 8], BF16)
    w2_sb = arB.sbuf_tensor("w2_sb", [128, 3 * 8 * 8 * 128], BF16)
    xq = arB.sbuf_tensor("xq", [128, 8 * 512], F32)
    x2c = arB.sbuf_tensor("x2c", [128, 8 * 513], F32)
    vtmp = arB.sbuf_tensor("vtmp", [128, 2 * 8 * 512], F32)
    sq2 = arB.sbuf_tensor("sq2", [128, 4 * 512], F32)
    mb_sb2 = arB.sbuf_tensor("mb_sb2", [128, 512], F32)
    ib_sb2 = arB.sbuf_tensor("ib_sb2", [128, 512], F32)
    xr_sb = arB.sbuf_tensor("xr_sb", [128, 8 * 512], BF16)
    xk_sb = arB.sbuf_tensor("xk_sb", [128, 8 * 512], BF16)
    ka_sb = arB.sbuf_tensor("ka_sb", [128, 8 * 512], BF16)
    rg_sb = arB.sbuf_tensor("rg_sb", [128, 8 * 512], F32)
    arB.allocate()
    t1_sb = None

    psA = psx_("psA")
    psB = psx_("psB")
    psO0 = psx_("psO0")
    psO1 = psx_("psO1")
    psL0 = psx_("psL0")
    psL1 = psx_("psL1")
    psBC = psx_("psBC")
    psX = psx_("psX")

    sC = Ctr(sem("sC"), 16)
    sWQ = Ctr(sem("sWQ"), 16)
    sX = [Ctr(sem("sX0"), 16), Ctr(sem("sX1"), 16)]
    sW2 = Ctr(sem("sW2"), 16)
    sAO = [Ctr(sem("sAO0"), 16), Ctr(sem("sAO1"), 16)]
    sZ = Ctr(sem("sZ"), 16)
    sLD = Ctr(sem("sLD"), 16)
    sLD0 = Ctr(sem("sLD0"), 16)
    sLD2 = Ctr(sem("sLD2"), 16)
    sWO = Ctr(sem("sWO"), 16)
    sAB = Ctr(sem("sAB"), 16)
    sXQL = Ctr(sem("sXQL"), 16)
    sXQ = Ctr(sem("sXQ"), 16)
    sOUT = Ctr(sem("sOUT"), 16)
    sgp = Ctr(sem("sgp"), 1)
    st = Ctr(sem("st"), 1)
    sv = Ctr(sem("sv"), 1)
    sa = Ctr(sem("sa"), 1)
    scc = Ctr(sem("scc"), 1)

    prog = {k: [] for k in ("t", "v", "a", "g", "s")}
    highwater = {}

    def op(eng, builder, inc=None, waits=()):
        if eng == "ad":
            eng = "a"
        elif eng == "gc":
            eng = "g"
            waits = list(waits) + [(sgp, sgp.v)]
            inc = sgp
        elif eng == "v":
            waits = list(waits) + [(sv, sv.v)]
            inc = sv
        elif eng == "a":
            waits = list(waits) + [(sa, sa.v)]
            inc = sa
        needed = []
        for ctr, val in waits:
            if val <= 0:
                continue
            key = (eng, id(ctr))
            if highwater.get(key, 0) >= val:
                continue
            highwater[key] = val
            needed.append((ctr.sem, val))
        inc_sem = None
        inc_n = 0
        if inc is not None:
            inc.bump()
            inc_sem = inc.sem
            inc_n = inc.step

        def f(e, builder=builder, needed=tuple(needed), inc_sem=inc_sem, inc_n=inc_n):
            for s_, v_ in needed:
                e.wait_ge(s_, v_)
            inst = builder(e)
            if inc_sem is not None:
                inst.then_inc(inc_sem, inc_n)

        prog[eng].append(f)
        return inc.v if inc is not None else None

    for dst, src in ((c_id[:, :], identp), (c_1b[:, :], onesb), (c_1c[:, :], onesc),
                     (c_1bm[:, :], onesbm), (c_1cm[:, :], onescm),
                     (c_1r[:, :], onesr), (c_tri[:, :], trim),
                     (c_d3[:, :], d3), (c_hs[:, :], hs2), (c_mur[:, :], mur),
                     (c_muk[:, :], muk), (c_eps[:, :], epsv),
                     (c_gix[0][:, :], gidx[0:1, :]), (c_gix[1][:, :], gidx[1:2, :]),
                     (c_gix[2][:, :], gidx[2:3, :]), (c_gix[3][:, :], gidx[3:4, :]),
                     (xql[:, :], xqlp)):
        op("s", lambda e, d=dst, s=src: e.dma_start(out=d, in_=s), inc=sC)
    consts_s = sC.v
    op("g", lambda e: e.dma_start(out=wq_sb[:, :], in_=wqkv), inc=sWQ)
    op("g", lambda e: e.dma_start(out=wo_sb[:, :], in_=wout), inc=sWO)
    op("g", lambda e: e.dma_start(out=wrd, in_=wr), inc=sW2)
    op("g", lambda e: e.dma_start(out=wkd, in_=wk), inc=sW2)
    op("g", lambda e: e.dma_start(out=wvd, in_=wv), inc=sW2)
    w2d_g = sW2.v

    zv = op("v", lambda e: e.memset(zcol[:, :], 0.0), inc=sv)
    for sh in (0, 4):
        for u in range(2):
            op("s", lambda e, sh=sh, u=u: e.dma_start(
                out=a2ai[NC2 - 1][sh, :, u, 0:1], in_=zcol[:, :],
                allow_slow_non_contiguous=True),
               inc=sZ, waits=[(sv, zv)])
    zero_ss = sZ.v

    xt3 = xt.rearrange("p (tc a t) -> p tc a t", tc=NT, a=8)
    xsb3 = xsb[:, :].rearrange("p (par a t) -> p par a t", par=2, a=8)
    wq4 = wq_sb[:, :].rearrange("p (s a d) -> p s a d", s=6, a=8)
    Qt2 = Qt[:, :].rearrange("p (u t) -> p u t", u=2)
    Kt2 = Kt[:, :].rearrange("p (u t) -> p u t", u=2)
    Vt3 = Vt[:, :].rearrange("p (u kt d) -> p u kt d", u=2, kt=KTT)

    x_done = []
    x_cons = []
    bank_free = {0: 0, 1: 0}
    psx_free = 0

    for tc in range(NT):
        par = tc % 2
        w_ = [] if tc < 2 else [(st, x_cons[tc - 2])]
        op("g", lambda e, par=par, tc=tc:
           e.dma_start(out=xsb3[:, par, :, :], in_=xt3[:, tc, :, :]),
           inc=sX[par], waits=w_)
        x_done.append((sX[par], sX[par].v))

        for ob in range(6):
            bname = (tc * 6 + ob) % 2
            bank = psA if bname == 0 else psB

            def mms(e, bank=bank, ob=ob, par=par):
                for a in range(8):
                    inst = e.matmul(bank[:, :], wq4[:, ob, a, :], xsb3[:, par, a, :],
                                    start=(a == 0), stop=(a == 7))
                return inst
            mmc = op("t", mms, inc=st,
                     waits=[x_done[tc], (sWQ, 16), (sv, bank_free[bname])])

            u = ob % 2
            kind = ob // 2
            if kind == 0:
                dst = Qt2[:, u, tc * 512:(tc + 1) * 512]
            elif kind == 1:
                dst = Kt2[:, u, tc * 512:(tc + 1) * 512]
            else:
                dst = vstg[:, :]
            bank_free[bname] = op("v", lambda e, d=dst, b=bank: e.tensor_copy(d, b[:, :]),
                                  inc=sv, waits=[(st, mmc)])

            if kind == 2:
                vst = sv.v

                def vtr(e):
                    psXb = psX[:, :].bitcast(BF16)
                    for jj in range(4):
                        inst = e.transpose(psXb[:, jj * 128:(jj + 1) * 128],
                                           vstg[:, jj * 128:(jj + 1) * 128], c_id[:, :])
                    return inst
                trc = op("t", vtr, inc=st,
                         waits=[(sv, vst), (sv, psx_free), (sC, consts_s)])
                psx_free = op("v", lambda e, u=u, tc=tc: e.tensor_copy(
                    Vt3[:, u, tc * 4:(tc + 1) * 4, :].rearrange("p k d -> p (k d)"),
                    psX[:, :].bitcast(BF16)[:, 0:512]), inc=sv, waits=[(st, trc)])
        x_cons.append(st.v)

    ph1_v = sv.v

    psbc_free = 0
    adj_frees = {0: 0, 1: 0}
    psl_frees = {0: 0, 1: 0}
    sq_frees = {0: 0, 1: 0}
    for u in range(2):
        hc = op("t", lambda e, u=u: e.matmul(psX[0:3, 0:1], c_1r[0:1, 0:3],
                                             c_hs[0:1, u:u + 1], start=True, stop=True),
                inc=st, waits=[(sC, consts_s), (sv, psx_free)])
        hv = op("v", lambda e, u=u: e.tensor_copy(hsr[:, u:u + 1], psX[0:3, 0:1]),
                inc=sv, waits=[(st, hc)])
        psx_free = hv
        a3done = op("v", lambda e, u=u: e.tensor_scalar(
            A3[:, 3 * u:3 * u + 3], c_d3[:, 3 * u:3 * u + 3], hsr[:, u:u + 1], None, OP.mult),
            inc=sv, waits=[(sv, hv), (sC, consts_s)])

        for tc in range(NT):
            abank = psBC if tc % 2 == 0 else psX
            amc = op("t", lambda e, u=u, tc=tc, abank=abank: e.matmul(
                abank[0:3, 0:512], A3[:, 3 * u:3 * u + 3],
                Qt2[0:3, u, tc * 512:(tc + 1) * 512], start=True, stop=True),
                inc=st, waits=[(sv, a3done), (sv, ph1_v),
                               (sv, adj_frees[tc % 2])])
            adj_frees[tc % 2] = op("v", lambda e, u=u, tc=tc, abank=abank:
                                   e.tensor_tensor(
                Qt2[0:3, u, tc * 512:(tc + 1) * 512],
                Qt2[0:3, u, tc * 512:(tc + 1) * 512],
                abank[0:3, 0:512], OP.add), inc=sv, waits=[(st, amc)])
        adj_v = sv.v
        psbc_free = max(psbc_free, adj_frees[0])
        psx_free = max(psx_free, adj_frees[1])

        for which in ("q", "k"):
            buf = Qt2 if which == "q" else Kt2
            base = 0 if which == "q" else 16
            dep = adj_v if which == "q" else ph1_v
            for tc in range(NT):
                sl = tc % 2
                bank = psL0 if sl == 0 else psL1
                sqc = op("a", lambda e, u=u, tc=tc, sl=sl, buf=buf: e.activation(
                    sq[:, sl * 512:sl * 512 + 512],
                    buf[:, u, tc * 512:(tc + 1) * 512], AF.Square),
                    waits=[(sv, dep), (st, sq_frees[sl])])
                oc = op("t", lambda e, sl=sl, bank=bank: e.matmul(
                    bank[0:1, :], c_1b[:, 0:1], sq[:, sl * 512:sl * 512 + 512],
                    start=True, stop=True),
                    inc=st, waits=[(sa, sqc), (sC, consts_s), (sv, psl_frees[sl])])
                sq_frees[sl] = oc
                psl_frees[sl] = op("v", lambda e, tc=tc, sl=sl, base=base, bank=bank:
                                   e.reduce_max(
                    mx[0:1, base + tc:base + tc + 1], bank[0:1, 0:512], axis=AX),
                    inc=sv, waits=[(st, oc)])

        op("v", lambda e: e.reduce_max(mx[0:1, 32:33], mx[0:1, 0:NT], axis=AX),
           inc=sv, waits=[(sv, max(psl_frees[0], psl_frees[1]))])
        f0 = op("v", lambda e: e.reduce_max(mx[0:1, 33:34], mx[0:1, 16:16 + NT], axis=AX),
                inc=sv)
        fc = op("v", lambda e: e.tensor_tensor(mx[0:1, 34:35], mx[0:1, 32:33],
                                               mx[0:1, 33:34], OP.mult),
                inc=sv, waits=[(sv, f0)])

        op("a", lambda e: e.activation(mx[0:1, 35:36], mx[0:1, 34:35], AF.Sqrt),
           waits=[(sv, fc)])
        cc_ = op("a", lambda e: e.mul(mx[0:1, 36:37], mx[0:1, 35:36], -SCL))
        cbc_c = op("t", lambda e: e.matmul(psBC[:, 0:1], c_1r[0:1, :], mx[0:1, 36:37],
                                           start=True, stop=True),
                   inc=st, waits=[(sa, cc_), (sv, psbc_free)])
        psbc_free = op("v", lambda e, u=u: e.tensor_copy(cneg[:, u:u + 1], psBC[:, 0:1]),
                       inc=sv, waits=[(st, cbc_c)])
    bounds_v = sv.v

    xqt3 = xqt.rearrange("p (ck a t) -> p ck a t", ck=NC2, a=8)

    wo4 = wo_sb[:, :].rearrange("p (a o d) -> p a o d", a=8, o=8)
    w24 = w2_sb[:, :].rearrange("p (w a o d) -> p w a o d", w=3, a=8, o=8)
    xq3 = xq[:, :].rearrange("p (a t) -> p a t", a=8)
    x2c3 = x2c[:, :].rearrange("p (a t) -> p a t", a=8)
    vtmp3p = vtmp[:, :].rearrange("p (par a t) -> p par a t", par=2, a=8)
    xr3 = xr_sb[:, :].rearrange("p (a t) -> p a t", a=8)
    xk3 = xk_sb[:, :].rearrange("p (a t) -> p a t", a=8)
    ka3 = ka_sb[:, :].rearrange("p (a t) -> p a t", a=8)
    rg3 = rg_sb[:, :].rearrange("p (a t) -> p a t", a=8)
    out3 = out.rearrange("(a p) t -> p a t", p=128)

    rounds = [NC2 - 1] + list(range(NC2 - 1))
    a2afl = [a2ao[j].rearrange("r p u t -> (r p) (u t)") for j in range(NC2)]
    scc_need = {jr: i + 1 for i, jr in enumerate(rounds)}
    prologue_state = {}

    def emit_prologue0():
        ab_ss = 0
        for j_ in range(4):
            ab_ss = op("g", lambda e, j_=j_: e.indirect_dma_start(
                aall[:, j_ * 1026:(j_ + 1) * 1026], None, a2afl[NC2 - 1][:, :],
                IndirectOffsetOnAxis(c_gix[j_][:, :], 0)),
                inc=sAB, waits=[(scc, scc_need[NC2 - 1]), (sC, consts_s)])
        abview = bass.AP(aall, 0, [[4104, 128], [1026, 4], [513, 2]])
        abcp = op("v", lambda e: e.tensor_copy(
            aall_b[:, :].rearrange("p (j u) -> p j u", u=2), abview),
            waits=[(sAB, ab_ss)])
        prologue_state["abcp"] = abcp

    def emit_prologue():
        nonlocal psx_free
        abcp = prologue_state["abcp"]

        def bpmm(e):
            for o in range(8):
                for a in range(8):
                    inst = e.matmul(psX[:, o:o + 1], wo4[:, a, o, :],
                                    aall_b[:, a:a + 1],
                                    start=(a == 0), stop=(a == 7))
            return inst
        bp_c = op("t", bpmm, inc=st,
                  waits=[(sv, abcp), (sWO, 16), (sv, psx_free)])
        bres_c = op("v", lambda e: e.tensor_tensor(vcol[:, :], xql[:, :], psX[:, 0:8], OP.add),
                    inc=sv, waits=[(st, bp_c), (sC, consts_s)])
        psx_free = bres_c
        prologue_state["bnd_v"] = bres_c

    def emit_prologue2():
        bres_c = prologue_state["bnd_v"]
        bsq_c = op("a", lambda e: e.activation(bsq8[:, :], vcol[:, :], AF.Square),
                   waits=[(sv, bres_c)])

        def bstat(e):
            e.matmul(psX[0:1, 16:24], c_1c[:, 0:1], vcol[:, :], start=True, stop=True)
            return e.matmul(psX[0:1, 32:40], c_1c[:, 0:1], bsq8[:, :], start=True, stop=True)
        bst_c = op("t", bstat, inc=st, waits=[(sa, bsq_c), (sv, bres_c)])

        op("v", lambda e: e.reduce_sum(mx[0:1, 40:41], psX[0:1, 16:24], axis=AX),
           inc=sv, waits=[(st, bst_c)])
        bm0 = op("v", lambda e: e.reduce_sum(mx[0:1, 41:42], psX[0:1, 32:40], axis=AX),
                 inc=sv)

        op("v", lambda e: e.tensor_scalar(mx[0:1, 40:41], mx[0:1, 40:41], 1.0 / C,
                                          None, OP.mult), waits=[(sv, bm0)])
        op("v", lambda e: e.tensor_scalar(mx[0:1, 41:42], mx[0:1, 41:42], 1.0 / C,
                                          None, OP.mult))
        op("v", lambda e: e.tensor_tensor(mx[0:1, 42:43], mx[0:1, 40:41], mx[0:1, 40:41],
                                          OP.mult))
        bmv_c = op("v", lambda e: e.tensor_tensor(mx[0:1, 41:42], mx[0:1, 41:42],
                                                  mx[0:1, 42:43], OP.subtract))
        bsi_c = op("a", lambda e: e.activation(mx[0:1, 41:42], mx[0:1, 41:42],
                                               AF.Sqrt, bias=c_eps[0:1, 0:1]),
                   inc=sa, waits=[(sv, bmv_c)])
        bir_c = op("v", lambda e: e.reciprocal(mx[0:1, 41:42], mx[0:1, 41:42]),
                   inc=sv, waits=[(sa, bsi_c)])
        bbc_c = op("t", lambda e: e.matmul(psX[:, 40:42], c_1r[0:1, :], mx[0:1, 40:42],
                                           start=True, stop=True),
                   inc=st, waits=[(sv, bir_c)])
        mi_ev = op("v", lambda e: e.tensor_copy(mi2[:, :], psX[:, 40:42]),
                   inc=sv, waits=[(st, bbc_c)])
        bnd_v = op("v", lambda e: e.tensor_scalar(
            bcol[:, :], vcol[:, :], mi2[:, 0:1], mi2[:, 1:2], OP.subtract, OP.mult),
            waits=[(sv, mi_ev)])
        prologue_state["bnd_v"] = bnd_v

    pt2 = pt[:, :].rearrange("p (r t) -> p r t", r=8)
    ls2 = lsum[:, :].rearrange("p (r t) -> p r t", r=2)
    aosg2 = aosg[:, :].rearrange("p (r t) -> p r t", r=2)

    exp_ring = {0: 0, 1: 0, 2: 0}
    pt_ring = {k: 0 for k in range(8)}
    pt_ring_v = {k: 0 for k in range(8)}
    lsum_free = {0: 0, 1: 0}
    ol_free = {0: 0, 1: 0}
    ao_dma = {0: 0, 1: 0}
    lb_free = 0

    qi = -1
    a2a_cs = {}
    for jr in rounds:
        for u in range(2):
          for qtr in range(4):
            qg = qtr * PQ + jr
            qi += 1
            ktmax = 4 * qg + 4
            ngrp = ktmax // 4
            obank = psO0 if qi % 2 == 0 else psO1
            lbank = psL0 if qi % 2 == 0 else psL1

            smm_cs = {}
            exp_cs = {}
            grp_add = {}

            def geom(kt):
                diag = kt - 4 * qg
                if diag >= 0:
                    return 128 * diag, 512 - 128 * diag
                return 0, 512

            sbanks = (psA, psB, psBC)
            for step in range(ktmax + 2):
                tri_c = None
                if step < ktmax:
                    kt = step
                    sbank = sbanks[kt % 3]
                    o0, ncols = geom(kt)
                    qc0 = qg * 512 + o0
                    smm_cs[kt] = op("t", lambda e, u=u, kt=kt, sbank=sbank, qc0=qc0,
                                    ncols=ncols, o0=o0:
                                    e.matmul(sbank[:, o0:o0 + ncols],
                                             Kt2[:, u, kt * 128:(kt + 1) * 128],
                                             Qt2[:, u, qc0:qc0 + ncols],
                                             start=True, stop=True),
                                    inc=st,
                                    waits=[(sv, bounds_v), (sa, exp_ring[kt % 3]),
                                           (sv, psbc_free if kt % 3 == 2 else 0)])
                    if kt - 4 * qg >= 0:
                        tri_c = op("v", lambda e, sbank=sbank, o0=o0: e.tensor_tensor(
                            sbank[:, o0:o0 + 128], sbank[:, o0:o0 + 128],
                            c_tri[:, :], OP.add),
                            waits=[(st, smm_cs[kt]), (sC, consts_s)])
                if step >= 2:
                    kt = step - 2
                    o0, ncols = geom(kt)
                    ow = [(sa, exp_cs[kt])]
                    if kt == 0:
                        ow.append((sv, ol_free[qi % 2]))
                    pt_ring[kt % 8] = op("t", lambda e, u=u, kt=kt, o0=o0, ncols=ncols,
                                         obank=obank,
                                         first=(kt == 0), last=(kt == ktmax - 1):
                                         e.matmul(obank[:, o0:o0 + ncols],
                                                  Vt3[:, u, kt, :],
                                                  pt2[:, kt % 8, o0:o0 + ncols],
                                                  start=first, stop=last),
                                         inc=st, waits=ow)
                    if kt % 4 == 1 and kt // 4 >= 1:
                        g = kt // 4 - 1
                        gp = 4 * (g % 2)
                        a1 = op("v", lambda e, g=g, gp=gp: e.tensor_tensor(
                            ls2[:, g % 2, :], pt2[:, gp, :], pt2[:, gp + 1, :],
                            OP.add),
                            waits=[(sa, exp_cs[4 * g + 3]),
                                   (st, lsum_free[g % 2])])
                        pt_ring_v[gp] = a1
                        pt_ring_v[gp + 1] = a1
                    if kt % 4 == 3 and kt // 4 >= 1:
                        g = kt // 4 - 1
                        gp = 4 * (g % 2)
                        op("v", lambda e, g=g, gp=gp: e.tensor_tensor(
                            ls2[:, g % 2, :], ls2[:, g % 2, :], pt2[:, gp + 2, :],
                            OP.add))
                        grp_add[g] = op("v", lambda e, g=g, gp=gp: e.tensor_tensor(
                            ls2[:, g % 2, :], ls2[:, g % 2, :], pt2[:, gp + 3, :],
                            OP.add))
                        pt_ring_v[gp + 2] = grp_add[g]
                        pt_ring_v[gp + 3] = grp_add[g]
                        ow2 = [(sv, grp_add[g])]
                        if g == 0:
                            ow2.append((sv, ol_free[qi % 2]))
                        lsum_free[g % 2] = op("t", lambda e, g=g, lbank=lbank,
                                              first=(g == 0):
                                              e.matmul(lbank[:, :], c_1bm[:, :],
                                                       ls2[:, g % 2, :],
                                                       start=first, stop=False),
                                              inc=st, waits=ow2)
                if step < ktmax:
                    kt = step
                    sbank = sbanks[kt % 3]
                    o0, ncols = geom(kt)
                    ew = [(st, smm_cs[kt]), (st, pt_ring[kt % 8]),
                          (sv, pt_ring_v[kt % 8])]
                    if tri_c is not None:
                        ew.append((sv, tri_c))
                    exp_cs[kt] = op("a", lambda e, u=u, sbank=sbank, kt=kt, o0=o0,
                                    ncols=ncols:
                                    e.activation(pt2[:, kt % 8, o0:o0 + ncols],
                                                 sbank[:, o0:o0 + ncols], AF.Exp,
                                                 bias=cneg[:, u:u + 1], scale=SCL),
                                    waits=ew)
                    exp_ring[kt % 3] = exp_cs[kt]

            g = ngrp - 1
            gp = 4 * (g % 2)
            for kk in range(4 * g + 1, 4 * g + 4):
                oo, nn = geom(kk)
                op("v", lambda e, kk=kk, oo=oo, gp=gp: e.memset(
                    pt2[:, gp + (kk % 4), 0:oo], 0.0),
                    waits=[(sa, exp_cs[kk]), (st, pt_ring[kk % 8])])
            op("v", lambda e, g=g, gp=gp: e.tensor_tensor(
                ls2[:, g % 2, :], pt2[:, gp, :], pt2[:, gp + 1, :], OP.add),
                waits=[(st, lsum_free[g % 2])])
            op("v", lambda e, g=g, gp=gp: e.tensor_tensor(
                ls2[:, g % 2, :], ls2[:, g % 2, :], pt2[:, gp + 2, :], OP.add))
            grp_add[g] = op("v", lambda e, g=g, gp=gp: e.tensor_tensor(
                ls2[:, g % 2, :], ls2[:, g % 2, :], pt2[:, gp + 3, :], OP.add))
            for kk in range(4 * g, 4 * g + 4):
                pt_ring_v[gp + (kk % 4)] = grp_add[g]
            ow2 = [(sv, grp_add[g])]
            if g == 0:
                ow2.append((sv, ol_free[qi % 2]))
            lsum_free[g % 2] = op("t", lambda e, g=g, lbank=lbank,
                                  first=(g == 0):
                                  e.matmul(lbank[:, :], c_1bm[:, :],
                                           ls2[:, g % 2, :],
                                           start=first, stop=True),
                                  inc=st, waits=ow2)

            kloop_t = st.v
            rec_c = op("v", lambda e, lbank=lbank: e.reciprocal(lb[:, :], lbank[:, :]),
                       waits=[(st, kloop_t), (sv, lb_free)])
            aev_c = op("v", lambda e, obank=obank, qi=qi: e.tensor_tensor(
                aosg2[:, qi % 2, :], obank[:, :], lb[:, :], OP.mult),
                waits=[(sv, rec_c), (sAO[qi % 2], ao_dma[qi % 2])])
            lb_free = aev_c
            ol_free[qi % 2] = aev_c

            extra = (jr == NC2 - 1) and (qtr < 3)
            sAO[qi % 2].bump((3 if extra else 1) * 16)
            bcol0 = 1 if jr == NC2 - 1 else 0

            def aodma(e, u=u, qi=qi, qtr=qtr, jr=jr, extra=extra, bcol0=bcol0,
                      sem_=sAO[qi % 2].sem):
                tin = a2ai[jr]
                if extra:
                    e.dma_start(out=tin[qtr + 1, :, u, 0:1],
                                in_=aosg2[:, qi % 2, 511:512],
                                allow_slow_non_contiguous=True).then_inc(sem_, 16)
                    e.dma_start(out=tin[qtr + 5, :, u, 0:1],
                                in_=aosg2[:, qi % 2, 511:512],
                                allow_slow_non_contiguous=True).then_inc(sem_, 16)
                e.dma_start(out=tin[qtr, :, u, bcol0:bcol0 + 512],
                            in_=aosg2[:, qi % 2, :]).then_inc(sem_, 16)
                return e.dma_start(
                    out=tin[qtr + 4, :, u, bcol0:bcol0 + 512], in_=aosg2[:, qi % 2, :])
            ao_dma[qi % 2] = op("s", aodma, inc=sAO[qi % 2], waits=[(sv, aev_c)])
        cw = [(sAO[0], sAO[0].v), (sAO[1], sAO[1].v)]
        if jr == NC2 - 1:
            cw.append((sZ, zero_ss))
        a2a_cs[jr] = op("g", lambda e, jr=jr: e.collective_compute(
            "AllToAll", OP.bypass, replica_groups=[[0, 1, 2, 3, 4, 5, 6, 7]],
            ins=[a2ai[jr].opt()], outs=[a2ao[jr].opt()]),
            inc=scc, waits=cw)
        if jr == 0:
            emit_prologue0()
            if NC2 > 1:
                for j_ in range(4):
                    op("g", lambda e, j_=j_: e.indirect_dma_start(
                        aall0[:, j_ * 1024:(j_ + 1) * 1024], None, a2afl[0][:, :],
                        IndirectOffsetOnAxis(c_gix[j_][:, :], 0)),
                        inc=sLD0, waits=[(scc, scc_need[0]), (sC, consts_s)])
                prologue_state["aall0_ss"] = sLD0.v
        if jr == rounds[min(2, len(rounds) - 1)]:
            emit_prologue()

    attn_ao0 = sAO[0].v
    attn_ao1 = sAO[1].v

    sWR = Ctr(sem("sWR"), 16)
    sWK = Ctr(sem("sWK"), 16)
    sWV = Ctr(sem("sWV"), 16)
    aow = [(sAO[0], attn_ao0), (sAO[1], attn_ao1), (sW2, w2d_g)]
    w23 = w2_sb[:, :].rearrange("p (w x) -> p w x", w=3)
    op("g", lambda e: e.dma_start(out=w23[:, 0, :], in_=wrd), inc=sWR, waits=aow)
    op("g", lambda e: e.dma_start(out=w23[:, 1, :], in_=wkd), inc=sWK)
    op("g", lambda e: e.dma_start(out=w23[:, 2, :], in_=wvd), inc=sWV)



    pair_free = {0: 0, 1: 0}
    sqb_t_free = {0: 0, 1: 0}

    def stats_and_apply(src3, dst3, dst_off, dep_v, bc_extra=(), pls=None):
        nonlocal psbc_free, psx_free
        key = 0 if pls is None else 1
        pl0, pl1 = (psL0, psL1) if pls is None else pls
        sqb = sqB if pls is None else sq2
        mb_sb = mb_sb2[:, :]
        ib_sb = ib_sb2[:, :]
        sqcs = {}
        for h in range(2):
            sqcs[h] = op("a", lambda e, h=h, src3=src3, sqb=sqb: e.activation(
                sqb[:, :], src3[:, 4 * h:4 * h + 4, :].rearrange("p a t -> p (a t)"),
                AF.Square),
                waits=[(sv, dep_v), (st, sqb_t_free[key]),
                       (sv, pair_free[key])])
            for a in range(4 * h, 4 * h + 4):
                def s2mm(e, a=a, src3=src3, pl0=pl0, pl1=pl1, sqb=sqb):
                    e.matmul(pl0[:, :], c_1cm[:, :], src3[:, a, :],
                             start=(a == 0), stop=(a == 7))
                    return e.matmul(pl1[:, :], c_1cm[:, :],
                                    sqb[:, (a % 4) * 512:(a % 4) * 512 + 512],
                                    start=(a == 0), stop=(a == 7))
                op("t", s2mm, inc=st,
                   waits=[(sa, sqcs[h]), (sv, dep_v), (sv, pair_free[key])]
                   + list(bc_extra))
            sqb_t_free[key] = st.v
        stats_t = st.v

        op("v", lambda e, pl0=pl0, mb_sb=mb_sb: e.tensor_scalar(
            mb_sb, pl0[:, :], 1.0 / C, None, OP.mult), waits=[(st, stats_t)])
        op("v", lambda e, pl1=pl1, ib_sb=ib_sb: e.tensor_scalar(
            ib_sb, pl1[:, :], 1.0 / C, None, OP.mult))
        op("v", lambda e, mb_sb=mb_sb, sqb=sqb: e.tensor_tensor(
            sqb[:, 0:512], mb_sb, mb_sb, OP.mult))
        mv_c = op("v", lambda e, ib_sb=ib_sb, sqb=sqb: e.tensor_tensor(
            ib_sb, ib_sb, sqb[:, 0:512], OP.subtract))
        pair_free[key] = mv_c
        isq_c = op("a", lambda e, ib_sb=ib_sb: e.activation(ib_sb, ib_sb, AF.Sqrt,
                                                            bias=c_eps[:, 0:1]),
                   waits=[(sv, mv_c)])
        bce_c = op("v", lambda e, ib_sb=ib_sb: e.reciprocal(ib_sb, ib_sb),
                   waits=[(sa, isq_c)])

        mbB = bass.AP(mb_sb2, 0, [[512, 128], [0, 4], [1, 512]])
        ibB = bass.AP(ib_sb2, 0, [[512, 128], [0, 4], [1, 512]])
        halves = {}
        for hh in range(2):
            op("v", lambda e, hh=hh, src3=src3, mbB=mbB: e.tensor_tensor(
                src3[:, 4 * hh:4 * hh + 4, :], src3[:, 4 * hh:4 * hh + 4, :],
                mbB, OP.subtract),
                waits=([(sv, bce_c)] if hh == 0 else []))
            halves[hh] = op("v", lambda e, hh=hh, src3=src3, dst3=dst3,
                            dst_off=dst_off, ibB=ibB:
                            e.tensor_tensor(
                                dst3[:, 4 * hh:4 * hh + 4, dst_off:dst_off + 512],
                                src3[:, 4 * hh:4 * hh + 4, :], ibB, OP.mult))
        return halves

    pb_free = {0: ol_free[0], 1: ol_free[1]}
    xq_free = 0
    out_dma = {0: 0, 1: 0}
    out_dma_all = 0
    ln1_v = 0
    sig_c = {}
    ka_c = {}
    rbank_free = {0: 0, 1: 0}
    kbank_free = {0: 0, 1: 0}
    vb_free = {0: 0, 1: 0}
    aall_free = 0
    aall0_free = 0
    zblk_vs = {}
    resid_vs = {}
    ka_prev = 0

    def emit_A(ck):
        nonlocal xq_free, aall_free, aall0_free
        xq_ss = op("s", lambda e, ck=ck: e.dma_start(
            out=xq3[:, :, :], in_=xqt3[:, ck, :, :]),
            inc=sXQ, waits=[(sv, xq_free), (sAO[0], attn_ao0), (sAO[1], attn_ao1)])
        wid = 513 if ck == NC2 - 1 else 512
        if ck % 2 == 0 and ck != NC2 - 1:
            abuf, spac, asem = aall0, 1024, sLD0
            if ck == 0:
                aall_ss = prologue_state["aall0_ss"]
            else:
                for j_ in range(4):
                    op("g", lambda e, ck=ck, j_=j_: e.indirect_dma_start(
                        aall0[:, j_ * 1024:(j_ + 1) * 1024], None, a2afl[ck][:, :],
                        IndirectOffsetOnAxis(c_gix[j_][:, :], 0)),
                        inc=sLD0, waits=[(st, aall0_free), (scc, scc_need[ck]),
                                         (sC, consts_s)])
                aall_ss = sLD0.v
        else:
            abuf, spac, asem = aall, 1026, sLD
            for j_ in range(4):
                op("g", lambda e, ck=ck, j_=j_, wid=wid: e.indirect_dma_start(
                    aall[:, j_ * 1026: j_ * 1026 + 2 * wid], None, a2afl[ck][:, :],
                    IndirectOffsetOnAxis(c_gix[j_][:, :], 0)),
                    inc=sLD, waits=[(st, aall_free), (scc, scc_need[ck]),
                                    (sC, consts_s), (sv, abcp)])
            aall_ss = sLD.v

        par = ck % 2
        off = 1 if ck == NC2 - 1 else 0
        for o in range(8):
            bank = psO0 if o % 2 == 0 else psO1
            pw = [(asem, aall_ss), (sWO, 16), (sv, pb_free[o % 2])]

            def pmm(e, o=o, bank=bank, off=off, wid=wid, abuf=abuf, spac=spac):
                for a in range(8):
                    base = (a // 2) * spac + (a % 2) * wid + off
                    inst = e.matmul(bank[:, :], wo4[:, a, o, :],
                                    abuf[:, base:base + 512],
                                    start=(a == 0), stop=(a == 7))
                return inst
            pc_ = op("t", pmm, inc=st, waits=pw)
            pb_free[o % 2] = op("v", lambda e, o=o, par=par, bank=bank:
                                e.tensor_tensor(
                vtmp3p[:, par, o, :], xq3[:, o, :], bank[:, :], OP.add),
                waits=[(st, pc_), (sXQ, xq_ss), (sOUT, out_dma[par])])
        resid_vs[ck] = sv.v
        xq_free = sv.v
        if ck % 2 == 0 and ck != NC2 - 1:
            aall0_free = st.v
        else:
            aall_free = st.v

    abcp = prologue_state["abcp"]
    emit_A(0)
    emit_prologue2()
    bnd_v = op("v", lambda e: e.tensor_copy(x2c3[:, :, 0], bcol[:, :]),
               waits=[(sv, prologue_state["bnd_v"]), (sAO[0], attn_ao0),
                      (sAO[1], attn_ao1)])
    ln1_v = bnd_v
    for ck in range(NC2):
        par = ck % 2
        src3 = vtmp3p[:, par, :, :]

        if ck > 0:
            ln1_v = op("v", lambda e: e.tensor_copy(x2c3[:, :, 0], x2c3[:, :, 512]),
                       waits=[(sv, ln1_v)])

        ln1_h = stats_and_apply(src3, x2c3, 1, resid_vs[ck])
        ln1_v = ln1_h[1]

        mixed = None
        mixed_h = {}
        for h in range(2):
            op("v", lambda e, h=h: e.tensor_tensor(
                sq2[:, :].rearrange("p (a t) -> p a t", a=4),
                x2c3[:, 4 * h:4 * h + 4, 1:513],
                x2c3[:, 4 * h:4 * h + 4, 0:512], OP.subtract),
                waits=[(sv, ln1_h[h]), (sC, consts_s), (st, sqb_t_free[1]),
                       (sv, ka_prev)])
            for a in range(4 * h, 4 * h + 4):
                op("v", lambda e, a=a: e.scalar_tensor_tensor(
                    xr3[:, a, :], sq2[:, (a % 4) * 512:(a % 4) * 512 + 512],
                    c_mur[:, a:a + 1], x2c3[:, a, 0:512], OP.mult, OP.add))
                mixed = op("v", lambda e, a=a: e.scalar_tensor_tensor(
                    xk3[:, a, :], sq2[:, (a % 4) * 512:(a % 4) * 512 + 512],
                    c_muk[:, a:a + 1], x2c3[:, a, 0:512], OP.mult, OP.add))
            mixed_h[h] = mixed

        if ck + 1 < NC2:
            emit_A(ck + 1)

        for o in range(8):
            bank = psA if o % 2 == 0 else psB

            def rmm0(e, o=o, bank=bank):
                for a in range(4):
                    inst = e.matmul(bank[:, :], w24[:, 0, a, o, :], xr3[:, a, :],
                                    start=(a == 0), stop=False)
                return inst
            op("t", rmm0, inc=st,
               waits=[(sv, mixed_h[0]), (sWR, 16), (sa, rbank_free[o % 2])])

            def rmm1(e, o=o, bank=bank):
                for a in range(4, 8):
                    inst = e.matmul(bank[:, :], w24[:, 0, a, o, :], xr3[:, a, :],
                                    start=False, stop=(a == 7))
                return inst
            rc = op("t", rmm1, inc=st, waits=[(sv, mixed_h[1])])
            sig_c[o] = op("a", lambda e, o=o, bank=bank: e.activation(
                rg3[:, o, :], bank[:, :], AF.Sigmoid), waits=[(st, rc)])
            rbank_free[o % 2] = sig_c[o]

        for o in range(8):
            bank = psBC if o % 2 == 0 else psX

            def kmm(e, o=o, bank=bank):
                for a in range(8):
                    inst = e.matmul(bank[:, :], w24[:, 1, a, o, :], xk3[:, a, :],
                                    start=(a == 0), stop=(a == 7))
                return inst
            kc = op("t", kmm, inc=st,
                    waits=[(sv, mixed), (sWK, 16), (sv, kbank_free[o % 2])])

            op("v", lambda e, o=o, bank=bank: e.tensor_relu(
                sq2[:, (o % 2) * 512:(o % 2) * 512 + 512], bank[:, :]),
                waits=[(st, kc), (st, sqb_t_free[1]), (sv, mixed)])
            ka_c[o] = op("v", lambda e, o=o: e.tensor_tensor(
                ka3[:, o, :], sq2[:, (o % 2) * 512:(o % 2) * 512 + 512],
                sq2[:, (o % 2) * 512:(o % 2) * 512 + 512], OP.mult))
            kbank_free[o % 2] = ka_c[o]
        ka_done = sv.v
        ka_prev = ka_done


        for o in range(8):
            bank = psA if o % 2 == 0 else psB

            def vmm(e, o=o, bank=bank):
                for a in range(8):
                    inst = e.matmul(bank[:, :], w24[:, 2, a, o, :], ka3[:, a, :],
                                    start=(a == 0), stop=(a == 7))
                return inst
            vc = op("t", vmm, inc=st,
                    waits=[(sv, ka_done), (sWV, 16), (sa, rbank_free[o % 2]),
                           (sv, vb_free[o % 2])])

            op("v", lambda e, o=o, par=par, bank=bank: e.tensor_tensor(
                vtmp3p[:, par, o, :], rg3[:, o, :], bank[:, :], OP.mult),
                waits=[(st, vc)])
            vb_free[o % 2] = op("v", lambda e, o=o, par=par: e.tensor_tensor(
                vtmp3p[:, par, o, :], vtmp3p[:, par, o, :], x2c3[:, o, 1:513], OP.add))
        ffn_v = sv.v

        ln2_v = stats_and_apply(src3, src3, 0, ffn_v, bc_extra=[(sv, ka_done)],
                                pls=(psBC, psX))[1]
        out_dma[par] = op("s", lambda e, ck=ck, par=par: e.dma_start(
            out=out3[:, :, ck * 512:(ck + 1) * 512], in_=vtmp3p[:, par, :, :]),
            inc=sOUT, waits=[(sv, ln2_v)])
        out_dma_all = sOUT.v

    final_ss = out_dma_all
    op("s", lambda e: e.nop(), waits=[(sOUT, final_ss)])

    with nc.Block() as block:
        @block.tensor
        def _(e):
            for f in prog["t"]:
                f(e)

        @block.vector
        def _(e):
            for f in prog["v"]:
                f(e)

        @block.scalar
        def _(e):
            for f in prog["a"]:
                f(e)

        @block.gpsimd
        def _(e):
            for f in prog["g"]:
                f(e)

        @block.sync
        def _(e):
            for f in prog["s"]:
                f(e)

    return nc, st_ctx


def stage_inputs(T, inputs):
    x = np.asarray(inputs["L0_attn_x"], dtype=np.float32)
    w_qkv = np.asarray(inputs["L0_attn_w_qkv"], dtype=np.float32)
    w_out = np.asarray(inputs["L0_attn_w_out"], dtype=np.float32)
    head_scales = np.asarray(inputs["L0_attn_head_scales"], dtype=np.float32)
    W_r = np.asarray(inputs["L0_ffn_W_r"], dtype=np.float32)
    W_k = np.asarray(inputs["L0_ffn_W_k"], dtype=np.float32)
    W_v = np.asarray(inputs["L0_ffn_W_v"], dtype=np.float32)
    mu_r = np.asarray(inputs["L0_ffn_mu_r"], dtype=np.float32)
    mu_k = np.asarray(inputs["L0_ffn_mu_k"], dtype=np.float32)

    TQ = T // 4
    dirs = _dirs()

    def wpre(w):
        return np.ascontiguousarray(
            w.reshape(8, 128, 8, 128).transpose(1, 0, 2, 3).reshape(128, 8192))

    ident = np.eye(128, dtype=np.float32).astype(ml_dtypes.bfloat16)
    onesb = np.ones((128, 1), dtype=np.float32).astype(ml_dtypes.bfloat16)
    onesc = np.ones((128, 1), dtype=np.float32)
    onesr = np.ones((1, 128), dtype=np.float32)
    tri = np.where(np.arange(128)[:, None] <= np.arange(128)[None, :], 0.0, NEG).astype(np.float32)
    mur_s = np.ascontiguousarray(mu_r.reshape(8, 128).T)
    muk_s = np.ascontiguousarray(mu_k.reshape(8, 128).T)
    wout_p = wpre(w_out)
    wr_p = wpre(np.ascontiguousarray(W_r.T))
    wk_p = wpre(np.ascontiguousarray(W_k.T))
    wv_p = wpre(np.ascontiguousarray(W_v.T))

    NTl = T // 512
    NC2l = TQ // 512
    in_maps = []
    for c in range(8):
        b, r = divmod(c, 4)
        h0, h1 = 2 * r, 2 * r + 1
        xt_ = np.ascontiguousarray(x[b].T)
        xt_pre = np.ascontiguousarray(
            xt_.reshape(8, 128, NTl, 512).transpose(1, 2, 0, 3).reshape(128, 8 * T))
        xq_slice = xt_[:, r * TQ:(r + 1) * TQ]
        xqt_pre = np.ascontiguousarray(
            xq_slice.reshape(8, 128, NC2l, 512).transpose(1, 2, 0, 3).reshape(128, 8 * TQ))
        if r == 0:
            xql_ = np.zeros((128, 8), dtype=np.float32)
        else:
            xql_ = np.ascontiguousarray(
                xt_[:, r * TQ - 1].reshape(8, 128).T)
        wq = np.empty((128, 6, 8, 128), dtype=np.float32)
        for si, (stype, h) in enumerate([(0, h0), (0, h1), (1, h0), (1, h1), (2, h0), (2, h1)]):
            blk = w_qkv[:, stype * 1024 + h * 128: stype * 1024 + h * 128 + 128]
            wq[:, si, :, :] = blk.reshape(8, 128, 128).transpose(1, 0, 2)
        hs = head_scales[[h0, h1]].reshape(1, 2).astype(np.float32)
        d3c = np.empty((3, 6), dtype=np.float32)
        for u, h in enumerate((h0, h1)):
            d = dirs[h]
            d3c[:, 3 * u:3 * u + 3] = (np.outer(d, d) / SCL).astype(np.float32)
        in_maps.append({
            "xt": xt_pre,
            "xqt": xqt_pre,
            "xqlp": xql_,
            "wqkv": np.ascontiguousarray(wq.reshape(128, 6 * 8 * 128)),
            "wout": wout_p,
            "wr": wr_p,
            "wk": wk_p,
            "wv": wv_p,
            "mur": mur_s,
            "muk": muk_s,
            "hs2": hs,
            "d3": d3c,
            "identp": ident,
            "onesb": onesb,
            "onesbm": np.ones((128, 128), dtype=np.float32).astype(ml_dtypes.bfloat16),
            "onescm": np.ones((128, 128), dtype=np.float32),
            "onesc": onesc,
            "onesr": onesr,
            "trim": tri,
            "epsv": np.full((128, 1), EPS, dtype=np.float32),
            "gidx": ((4 * b + np.arange(4, dtype=np.int32))[:, None] * 128
                     + np.arange(128, dtype=np.int32)[None, :]).astype(np.int32),
        })
    return in_maps


_CACHE = {}


def kernel(**inputs):
    T = np.asarray(inputs["L0_attn_x"]).shape[1]
    if T not in _CACHE:
        _CACHE[T] = build_program(T)
    nc, _ctx = _CACHE[T]
    in_maps = stage_inputs(T, inputs)
    res = run_bass_kernel_spmd(nc, in_maps, core_ids=list(range(8)))
    TQ = T // 4
    full = np.empty((2, T, 1024), dtype=np.float32)
    for c in range(8):
        b, r = divmod(c, 4)
        full[b, r * TQ:(r + 1) * TQ, :] = res.results[c]["out"].T
    return full


# version 23 (final)
# speedup vs baseline: 1.1940x; 1.0083x over previous
import os

os.environ.setdefault("TRN_ALLOW_ICI_ALL_TO_ALL", "1")

import math
import numpy as np
import ml_dtypes

import concourse.bass as bass
import concourse.mybir as mybir
from concourse.bass import AutoArena, IndirectOffsetOnAxis
from concourse import dve_ops
from concourse.bass_utils import run_bass_kernel_spmd

F32 = mybir.dt.float32
BF16 = mybir.dt.bfloat16
AX = mybir.AxisListType.X
OP = mybir.AluOpType
AF = mybir.ActivationFunctionType

C = 1024
HD = 128
SCL = 1.0 / math.sqrt(HD)
EPS = 1e-5
NEG = -1.0e9


def _dirs():
    t = np.array(
        [[(i >> 2 & 1) * 2 - 1, (i >> 1 & 1) * 2 - 1, (i & 1) * 2 - 1] for i in range(8)],
        dtype=np.float64,
    )
    return t / np.sqrt(3.0)


class Ctr:
    def __init__(self, sem, step=1):
        self.sem = sem
        self.step = step
        self.v = 0

    def bump(self, n=None):
        self.v += self.step if n is None else n
        return self.v


def build_program(T):
    TQ = T // 4
    NT = T // 512
    QG = T // 512
    PQ = max(QG // 4, 1)
    NC2 = TQ // 512
    KTT = T // 128

    nc = bass.Bass("TRN2", target_bir_lowering=False)

    xt = nc.dram_tensor("xt", [128, 8 * T], F32, kind="ExternalInput").ap()
    xqt = nc.dram_tensor("xqt", [128, 8 * TQ], F32, kind="ExternalInput").ap()
    xqlp = nc.dram_tensor("xqlp", [128, 8], F32, kind="ExternalInput").ap()
    wqkv = nc.dram_tensor("wqkv", [128, 6 * 8 * 128], F32, kind="ExternalInput").ap()
    wout = nc.dram_tensor("wout", [128, 8 * 8 * 128], F32, kind="ExternalInput").ap()
    wr = nc.dram_tensor("wr", [128, 8 * 8 * 128], F32, kind="ExternalInput").ap()
    wk = nc.dram_tensor("wk", [128, 8 * 8 * 128], F32, kind="ExternalInput").ap()
    wv = nc.dram_tensor("wv", [128, 8 * 8 * 128], F32, kind="ExternalInput").ap()
    mur = nc.dram_tensor("mur", [128, 8], F32, kind="ExternalInput").ap()
    muk = nc.dram_tensor("muk", [128, 8], F32, kind="ExternalInput").ap()
    hs2 = nc.dram_tensor("hs2", [1, 2], F32, kind="ExternalInput").ap()
    d3 = nc.dram_tensor("d3", [3, 6], F32, kind="ExternalInput").ap()
    identp = nc.dram_tensor("identp", [128, 128], BF16, kind="ExternalInput").ap()
    onesb = nc.dram_tensor("onesb", [128, 1], BF16, kind="ExternalInput").ap()
    onesbm = nc.dram_tensor("onesbm", [128, 128], BF16, kind="ExternalInput").ap()
    onescm = nc.dram_tensor("onescm", [128, 128], F32, kind="ExternalInput").ap()
    onesc = nc.dram_tensor("onesc", [128, 1], F32, kind="ExternalInput").ap()
    onesr = nc.dram_tensor("onesr", [1, 128], F32, kind="ExternalInput").ap()
    trim = nc.dram_tensor("trim", [128, 128], F32, kind="ExternalInput").ap()
    epsv = nc.dram_tensor("epsv", [128, 1], F32, kind="ExternalInput").ap()
    gidx = nc.dram_tensor("gidx", [4, 128], mybir.dt.int32, kind="ExternalInput").ap()
    out = nc.dram_tensor("out", [C, TQ], F32, kind="ExternalOutput").ap()

    wrd = nc.dram_tensor("wrd", [128, 8 * 8 * 128], BF16).ap()
    wkd = nc.dram_tensor("wkd", [128, 8 * 8 * 128], BF16).ap()
    wvd = nc.dram_tensor("wvd", [128, 8 * 8 * 128], BF16).ap()
    a2ai = [nc.dram_tensor(f"a2ai{j}", [8, 128, 2, 512 + (1 if j == NC2 - 1 else 0)],
                           BF16).ap() for j in range(NC2)]
    a2ao = [nc.dram_tensor(f"a2ao{j}", [8, 128, 2, 512 + (1 if j == NC2 - 1 else 0)],
                           BF16).ap() for j in range(NC2)]

    import contextlib

    st_ctx = contextlib.ExitStack()
    sbp = lambda name, shape, dt: st_ctx.enter_context(nc.sbuf_tensor(name, shape, dt))
    psx_ = lambda name: st_ctx.enter_context(nc.psum_tensor(name, [128, 512], F32))
    sem = lambda name: st_ctx.enter_context(nc.semaphore(name))

    c_id = sbp("c_id", [128, 128], BF16)
    c_1b = sbp("c_1b", [128, 1], BF16)
    c_1bm = sbp("c_1bm", [128, 128], BF16)
    c_1cm = sbp("c_1cm", [128, 128], F32)
    c_1c = sbp("c_1c", [128, 1], F32)
    c_1r = sbp("c_1r", [1, 128], F32)
    c_tri = sbp("c_tri", [128, 128], F32)
    c_d3 = sbp("c_d3", [3, 6], F32)
    c_hs = sbp("c_hs", [1, 2], F32)
    c_mur = sbp("c_mur", [128, 8], F32)
    c_muk = sbp("c_muk", [128, 8], F32)
    zcol = sbp("zcol", [128, 1], BF16)
    c_eps = sbp("c_eps", [128, 1], F32)
    c_gix = [sbp(f"c_gix{j}", [128, 1], mybir.dt.int32) for j in range(4)]
    mx = sbp("mx", [1, 64], F32)
    hsr = sbp("hsr", [3, 2], F32)
    A3 = sbp("A3", [3, 6], BF16)
    cneg = sbp("cneg", [128, 2], F32)
    xql = sbp("xql", [128, 8], F32)
    wo_sb = sbp("wo_sb", [128, 8 * 8 * 128], BF16)
    vcol = sbp("vcol", [128, 8], F32)
    mi2 = sbp("mi2", [128, 2], F32)
    bsq8 = sbp("bsq8", [128, 8], F32)
    bcol = sbp("bcol", [128, 8], F32)
    aall0 = sbp("aall0", [128, 4 * 1024], BF16)
    sqB = sbp("sqB", [128, 4 * 512], F32)
    c_idf = sbp("c_idf", [128, 128], F32)

    arA = AutoArena(nc)
    wq_sb = arA.sbuf_tensor("wq_sb", [128, 6 * 8 * 128], BF16)
    xsb = arA.sbuf_tensor("xsb", [128, 2 * 8 * 512], BF16)
    Qt = arA.sbuf_tensor("Qt", [128, 2 * T], BF16)
    Kt = arA.sbuf_tensor("Kt", [128, 2 * T], BF16)
    Vt = arA.sbuf_tensor("Vt", [128, 2 * T], BF16)
    vstg = arA.sbuf_tensor("vstg", [128, 512], BF16)
    pt = arA.sbuf_tensor("pt", [128, 8 * 512], BF16)
    lsum = arA.sbuf_tensor("lsum", [128, 2 * 512], BF16)
    sq = arA.sbuf_tensor("sq", [128, 2 * 512], BF16)
    lb = arA.sbuf_tensor("lb", [128, 512], F32)
    rscr = arA.sbuf_tensor("rscr", [128, 512], F32)
    aosg = arA.sbuf_tensor("aosg", [128, 2 * 512], BF16)
    arA.allocate()

    arA.free()
    arB = AutoArena(nc)
    aall = arB.sbuf_tensor("aall", [128, 4 * 1026], BF16)
    aall_b = arB.sbuf_tensor("aall_b", [128, 8], BF16)
    w2_sb = arB.sbuf_tensor("w2_sb", [128, 3 * 8 * 8 * 128], BF16)
    xq = arB.sbuf_tensor("xq", [128, 8 * 512], F32)
    x2c = arB.sbuf_tensor("x2c", [128, 8 * 513], F32)
    vtmp = arB.sbuf_tensor("vtmp", [128, 2 * 8 * 512], F32)
    sq2 = arB.sbuf_tensor("sq2", [128, 4 * 512], F32)
    mb_sb2 = arB.sbuf_tensor("mb_sb2", [128, 512], F32)
    ib_sb2 = arB.sbuf_tensor("ib_sb2", [128, 512], F32)
    xr_sb = arB.sbuf_tensor("xr_sb", [128, 8 * 512], BF16)
    xk_sb = arB.sbuf_tensor("xk_sb", [128, 8 * 512], BF16)
    ka_sb = arB.sbuf_tensor("ka_sb", [128, 8 * 512], BF16)
    rg_sb = arB.sbuf_tensor("rg_sb", [128, 8 * 512], F32)
    arB.allocate()
    t1_sb = None

    psA = psx_("psA")
    psB = psx_("psB")
    psO0 = psx_("psO0")
    psO1 = psx_("psO1")
    psL0 = psx_("psL0")
    psL1 = psx_("psL1")
    psBC = psx_("psBC")
    psX = psx_("psX")

    sC = Ctr(sem("sC"), 16)
    sWQ = Ctr(sem("sWQ"), 16)
    sX = [Ctr(sem("sX0"), 16), Ctr(sem("sX1"), 16)]
    sW2 = Ctr(sem("sW2"), 16)
    sAO = [Ctr(sem("sAO0"), 16), Ctr(sem("sAO1"), 16)]
    sZ = Ctr(sem("sZ"), 16)
    sLD = Ctr(sem("sLD"), 16)
    sLD0 = Ctr(sem("sLD0"), 16)
    sLD2 = Ctr(sem("sLD2"), 16)
    sWO = Ctr(sem("sWO"), 16)
    sAB = Ctr(sem("sAB"), 16)
    sXQL = Ctr(sem("sXQL"), 16)
    sXQ = Ctr(sem("sXQ"), 16)
    sOUT = Ctr(sem("sOUT"), 16)
    sgp = Ctr(sem("sgp"), 1)
    st = Ctr(sem("st"), 1)
    sv = Ctr(sem("sv"), 1)
    sa = Ctr(sem("sa"), 1)
    scc = Ctr(sem("scc"), 1)

    prog = {k: [] for k in ("t", "v", "a", "g", "s")}
    highwater = {}

    def op(eng, builder, inc=None, waits=()):
        if eng == "ad":
            eng = "a"
        elif eng == "gc":
            eng = "g"
            waits = list(waits) + [(sgp, sgp.v)]
            inc = sgp
        elif eng == "v":
            waits = list(waits) + [(sv, sv.v)]
            inc = sv
        elif eng == "a":
            waits = list(waits) + [(sa, sa.v)]
            inc = sa
        needed = []
        for ctr, val in waits:
            if val <= 0:
                continue
            key = (eng, id(ctr))
            if highwater.get(key, 0) >= val:
                continue
            highwater[key] = val
            needed.append((ctr.sem, val))
        inc_sem = None
        inc_n = 0
        if inc is not None:
            inc.bump()
            inc_sem = inc.sem
            inc_n = inc.step

        def f(e, builder=builder, needed=tuple(needed), inc_sem=inc_sem, inc_n=inc_n):
            for s_, v_ in needed:
                e.wait_ge(s_, v_)
            inst = builder(e)
            if inc_sem is not None:
                inst.then_inc(inc_sem, inc_n)

        prog[eng].append(f)
        return inc.v if inc is not None else None

    for dst, src in ((c_id[:, :], identp), (c_1b[:, :], onesb), (c_1c[:, :], onesc),
                     (c_1bm[:, :], onesbm), (c_1cm[:, :], onescm),
                     (c_1r[:, :], onesr), (c_tri[:, :], trim),
                     (c_d3[:, :], d3), (c_hs[:, :], hs2), (c_mur[:, :], mur),
                     (c_muk[:, :], muk), (c_eps[:, :], epsv),
                     (c_gix[0][:, :], gidx[0:1, :]), (c_gix[1][:, :], gidx[1:2, :]),
                     (c_gix[2][:, :], gidx[2:3, :]), (c_gix[3][:, :], gidx[3:4, :]),
                     (xql[:, :], xqlp)):
        op("s", lambda e, d=dst, s=src: e.dma_start(out=d, in_=s), inc=sC)
    consts_s = sC.v
    idf_c = op("v", lambda e: e.tensor_copy(c_idf[:, :], c_id[:, :]),
               waits=[(sC, consts_s)])
    op("g", lambda e: e.dma_start(out=wq_sb[:, :], in_=wqkv), inc=sWQ)
    op("g", lambda e: e.dma_start(out=wo_sb[:, :], in_=wout), inc=sWO)
    op("g", lambda e: e.dma_start(out=wrd, in_=wr), inc=sW2)
    op("g", lambda e: e.dma_start(out=wkd, in_=wk), inc=sW2)
    op("g", lambda e: e.dma_start(out=wvd, in_=wv), inc=sW2)
    w2d_g = sW2.v

    zv = op("v", lambda e: e.memset(zcol[:, :], 0.0), inc=sv)
    for sh in (0, 4):
        for u in range(2):
            op("s", lambda e, sh=sh, u=u: e.dma_start(
                out=a2ai[NC2 - 1][sh, :, u, 0:1], in_=zcol[:, :],
                allow_slow_non_contiguous=True),
               inc=sZ, waits=[(sv, zv)])
    zero_ss = sZ.v

    xt3 = xt.rearrange("p (tc a t) -> p tc a t", tc=NT, a=8)
    xsb3 = xsb[:, :].rearrange("p (par a t) -> p par a t", par=2, a=8)
    wq4 = wq_sb[:, :].rearrange("p (s a d) -> p s a d", s=6, a=8)
    Qt2 = Qt[:, :].rearrange("p (u t) -> p u t", u=2)
    Kt2 = Kt[:, :].rearrange("p (u t) -> p u t", u=2)
    Vt3 = Vt[:, :].rearrange("p (u kt d) -> p u kt d", u=2, kt=KTT)

    x_done = []
    x_cons = []
    bank_free = {0: 0, 1: 0}
    psx_free = 0

    for tc in range(NT):
        par = tc % 2
        w_ = [] if tc < 2 else [(st, x_cons[tc - 2])]
        op("g", lambda e, par=par, tc=tc:
           e.dma_start(out=xsb3[:, par, :, :], in_=xt3[:, tc, :, :]),
           inc=sX[par], waits=w_)
        x_done.append((sX[par], sX[par].v))

        for ob in range(6):
            bname = (tc * 6 + ob) % 2
            bank = psA if bname == 0 else psB

            def mms(e, bank=bank, ob=ob, par=par):
                for a in range(8):
                    inst = e.matmul(bank[:, :], wq4[:, ob, a, :], xsb3[:, par, a, :],
                                    start=(a == 0), stop=(a == 7))
                return inst
            mmc = op("t", mms, inc=st,
                     waits=[x_done[tc], (sWQ, 16), (sv, bank_free[bname])])

            u = ob % 2
            kind = ob // 2
            if kind == 0:
                dst = Qt2[:, u, tc * 512:(tc + 1) * 512]
            elif kind == 1:
                dst = Kt2[:, u, tc * 512:(tc + 1) * 512]
            else:
                dst = vstg[:, :]
            bank_free[bname] = op("v", lambda e, d=dst, b=bank: e.tensor_copy(d, b[:, :]),
                                  inc=sv, waits=[(st, mmc)])

            if kind == 2:
                vst = sv.v

                def vtr(e):
                    psXb = psX[:, :].bitcast(BF16)
                    for jj in range(4):
                        inst = e.transpose(psXb[:, jj * 128:(jj + 1) * 128],
                                           vstg[:, jj * 128:(jj + 1) * 128], c_id[:, :])
                    return inst
                trc = op("t", vtr, inc=st,
                         waits=[(sv, vst), (sv, psx_free), (sC, consts_s)])
                psx_free = op("v", lambda e, u=u, tc=tc: e.tensor_copy(
                    Vt3[:, u, tc * 4:(tc + 1) * 4, :].rearrange("p k d -> p (k d)"),
                    psX[:, :].bitcast(BF16)[:, 0:512]), inc=sv, waits=[(st, trc)])
        x_cons.append(st.v)

    ph1_v = sv.v

    psbc_free = 0
    adj_frees = {0: 0, 1: 0}
    psl_frees = {0: 0, 1: 0}
    sq_frees = {0: 0, 1: 0}
    for u in range(2):
        hc = op("t", lambda e, u=u: e.matmul(psX[0:3, 0:1], c_1r[0:1, 0:3],
                                             c_hs[0:1, u:u + 1], start=True, stop=True),
                inc=st, waits=[(sC, consts_s), (sv, psx_free)])
        hv = op("v", lambda e, u=u: e.tensor_copy(hsr[:, u:u + 1], psX[0:3, 0:1]),
                inc=sv, waits=[(st, hc)])
        psx_free = hv
        a3done = op("v", lambda e, u=u: e.tensor_scalar(
            A3[:, 3 * u:3 * u + 3], c_d3[:, 3 * u:3 * u + 3], hsr[:, u:u + 1], None, OP.mult),
            inc=sv, waits=[(sv, hv), (sC, consts_s)])

        for tc in range(NT):
            abank = psBC if tc % 2 == 0 else psX
            amc = op("t", lambda e, u=u, tc=tc, abank=abank: e.matmul(
                abank[0:3, 0:512], A3[:, 3 * u:3 * u + 3],
                Qt2[0:3, u, tc * 512:(tc + 1) * 512], start=True, stop=True),
                inc=st, waits=[(sv, a3done), (sv, ph1_v),
                               (sv, adj_frees[tc % 2])])
            adj_frees[tc % 2] = op("v", lambda e, u=u, tc=tc, abank=abank:
                                   e.tensor_tensor(
                Qt2[0:3, u, tc * 512:(tc + 1) * 512],
                Qt2[0:3, u, tc * 512:(tc + 1) * 512],
                abank[0:3, 0:512], OP.add), inc=sv, waits=[(st, amc)])
        adj_v = sv.v
        psbc_free = max(psbc_free, adj_frees[0])
        psx_free = max(psx_free, adj_frees[1])

        for which in ("q", "k"):
            buf = Qt2 if which == "q" else Kt2
            base = 0 if which == "q" else 16
            dep = adj_v if which == "q" else ph1_v
            for tc in range(NT):
                sl = tc % 2
                bank = psL0 if sl == 0 else psL1
                sqc = op("a", lambda e, u=u, tc=tc, sl=sl, buf=buf: e.activation(
                    sq[:, sl * 512:sl * 512 + 512],
                    buf[:, u, tc * 512:(tc + 1) * 512], AF.Square),
                    waits=[(sv, dep), (st, sq_frees[sl])])
                oc = op("t", lambda e, sl=sl, bank=bank: e.matmul(
                    bank[0:1, :], c_1b[:, 0:1], sq[:, sl * 512:sl * 512 + 512],
                    start=True, stop=True),
                    inc=st, waits=[(sa, sqc), (sC, consts_s), (sv, psl_frees[sl])])
                sq_frees[sl] = oc
                psl_frees[sl] = op("v", lambda e, tc=tc, sl=sl, base=base, bank=bank:
                                   e.reduce_max(
                    mx[0:1, base + tc:base + tc + 1], bank[0:1, 0:512], axis=AX),
                    inc=sv, waits=[(st, oc)])

        op("v", lambda e: e.reduce_max(mx[0:1, 32:33], mx[0:1, 0:NT], axis=AX),
           inc=sv, waits=[(sv, max(psl_frees[0], psl_frees[1]))])
        f0 = op("v", lambda e: e.reduce_max(mx[0:1, 33:34], mx[0:1, 16:16 + NT], axis=AX),
                inc=sv)
        fc = op("v", lambda e: e.tensor_tensor(mx[0:1, 34:35], mx[0:1, 32:33],
                                               mx[0:1, 33:34], OP.mult),
                inc=sv, waits=[(sv, f0)])

        op("a", lambda e: e.activation(mx[0:1, 35:36], mx[0:1, 34:35], AF.Sqrt),
           waits=[(sv, fc)])
        cc_ = op("a", lambda e: e.mul(mx[0:1, 36:37], mx[0:1, 35:36], -SCL))
        cbc_c = op("t", lambda e: e.matmul(psBC[:, 0:1], c_1r[0:1, :], mx[0:1, 36:37],
                                           start=True, stop=True),
                   inc=st, waits=[(sa, cc_), (sv, psbc_free)])
        psbc_free = op("v", lambda e, u=u: e.tensor_copy(cneg[:, u:u + 1], psBC[:, 0:1]),
                       inc=sv, waits=[(st, cbc_c)])
    bounds_v = sv.v

    xqt3 = xqt.rearrange("p (ck a t) -> p ck a t", ck=NC2, a=8)

    wo4 = wo_sb[:, :].rearrange("p (a o d) -> p a o d", a=8, o=8)
    w24 = w2_sb[:, :].rearrange("p (w a o d) -> p w a o d", w=3, a=8, o=8)
    xq3 = xq[:, :].rearrange("p (a t) -> p a t", a=8)
    x2c3 = x2c[:, :].rearrange("p (a t) -> p a t", a=8)
    vtmp3p = vtmp[:, :].rearrange("p (par a t) -> p par a t", par=2, a=8)
    xr3 = xr_sb[:, :].rearrange("p (a t) -> p a t", a=8)
    xk3 = xk_sb[:, :].rearrange("p (a t) -> p a t", a=8)
    ka3 = ka_sb[:, :].rearrange("p (a t) -> p a t", a=8)
    rg3 = rg_sb[:, :].rearrange("p (a t) -> p a t", a=8)
    out3 = out.rearrange("(a p) t -> p a t", p=128)

    rounds = [NC2 - 1] + list(range(NC2 - 1))
    a2afl = [a2ao[j].rearrange("r p u t -> (r p) (u t)") for j in range(NC2)]
    scc_need = {jr: i + 1 for i, jr in enumerate(rounds)}
    prologue_state = {}

    def emit_prologue0():
        ab_ss = 0
        for j_ in range(4):
            ab_ss = op("g", lambda e, j_=j_: e.indirect_dma_start(
                aall[:, j_ * 1026:(j_ + 1) * 1026], None, a2afl[NC2 - 1][:, :],
                IndirectOffsetOnAxis(c_gix[j_][:, :], 0)),
                inc=sAB, waits=[(scc, scc_need[NC2 - 1]), (sC, consts_s)])
        abview = bass.AP(aall, 0, [[4104, 128], [1026, 4], [513, 2]])
        abcp = op("v", lambda e: e.tensor_copy(
            aall_b[:, :].rearrange("p (j u) -> p j u", u=2), abview),
            waits=[(sAB, ab_ss)])
        prologue_state["abcp"] = abcp

    def emit_prologue():
        nonlocal psx_free
        abcp = prologue_state["abcp"]

        def bpmm(e):
            for o in range(8):
                for a in range(8):
                    inst = e.matmul(psX[:, o:o + 1], wo4[:, a, o, :],
                                    aall_b[:, a:a + 1],
                                    start=(a == 0), stop=(a == 7))
            return inst
        bp_c = op("t", bpmm, inc=st,
                  waits=[(sv, abcp), (sWO, 16), (sv, psx_free)])
        bres_c = op("v", lambda e: e.tensor_tensor(vcol[:, :], xql[:, :], psX[:, 0:8], OP.add),
                    inc=sv, waits=[(st, bp_c), (sC, consts_s)])
        psx_free = bres_c
        prologue_state["bnd_v"] = bres_c

    def emit_prologue2():
        bres_c = prologue_state["bnd_v"]
        bsq_c = op("a", lambda e: e.activation(bsq8[:, :], vcol[:, :], AF.Square),
                   waits=[(sv, bres_c)])

        def bstat(e):
            e.matmul(psX[0:1, 16:24], c_1c[:, 0:1], vcol[:, :], start=True, stop=True)
            return e.matmul(psX[0:1, 32:40], c_1c[:, 0:1], bsq8[:, :], start=True, stop=True)
        bst_c = op("t", bstat, inc=st, waits=[(sa, bsq_c), (sv, bres_c)])

        op("v", lambda e: e.reduce_sum(mx[0:1, 40:41], psX[0:1, 16:24], axis=AX),
           inc=sv, waits=[(st, bst_c)])
        bm0 = op("v", lambda e: e.reduce_sum(mx[0:1, 41:42], psX[0:1, 32:40], axis=AX),
                 inc=sv)

        op("v", lambda e: e.tensor_scalar(mx[0:1, 40:41], mx[0:1, 40:41], 1.0 / C,
                                          None, OP.mult), waits=[(sv, bm0)])
        op("v", lambda e: e.tensor_scalar(mx[0:1, 41:42], mx[0:1, 41:42], 1.0 / C,
                                          None, OP.mult))
        op("v", lambda e: e.tensor_tensor(mx[0:1, 42:43], mx[0:1, 40:41], mx[0:1, 40:41],
                                          OP.mult))
        bmv_c = op("v", lambda e: e.tensor_tensor(mx[0:1, 41:42], mx[0:1, 41:42],
                                                  mx[0:1, 42:43], OP.subtract))
        bsi_c = op("a", lambda e: e.activation(mx[0:1, 41:42], mx[0:1, 41:42],
                                               AF.Sqrt, bias=c_eps[0:1, 0:1]),
                   inc=sa, waits=[(sv, bmv_c)])
        bir_c = op("v", lambda e: e.reciprocal(mx[0:1, 41:42], mx[0:1, 41:42]),
                   inc=sv, waits=[(sa, bsi_c)])
        bbc_c = op("t", lambda e: e.matmul(psX[:, 40:42], c_1r[0:1, :], mx[0:1, 40:42],
                                           start=True, stop=True),
                   inc=st, waits=[(sv, bir_c)])
        mi_ev = op("v", lambda e: e.tensor_copy(mi2[:, :], psX[:, 40:42]),
                   inc=sv, waits=[(st, bbc_c)])
        bnd_v = op("v", lambda e: e.tensor_scalar(
            bcol[:, :], vcol[:, :], mi2[:, 0:1], mi2[:, 1:2], OP.subtract, OP.mult),
            waits=[(sv, mi_ev)])
        prologue_state["bnd_v"] = bnd_v

    pt2 = pt[:, :].rearrange("p (r t) -> p r t", r=8)
    ls2 = lsum[:, :].rearrange("p (r t) -> p r t", r=2)
    aosg2 = aosg[:, :].rearrange("p (r t) -> p r t", r=2)

    exp_ring = {0: 0, 1: 0, 2: 0}
    pt_ring = {k: 0 for k in range(8)}
    pt_ring_v = {k: 0 for k in range(8)}
    lsum_free = {0: 0, 1: 0}
    ol_free = {0: 0, 1: 0}
    ao_dma = {0: 0, 1: 0}
    lb_free = 0

    qi = -1
    a2a_cs = {}
    for jr in rounds:
        for u in range(2):
          for qtr in range(4):
            qg = qtr * PQ + jr
            qi += 1
            ktmax = 4 * qg + 4
            ngrp = ktmax // 4
            obank = psO0 if qi % 2 == 0 else psO1
            lbank = psL0 if qi % 2 == 0 else psL1

            smm_cs = {}
            exp_cs = {}
            grp_add = {}

            def geom(kt):
                diag = kt - 4 * qg
                if diag >= 0:
                    return 128 * diag, 512 - 128 * diag
                return 0, 512

            sbanks = (psA, psB, psBC)
            for step in range(ktmax + 2):
                tri_c = None
                if step < ktmax:
                    kt = step
                    sbank = sbanks[kt % 3]
                    o0, ncols = geom(kt)
                    qc0 = qg * 512 + o0
                    smm_cs[kt] = op("t", lambda e, u=u, kt=kt, sbank=sbank, qc0=qc0,
                                    ncols=ncols, o0=o0:
                                    e.matmul(sbank[:, o0:o0 + ncols],
                                             Kt2[:, u, kt * 128:(kt + 1) * 128],
                                             Qt2[:, u, qc0:qc0 + ncols],
                                             start=True, stop=True),
                                    inc=st,
                                    waits=[(sv, bounds_v), (sa, exp_ring[kt % 3]),
                                           (sv, psbc_free if kt % 3 == 2 else 0)])
                    if kt - 4 * qg >= 0:
                        tri_c = op("v", lambda e, sbank=sbank, o0=o0: e.tensor_tensor(
                            sbank[:, o0:o0 + 128], sbank[:, o0:o0 + 128],
                            c_tri[:, :], OP.add),
                            waits=[(st, smm_cs[kt]), (sC, consts_s)])
                if step >= 2:
                    kt = step - 2
                    o0, ncols = geom(kt)
                    ow = [(sa, exp_cs[kt])]
                    if kt == 0:
                        ow.append((sv, ol_free[qi % 2]))
                    pt_ring[kt % 8] = op("t", lambda e, u=u, kt=kt, o0=o0, ncols=ncols,
                                         obank=obank,
                                         first=(kt == 0), last=(kt == ktmax - 1):
                                         e.matmul(obank[:, o0:o0 + ncols],
                                                  Vt3[:, u, kt, :],
                                                  pt2[:, kt % 8, o0:o0 + ncols],
                                                  start=first, stop=last),
                                         inc=st, waits=ow)
                    if kt % 4 == 1 and kt // 4 >= 1:
                        g = kt // 4 - 1
                        gp = 4 * (g % 2)
                        a1 = op("v", lambda e, g=g, gp=gp: e.tensor_tensor(
                            ls2[:, g % 2, :], pt2[:, gp, :], pt2[:, gp + 1, :],
                            OP.add),
                            waits=[(sa, exp_cs[4 * g + 3]),
                                   (st, lsum_free[g % 2])])
                        pt_ring_v[gp] = a1
                        pt_ring_v[gp + 1] = a1
                    if kt % 4 == 3 and kt // 4 >= 1:
                        g = kt // 4 - 1
                        gp = 4 * (g % 2)
                        op("v", lambda e, g=g, gp=gp: e.tensor_tensor(
                            ls2[:, g % 2, :], ls2[:, g % 2, :], pt2[:, gp + 2, :],
                            OP.add))
                        grp_add[g] = op("v", lambda e, g=g, gp=gp: e.tensor_tensor(
                            ls2[:, g % 2, :], ls2[:, g % 2, :], pt2[:, gp + 3, :],
                            OP.add))
                        pt_ring_v[gp + 2] = grp_add[g]
                        pt_ring_v[gp + 3] = grp_add[g]
                        ow2 = [(sv, grp_add[g])]
                        if g == 0:
                            ow2.append((sv, ol_free[qi % 2]))
                        lsum_free[g % 2] = op("t", lambda e, g=g, lbank=lbank,
                                              first=(g == 0):
                                              e.matmul(lbank[:, :], c_1bm[:, :],
                                                       ls2[:, g % 2, :],
                                                       start=first, stop=False),
                                              inc=st, waits=ow2)
                if step < ktmax:
                    kt = step
                    sbank = sbanks[kt % 3]
                    o0, ncols = geom(kt)
                    ew = [(st, smm_cs[kt]), (st, pt_ring[kt % 8]),
                          (sv, pt_ring_v[kt % 8])]
                    if tri_c is not None:
                        ew.append((sv, tri_c))
                    exp_cs[kt] = op("a", lambda e, u=u, sbank=sbank, kt=kt, o0=o0,
                                    ncols=ncols:
                                    e.activation(pt2[:, kt % 8, o0:o0 + ncols],
                                                 sbank[:, o0:o0 + ncols], AF.Exp,
                                                 bias=cneg[:, u:u + 1], scale=SCL),
                                    waits=ew)
                    exp_ring[kt % 3] = exp_cs[kt]

            g = ngrp - 1
            gp = 4 * (g % 2)
            for kk in range(4 * g + 1, 4 * g + 4):
                oo, nn = geom(kk)
                op("v", lambda e, kk=kk, oo=oo, gp=gp: e.memset(
                    pt2[:, gp + (kk % 4), 0:oo], 0.0),
                    waits=[(sa, exp_cs[kk]), (st, pt_ring[kk % 8])])
            op("v", lambda e, g=g, gp=gp: e.tensor_tensor(
                ls2[:, g % 2, :], pt2[:, gp, :], pt2[:, gp + 1, :], OP.add),
                waits=[(st, lsum_free[g % 2])])
            op("v", lambda e, g=g, gp=gp: e.tensor_tensor(
                ls2[:, g % 2, :], ls2[:, g % 2, :], pt2[:, gp + 2, :], OP.add))
            grp_add[g] = op("v", lambda e, g=g, gp=gp: e.tensor_tensor(
                ls2[:, g % 2, :], ls2[:, g % 2, :], pt2[:, gp + 3, :], OP.add))
            for kk in range(4 * g, 4 * g + 4):
                pt_ring_v[gp + (kk % 4)] = grp_add[g]
            ow2 = [(sv, grp_add[g])]
            if g == 0:
                ow2.append((sv, ol_free[qi % 2]))
            lsum_free[g % 2] = op("t", lambda e, g=g, lbank=lbank,
                                  first=(g == 0):
                                  e.matmul(lbank[:, :], c_1bm[:, :],
                                           ls2[:, g % 2, :],
                                           start=first, stop=True),
                                  inc=st, waits=ow2)

            kloop_t = st.v
            rec_c = op("v", lambda e, lbank=lbank: e.reciprocal(lb[:, :], lbank[:, :]),
                       waits=[(st, kloop_t), (sv, lb_free)])
            aev_c = op("v", lambda e, obank=obank, qi=qi: e.tensor_tensor(
                aosg2[:, qi % 2, :], obank[:, :], lb[:, :], OP.mult),
                waits=[(sv, rec_c), (sAO[qi % 2], ao_dma[qi % 2])])
            lb_free = aev_c
            ol_free[qi % 2] = aev_c

            extra = (jr == NC2 - 1) and (qtr < 3)
            sAO[qi % 2].bump((3 if extra else 1) * 16)
            bcol0 = 1 if jr == NC2 - 1 else 0

            def aodma(e, u=u, qi=qi, qtr=qtr, jr=jr, extra=extra, bcol0=bcol0,
                      sem_=sAO[qi % 2].sem):
                tin = a2ai[jr]
                if extra:
                    e.dma_start(out=tin[qtr + 1, :, u, 0:1],
                                in_=aosg2[:, qi % 2, 511:512],
                                allow_slow_non_contiguous=True).then_inc(sem_, 16)
                    e.dma_start(out=tin[qtr + 5, :, u, 0:1],
                                in_=aosg2[:, qi % 2, 511:512],
                                allow_slow_non_contiguous=True).then_inc(sem_, 16)
                e.dma_start(out=tin[qtr, :, u, bcol0:bcol0 + 512],
                            in_=aosg2[:, qi % 2, :]).then_inc(sem_, 16)
                return e.dma_start(
                    out=tin[qtr + 4, :, u, bcol0:bcol0 + 512], in_=aosg2[:, qi % 2, :])
            ao_dma[qi % 2] = op("s", aodma, inc=sAO[qi % 2], waits=[(sv, aev_c)])
        cw = [(sAO[0], sAO[0].v), (sAO[1], sAO[1].v)]
        if jr == NC2 - 1:
            cw.append((sZ, zero_ss))
        a2a_cs[jr] = op("g", lambda e, jr=jr: e.collective_compute(
            "AllToAll", OP.bypass, replica_groups=[[0, 1, 2, 3, 4, 5, 6, 7]],
            ins=[a2ai[jr].opt()], outs=[a2ao[jr].opt()]),
            inc=scc, waits=cw)
        if jr == 0:
            emit_prologue0()
            if NC2 > 1:
                for j_ in range(4):
                    op("g", lambda e, j_=j_: e.indirect_dma_start(
                        aall0[:, j_ * 1024:(j_ + 1) * 1024], None, a2afl[0][:, :],
                        IndirectOffsetOnAxis(c_gix[j_][:, :], 0)),
                        inc=sLD0, waits=[(scc, scc_need[0]), (sC, consts_s)])
                prologue_state["aall0_ss"] = sLD0.v
        if jr == rounds[min(2, len(rounds) - 1)]:
            emit_prologue()

    attn_ao0 = sAO[0].v
    attn_ao1 = sAO[1].v

    sWR = Ctr(sem("sWR"), 16)
    sWK = Ctr(sem("sWK"), 16)
    sWV = Ctr(sem("sWV"), 16)
    aow = [(sAO[0], attn_ao0), (sAO[1], attn_ao1), (sW2, w2d_g)]
    w23 = w2_sb[:, :].rearrange("p (w x) -> p w x", w=3)
    op("g", lambda e: e.dma_start(out=w23[:, 0, :], in_=wrd), inc=sWR, waits=aow)
    op("g", lambda e: e.dma_start(out=w23[:, 1, :], in_=wkd), inc=sWK)
    op("g", lambda e: e.dma_start(out=w23[:, 2, :], in_=wvd), inc=sWV)



    pair_free = {0: 0, 1: 0}
    sqb_t_free = {0: 0, 1: 0}

    def stats_and_apply(src3, dst3, dst_off, dep, bc_extra=(), pls=None):
        nonlocal psbc_free, psx_free
        key = 0 if pls is None else 1
        pl0, pl1 = (psL0, psL1) if pls is None else pls
        sqb = sqB if pls is None else sq2
        mb_sb = mb_sb2[:, :]
        ib_sb = ib_sb2[:, :]
        sqcs = {}
        for h in range(2):
            sqcs[h] = op("a", lambda e, h=h, src3=src3, sqb=sqb: e.activation(
                sqb[:, :], src3[:, 4 * h:4 * h + 4, :].rearrange("p a t -> p (a t)"),
                AF.Square),
                waits=list(dep) + [(st, sqb_t_free[key]),
                                   (sv, pair_free[key])])
            for a in range(4 * h, 4 * h + 4):
                def s2mm(e, a=a, src3=src3, pl0=pl0, pl1=pl1, sqb=sqb):
                    e.matmul(pl0[:, :], c_1cm[:, :], src3[:, a, :],
                             start=(a == 0), stop=(a == 7))
                    return e.matmul(pl1[:, :], c_1cm[:, :],
                                    sqb[:, (a % 4) * 512:(a % 4) * 512 + 512],
                                    start=(a == 0), stop=(a == 7))
                op("t", s2mm, inc=st,
                   waits=[(sa, sqcs[h]), (sv, pair_free[key])]
                   + list(dep) + list(bc_extra))
            sqb_t_free[key] = st.v
        stats_t = st.v

        op("v", lambda e, pl0=pl0, mb_sb=mb_sb: e.tensor_scalar(
            mb_sb, pl0[:, :], 1.0 / C, None, OP.mult), waits=[(st, stats_t)])
        op("v", lambda e, pl1=pl1, ib_sb=ib_sb: e.tensor_scalar(
            ib_sb, pl1[:, :], 1.0 / C, None, OP.mult))
        op("v", lambda e, mb_sb=mb_sb, sqb=sqb: e.tensor_tensor(
            sqb[:, 0:512], mb_sb, mb_sb, OP.mult))
        mv_c = op("v", lambda e, ib_sb=ib_sb, sqb=sqb: e.tensor_tensor(
            ib_sb, ib_sb, sqb[:, 0:512], OP.subtract))
        pair_free[key] = mv_c
        isq_c = op("a", lambda e, ib_sb=ib_sb: e.activation(ib_sb, ib_sb, AF.Sqrt,
                                                            bias=c_eps[:, 0:1]),
                   waits=[(sv, mv_c)])
        bce_c = op("v", lambda e, ib_sb=ib_sb: e.reciprocal(ib_sb, ib_sb),
                   waits=[(sa, isq_c)])

        mbB = bass.AP(mb_sb2, 0, [[512, 128], [0, 4], [1, 512]])
        ibB = bass.AP(ib_sb2, 0, [[512, 128], [0, 4], [1, 512]])
        halves = {}
        for hh in range(2):
            op("v", lambda e, hh=hh, src3=src3, mbB=mbB: e.tensor_tensor(
                src3[:, 4 * hh:4 * hh + 4, :], src3[:, 4 * hh:4 * hh + 4, :],
                mbB, OP.subtract),
                waits=([(sv, bce_c)] if hh == 0 else []))
            halves[hh] = op("v", lambda e, hh=hh, src3=src3, dst3=dst3,
                            dst_off=dst_off, ibB=ibB:
                            e.tensor_tensor(
                                dst3[:, 4 * hh:4 * hh + 4, dst_off:dst_off + 512],
                                src3[:, 4 * hh:4 * hh + 4, :], ibB, OP.mult))
        return halves

    pb_free = {0: (sv, ol_free[0]), 1: (sv, ol_free[1])}
    xq_free = 0
    out_dma = {0: 0, 1: 0}
    out_dma_all = 0
    ln1_v = 0
    sig_c = {}
    ka_c = {}
    rbank_free = {0: 0, 1: 0}
    kbank_free = {0: 0, 1: 0}
    vb_free = {0: 0, 1: 0}
    aall_free = 0
    aall0_free = 0
    zblk_vs = {}
    resid_vs = {}
    ka_prev = 0

    def emit_A(ck):
        nonlocal xq_free, aall_free, aall0_free
        xq_ss = op("s", lambda e, ck=ck: e.dma_start(
            out=xq3[:, :, :], in_=xqt3[:, ck, :, :]),
            inc=sXQ, waits=[(st, xq_free), (sAO[0], attn_ao0), (sAO[1], attn_ao1)])
        wid = 513 if ck == NC2 - 1 else 512
        if ck % 2 == 0 and ck != NC2 - 1:
            abuf, spac, asem = aall0, 1024, sLD0
            if ck == 0:
                aall_ss = prologue_state["aall0_ss"]
            else:
                for j_ in range(4):
                    op("g", lambda e, ck=ck, j_=j_: e.indirect_dma_start(
                        aall0[:, j_ * 1024:(j_ + 1) * 1024], None, a2afl[ck][:, :],
                        IndirectOffsetOnAxis(c_gix[j_][:, :], 0)),
                        inc=sLD0, waits=[(st, aall0_free), (scc, scc_need[ck]),
                                         (sC, consts_s)])
                aall_ss = sLD0.v
        else:
            abuf, spac, asem = aall, 1026, sLD
            for j_ in range(4):
                op("g", lambda e, ck=ck, j_=j_, wid=wid: e.indirect_dma_start(
                    aall[:, j_ * 1026: j_ * 1026 + 2 * wid], None, a2afl[ck][:, :],
                    IndirectOffsetOnAxis(c_gix[j_][:, :], 0)),
                    inc=sLD, waits=[(st, aall_free), (scc, scc_need[ck]),
                                    (sC, consts_s), (sv, abcp)])
            aall_ss = sLD.v

        par = ck % 2
        off = 1 if ck == NC2 - 1 else 0
        for o in range(8):
            bank = psO0 if o % 2 == 0 else psO1
            pw = [(asem, aall_ss), (sWO, 16), pb_free[o % 2]]

            def pmm(e, o=o, bank=bank, off=off, wid=wid, abuf=abuf, spac=spac):
                for a in range(8):
                    base = (a // 2) * spac + (a % 2) * wid + off
                    inst = e.matmul(bank[:, :], wo4[:, a, o, :],
                                    abuf[:, base:base + 512],
                                    start=(a == 0), stop=False)
                return inst
            op("t", pmm, inc=st, waits=pw)
            pc_ = op("t", lambda e, o=o, bank=bank:
                     e.matmul(bank[:, :], c_idf[:, :], xq3[:, o, :],
                              start=False, stop=True),
                     inc=st, waits=[(sXQ, xq_ss), (sv, idf_c)])
            pb_free[o % 2] = (sa, op("a", lambda e, o=o, par=par, bank=bank:
                                     e.activation(vtmp3p[:, par, o, :],
                                                  bank[:, :], AF.Copy),
                                     waits=[(st, pc_), (sOUT, out_dma[par])]))
        resid_vs[ck] = sa.v
        xq_free = st.v
        if ck % 2 == 0 and ck != NC2 - 1:
            aall0_free = st.v
        else:
            aall_free = st.v

    abcp = prologue_state["abcp"]
    emit_A(0)
    emit_prologue2()
    bnd_v = op("v", lambda e: e.tensor_copy(x2c3[:, :, 0], bcol[:, :]),
               waits=[(sv, prologue_state["bnd_v"]), (sAO[0], attn_ao0),
                      (sAO[1], attn_ao1)])
    ln1_v = bnd_v
    for ck in range(NC2):
        par = ck % 2
        src3 = vtmp3p[:, par, :, :]

        if ck > 0:
            ln1_v = op("v", lambda e: e.tensor_copy(x2c3[:, :, 0], x2c3[:, :, 512]),
                       waits=[(sv, ln1_v)])

        ln1_h = stats_and_apply(src3, x2c3, 1, [(sa, resid_vs[ck])])
        ln1_v = ln1_h[1]

        mixed = None
        mixed_h = {}
        for h in range(2):
            op("v", lambda e, h=h: e.tensor_tensor(
                sq2[:, :].rearrange("p (a t) -> p a t", a=4),
                x2c3[:, 4 * h:4 * h + 4, 1:513],
                x2c3[:, 4 * h:4 * h + 4, 0:512], OP.subtract),
                waits=[(sv, ln1_h[h]), (sC, consts_s), (st, sqb_t_free[1]),
                       (sv, ka_prev)])
            for a in range(4 * h, 4 * h + 4):
                op("v", lambda e, a=a: e.scalar_tensor_tensor(
                    xr3[:, a, :], sq2[:, (a % 4) * 512:(a % 4) * 512 + 512],
                    c_mur[:, a:a + 1], x2c3[:, a, 0:512], OP.mult, OP.add))
                mixed = op("v", lambda e, a=a: e.scalar_tensor_tensor(
                    xk3[:, a, :], sq2[:, (a % 4) * 512:(a % 4) * 512 + 512],
                    c_muk[:, a:a + 1], x2c3[:, a, 0:512], OP.mult, OP.add))
            mixed_h[h] = mixed

        if ck + 1 < NC2:
            emit_A(ck + 1)

        for o in range(8):
            bank = psA if o % 2 == 0 else psB

            def rmm0(e, o=o, bank=bank):
                for a in range(4):
                    inst = e.matmul(bank[:, :], w24[:, 0, a, o, :], xr3[:, a, :],
                                    start=(a == 0), stop=False)
                return inst
            op("t", rmm0, inc=st,
               waits=[(sv, mixed_h[0]), (sWR, 16), (sa, rbank_free[o % 2])])

            def rmm1(e, o=o, bank=bank):
                for a in range(4, 8):
                    inst = e.matmul(bank[:, :], w24[:, 0, a, o, :], xr3[:, a, :],
                                    start=False, stop=(a == 7))
                return inst
            rc = op("t", rmm1, inc=st, waits=[(sv, mixed_h[1])])
            sig_c[o] = op("a", lambda e, o=o, bank=bank: e.activation(
                rg3[:, o, :], bank[:, :], AF.Sigmoid), waits=[(st, rc)])
            rbank_free[o % 2] = sig_c[o]

        for o in range(8):
            bank = psBC if o % 2 == 0 else psX

            def kmm(e, o=o, bank=bank):
                for a in range(8):
                    inst = e.matmul(bank[:, :], w24[:, 1, a, o, :], xk3[:, a, :],
                                    start=(a == 0), stop=(a == 7))
                return inst
            kc = op("t", kmm, inc=st,
                    waits=[(sv, mixed), (sWK, 16), (sv, kbank_free[o % 2])])

            op("v", lambda e, o=o, bank=bank: e.tensor_relu(
                sq2[:, (o % 2) * 512:(o % 2) * 512 + 512], bank[:, :]),
                waits=[(st, kc), (st, sqb_t_free[1]), (sv, mixed)])
            ka_c[o] = op("v", lambda e, o=o: e.tensor_tensor(
                ka3[:, o, :], sq2[:, (o % 2) * 512:(o % 2) * 512 + 512],
                sq2[:, (o % 2) * 512:(o % 2) * 512 + 512], OP.mult))
            kbank_free[o % 2] = ka_c[o]
        ka_done = sv.v
        ka_prev = ka_done


        for o in range(8):
            bank = psA if o % 2 == 0 else psB

            def vmm(e, o=o, bank=bank):
                for a in range(8):
                    inst = e.matmul(bank[:, :], w24[:, 2, a, o, :], ka3[:, a, :],
                                    start=(a == 0), stop=(a == 7))
                return inst
            vc = op("t", vmm, inc=st,
                    waits=[(sv, ka_done), (sWV, 16), (sa, rbank_free[o % 2]),
                           (sv, vb_free[o % 2])])

            op("v", lambda e, o=o, par=par, bank=bank: e.tensor_tensor(
                vtmp3p[:, par, o, :], rg3[:, o, :], bank[:, :], OP.mult),
                waits=[(st, vc)])
            vb_free[o % 2] = op("v", lambda e, o=o, par=par: e.tensor_tensor(
                vtmp3p[:, par, o, :], vtmp3p[:, par, o, :], x2c3[:, o, 1:513], OP.add))
        ffn_v = sv.v

        ln2_v = stats_and_apply(src3, src3, 0, [(sv, ffn_v)],
                                bc_extra=[(sv, ka_done)], pls=(psBC, psX))[1]
        out_dma[par] = op("s", lambda e, ck=ck, par=par: e.dma_start(
            out=out3[:, :, ck * 512:(ck + 1) * 512], in_=vtmp3p[:, par, :, :]),
            inc=sOUT, waits=[(sv, ln2_v)])
        out_dma_all = sOUT.v

    final_ss = out_dma_all
    op("s", lambda e: e.nop(), waits=[(sOUT, final_ss)])

    with nc.Block() as block:
        @block.tensor
        def _(e):
            for f in prog["t"]:
                f(e)

        @block.vector
        def _(e):
            for f in prog["v"]:
                f(e)

        @block.scalar
        def _(e):
            for f in prog["a"]:
                f(e)

        @block.gpsimd
        def _(e):
            for f in prog["g"]:
                f(e)

        @block.sync
        def _(e):
            for f in prog["s"]:
                f(e)

    return nc, st_ctx


def stage_inputs(T, inputs):
    x = np.asarray(inputs["L0_attn_x"], dtype=np.float32)
    w_qkv = np.asarray(inputs["L0_attn_w_qkv"], dtype=np.float32)
    w_out = np.asarray(inputs["L0_attn_w_out"], dtype=np.float32)
    head_scales = np.asarray(inputs["L0_attn_head_scales"], dtype=np.float32)
    W_r = np.asarray(inputs["L0_ffn_W_r"], dtype=np.float32)
    W_k = np.asarray(inputs["L0_ffn_W_k"], dtype=np.float32)
    W_v = np.asarray(inputs["L0_ffn_W_v"], dtype=np.float32)
    mu_r = np.asarray(inputs["L0_ffn_mu_r"], dtype=np.float32)
    mu_k = np.asarray(inputs["L0_ffn_mu_k"], dtype=np.float32)

    TQ = T // 4
    dirs = _dirs()

    def wpre(w):
        return np.ascontiguousarray(
            w.reshape(8, 128, 8, 128).transpose(1, 0, 2, 3).reshape(128, 8192))

    ident = np.eye(128, dtype=np.float32).astype(ml_dtypes.bfloat16)
    onesb = np.ones((128, 1), dtype=np.float32).astype(ml_dtypes.bfloat16)
    onesc = np.ones((128, 1), dtype=np.float32)
    onesr = np.ones((1, 128), dtype=np.float32)
    tri = np.where(np.arange(128)[:, None] <= np.arange(128)[None, :], 0.0, NEG).astype(np.float32)
    mur_s = np.ascontiguousarray(mu_r.reshape(8, 128).T)
    muk_s = np.ascontiguousarray(mu_k.reshape(8, 128).T)
    wout_p = wpre(w_out)
    wr_p = wpre(np.ascontiguousarray(W_r.T))
    wk_p = wpre(np.ascontiguousarray(W_k.T))
    wv_p = wpre(np.ascontiguousarray(W_v.T))

    NTl = T // 512
    NC2l = TQ // 512
    in_maps = []
    for c in range(8):
        b, r = divmod(c, 4)
        h0, h1 = 2 * r, 2 * r + 1
        xt_ = np.ascontiguousarray(x[b].T)
        xt_pre = np.ascontiguousarray(
            xt_.reshape(8, 128, NTl, 512).transpose(1, 2, 0, 3).reshape(128, 8 * T))
        xq_slice = xt_[:, r * TQ:(r + 1) * TQ]
        xqt_pre = np.ascontiguousarray(
            xq_slice.reshape(8, 128, NC2l, 512).transpose(1, 2, 0, 3).reshape(128, 8 * TQ))
        if r == 0:
            xql_ = np.zeros((128, 8), dtype=np.float32)
        else:
            xql_ = np.ascontiguousarray(
                xt_[:, r * TQ - 1].reshape(8, 128).T)
        wq = np.empty((128, 6, 8, 128), dtype=np.float32)
        for si, (stype, h) in enumerate([(0, h0), (0, h1), (1, h0), (1, h1), (2, h0), (2, h1)]):
            blk = w_qkv[:, stype * 1024 + h * 128: stype * 1024 + h * 128 + 128]
            wq[:, si, :, :] = blk.reshape(8, 128, 128).transpose(1, 0, 2)
        hs = head_scales[[h0, h1]].reshape(1, 2).astype(np.float32)
        d3c = np.empty((3, 6), dtype=np.float32)
        for u, h in enumerate((h0, h1)):
            d = dirs[h]
            d3c[:, 3 * u:3 * u + 3] = (np.outer(d, d) / SCL).astype(np.float32)
        in_maps.append({
            "xt": xt_pre,
            "xqt": xqt_pre,
            "xqlp": xql_,
            "wqkv": np.ascontiguousarray(wq.reshape(128, 6 * 8 * 128)),
            "wout": wout_p,
            "wr": wr_p,
            "wk": wk_p,
            "wv": wv_p,
            "mur": mur_s,
            "muk": muk_s,
            "hs2": hs,
            "d3": d3c,
            "identp": ident,
            "onesb": onesb,
            "onesbm": np.ones((128, 128), dtype=np.float32).astype(ml_dtypes.bfloat16),
            "onescm": np.ones((128, 128), dtype=np.float32),
            "onesc": onesc,
            "onesr": onesr,
            "trim": tri,
            "epsv": np.full((128, 1), EPS, dtype=np.float32),
            "gidx": ((4 * b + np.arange(4, dtype=np.int32))[:, None] * 128
                     + np.arange(128, dtype=np.int32)[None, :]).astype(np.int32),
        })
    return in_maps


_CACHE = {}


def kernel(**inputs):
    T = np.asarray(inputs["L0_attn_x"]).shape[1]
    if T not in _CACHE:
        _CACHE[T] = build_program(T)
    nc, _ctx = _CACHE[T]
    in_maps = stage_inputs(T, inputs)
    res = run_bass_kernel_spmd(nc, in_maps, core_ids=list(range(8)))
    TQ = T // 4
    full = np.empty((2, T, 1024), dtype=np.float32)
    for c in range(8):
        b, r = divmod(c, 4)
        full[b, r * TQ:(r + 1) * TQ, :] = res.results[c]["out"].T
    return full


# version 24
# speedup vs baseline: 1.2002x; 1.0052x over previous
import os

os.environ.setdefault("TRN_ALLOW_ICI_ALL_TO_ALL", "1")

import math
import numpy as np
import ml_dtypes

import concourse.bass as bass
import concourse.mybir as mybir
from concourse.bass import AutoArena, IndirectOffsetOnAxis
from concourse import dve_ops
from concourse.bass_utils import run_bass_kernel_spmd

F32 = mybir.dt.float32
BF16 = mybir.dt.bfloat16
AX = mybir.AxisListType.X
OP = mybir.AluOpType
AF = mybir.ActivationFunctionType

C = 1024
HD = 128
SCL = 1.0 / math.sqrt(HD)
EPS = 1e-5
NEG = -1.0e9


def _dirs():
    t = np.array(
        [[(i >> 2 & 1) * 2 - 1, (i >> 1 & 1) * 2 - 1, (i & 1) * 2 - 1] for i in range(8)],
        dtype=np.float64,
    )
    return t / np.sqrt(3.0)


class Ctr:
    def __init__(self, sem, step=1):
        self.sem = sem
        self.step = step
        self.v = 0

    def bump(self, n=None):
        self.v += self.step if n is None else n
        return self.v


def build_program(T):
    TQ = T // 4
    NT = T // 512
    QG = T // 512
    PQ = max(QG // 4, 1)
    NC2 = TQ // 512
    KTT = T // 128

    nc = bass.Bass("TRN2", target_bir_lowering=False)

    xt = nc.dram_tensor("xt", [128, 8 * T], F32, kind="ExternalInput").ap()
    xqt = nc.dram_tensor("xqt", [128, 8 * TQ], F32, kind="ExternalInput").ap()
    xqlp = nc.dram_tensor("xqlp", [128, 8], F32, kind="ExternalInput").ap()
    wqkv = nc.dram_tensor("wqkv", [128, 6 * 8 * 128], F32, kind="ExternalInput").ap()
    wout = nc.dram_tensor("wout", [128, 8 * 8 * 128], F32, kind="ExternalInput").ap()
    wr = nc.dram_tensor("wr", [128, 8 * 8 * 128], F32, kind="ExternalInput").ap()
    wk = nc.dram_tensor("wk", [128, 8 * 8 * 128], F32, kind="ExternalInput").ap()
    wv = nc.dram_tensor("wv", [128, 8 * 8 * 128], F32, kind="ExternalInput").ap()
    mur = nc.dram_tensor("mur", [128, 8], F32, kind="ExternalInput").ap()
    muk = nc.dram_tensor("muk", [128, 8], F32, kind="ExternalInput").ap()
    hs2 = nc.dram_tensor("hs2", [1, 2], F32, kind="ExternalInput").ap()
    d3 = nc.dram_tensor("d3", [3, 6], F32, kind="ExternalInput").ap()
    identp = nc.dram_tensor("identp", [128, 128], BF16, kind="ExternalInput").ap()
    onesb = nc.dram_tensor("onesb", [128, 1], BF16, kind="ExternalInput").ap()
    onesbm = nc.dram_tensor("onesbm", [128, 128], BF16, kind="ExternalInput").ap()
    onescm = nc.dram_tensor("onescm", [128, 128], F32, kind="ExternalInput").ap()
    onesc = nc.dram_tensor("onesc", [128, 1], F32, kind="ExternalInput").ap()
    onesr = nc.dram_tensor("onesr", [1, 128], F32, kind="ExternalInput").ap()
    trim = nc.dram_tensor("trim", [128, 128], F32, kind="ExternalInput").ap()
    epsv = nc.dram_tensor("epsv", [128, 1], F32, kind="ExternalInput").ap()
    gidx = nc.dram_tensor("gidx", [4, 128], mybir.dt.int32, kind="ExternalInput").ap()
    out = nc.dram_tensor("out", [C, TQ], F32, kind="ExternalOutput").ap()

    wrd = nc.dram_tensor("wrd", [128, 8 * 8 * 128], BF16).ap()
    wkd = nc.dram_tensor("wkd", [128, 8 * 8 * 128], BF16).ap()
    wvd = nc.dram_tensor("wvd", [128, 8 * 8 * 128], BF16).ap()
    a2ai = [nc.dram_tensor(f"a2ai{j}", [8, 128, 2, 512 + (1 if j == NC2 - 1 else 0)],
                           BF16).ap() for j in range(NC2)]
    a2ao = [nc.dram_tensor(f"a2ao{j}", [8, 128, 2, 512 + (1 if j == NC2 - 1 else 0)],
                           BF16).ap() for j in range(NC2)]

    import contextlib

    st_ctx = contextlib.ExitStack()
    sbp = lambda name, shape, dt: st_ctx.enter_context(nc.sbuf_tensor(name, shape, dt))
    psx_ = lambda name: st_ctx.enter_context(nc.psum_tensor(name, [128, 512], F32))
    sem = lambda name: st_ctx.enter_context(nc.semaphore(name))

    c_id = sbp("c_id", [128, 128], BF16)
    c_1b = sbp("c_1b", [128, 1], BF16)
    c_1bm = sbp("c_1bm", [128, 128], BF16)
    c_1cm = sbp("c_1cm", [128, 128], F32)
    c_1c = sbp("c_1c", [128, 1], F32)
    c_1r = sbp("c_1r", [1, 128], F32)
    c_tri = sbp("c_tri", [128, 128], F32)
    c_d3 = sbp("c_d3", [3, 6], F32)
    c_hs = sbp("c_hs", [1, 2], F32)
    c_mur = sbp("c_mur", [128, 8], F32)
    c_muk = sbp("c_muk", [128, 8], F32)
    zcol = sbp("zcol", [128, 1], BF16)
    c_eps = sbp("c_eps", [128, 1], F32)
    c_gix = [sbp(f"c_gix{j}", [128, 1], mybir.dt.int32) for j in range(4)]
    mx = sbp("mx", [1, 64], F32)
    hsr = sbp("hsr", [3, 2], F32)
    A3 = sbp("A3", [3, 6], BF16)
    cneg = sbp("cneg", [128, 2], F32)
    xql = sbp("xql", [128, 8], F32)
    wo_sb = sbp("wo_sb", [128, 8 * 8 * 128], BF16)
    vcol = sbp("vcol", [128, 8], F32)
    mi2 = sbp("mi2", [128, 2], F32)
    bsq8 = sbp("bsq8", [128, 8], F32)
    bcol = sbp("bcol", [128, 8], F32)
    aall0 = sbp("aall0", [128, 4 * 1024], BF16)
    sqB = sbp("sqB", [128, 4 * 512], F32)
    c_idf = sbp("c_idf", [128, 128], F32)

    arA = AutoArena(nc)
    wq_sb = arA.sbuf_tensor("wq_sb", [128, 6 * 8 * 128], BF16)
    xsb = arA.sbuf_tensor("xsb", [128, 2 * 8 * 512], BF16)
    Qt = arA.sbuf_tensor("Qt", [128, 2 * T], BF16)
    Kt = arA.sbuf_tensor("Kt", [128, 2 * T], BF16)
    Vt = arA.sbuf_tensor("Vt", [128, 2 * T], BF16)
    vstg = arA.sbuf_tensor("vstg", [128, 512], BF16)
    pt = arA.sbuf_tensor("pt", [128, 8 * 512], BF16)
    lsum = arA.sbuf_tensor("lsum", [128, 2 * 512], BF16)
    sq = arA.sbuf_tensor("sq", [128, 2 * 512], BF16)
    lb = arA.sbuf_tensor("lb", [128, 512], F32)
    rscr = arA.sbuf_tensor("rscr", [128, 512], F32)
    aosg = arA.sbuf_tensor("aosg", [128, 2 * 512], BF16)
    arA.allocate()

    arA.free()
    arB = AutoArena(nc)
    aall = arB.sbuf_tensor("aall", [128, 4 * 1026], BF16)
    aall_b = arB.sbuf_tensor("aall_b", [128, 8], BF16)
    w2_sb = arB.sbuf_tensor("w2_sb", [128, 3 * 8 * 8 * 128], BF16)
    xq = arB.sbuf_tensor("xq", [128, 8 * 512], F32)
    x2c = arB.sbuf_tensor("x2c", [128, 8 * 513], F32)
    vtmp = arB.sbuf_tensor("vtmp", [128, 2 * 8 * 512], F32)
    sq2 = arB.sbuf_tensor("sq2", [128, 4 * 512], F32)
    mb_sb2 = arB.sbuf_tensor("mb_sb2", [128, 512], F32)
    ib_sb2 = arB.sbuf_tensor("ib_sb2", [128, 512], F32)
    xr_sb = arB.sbuf_tensor("xr_sb", [128, 8 * 512], BF16)
    xk_sb = arB.sbuf_tensor("xk_sb", [128, 8 * 512], BF16)
    ka_sb = arB.sbuf_tensor("ka_sb", [128, 8 * 512], BF16)
    rg_sb = arB.sbuf_tensor("rg_sb", [128, 8 * 512], F32)
    arB.allocate()
    t1_sb = None

    psA = psx_("psA")
    psB = psx_("psB")
    psO0 = psx_("psO0")
    psO1 = psx_("psO1")
    psL0 = psx_("psL0")
    psL1 = psx_("psL1")
    psBC = psx_("psBC")
    psX = psx_("psX")

    sC = Ctr(sem("sC"), 16)
    sWQ = Ctr(sem("sWQ"), 16)
    sX = [Ctr(sem("sX0"), 16), Ctr(sem("sX1"), 16)]
    sW2 = Ctr(sem("sW2"), 16)
    sAO = [Ctr(sem("sAO0"), 16), Ctr(sem("sAO1"), 16)]
    sZ = Ctr(sem("sZ"), 16)
    sLD = Ctr(sem("sLD"), 16)
    sLD0 = Ctr(sem("sLD0"), 16)
    sLD2 = Ctr(sem("sLD2"), 16)
    sWO = Ctr(sem("sWO"), 16)
    sAB = Ctr(sem("sAB"), 16)
    sXQL = Ctr(sem("sXQL"), 16)
    sXQ = Ctr(sem("sXQ"), 16)
    sOUT = Ctr(sem("sOUT"), 16)
    sgp = Ctr(sem("sgp"), 1)
    st = Ctr(sem("st"), 1)
    sv = Ctr(sem("sv"), 1)
    sa = Ctr(sem("sa"), 1)
    scc = Ctr(sem("scc"), 1)

    prog = {k: [] for k in ("t", "v", "a", "g", "s")}
    highwater = {}

    def op(eng, builder, inc=None, waits=()):
        if eng == "ad":
            eng = "a"
        elif eng == "gc":
            eng = "g"
            waits = list(waits) + [(sgp, sgp.v)]
            inc = sgp
        elif eng == "v":
            waits = list(waits) + [(sv, sv.v)]
            inc = sv
        elif eng == "a":
            waits = list(waits) + [(sa, sa.v)]
            inc = sa
        needed = []
        for ctr, val in waits:
            if val <= 0:
                continue
            key = (eng, id(ctr))
            if highwater.get(key, 0) >= val:
                continue
            highwater[key] = val
            needed.append((ctr.sem, val))
        inc_sem = None
        inc_n = 0
        if inc is not None:
            inc.bump()
            inc_sem = inc.sem
            inc_n = inc.step

        def f(e, builder=builder, needed=tuple(needed), inc_sem=inc_sem, inc_n=inc_n):
            for s_, v_ in needed:
                e.wait_ge(s_, v_)
            inst = builder(e)
            if inc_sem is not None:
                inst.then_inc(inc_sem, inc_n)

        prog[eng].append(f)
        return inc.v if inc is not None else None

    for dst, src in ((c_id[:, :], identp), (c_1b[:, :], onesb), (c_1c[:, :], onesc),
                     (c_1bm[:, :], onesbm), (c_1cm[:, :], onescm),
                     (c_1r[:, :], onesr), (c_tri[:, :], trim),
                     (c_d3[:, :], d3), (c_hs[:, :], hs2), (c_mur[:, :], mur),
                     (c_muk[:, :], muk), (c_eps[:, :], epsv),
                     (c_gix[0][:, :], gidx[0:1, :]), (c_gix[1][:, :], gidx[1:2, :]),
                     (c_gix[2][:, :], gidx[2:3, :]), (c_gix[3][:, :], gidx[3:4, :]),
                     (xql[:, :], xqlp)):
        op("s", lambda e, d=dst, s=src: e.dma_start(out=d, in_=s), inc=sC)
    consts_s = sC.v
    idf_c = op("v", lambda e: e.tensor_copy(c_idf[:, :], c_id[:, :]),
               waits=[(sC, consts_s)])
    op("g", lambda e: e.dma_start(out=wq_sb[:, :], in_=wqkv), inc=sWQ)
    op("g", lambda e: e.dma_start(out=wo_sb[:, :], in_=wout), inc=sWO)
    op("g", lambda e: e.dma_start(out=wrd, in_=wr), inc=sW2)
    op("g", lambda e: e.dma_start(out=wkd, in_=wk), inc=sW2)
    op("g", lambda e: e.dma_start(out=wvd, in_=wv), inc=sW2)
    w2d_g = sW2.v

    zv = op("v", lambda e: e.memset(zcol[:, :], 0.0), inc=sv)
    for sh in (0, 4):
        for u in range(2):
            op("s", lambda e, sh=sh, u=u: e.dma_start(
                out=a2ai[NC2 - 1][sh, :, u, 0:1], in_=zcol[:, :],
                allow_slow_non_contiguous=True),
               inc=sZ, waits=[(sv, zv)])
    zero_ss = sZ.v

    xt3 = xt.rearrange("p (tc a t) -> p tc a t", tc=NT, a=8)
    xsb3 = xsb[:, :].rearrange("p (par a t) -> p par a t", par=2, a=8)
    wq4 = wq_sb[:, :].rearrange("p (s a d) -> p s a d", s=6, a=8)
    Qt2 = Qt[:, :].rearrange("p (u t) -> p u t", u=2)
    Kt2 = Kt[:, :].rearrange("p (u t) -> p u t", u=2)
    Vt3 = Vt[:, :].rearrange("p (u kt d) -> p u kt d", u=2, kt=KTT)

    x_done = []
    x_cons = []
    bank_free = {0: 0, 1: 0}
    psx_free = 0

    for tc in range(NT):
        par = tc % 2
        w_ = [] if tc < 2 else [(st, x_cons[tc - 2])]
        op("g", lambda e, par=par, tc=tc:
           e.dma_start(out=xsb3[:, par, :, :], in_=xt3[:, tc, :, :]),
           inc=sX[par], waits=w_)
        x_done.append((sX[par], sX[par].v))

        for ob in range(6):
            bname = (tc * 6 + ob) % 2
            bank = psA if bname == 0 else psB

            def mms(e, bank=bank, ob=ob, par=par):
                for a in range(8):
                    inst = e.matmul(bank[:, :], wq4[:, ob, a, :], xsb3[:, par, a, :],
                                    start=(a == 0), stop=(a == 7))
                return inst
            mmc = op("t", mms, inc=st,
                     waits=[x_done[tc], (sWQ, 16), (sv, bank_free[bname])])

            u = ob % 2
            kind = ob // 2
            if kind == 0:
                dst = Qt2[:, u, tc * 512:(tc + 1) * 512]
            elif kind == 1:
                dst = Kt2[:, u, tc * 512:(tc + 1) * 512]
            else:
                dst = vstg[:, :]
            bank_free[bname] = op("v", lambda e, d=dst, b=bank: e.tensor_copy(d, b[:, :]),
                                  inc=sv, waits=[(st, mmc)])

            if kind == 2:
                vst = sv.v

                def vtr(e):
                    psXb = psX[:, :].bitcast(BF16)
                    for jj in range(4):
                        inst = e.transpose(psXb[:, jj * 128:(jj + 1) * 128],
                                           vstg[:, jj * 128:(jj + 1) * 128], c_id[:, :])
                    return inst
                trc = op("t", vtr, inc=st,
                         waits=[(sv, vst), (sv, psx_free), (sC, consts_s)])
                psx_free = op("v", lambda e, u=u, tc=tc: e.tensor_copy(
                    Vt3[:, u, tc * 4:(tc + 1) * 4, :].rearrange("p k d -> p (k d)"),
                    psX[:, :].bitcast(BF16)[:, 0:512]), inc=sv, waits=[(st, trc)])
        x_cons.append(st.v)

    ph1_v = sv.v

    psbc_free = 0
    adj_frees = {0: 0, 1: 0}
    psl_frees = {0: 0, 1: 0}
    sq_frees = {0: 0, 1: 0}
    for u in range(2):
        hc = op("t", lambda e, u=u: e.matmul(psX[0:3, 0:1], c_1r[0:1, 0:3],
                                             c_hs[0:1, u:u + 1], start=True, stop=True),
                inc=st, waits=[(sC, consts_s), (sv, psx_free)])
        hv = op("v", lambda e, u=u: e.tensor_copy(hsr[:, u:u + 1], psX[0:3, 0:1]),
                inc=sv, waits=[(st, hc)])
        psx_free = hv
        a3done = op("v", lambda e, u=u: e.tensor_scalar(
            A3[:, 3 * u:3 * u + 3], c_d3[:, 3 * u:3 * u + 3], hsr[:, u:u + 1], None, OP.mult),
            inc=sv, waits=[(sv, hv), (sC, consts_s)])

        for tc in range(NT):
            abank = psBC if tc % 2 == 0 else psX
            amc = op("t", lambda e, u=u, tc=tc, abank=abank: e.matmul(
                abank[0:3, 0:512], A3[:, 3 * u:3 * u + 3],
                Qt2[0:3, u, tc * 512:(tc + 1) * 512], start=True, stop=True),
                inc=st, waits=[(sv, a3done), (sv, ph1_v),
                               (sv, adj_frees[tc % 2])])
            adj_frees[tc % 2] = op("v", lambda e, u=u, tc=tc, abank=abank:
                                   e.tensor_tensor(
                Qt2[0:3, u, tc * 512:(tc + 1) * 512],
                Qt2[0:3, u, tc * 512:(tc + 1) * 512],
                abank[0:3, 0:512], OP.add), inc=sv, waits=[(st, amc)])
        adj_v = sv.v
        psbc_free = max(psbc_free, adj_frees[0])
        psx_free = max(psx_free, adj_frees[1])

        for which in ("q", "k"):
            buf = Qt2 if which == "q" else Kt2
            base = 0 if which == "q" else 16
            dep = adj_v if which == "q" else ph1_v
            for tc in range(NT):
                sl = tc % 2
                bank = psL0 if sl == 0 else psL1
                sqc = op("a", lambda e, u=u, tc=tc, sl=sl, buf=buf: e.activation(
                    sq[:, sl * 512:sl * 512 + 512],
                    buf[:, u, tc * 512:(tc + 1) * 512], AF.Square),
                    waits=[(sv, dep), (st, sq_frees[sl])])
                oc = op("t", lambda e, sl=sl, bank=bank: e.matmul(
                    bank[0:1, :], c_1b[:, 0:1], sq[:, sl * 512:sl * 512 + 512],
                    start=True, stop=True),
                    inc=st, waits=[(sa, sqc), (sC, consts_s), (sv, psl_frees[sl])])
                sq_frees[sl] = oc
                psl_frees[sl] = op("v", lambda e, tc=tc, sl=sl, base=base, bank=bank:
                                   e.reduce_max(
                    mx[0:1, base + tc:base + tc + 1], bank[0:1, 0:512], axis=AX),
                    inc=sv, waits=[(st, oc)])

        op("v", lambda e: e.reduce_max(mx[0:1, 32:33], mx[0:1, 0:NT], axis=AX),
           inc=sv, waits=[(sv, max(psl_frees[0], psl_frees[1]))])
        f0 = op("v", lambda e: e.reduce_max(mx[0:1, 33:34], mx[0:1, 16:16 + NT], axis=AX),
                inc=sv)
        fc = op("v", lambda e: e.tensor_tensor(mx[0:1, 34:35], mx[0:1, 32:33],
                                               mx[0:1, 33:34], OP.mult),
                inc=sv, waits=[(sv, f0)])

        op("a", lambda e: e.activation(mx[0:1, 35:36], mx[0:1, 34:35], AF.Sqrt),
           waits=[(sv, fc)])
        cc_ = op("a", lambda e: e.mul(mx[0:1, 36:37], mx[0:1, 35:36], -SCL))
        cbc_c = op("t", lambda e: e.matmul(psBC[:, 0:1], c_1r[0:1, :], mx[0:1, 36:37],
                                           start=True, stop=True),
                   inc=st, waits=[(sa, cc_), (sv, psbc_free)])
        psbc_free = op("v", lambda e, u=u: e.tensor_copy(cneg[:, u:u + 1], psBC[:, 0:1]),
                       inc=sv, waits=[(st, cbc_c)])
    bounds_v = sv.v

    xqt3 = xqt.rearrange("p (ck a t) -> p ck a t", ck=NC2, a=8)

    wo4 = wo_sb[:, :].rearrange("p (a o d) -> p a o d", a=8, o=8)
    w24 = w2_sb[:, :].rearrange("p (w a o d) -> p w a o d", w=3, a=8, o=8)
    xq3 = xq[:, :].rearrange("p (a t) -> p a t", a=8)
    x2c3 = x2c[:, :].rearrange("p (a t) -> p a t", a=8)
    vtmp3p = vtmp[:, :].rearrange("p (par a t) -> p par a t", par=2, a=8)
    xr3 = xr_sb[:, :].rearrange("p (a t) -> p a t", a=8)
    xk3 = xk_sb[:, :].rearrange("p (a t) -> p a t", a=8)
    ka3 = ka_sb[:, :].rearrange("p (a t) -> p a t", a=8)
    rg3 = rg_sb[:, :].rearrange("p (a t) -> p a t", a=8)
    out3 = out.rearrange("(a p) t -> p a t", p=128)

    rounds = [NC2 - 1] + list(range(NC2 - 1))
    a2afl = [a2ao[j].rearrange("r p u t -> (r p) (u t)") for j in range(NC2)]
    scc_need = {jr: i + 1 for i, jr in enumerate(rounds)}
    prologue_state = {}

    def emit_prologue0():
        ab_ss = 0
        for j_ in range(4):
            ab_ss = op("g", lambda e, j_=j_: e.indirect_dma_start(
                aall[:, j_ * 1026:(j_ + 1) * 1026], None, a2afl[NC2 - 1][:, :],
                IndirectOffsetOnAxis(c_gix[j_][:, :], 0)),
                inc=sAB, waits=[(scc, scc_need[NC2 - 1]), (sC, consts_s)])
        abview = bass.AP(aall, 0, [[4104, 128], [1026, 4], [513, 2]])
        abcp = op("v", lambda e: e.tensor_copy(
            aall_b[:, :].rearrange("p (j u) -> p j u", u=2), abview),
            waits=[(sAB, ab_ss)])
        prologue_state["abcp"] = abcp

    def emit_prologue():
        nonlocal psx_free
        abcp = prologue_state["abcp"]

        def bpmm(e):
            for o in range(8):
                for a in range(8):
                    inst = e.matmul(psX[:, o:o + 1], wo4[:, a, o, :],
                                    aall_b[:, a:a + 1],
                                    start=(a == 0), stop=(a == 7))
            return inst
        bp_c = op("t", bpmm, inc=st,
                  waits=[(sv, abcp), (sWO, 16), (sv, psx_free)])
        bres_c = op("v", lambda e: e.tensor_tensor(vcol[:, :], xql[:, :], psX[:, 0:8], OP.add),
                    inc=sv, waits=[(st, bp_c), (sC, consts_s)])
        psx_free = bres_c
        prologue_state["bnd_v"] = bres_c

    def emit_prologue2():
        bres_c = prologue_state["bnd_v"]
        bsq_c = op("a", lambda e: e.activation(bsq8[:, :], vcol[:, :], AF.Square),
                   waits=[(sv, bres_c)])

        def bstat(e):
            e.matmul(psX[0:1, 16:24], c_1c[:, 0:1], vcol[:, :], start=True, stop=True)
            return e.matmul(psX[0:1, 32:40], c_1c[:, 0:1], bsq8[:, :], start=True, stop=True)
        bst_c = op("t", bstat, inc=st, waits=[(sa, bsq_c), (sv, bres_c)])

        op("v", lambda e: e.reduce_sum(mx[0:1, 40:41], psX[0:1, 16:24], axis=AX),
           inc=sv, waits=[(st, bst_c)])
        bm0 = op("v", lambda e: e.reduce_sum(mx[0:1, 41:42], psX[0:1, 32:40], axis=AX),
                 inc=sv)

        op("v", lambda e: e.tensor_scalar(mx[0:1, 40:41], mx[0:1, 40:41], 1.0 / C,
                                          None, OP.mult), waits=[(sv, bm0)])
        op("v", lambda e: e.tensor_scalar(mx[0:1, 41:42], mx[0:1, 41:42], 1.0 / C,
                                          None, OP.mult))
        op("v", lambda e: e.tensor_tensor(mx[0:1, 42:43], mx[0:1, 40:41], mx[0:1, 40:41],
                                          OP.mult))
        bmv_c = op("v", lambda e: e.tensor_tensor(mx[0:1, 41:42], mx[0:1, 41:42],
                                                  mx[0:1, 42:43], OP.subtract))
        bsi_c = op("a", lambda e: e.activation(mx[0:1, 41:42], mx[0:1, 41:42],
                                               AF.Sqrt, bias=c_eps[0:1, 0:1]),
                   inc=sa, waits=[(sv, bmv_c)])
        bir_c = op("v", lambda e: e.reciprocal(mx[0:1, 41:42], mx[0:1, 41:42]),
                   inc=sv, waits=[(sa, bsi_c)])
        bbc_c = op("t", lambda e: e.matmul(psX[:, 40:42], c_1r[0:1, :], mx[0:1, 40:42],
                                           start=True, stop=True),
                   inc=st, waits=[(sv, bir_c)])
        mi_ev = op("v", lambda e: e.tensor_copy(mi2[:, :], psX[:, 40:42]),
                   inc=sv, waits=[(st, bbc_c)])
        bnd_v = op("v", lambda e: e.tensor_scalar(
            bcol[:, :], vcol[:, :], mi2[:, 0:1], mi2[:, 1:2], OP.subtract, OP.mult),
            waits=[(sv, mi_ev)])
        prologue_state["bnd_v"] = bnd_v

    pt2 = pt[:, :].rearrange("p (r t) -> p r t", r=8)
    ls2 = lsum[:, :].rearrange("p (r t) -> p r t", r=2)
    aosg2 = aosg[:, :].rearrange("p (r t) -> p r t", r=2)

    exp_ring = {0: 0, 1: 0, 2: 0}
    pt_ring = {k: 0 for k in range(8)}
    pt_ring_v = {k: 0 for k in range(8)}
    lsum_free = {0: 0, 1: 0}
    ol_free = {0: 0, 1: 0}
    ao_dma = {0: 0, 1: 0}
    lb_free = 0

    qi = -1
    a2a_cs = {}
    for jr in rounds:
        for u in range(2):
          for qtr in range(4):
            qg = qtr * PQ + jr
            qi += 1
            ktmax = 4 * qg + 4
            ngrp = ktmax // 4
            obank = psO0 if qi % 2 == 0 else psO1
            lbank = psL0 if qi % 2 == 0 else psL1

            smm_cs = {}
            exp_cs = {}
            grp_add = {}

            def geom(kt):
                diag = kt - 4 * qg
                if diag >= 0:
                    return 128 * diag, 512 - 128 * diag
                return 0, 512

            sbanks = (psA, psB, psBC)
            for step in range(ktmax + 2):
                tri_c = None
                if step < ktmax:
                    kt = step
                    sbank = sbanks[kt % 3]
                    o0, ncols = geom(kt)
                    qc0 = qg * 512 + o0
                    smm_cs[kt] = op("t", lambda e, u=u, kt=kt, sbank=sbank, qc0=qc0,
                                    ncols=ncols, o0=o0:
                                    e.matmul(sbank[:, o0:o0 + ncols],
                                             Kt2[:, u, kt * 128:(kt + 1) * 128],
                                             Qt2[:, u, qc0:qc0 + ncols],
                                             start=True, stop=True),
                                    inc=st,
                                    waits=[(sv, bounds_v), (sa, exp_ring[kt % 3]),
                                           (sv, psbc_free if kt % 3 == 2 else 0)])
                    if kt - 4 * qg >= 0:
                        tri_c = op("v", lambda e, sbank=sbank, o0=o0: e.tensor_tensor(
                            sbank[:, o0:o0 + 128], sbank[:, o0:o0 + 128],
                            c_tri[:, :], OP.add),
                            waits=[(st, smm_cs[kt]), (sC, consts_s)])
                if step >= 2:
                    kt = step - 2
                    o0, ncols = geom(kt)
                    ow = [(sa, exp_cs[kt])]
                    if kt == 0:
                        ow.append((sv, ol_free[qi % 2]))
                    pt_ring[kt % 8] = op("t", lambda e, u=u, kt=kt, o0=o0, ncols=ncols,
                                         obank=obank,
                                         first=(kt == 0), last=(kt == ktmax - 1):
                                         e.matmul(obank[:, o0:o0 + ncols],
                                                  Vt3[:, u, kt, :],
                                                  pt2[:, kt % 8, o0:o0 + ncols],
                                                  start=first, stop=last),
                                         inc=st, waits=ow)
                    if kt % 4 == 1 and kt // 4 >= 1:
                        g = kt // 4 - 1
                        gp = 4 * (g % 2)
                        a1 = op("v", lambda e, g=g, gp=gp: e.tensor_tensor(
                            ls2[:, g % 2, :], pt2[:, gp, :], pt2[:, gp + 1, :],
                            OP.add),
                            waits=[(sa, exp_cs[4 * g + 3]),
                                   (st, lsum_free[g % 2])])
                        pt_ring_v[gp] = a1
                        pt_ring_v[gp + 1] = a1
                    if kt % 4 == 3 and kt // 4 >= 1:
                        g = kt // 4 - 1
                        gp = 4 * (g % 2)
                        op("v", lambda e, g=g, gp=gp: e.tensor_tensor(
                            ls2[:, g % 2, :], ls2[:, g % 2, :], pt2[:, gp + 2, :],
                            OP.add))
                        grp_add[g] = op("v", lambda e, g=g, gp=gp: e.tensor_tensor(
                            ls2[:, g % 2, :], ls2[:, g % 2, :], pt2[:, gp + 3, :],
                            OP.add))
                        pt_ring_v[gp + 2] = grp_add[g]
                        pt_ring_v[gp + 3] = grp_add[g]
                        ow2 = [(sv, grp_add[g])]
                        if g == 0:
                            ow2.append((sv, ol_free[qi % 2]))
                        lsum_free[g % 2] = op("t", lambda e, g=g, lbank=lbank,
                                              first=(g == 0):
                                              e.matmul(lbank[:, :], c_1bm[:, :],
                                                       ls2[:, g % 2, :],
                                                       start=first, stop=False),
                                              inc=st, waits=ow2)
                if step < ktmax:
                    kt = step
                    sbank = sbanks[kt % 3]
                    o0, ncols = geom(kt)
                    ew = [(st, smm_cs[kt]), (st, pt_ring[kt % 8]),
                          (sv, pt_ring_v[kt % 8])]
                    if tri_c is not None:
                        ew.append((sv, tri_c))
                    exp_cs[kt] = op("a", lambda e, u=u, sbank=sbank, kt=kt, o0=o0,
                                    ncols=ncols:
                                    e.activation(pt2[:, kt % 8, o0:o0 + ncols],
                                                 sbank[:, o0:o0 + ncols], AF.Exp,
                                                 bias=cneg[:, u:u + 1], scale=SCL),
                                    waits=ew)
                    exp_ring[kt % 3] = exp_cs[kt]

            g = ngrp - 1
            gp = 4 * (g % 2)
            for kk in range(4 * g + 1, 4 * g + 4):
                oo, nn = geom(kk)
                op("v", lambda e, kk=kk, oo=oo, gp=gp: e.memset(
                    pt2[:, gp + (kk % 4), 0:oo], 0.0),
                    waits=[(sa, exp_cs[kk]), (st, pt_ring[kk % 8])])
            op("v", lambda e, g=g, gp=gp: e.tensor_tensor(
                ls2[:, g % 2, :], pt2[:, gp, :], pt2[:, gp + 1, :], OP.add),
                waits=[(st, lsum_free[g % 2])])
            op("v", lambda e, g=g, gp=gp: e.tensor_tensor(
                ls2[:, g % 2, :], ls2[:, g % 2, :], pt2[:, gp + 2, :], OP.add))
            grp_add[g] = op("v", lambda e, g=g, gp=gp: e.tensor_tensor(
                ls2[:, g % 2, :], ls2[:, g % 2, :], pt2[:, gp + 3, :], OP.add))
            for kk in range(4 * g, 4 * g + 4):
                pt_ring_v[gp + (kk % 4)] = grp_add[g]
            ow2 = [(sv, grp_add[g])]
            if g == 0:
                ow2.append((sv, ol_free[qi % 2]))
            lsum_free[g % 2] = op("t", lambda e, g=g, lbank=lbank,
                                  first=(g == 0):
                                  e.matmul(lbank[:, :], c_1bm[:, :],
                                           ls2[:, g % 2, :],
                                           start=first, stop=True),
                                  inc=st, waits=ow2)

            kloop_t = st.v
            rec_c = op("v", lambda e, lbank=lbank: e.reciprocal(lb[:, :], lbank[:, :]),
                       waits=[(st, kloop_t), (sv, lb_free)])
            aev_c = op("v", lambda e, obank=obank, qi=qi: e.tensor_tensor(
                aosg2[:, qi % 2, :], obank[:, :], lb[:, :], OP.mult),
                waits=[(sv, rec_c), (sAO[qi % 2], ao_dma[qi % 2])])
            lb_free = aev_c
            ol_free[qi % 2] = aev_c

            extra = (jr == NC2 - 1) and (qtr < 3)
            sAO[qi % 2].bump((3 if extra else 1) * 16)
            bcol0 = 1 if jr == NC2 - 1 else 0

            def aodma(e, u=u, qi=qi, qtr=qtr, jr=jr, extra=extra, bcol0=bcol0,
                      sem_=sAO[qi % 2].sem):
                tin = a2ai[jr]
                if extra:
                    e.dma_start(out=tin[qtr + 1, :, u, 0:1],
                                in_=aosg2[:, qi % 2, 511:512],
                                allow_slow_non_contiguous=True).then_inc(sem_, 16)
                    e.dma_start(out=tin[qtr + 5, :, u, 0:1],
                                in_=aosg2[:, qi % 2, 511:512],
                                allow_slow_non_contiguous=True).then_inc(sem_, 16)
                e.dma_start(out=tin[qtr, :, u, bcol0:bcol0 + 512],
                            in_=aosg2[:, qi % 2, :]).then_inc(sem_, 16)
                return e.dma_start(
                    out=tin[qtr + 4, :, u, bcol0:bcol0 + 512], in_=aosg2[:, qi % 2, :])
            ao_dma[qi % 2] = op("s", aodma, inc=sAO[qi % 2], waits=[(sv, aev_c)])
        cw = [(sAO[0], sAO[0].v), (sAO[1], sAO[1].v)]
        if jr == NC2 - 1:
            cw.append((sZ, zero_ss))
        a2a_cs[jr] = op("g", lambda e, jr=jr: e.collective_compute(
            "AllToAll", OP.bypass, replica_groups=[[0, 1, 2, 3, 4, 5, 6, 7]],
            ins=[a2ai[jr].opt()], outs=[a2ao[jr].opt()]),
            inc=scc, waits=cw)
        if jr == 0:
            emit_prologue0()
            if NC2 > 1:
                for j_ in range(4):
                    op("g", lambda e, j_=j_: e.indirect_dma_start(
                        aall0[:, j_ * 1024:(j_ + 1) * 1024], None, a2afl[0][:, :],
                        IndirectOffsetOnAxis(c_gix[j_][:, :], 0)),
                        inc=sLD0, waits=[(scc, scc_need[0]), (sC, consts_s)])
                prologue_state["aall0_ss"] = sLD0.v
        if jr == rounds[min(2, len(rounds) - 1)]:
            emit_prologue()

    attn_ao0 = sAO[0].v
    attn_ao1 = sAO[1].v

    sWR = Ctr(sem("sWR"), 16)
    sWK = Ctr(sem("sWK"), 16)
    sWV = Ctr(sem("sWV"), 16)
    aow = [(sAO[0], attn_ao0), (sAO[1], attn_ao1), (sW2, w2d_g)]
    w23 = w2_sb[:, :].rearrange("p (w x) -> p w x", w=3)
    op("g", lambda e: e.dma_start(out=w23[:, 0, :], in_=wrd), inc=sWR, waits=aow)
    op("g", lambda e: e.dma_start(out=w23[:, 1, :], in_=wkd), inc=sWK)
    op("g", lambda e: e.dma_start(out=w23[:, 2, :], in_=wvd), inc=sWV)



    pair_free = {0: 0, 1: 0}
    sqb_t_free = {0: 0, 1: 0}

    def stats_and_apply(src3, dst3, dst_off, dep, bc_extra=(), pls=None):
        nonlocal psbc_free, psx_free
        key = 0 if pls is None else 1
        pl0, pl1 = (psL0, psL1) if pls is None else pls
        sqb = sqB if pls is None else sq2
        mb_sb = mb_sb2[:, :]
        ib_sb = ib_sb2[:, :]
        sqcs = {}
        for h in range(2):
            sqcs[h] = op("a", lambda e, h=h, src3=src3, sqb=sqb: e.activation(
                sqb[:, :], src3[:, 4 * h:4 * h + 4, :].rearrange("p a t -> p (a t)"),
                AF.Square),
                waits=list(dep) + [(st, sqb_t_free[key]),
                                   (sv, pair_free[key])])
            for a in range(4 * h, 4 * h + 4):
                def s2mm(e, a=a, src3=src3, pl0=pl0, pl1=pl1, sqb=sqb):
                    e.matmul(pl0[:, :], c_1cm[:, :], src3[:, a, :],
                             start=(a == 0), stop=(a == 7))
                    return e.matmul(pl1[:, :], c_1cm[:, :],
                                    sqb[:, (a % 4) * 512:(a % 4) * 512 + 512],
                                    start=(a == 0), stop=(a == 7))
                op("t", s2mm, inc=st,
                   waits=[(sa, sqcs[h]), (sv, pair_free[key])]
                   + list(dep) + list(bc_extra))
            sqb_t_free[key] = st.v
        stats_t = st.v

        op("v", lambda e, pl0=pl0, mb_sb=mb_sb: e.tensor_scalar(
            mb_sb, pl0[:, :], 1.0 / C, None, OP.mult), waits=[(st, stats_t)])
        op("v", lambda e, pl1=pl1, ib_sb=ib_sb: e.tensor_scalar(
            ib_sb, pl1[:, :], 1.0 / C, None, OP.mult))
        op("v", lambda e, mb_sb=mb_sb, sqb=sqb: e.tensor_tensor(
            sqb[:, 0:512], mb_sb, mb_sb, OP.mult))
        mv_c = op("v", lambda e, ib_sb=ib_sb, sqb=sqb: e.tensor_tensor(
            ib_sb, ib_sb, sqb[:, 0:512], OP.subtract))
        pair_free[key] = mv_c
        isq_c = op("a", lambda e, ib_sb=ib_sb: e.activation(ib_sb, ib_sb, AF.Sqrt,
                                                            bias=c_eps[:, 0:1]),
                   waits=[(sv, mv_c)])
        bce_c = op("v", lambda e, ib_sb=ib_sb: e.reciprocal(ib_sb, ib_sb),
                   waits=[(sa, isq_c)])

        mbB = bass.AP(mb_sb2, 0, [[512, 128], [0, 4], [1, 512]])
        ibB = bass.AP(ib_sb2, 0, [[512, 128], [0, 4], [1, 512]])
        halves = {}
        for hh in range(2):
            op("v", lambda e, hh=hh, src3=src3, mbB=mbB: e.tensor_tensor(
                src3[:, 4 * hh:4 * hh + 4, :], src3[:, 4 * hh:4 * hh + 4, :],
                mbB, OP.subtract),
                waits=([(sv, bce_c)] if hh == 0 else []))
            halves[hh] = op("v", lambda e, hh=hh, src3=src3, dst3=dst3,
                            dst_off=dst_off, ibB=ibB:
                            e.tensor_tensor(
                                dst3[:, 4 * hh:4 * hh + 4, dst_off:dst_off + 512],
                                src3[:, 4 * hh:4 * hh + 4, :], ibB, OP.mult))
        return halves

    pb_free = {0: (sv, ol_free[0]), 1: (sv, ol_free[1])}
    xq_free = 0
    out_dma = {0: 0, 1: 0}
    out_dma_all = 0
    ln1_v = 0
    sig_c = {}
    ka_c = {}
    rbank_free = {0: 0, 1: 0}
    kbank_free = {0: 0, 1: 0}
    vb_free = {0: 0, 1: 0}
    aall_free = 0
    aall0_free = 0
    zblk_vs = {}
    resid_vs = {}
    ka_prev = 0

    def emit_A(ck):
        nonlocal xq_free, aall_free, aall0_free
        xq_ss = op("s", lambda e, ck=ck: e.dma_start(
            out=xq3[:, :, :], in_=xqt3[:, ck, :, :]),
            inc=sXQ, waits=[(st, xq_free), (sAO[0], attn_ao0), (sAO[1], attn_ao1)])
        wid = 513 if ck == NC2 - 1 else 512
        if ck % 2 == 0 and ck != NC2 - 1:
            abuf, spac, asem = aall0, 1024, sLD0
            if ck == 0:
                aall_ss = prologue_state["aall0_ss"]
            else:
                for j_ in range(4):
                    op("g", lambda e, ck=ck, j_=j_: e.indirect_dma_start(
                        aall0[:, j_ * 1024:(j_ + 1) * 1024], None, a2afl[ck][:, :],
                        IndirectOffsetOnAxis(c_gix[j_][:, :], 0)),
                        inc=sLD0, waits=[(st, aall0_free), (scc, scc_need[ck]),
                                         (sC, consts_s)])
                aall_ss = sLD0.v
        else:
            abuf, spac, asem = aall, 1026, sLD
            for j_ in range(4):
                op("g", lambda e, ck=ck, j_=j_, wid=wid: e.indirect_dma_start(
                    aall[:, j_ * 1026: j_ * 1026 + 2 * wid], None, a2afl[ck][:, :],
                    IndirectOffsetOnAxis(c_gix[j_][:, :], 0)),
                    inc=sLD, waits=[(st, aall_free), (scc, scc_need[ck]),
                                    (sC, consts_s), (sv, abcp)])
            aall_ss = sLD.v

        par = ck % 2
        off = 1 if ck == NC2 - 1 else 0
        for o in range(8):
            bank = psO0 if o % 2 == 0 else psO1
            pw = [(asem, aall_ss), (sWO, 16), pb_free[o % 2]]

            def pmm(e, o=o, bank=bank, off=off, wid=wid, abuf=abuf, spac=spac):
                for a in range(8):
                    base = (a // 2) * spac + (a % 2) * wid + off
                    inst = e.matmul(bank[:, :], wo4[:, a, o, :],
                                    abuf[:, base:base + 512],
                                    start=(a == 0), stop=False)
                return inst
            op("t", pmm, inc=st, waits=pw)
            pc_ = op("t", lambda e, o=o, bank=bank:
                     e.matmul(bank[:, :], c_idf[:, :], xq3[:, o, :],
                              start=False, stop=True),
                     inc=st, waits=[(sXQ, xq_ss), (sv, idf_c)])
            pb_free[o % 2] = (sa, op("a", lambda e, o=o, par=par, bank=bank:
                                     e.activation(vtmp3p[:, par, o, :],
                                                  bank[:, :], AF.Copy),
                                     waits=[(st, pc_), (sOUT, out_dma[par])]))
        resid_vs[ck] = sa.v
        xq_free = st.v
        if ck % 2 == 0 and ck != NC2 - 1:
            aall0_free = st.v
        else:
            aall_free = st.v

    abcp = prologue_state["abcp"]
    emit_A(0)
    emit_prologue2()
    bnd_v = op("v", lambda e: e.tensor_copy(x2c3[:, :, 0], bcol[:, :]),
               waits=[(sv, prologue_state["bnd_v"]), (sAO[0], attn_ao0),
                      (sAO[1], attn_ao1)])
    ln1_v = bnd_v
    for ck in range(NC2):
        par = ck % 2
        src3 = vtmp3p[:, par, :, :]

        if ck > 0:
            ln1_v = op("v", lambda e: e.tensor_copy(x2c3[:, :, 0], x2c3[:, :, 512]),
                       waits=[(sv, ln1_v)])

        ln1_h = stats_and_apply(src3, x2c3, 1, [(sa, resid_vs[ck])])
        ln1_v = ln1_h[1]

        mixed = None
        mixed_h = {}
        for h in range(2):
            op("v", lambda e, h=h: e.tensor_tensor(
                sq2[:, :].rearrange("p (a t) -> p a t", a=4),
                x2c3[:, 4 * h:4 * h + 4, 1:513],
                x2c3[:, 4 * h:4 * h + 4, 0:512], OP.subtract),
                waits=[(sv, ln1_h[h]), (sC, consts_s), (st, sqb_t_free[1]),
                       (sv, ka_prev)])
            for a in range(4 * h, 4 * h + 4):
                mixed_h[h] = op("v", lambda e, a=a: e.scalar_tensor_tensor(
                    xr3[:, a, :], sq2[:, (a % 4) * 512:(a % 4) * 512 + 512],
                    c_mur[:, a:a + 1], x2c3[:, a, 0:512], OP.mult, OP.add))
            for a in range(4 * h, 4 * h + 4):
                mixed = op("v", lambda e, a=a: e.scalar_tensor_tensor(
                    xk3[:, a, :], sq2[:, (a % 4) * 512:(a % 4) * 512 + 512],
                    c_muk[:, a:a + 1], x2c3[:, a, 0:512], OP.mult, OP.add))

        if ck + 1 < NC2:
            emit_A(ck + 1)

        for o in range(8):
            bank = psA if o % 2 == 0 else psB

            def rmm0(e, o=o, bank=bank):
                for a in range(4):
                    inst = e.matmul(bank[:, :], w24[:, 0, a, o, :], xr3[:, a, :],
                                    start=(a == 0), stop=False)
                return inst
            op("t", rmm0, inc=st,
               waits=[(sv, mixed_h[0]), (sWR, 16), (sa, rbank_free[o % 2])])

            def rmm1(e, o=o, bank=bank):
                for a in range(4, 8):
                    inst = e.matmul(bank[:, :], w24[:, 0, a, o, :], xr3[:, a, :],
                                    start=False, stop=(a == 7))
                return inst
            rc = op("t", rmm1, inc=st, waits=[(sv, mixed_h[1])])
            sig_c[o] = op("a", lambda e, o=o, bank=bank: e.activation(
                rg3[:, o, :], bank[:, :], AF.Sigmoid), waits=[(st, rc)])
            rbank_free[o % 2] = sig_c[o]

        for o in range(8):
            bank = psBC if o % 2 == 0 else psX

            def kmm(e, o=o, bank=bank):
                for a in range(8):
                    inst = e.matmul(bank[:, :], w24[:, 1, a, o, :], xk3[:, a, :],
                                    start=(a == 0), stop=(a == 7))
                return inst
            kc = op("t", kmm, inc=st,
                    waits=[(sv, mixed), (sWK, 16), (sv, kbank_free[o % 2])])

            op("v", lambda e, o=o, bank=bank: e.tensor_relu(
                sq2[:, (o % 2) * 512:(o % 2) * 512 + 512], bank[:, :]),
                waits=[(st, kc), (st, sqb_t_free[1]), (sv, mixed)])
            ka_c[o] = op("v", lambda e, o=o: e.tensor_tensor(
                ka3[:, o, :], sq2[:, (o % 2) * 512:(o % 2) * 512 + 512],
                sq2[:, (o % 2) * 512:(o % 2) * 512 + 512], OP.mult))
            kbank_free[o % 2] = ka_c[o]
        ka_done = sv.v
        ka_prev = ka_done


        for o in range(8):
            bank = psA if o % 2 == 0 else psB

            def vmm(e, o=o, bank=bank):
                for a in range(8):
                    inst = e.matmul(bank[:, :], w24[:, 2, a, o, :], ka3[:, a, :],
                                    start=(a == 0), stop=(a == 7))
                return inst
            vc = op("t", vmm, inc=st,
                    waits=[(sv, ka_done), (sWV, 16), (sa, rbank_free[o % 2]),
                           (sv, vb_free[o % 2])])

            op("v", lambda e, o=o, par=par, bank=bank: e.tensor_tensor(
                vtmp3p[:, par, o, :], rg3[:, o, :], bank[:, :], OP.mult),
                waits=[(st, vc)])
            vb_free[o % 2] = op("v", lambda e, o=o, par=par: e.tensor_tensor(
                vtmp3p[:, par, o, :], vtmp3p[:, par, o, :], x2c3[:, o, 1:513], OP.add))
        ffn_v = sv.v

        ln2_v = stats_and_apply(src3, src3, 0, [(sv, ffn_v)],
                                bc_extra=[(sv, ka_done)], pls=(psBC, psX))[1]
        out_dma[par] = op("s", lambda e, ck=ck, par=par: e.dma_start(
            out=out3[:, :, ck * 512:(ck + 1) * 512], in_=vtmp3p[:, par, :, :]),
            inc=sOUT, waits=[(sv, ln2_v)])
        out_dma_all = sOUT.v

    final_ss = out_dma_all
    op("s", lambda e: e.nop(), waits=[(sOUT, final_ss)])

    with nc.Block() as block:
        @block.tensor
        def _(e):
            for f in prog["t"]:
                f(e)

        @block.vector
        def _(e):
            for f in prog["v"]:
                f(e)

        @block.scalar
        def _(e):
            for f in prog["a"]:
                f(e)

        @block.gpsimd
        def _(e):
            for f in prog["g"]:
                f(e)

        @block.sync
        def _(e):
            for f in prog["s"]:
                f(e)

    return nc, st_ctx


def stage_inputs(T, inputs):
    x = np.asarray(inputs["L0_attn_x"], dtype=np.float32)
    w_qkv = np.asarray(inputs["L0_attn_w_qkv"], dtype=np.float32)
    w_out = np.asarray(inputs["L0_attn_w_out"], dtype=np.float32)
    head_scales = np.asarray(inputs["L0_attn_head_scales"], dtype=np.float32)
    W_r = np.asarray(inputs["L0_ffn_W_r"], dtype=np.float32)
    W_k = np.asarray(inputs["L0_ffn_W_k"], dtype=np.float32)
    W_v = np.asarray(inputs["L0_ffn_W_v"], dtype=np.float32)
    mu_r = np.asarray(inputs["L0_ffn_mu_r"], dtype=np.float32)
    mu_k = np.asarray(inputs["L0_ffn_mu_k"], dtype=np.float32)

    TQ = T // 4
    dirs = _dirs()

    def wpre(w):
        return np.ascontiguousarray(
            w.reshape(8, 128, 8, 128).transpose(1, 0, 2, 3).reshape(128, 8192))

    ident = np.eye(128, dtype=np.float32).astype(ml_dtypes.bfloat16)
    onesb = np.ones((128, 1), dtype=np.float32).astype(ml_dtypes.bfloat16)
    onesc = np.ones((128, 1), dtype=np.float32)
    onesr = np.ones((1, 128), dtype=np.float32)
    tri = np.where(np.arange(128)[:, None] <= np.arange(128)[None, :], 0.0, NEG).astype(np.float32)
    mur_s = np.ascontiguousarray(mu_r.reshape(8, 128).T)
    muk_s = np.ascontiguousarray(mu_k.reshape(8, 128).T)
    wout_p = wpre(w_out)
    wr_p = wpre(np.ascontiguousarray(W_r.T))
    wk_p = wpre(np.ascontiguousarray(W_k.T))
    wv_p = wpre(np.ascontiguousarray(W_v.T))

    NTl = T // 512
    NC2l = TQ // 512
    in_maps = []
    for c in range(8):
        b, r = divmod(c, 4)
        h0, h1 = 2 * r, 2 * r + 1
        xt_ = np.ascontiguousarray(x[b].T)
        xt_pre = np.ascontiguousarray(
            xt_.reshape(8, 128, NTl, 512).transpose(1, 2, 0, 3).reshape(128, 8 * T))
        xq_slice = xt_[:, r * TQ:(r + 1) * TQ]
        xqt_pre = np.ascontiguousarray(
            xq_slice.reshape(8, 128, NC2l, 512).transpose(1, 2, 0, 3).reshape(128, 8 * TQ))
        if r == 0:
            xql_ = np.zeros((128, 8), dtype=np.float32)
        else:
            xql_ = np.ascontiguousarray(
                xt_[:, r * TQ - 1].reshape(8, 128).T)
        wq = np.empty((128, 6, 8, 128), dtype=np.float32)
        for si, (stype, h) in enumerate([(0, h0), (0, h1), (1, h0), (1, h1), (2, h0), (2, h1)]):
            blk = w_qkv[:, stype * 1024 + h * 128: stype * 1024 + h * 128 + 128]
            wq[:, si, :, :] = blk.reshape(8, 128, 128).transpose(1, 0, 2)
        hs = head_scales[[h0, h1]].reshape(1, 2).astype(np.float32)
        d3c = np.empty((3, 6), dtype=np.float32)
        for u, h in enumerate((h0, h1)):
            d = dirs[h]
            d3c[:, 3 * u:3 * u + 3] = (np.outer(d, d) / SCL).astype(np.float32)
        in_maps.append({
            "xt": xt_pre,
            "xqt": xqt_pre,
            "xqlp": xql_,
            "wqkv": np.ascontiguousarray(wq.reshape(128, 6 * 8 * 128)),
            "wout": wout_p,
            "wr": wr_p,
            "wk": wk_p,
            "wv": wv_p,
            "mur": mur_s,
            "muk": muk_s,
            "hs2": hs,
            "d3": d3c,
            "identp": ident,
            "onesb": onesb,
            "onesbm": np.ones((128, 128), dtype=np.float32).astype(ml_dtypes.bfloat16),
            "onescm": np.ones((128, 128), dtype=np.float32),
            "onesc": onesc,
            "onesr": onesr,
            "trim": tri,
            "epsv": np.full((128, 1), EPS, dtype=np.float32),
            "gidx": ((4 * b + np.arange(4, dtype=np.int32))[:, None] * 128
                     + np.arange(128, dtype=np.int32)[None, :]).astype(np.int32),
        })
    return in_maps


_CACHE = {}


def kernel(**inputs):
    T = np.asarray(inputs["L0_attn_x"]).shape[1]
    if T not in _CACHE:
        _CACHE[T] = build_program(T)
    nc, _ctx = _CACHE[T]
    in_maps = stage_inputs(T, inputs)
    res = run_bass_kernel_spmd(nc, in_maps, core_ids=list(range(8)))
    TQ = T // 4
    full = np.empty((2, T, 1024), dtype=np.float32)
    for c in range(8):
        b, r = divmod(c, 4)
        full[b, r * TQ:(r + 1) * TQ, :] = res.results[c]["out"].T
    return full
